# version 06
# baseline (speedup 1.0000x reference)
import numpy as np
import ml_dtypes
import concourse.bass as bass
import concourse.mybir as mybir
from concourse.bass_utils import run_bass_kernel_spmd

F32 = mybir.dt.float32
BF16 = mybir.dt.bfloat16
I32 = mybir.dt.int32
AF = mybir.ActivationFunctionType
ALU = mybir.AluOpType
AX = mybir.AxisListType

T = 4096
D = 1024
NCORES = 8
EPS = 1e-5
BF = ml_dtypes.bfloat16
RHD = 0.49999997

WSPEC = [
    ("l0_wqT", (D, D)), ("l0_wkT", (D, D)), ("l0_wvT", (D, D)), ("l0_woT", (D, D)),
    ("l0_bq", (D,)), ("l0_bk", (D,)), ("l0_bv", (D,)), ("l0_bo", (128, 8)),
    ("l0_wgate", (2816, D)), ("l0_wup", (2816, D)), ("l0_wdown", (D, 2816)),
    ("l1_wqT", (D, D)), ("l1_wkT", (D, D)), ("l1_wvT", (D, D)), ("l1_wpT", (D, D)),
    ("l1_bqT", (D, 8)), ("l1_bkT", (D, 8)), ("l1_bvT", (D, 8)),
    ("l1_aqT", (8, D)), ("l1_akT", (8, D)), ("l1_avT", (8, D)),
    ("l1_w1T", (D, 4096)), ("l1_w2T", (D, 4096)),
    ("l1_h1T", (4096, 4096)), ("l1_h2T", (4096, 4096)), ("l1_w3T", (4096, D)),
    ("l1_s1", (128, 32)), ("l1_s2", (128, 32)), ("l1_h1s", (128, 32)), ("l1_h2s", (128, 32)),
    ("l1_s3", (128, 8)),
    ("l2_wqT", (D, D)), ("l2_wkT", (D, D)), ("l2_wvT", (D, D)),
    ("l2_wrT", (D, D)), ("l2_wkT2", (D, D)), ("l2_wvT2", (D, D)),
    ("l2_mur", (128, 8)), ("l2_muk", (128, 8)),
    ("l3_winT", (D, 4096)), ("l3_convw", (128, 16, 4)), ("l3_convb", (128, 16)),
    ("l3_wxcT", (2048, 16)), ("l3_blogT", (16, 2048)),
    ("l3_woutT", (2048, D)), ("l3_lng", (128, 8)), ("l3_lnb", (128, 8)),
    ("l3_wQT", (D, D)), ("l3_wKT", (D, D)), ("l3_wOT", (D, D)), ("l3_winvT", (D, D)),
    ("l3_vfwd", (D, 512)), ("l3_wfwd", (D, 512)), ("l3_ufwd", (512, D)), ("l3_bfwd", (8, D)),
    ("l3_vinv", (D, 512)), ("l3_winvexp", (D, 512)), ("l3_uinv", (512, D)), ("l3_binv", (8, D)),
    ("l3_rw1T", (D, D)), ("l3_rb1", (128, 8)), ("l3_rw2T", (D, 8)), ("l3_rb2e", (8,)),
    ("l3_alpha", (1,)),
]

def _blob_offsets():
    offs, cur = {}, 0
    for key, shape in WSPEC:
        n = int(np.prod(shape))
        offs[key] = (cur, shape)
        cur += ((n + 511) // 512) * 512
    return offs, cur

BLOB_OFFS, BLOB_N = _blob_offsets()

def _prep_blob(inp):
    t = lambda a: np.ascontiguousarray(np.asarray(a).T.astype(np.float32))
    c = lambda a: np.ascontiguousarray(np.asarray(a).astype(np.float32))
    def pcol(a):
        a = np.asarray(a).astype(np.float32).reshape(-1, 128)
        return np.ascontiguousarray(a.T)
    d = {}
    d["l0_wqT"] = t(inp["L0_attn_Wq"]); d["l0_wkT"] = t(inp["L0_attn_Wk"])
    d["l0_wvT"] = t(inp["L0_attn_Wv"]); d["l0_woT"] = t(inp["L0_attn_Wo"])
    d["l0_bq"] = c(inp["L0_attn_bq"]); d["l0_bk"] = c(inp["L0_attn_bk"])
    d["l0_bv"] = c(inp["L0_attn_bv"]); d["l0_bo"] = pcol(inp["L0_attn_bo"])
    d["l0_wgate"] = c(inp["L0_ffn_w_gate"]); d["l0_wup"] = c(inp["L0_ffn_w_up"])
    d["l0_wdown"] = c(inp["L0_ffn_w_down"])
    d["l1_wqT"] = t(inp["L1_attn_Wq"]); d["l1_wkT"] = t(inp["L1_attn_Wk"])
    d["l1_wvT"] = t(inp["L1_attn_Wv"]); d["l1_wpT"] = t(inp["L1_attn_Wp"])
    d["l1_bqT"] = t(inp["L1_attn_Bq"]); d["l1_bkT"] = t(inp["L1_attn_Bk"])
    d["l1_bvT"] = t(inp["L1_attn_Bv"])
    d["l1_aqT"] = t(inp["L1_attn_Aq"]); d["l1_akT"] = t(inp["L1_attn_Ak"])
    d["l1_avT"] = t(inp["L1_attn_Av"])
    d["l1_w1T"] = t(inp["L1_ffn_w1_w"]); d["l1_w2T"] = t(inp["L1_ffn_w2_w"])
    d["l1_h1T"] = t(inp["L1_ffn_h1_w"]); d["l1_h2T"] = t(inp["L1_ffn_h2_w"])
    d["l1_w3T"] = t(inp["L1_ffn_w3_w"])
    d["l1_s1"] = pcol(inp["L1_ffn_w1_s"]); d["l1_s2"] = pcol(inp["L1_ffn_w2_s"])
    d["l1_h1s"] = pcol(inp["L1_ffn_h1_s"]); d["l1_h2s"] = pcol(inp["L1_ffn_h2_s"])
    d["l1_s3"] = pcol(inp["L1_ffn_w3_s"])
    wqkv = np.asarray(inp["L2_attn_W_qkv"])
    idx = np.arange(16)[:, None] * 192 + np.arange(64)[None, :]
    d["l2_wqT"] = t(wqkv[(idx + 0).reshape(-1)])
    d["l2_wkT"] = t(wqkv[(idx + 64).reshape(-1)])
    d["l2_wvT"] = t(wqkv[(idx + 128).reshape(-1)])
    d["l2_wrT"] = t(inp["L2_ffn_W_r"]); d["l2_wkT2"] = t(inp["L2_ffn_W_k"])
    d["l2_wvT2"] = t(inp["L2_ffn_W_v"])
    d["l2_mur"] = pcol(inp["L2_ffn_mu_r"]); d["l2_muk"] = pcol(inp["L2_ffn_mu_k"])
    d["l3_winT"] = t(inp["L3_attn_W_in"])
    d["l3_convw"] = np.ascontiguousarray(np.asarray(inp["L3_attn_conv_w"]).astype(np.float32).reshape(16, 128, 4).transpose(1, 0, 2))
    d["l3_convb"] = pcol(inp["L3_attn_conv_b"])
    d["l3_wxcT"] = t(np.asarray(inp["L3_attn_W_x"])[16:32])
    d["l3_blogT"] = t(inp["L3_attn_B_log"])
    d["l3_woutT"] = t(inp["L3_attn_W_out"])
    d["l3_lng"] = pcol(inp["L3_attn_ln_g"]); d["l3_lnb"] = pcol(inp["L3_attn_ln_b"])
    d["l3_wQT"] = t(inp["L3_ffn_W_Q"]); d["l3_wKT"] = t(inp["L3_ffn_W_K"])
    d["l3_wOT"] = t(inp["L3_ffn_W_O"]); d["l3_winvT"] = t(inp["L3_ffn_W_inv"])
    flat_d_kr = lambda a: c(np.asarray(a).transpose(1, 0, 2).reshape(D, 512))
    flat_kr_d = lambda a: c(np.asarray(a).transpose(0, 2, 1).reshape(512, D))
    d["l3_vfwd"] = flat_d_kr(inp["L3_ffn_V_fwd"])
    d["l3_wfwd"] = flat_d_kr(inp["L3_ffn_W_fwd"])
    d["l3_ufwd"] = flat_kr_d(inp["L3_ffn_U_fwd"])
    d["l3_bfwd"] = c(inp["L3_ffn_b_fwd"])
    d["l3_vinv"] = flat_d_kr(inp["L3_ffn_W_inv_exp"])
    d["l3_winvexp"] = flat_d_kr(inp["L3_ffn_V_inv"])
    d["l3_uinv"] = flat_kr_d(inp["L3_ffn_U_inv"])
    d["l3_binv"] = c(inp["L3_ffn_b_inv"])
    d["l3_rw1T"] = t(inp["L3_ffn_router_w1"]); d["l3_rb1"] = pcol(inp["L3_ffn_router_b1"])
    d["l3_rw2T"] = t(inp["L3_ffn_router_w2"])
    d["l3_rb2e"] = c(np.asarray(inp["L3_ffn_router_b2"]) + np.asarray(inp["L3_ffn_expert_bias"]))
    d["l3_alpha"] = c(np.asarray(inp["L3_ffn_alpha_bi"]).reshape(1))
    blob = np.zeros(BLOB_N, np.float32)
    for key, (off, shape) in BLOB_OFFS.items():
        a = d[key]
        assert a.shape == tuple(shape), (key, a.shape, shape)
        blob[off:off + a.size] = a.reshape(-1)
    return blob

def _prep_consts():
    import math
    p = np.arange(128)
    ident = np.eye(128, dtype=np.float32)
    stair = (p[:, None] // 32 <= p[None, :] // 32).astype(np.float32)
    tri = (p[:, None] <= p[None, :]).astype(np.float32)
    blk = np.arange(T) // 32
    U = (np.arange(128)[:, None] > blk[None, :]).astype(np.float32)
    blockind = (p[:, None] // 32 == np.arange(4)[None, :]).astype(np.float32)
    start = 2.0 ** (-2.0 ** (-(math.log2(16) - 3)))
    slopes = np.array([start * start ** i for i in range(16)], np.float32)
    alibi = np.zeros((128, 512), np.float32)
    for kt in range(32):
        for h in range(16):
            alibi[:, kt * 16 + h] = -slopes[h] * (128 * kt + p)
    nvi = (1.0 / np.arange(1, T + 1, dtype=np.float64)).astype(np.float32)[None, :]
    kr = np.arange(512)
    selk = (kr[None, :] // 64 == np.arange(8)[:, None]).astype(np.float32)
    half0 = np.zeros((1, 128), np.float32); half0[0, 0:64] = 1.0
    half1 = np.zeros((1, 128), np.float32); half1[0, 64:128] = 1.0
    cbf = {
        "identb": ident, "stair": stair, "tri": tri, "U": U, "blockind": blockind,
        "onescol_b": np.ones((128, 1), np.float32), "selk": selk,
        "half0": half0, "half1": half1,
    }
    cf = {
        "identf": ident, "alibi": alibi, "nvi": nvi,
        "onesrow_f": np.ones((1, 128), np.float32),
        "eps1": np.full((1, 1), EPS, np.float32),
        "selrows": np.array([[-1.0, 0.0, 0.0, 1.0]], np.float32),
    }
    def pack(dd, dt):
        offs, cur = {}, 0
        for k, a in dd.items():
            offs[k] = (cur, a.shape)
            cur += ((a.size + 511) // 512) * 512
        blob = np.zeros(cur, np.float32)
        for k, (o, s) in offs.items():
            blob[o:o + int(np.prod(s))] = dd[k].reshape(-1)
        return blob.astype(dt), offs
    bblob, bo = pack(cbf, np.float32)
    fblob, fo = pack(cf, np.float32)
    return bblob, bo, fblob, fo

CB_BLOB, CB_OFFS, CF_BLOB, CF_OFFS = _prep_consts()

ENGINES = ["tensor", "vector", "scalar", "gpsimd", "sync"]

class _Op:
    __slots__ = ("eng", "fn", "deps", "idx", "need_inc", "inc_seq", "is_dma",
                 "narrow", "tag", "sem_idx")
    def __init__(self, eng, fn, is_dma, narrow, tag):
        self.eng = eng
        self.fn = fn
        self.deps = []
        self.idx = 0
        self.need_inc = is_dma
        self.inc_seq = 0
        self.is_dma = is_dma
        self.narrow = narrow
        self.tag = tag

class Sched:
    def __init__(self, nc):
        self.nc = nc
        self.ops = {e: [] for e in ENGINES}
        self.last_writer = {}
        self.readers = {}
        self.all = []

    def _record(self, eng, fn, reads, writes, is_dma, narrow, tag):
        op = _Op(eng, fn, is_dma, narrow, tag)
        op.idx = len(self.ops[eng])
        writes = list(writes) + [("__last__", eng)]
        deps = []
        for k in reads:
            w = self.last_writer.get(k)
            if w is not None:
                deps.append(w)
        for k in writes:
            w = self.last_writer.get(k)
            if w is not None:
                deps.append(w)
            deps.extend(self.readers.get(k, ()))
        op.deps = list({id(d): d for d in deps if d is not op}.values())
        for k in writes:
            self.last_writer[k] = op
            self.readers[k] = []
        for k in reads:
            self.readers.setdefault(k, []).append(op)
        self.ops[eng].append(op)
        self.all.append(op)
        return op

    def op(self, eng, fn, reads=(), writes=(), narrow=False, tag=""):
        return self._record(eng, fn, reads, writes, False, narrow, tag)

    def dma(self, eng, fn, reads=(), writes=(), tag=""):
        return self._record(eng, fn, reads, writes, True, False, tag)

    def fence(self):
        for e in ENGINES:
            self.op(e, lambda en: en.nop(nofuse=True),
                    reads=[("__last__", x) for x in ENGINES if x != e], tag="fence")

    def emit(self):
        nc = self.nc
        for op in self.all:
            for d in op.deps:
                d.need_inc = True
        NDSEM = 10
        cnt = {e: 0 for e in ENGINES}
        dcnt = {e: 0 for e in ENGINES}
        for e in ENGINES:
            for op in self.ops[e]:
                if op.is_dma:
                    n = dcnt[e]
                    dcnt[e] += 1
                    op.inc_seq = (n // NDSEM) + 1
                    op.sem_idx = n % NDSEM
                elif op.need_inc:
                    cnt[e] += 1
                    op.inc_seq = cnt[e]
        sems = {}
        dsems = {}
        for e in ENGINES:
            sems[e] = nc.semaphore(f"se_{e}").__enter__()
            dsems[e] = [nc.semaphore(f"sd_{e}{i}").__enter__() for i in range(NDSEM)]
        waited = {e: {} for e in ENGINES}

        def body(e_handle, ename):
            wd = waited[ename]
            for op in self.ops[ename]:
                wants = {}
                for d in op.deps:
                    if d.is_dma:
                        key = ("d", d.eng, d.sem_idx)
                        val = 16 * d.inc_seq
                    else:
                        key = ("c", d.eng)
                        val = d.inc_seq
                    if wants.get(key, 0) < val:
                        wants[key] = val
                if op.is_dma and op.inc_seq > 1:
                    key = ("d", ename, op.sem_idx)
                    val = 16 * (op.inc_seq - 1)
                    if wants.get(key, 0) < val:
                        wants[key] = val
                for key, val in sorted(wants.items()):
                    if wd.get(key, 0) >= val:
                        continue
                    sem = dsems[key[1]][key[2]] if key[0] == "d" else sems[key[1]]
                    e_handle.wait_ge(sem, val)
                    wd[key] = val
                inst = op.fn(e_handle)
                if op.need_inc:
                    if op.is_dma:
                        inst.then_inc(dsems[ename][op.sem_idx], 16)
                    else:
                        if inst is None:
                            inst = e_handle.nop(nofuse=True)
                        inst.then_inc(sems[ename], 1)

        with nc.Block("main") as blk:
            blk.tensor(lambda e: body(e, "tensor"))
            blk.vector(lambda e: body(e, "vector"))
            blk.scalar(lambda e: body(e, "scalar"))
            blk.gpsimd(lambda e: body(e, "gpsimd"))
            blk.sync(lambda e: body(e, "sync"))

class Ctx:
    pass

def _mk_ctx(nc):
    ctx = Ctx()
    ctx.nc = nc
    ctx.S = Sched(nc)
    ctx.xin = nc.dram_tensor("xin", [D, T], F32, kind="ExternalInput")
    ctx.wblob = nc.dram_tensor("wblob", [BLOB_N], F32, kind="ExternalInput")
    ctx.cbin = nc.dram_tensor("cbin", [CB_BLOB.size], F32, kind="ExternalInput")
    ctx.cfin = nc.dram_tensor("cfin", [CF_BLOB.size], F32, kind="ExternalInput")
    ctx.xout = nc.dram_tensor("xout", [D, T], F32, kind="ExternalOutput")
    ctx.xA = nc.dram_tensor("xA", [D, T], F32)
    ctx.xB = nc.dram_tensor("xB", [D, T], F32)
    ctx.ctxd = nc.dram_tensor("ctxd", [8, 8, 128, 512], BF16)
    ctx.hD = nc.dram_tensor("hD", [32, 8, 128, 512], BF16)
    ctx.g1D = nc.dram_tensor("g1D", [32, 8, 128, 512], BF16)
    ctx.g2D = nc.dram_tensor("g2D", [32, 8, 128, 512], BF16)

    ctx.live = []
    def keep(cm):
        v = cm.__enter__()
        ctx.live.append(cm)
        return v
    ctx.keep = keep

    ctx.identb = keep(nc.sbuf_tensor("identb", [128, 128], BF16))
    ctx.stair = keep(nc.sbuf_tensor("stair", [128, 128], BF16))
    ctx.tri = keep(nc.sbuf_tensor("tri", [128, 128], BF16))
    ctx.blockind = keep(nc.sbuf_tensor("blockind", [128, 4], BF16))
    ctx.selk = keep(nc.sbuf_tensor("selk", [8, 512], BF16))
    ctx.half0 = keep(nc.sbuf_tensor("half0", [1, 128], BF16))
    ctx.half1 = keep(nc.sbuf_tensor("half1", [1, 128], BF16))
    ctx.onescol_b = keep(nc.sbuf_tensor("onescol_b", [128, 1], BF16))
    ctx.identf = keep(nc.sbuf_tensor("identf", [128, 128], F32))
    ctx.alibi = keep(nc.sbuf_tensor("alibi", [128, 512], F32))
    ctx.onesrow_f = keep(nc.sbuf_tensor("onesrow_f", [1, 128], F32))
    ctx.selrows = keep(nc.sbuf_tensor("selrows", [1, 4], F32))
    ctx.onesrow_b = keep(nc.sbuf_tensor("onesrow_b", [1, 128], BF16))
    ctx.ones64_b = keep(nc.sbuf_tensor("ones64_b", [65, 128], BF16))
    ctx.eps1 = keep(nc.sbuf_tensor("eps1", [1, 1], F32))
    ctx.pA = keep(nc.psum_tensor("pA", [128, 512], F32))
    ctx.pB = keep(nc.psum_tensor("pB", [128, 512], F32))
    ctx.pC = keep(nc.psum_tensor("pC", [128, 512], F32))
    ctx.pD = keep(nc.psum_tensor("pD", [128, 512], F32))
    ctx.pE = keep(nc.psum_tensor("pE", [128, 512], F32))
    ctx.pF = keep(nc.psum_tensor("pF", [128, 512], F32))
    ctx.pS = keep(nc.psum_tensor("pS", [128, 512], F32))
    ctx.pTb = keep(nc.psum_tensor("pTb", [128, 1024], BF16))
    return ctx

def wap2(ctx, key):
    off, shape = BLOB_OFFS[key]
    n = int(np.prod(shape))
    a = ctx.wblob[off:off + n]
    if len(shape) == 2:
        a = a.rearrange("(r c) -> r c", c=shape[1])
    elif len(shape) == 3:
        a = a.rearrange("(r c k) -> r c k", c=shape[1], k=shape[2])
    return a

def _emit_preamble(ctx):
    S = ctx.S
    def ld(dst, src, offs, key):
        o, shape = offs[key]
        rows = shape[0] if len(shape) == 2 else 1
        cols = shape[1] if len(shape) == 2 else shape[0]
        ap = src[o:o + rows * cols].rearrange("(r c) -> r c", c=cols)
        q = "gpsimd" if dst.ap().dtype == BF16 else "sync"
        S.dma(q, lambda e, d=dst, a=ap, r=rows, cc=cols: e.dma_start(out=d[:r, :cc], in_=a),
              reads=(), writes=[("c", key)], tag=f"ld_{key}")
    ld(ctx.identb, ctx.cbin, CB_OFFS, "identb")
    ld(ctx.stair, ctx.cbin, CB_OFFS, "stair")
    ld(ctx.tri, ctx.cbin, CB_OFFS, "tri")
    ld(ctx.blockind, ctx.cbin, CB_OFFS, "blockind")
    ld(ctx.selk, ctx.cbin, CB_OFFS, "selk")
    ld(ctx.half0, ctx.cbin, CB_OFFS, "half0")
    ld(ctx.half1, ctx.cbin, CB_OFFS, "half1")
    ld(ctx.onescol_b, ctx.cbin, CB_OFFS, "onescol_b")
    ld(ctx.identf, ctx.cfin, CF_OFFS, "identf")
    ld(ctx.alibi, ctx.cfin, CF_OFFS, "alibi")
    ld(ctx.onesrow_f, ctx.cfin, CF_OFFS, "onesrow_f")
    ld(ctx.selrows, ctx.cfin, CF_OFFS, "selrows")
    ld(ctx.eps1, ctx.cfin, CF_OFFS, "eps1")
    S.op("vector", lambda e: e.tensor_copy(ctx.onesrow_b[:], ctx.onesrow_f[:]),
         reads=[("c", "onesrow_f")], writes=[("c", "onesrow_b")])
    S.op("vector", lambda e: e.memset(ctx.ones64_b[64:65, :], 1.0),
         writes=[("c", "ones64")])

def emit_stats(ctx, xsrc, pes, want_col=False, tag="", xb_dst=None):
    import contextlib
    nc, S = ctx.nc, ctx.S
    ctx.rowsNB = pes(nc.sbuf_tensor(f"rowsNB{tag}", [2, T], BF16))
    ctx.srep = pes(nc.sbuf_tensor(f"srep{tag}", [128, 8, 512], BF16))
    if want_col:
        ctx.rstd_col = pes(nc.sbuf_tensor(f"rstdc{tag}", [128, 32], F32))
    rowsNB_, srep_ = ctx.rowsNB, ctx.srep
    rstdcol_ = ctx.rstd_col if want_col else None
    with contextlib.ExitStack() as est:
        es = est.enter_context
        xsl = es(nc.sbuf_tensor(f"st_x{tag}", [128, 2, 8, 512], F32))
        x2t = es(nc.sbuf_tensor(f"st_x2{tag}", [128, 2, 8, 512], BF16))
        xbt = es(nc.sbuf_tensor(f"st_xb{tag}", [128, 2, 8, 512], BF16))
        var = es(nc.sbuf_tensor(f"st_var{tag}", [1, 512], F32))
        musq = es(nc.sbuf_tensor(f"st_musq{tag}", [1, 512], F32))
        sig = es(nc.sbuf_tensor(f"st_sig{tag}", [1, 512], F32))
        murow = es(nc.sbuf_tensor(f"st_mu{tag}", [1, 512], F32))
        rstd = es(nc.sbuf_tensor(f"st_rstd{tag}", [1, 512], F32))
        xr = xsrc.rearrange("(c p) t -> p c t", p=128)
        for s in range(8):
            b = s % 2
            sl = slice(s * 512, (s + 1) * 512)
            xbap = xbt[:, b] if xb_dst is None else xb_dst[:, :, sl]
            S.dma("sync", lambda e, b=b, sl=sl: e.dma_start(out=xsl[:, b], in_=xr[:, :, sl]),
                  reads=[("xslfree", b)], writes=[("xsl", b)], tag=f"st_ld{s}")
            S.op("vector", lambda e, b=b, xbap=xbap: e.tensor_copy(xbap, xsl[:, b]),
                 reads=[("xsl", b)], writes=[("stxb", b), ("xbw", s)], tag=f"st_cast{s}")
            S.op("scalar", lambda e, b=b: e.activation(
                    x2t[:, b].rearrange("p c t -> p (c t)"), xsl[:, b].rearrange("p c t -> p (c t)"),
                    AF.Square),
                 reads=[("xsl", b)], writes=[("x2t", b), ("xslfree", b)], tag=f"st_sq{s}")
            def mm_sum(e, src_ap, ps):
                r = None
                for ci in range(8):
                    r = e.matmul(ps[0:1, :], ctx.onescol_b[:], src_ap[:, ci], start=(ci == 0), stop=(ci == 7))
                return r
            S.op("tensor", lambda e, b=b, xbap=xbap: mm_sum(e, xbap, ctx.pA),
                 reads=[("stxb", b), ("c", "onescol_b")], writes=["pA"], tag=f"st_mmx{s}")
            S.op("tensor", lambda e, b=b: mm_sum(e, x2t[:, b], ctx.pB),
                 reads=[("x2t", b), ("c", "onescol_b")], writes=["pB"], tag=f"st_mmx2{s}")
            S.op("vector", lambda e: e.tensor_scalar(murow[:], ctx.pA[0:1, :], 1.0 / D, None, ALU.mult),
                 reads=["pA"], writes=["mu"], tag=f"st_mu{s}")
            S.op("scalar", lambda e: e.activation(musq[:], murow[:], AF.Square),
                 reads=["mu"], writes=["musq"], tag=f"st_musq{s}")
            S.op("vector", lambda e: e.scalar_tensor_tensor(var[:], ctx.pB[0:1, :], 1.0 / D, musq[:],
                                                            ALU.mult, ALU.subtract),
                 reads=["pB", "musq"], writes=["var"], tag=f"st_var{s}")
            S.op("scalar", lambda e: e.activation(sig[:], var[:], AF.Sqrt, bias=ctx.eps1[0:1, :]),
                 reads=["var", ("c", "eps1")], writes=["sig"], tag=f"st_sig{s}")
            S.op("vector", lambda e: e.reciprocal(rstd[:], sig[:]),
                 reads=["sig"], writes=["rstd"], tag=f"st_rstd{s}")
            def rowmm(e):
                e.matmul(ctx.pE[0:2, :], ctx.selrows[0:1, 0:2], murow[:], start=True, stop=False)
                return e.matmul(ctx.pE[0:2, :], ctx.selrows[0:1, 2:4], sig[:], start=False, stop=True)
            S.op("tensor", rowmm, reads=["mu", "sig", ("c", "selrows")], writes=["pE"],
                 tag=f"st_rowmm{s}")
            S.op("vector", lambda e, sl=sl: e.tensor_copy(rowsNB_[0:2, sl], ctx.pE[0:2, :]),
                 reads=["pE"], writes=[("rowN", s), ("rowS", s)], tag=f"st_rows{s}")
            S.op("tensor", lambda e: e.matmul(ctx.pS[:, :], ctx.onesrow_f[:],
                                              rstd[:], start=True, stop=True),
                 reads=["rstd", ("c", "onesrow_f")], writes=["pS"], tag=f"st_srepmm{s}")
            S.op("vector", lambda e, s=s: e.tensor_copy(srep_[:, s, :], ctx.pS[:, :]),
                 reads=["pS"], writes=[("srep", s)], tag=f"st_srepev{s}")
            if want_col:
                def tp(e):
                    r = None
                    for j in range(4):
                        r = e.transpose(ctx.pF[:, j:j + 1], rstd[:, j * 128:(j + 1) * 128],
                                        ctx.identf[0:1, 0:1])
                    return r
                S.op("tensor", tp, reads=["rstd", ("c", "identf")], writes=["pF"],
                     tag=f"st_coltp{s}")
                S.op("vector", lambda e, s=s: e.tensor_copy(rstdcol_[:, s * 4:s * 4 + 4],
                                                            ctx.pF[:, 0:4]),
                     reads=["pF"], writes=[("rstd_col", s)], narrow=True, tag=f"st_colev{s}")
        S.fence()

def emit_attn(ctx, L, xsrc, xdst):
    import contextlib
    nc, S = ctx.nc, ctx.S
    wq, wk, wv = [["l0_wqT", "l0_wkT", "l0_wvT"], ["l1_wqT", "l1_wkT", "l1_wvT"],
                  ["l2_wqT", "l2_wkT", "l2_wvT"]][L]
    has_bias = (L == 0)
    has_lora = (L == 1)
    xr = xsrc.rearrange("(c p) t -> p c t", p=128)

    with contextlib.ExitStack() as est:
        es = est.enter_context
        emit_stats(ctx, xsrc, es, want_col=True, tag=f"a{L}")
        rowsNB_, srep_ = ctx.rowsNB, ctx.srep
        rstdcol_ = getattr(ctx, 'rstd_col', None)
        K_sb = es(nc.sbuf_tensor(f"K_sb{L}", [128, 8, T], BF16))
        Vaug = es(nc.sbuf_tensor(f"Vaug{L}", [128, 32, 16, 66], BF16))
        Q_sb = es(nc.sbuf_tensor(f"Q_sb{L}", [128, 8, 512], BF16))
        xb = es(nc.sbuf_tensor(f"at_xb{L}", [128, 8, 512], BF16))
        wt = es(nc.sbuf_tensor(f"at_wt{L}", [128, 2, 8, 256], BF16))
        wcb = es(nc.sbuf_tensor(f"at_wcb{L}", [2, 3, 1024], BF16))
        if has_lora:
            wbq = es(nc.sbuf_tensor(f"at_wbq{L}", [128, 3, 8, 8], BF16))
            waq = es(nc.sbuf_tensor(f"at_waq{L}", [8, 3, 1024], BF16))
            wcbl = es(nc.sbuf_tensor(f"at_wcbl{L}", [2, 3, 8], BF16))
            mids = es(nc.sbuf_tensor(f"at_mids{L}", [8, 3, 512], BF16))
        P_sb = es(nc.sbuf_tensor(f"at_P{L}", [128, 8, 512], BF16))
        rec65 = es(nc.sbuf_tensor(f"at_rec65{L}", [65, 2, 512], BF16))
        rzrep = es(nc.sbuf_tensor(f"at_rzrep{L}", [128, 512], BF16))
        ctxn = es(nc.sbuf_tensor(f"at_ctxn{L}", [128, 2, 512], BF16))
        hsel = es(nc.sbuf_tensor(f"at_hsel{L}", [65, 2, 128], BF16))
        S.op("vector", lambda e: e.memset(hsel[64:65, :, :], 0.0), writes=["hselz"],
             narrow=True, tag="hselz")
        S.op("vector", lambda e: e.memset(hsel[64:65, 0, 0:64], 1.0),
             reads=["hselz"], writes=[("hsel", 0)], narrow=True, tag="hsel0")
        S.op("vector", lambda e: e.memset(hsel[64:65, 1, 64:128], 1.0),
             reads=["hselz"], writes=[("hsel", 1)], narrow=True, tag="hsel1")
        if L == 0:
            vblkT = es(nc.sbuf_tensor(f"at_vblkT{L}", [66, 128], BF16))
            vblk = es(nc.sbuf_tensor(f"at_vblk{L}", [128, 16, 66], BF16))
            Ucst = es(nc.sbuf_tensor(f"at_U{L}", [128, T], BF16))
            o_U = CB_OFFS["U"][0]
            S.dma("gpsimd", lambda e: e.dma_start(
                    out=Ucst[:, :], in_=ctx.cbin[o_U:o_U + 128 * T].rearrange("(r c) -> r c", c=T)),
                  writes=[("c", "U")], tag="ldU")

        S.op("vector", lambda e: e.memset(wcb[0:2, :, :], 0.0),
             writes=["wcbz"] + [("wcb_b", i) for i in range(3)], tag="zb")
        if has_bias:
            for i, bk in enumerate(["l0_bq", "l0_bk", "l0_bv"]):
                S.dma("gpsimd", lambda e, i=i, bk=bk: e.dma_start(
                        out=wcb[1:2, i, :], in_=wap2(ctx, bk).rearrange("(r c) -> r c", r=1)),
                      reads=["wcbz"], writes=[("wcb_b", i)], tag=f"ldb{i}")
        if has_lora:
            for i, (bk, ak) in enumerate([("l1_bqT", "l1_aqT"), ("l1_bkT", "l1_akT"),
                                          ("l1_bvT", "l1_avT")]):
                S.dma("gpsimd", lambda e, i=i, bk=bk: e.dma_start(
                        out=wbq[:, i], in_=wap2(ctx, bk).rearrange("(c p) o -> p c o", p=128)),
                      writes=[("wbq", i)], tag=f"ldlb{i}")
                S.dma("gpsimd", lambda e, i=i, ak=ak: e.dma_start(
                        out=waq[:, i, :], in_=wap2(ctx, ak)),
                      writes=[("waq", i)], tag=f"ldla{i}")
            S.op("vector", lambda e: e.memset(wcbl[0:2, :, :], 0.0), writes=["wcbl_b"], tag="zlb")
            for i in range(3):
                def cs(e, i=i):
                    r = None
                    for ci in range(8):
                        r = e.matmul(ctx.pS[0:1, 0:8], ctx.onescol_b[:], wbq[:, i, ci],
                                     start=(ci == 0), stop=(ci == 7))
                    return r
                S.op("tensor", cs, reads=[("wbq", i), ("c", "onescol_b")], writes=["pS"],
                     tag=f"lcs{i}")
                S.op("vector", lambda e, i=i: e.tensor_copy(wcbl[0:1, i, :], ctx.pS[0:1, 0:8]),
                     reads=["pS"], writes=[("wcbl", i)], narrow=True, tag=f"lcsev{i}")

        S.op("vector", lambda e: e.memset(Vaug[:, :, :, 64:65], 1.0), writes=["vones"], tag="vones")
        S.op("vector", lambda e: e.memset(Vaug[:, :, :, 65:66], 0.0), writes=["vzpad"], tag="vzpad")

        wkeys = [wq, wk, wv]
        def load_w(p, ob, buf, first):
            ap = wap2(ctx, wkeys[p]).rearrange("(c p) o -> p c o", p=128)[:, :, ob * 256:(ob + 1) * 256]
            S.dma("gpsimd", lambda e, ap=ap, buf=buf: e.dma_start(out=wt[:, buf], in_=ap),
                  writes=[("wt", buf)], tag=f"ldw{p}{ob}")
            if first:
                def cs(e, buf=buf):
                    r = None
                    for ci in range(8):
                        r = e.matmul(ctx.pS[0:1, 0:256], ctx.onescol_b[:], wt[:, buf, ci],
                                     start=(ci == 0), stop=(ci == 7))
                    return r
                S.op("tensor", cs, reads=[("wt", buf), ("c", "onescol_b")], writes=["pS"],
                     tag=f"cs{p}{ob}")
                S.op("vector", lambda e, p=p, ob=ob: e.tensor_copy(
                        wcb[0:1, p, ob * 256:(ob + 1) * 256], ctx.pS[0:1, 0:256]),
                     reads=["pS", "wcbz"], writes=[("wcb", p, ob)], tag=f"csev{p}{ob}")

        def load_xb(sg):
            S.dma("gpsimd", lambda e, sg=sg: e.dma_start(
                    out=xb[:, :, :], in_=xr[:, :, sg * 512:(sg + 1) * 512]),
                  reads=[("xb",)], writes=[("xb",)], tag=f"ldxb{sg}")

        def lora_mids(sg):
            for i in range(3):
                def mmid(e, i=i, sg=sg):
                    for ci in range(8):
                        e.matmul(ctx.pS[0:8, :], wbq[:, i, ci], xb[:, ci, :],
                                 start=(ci == 0), stop=False)
                    return e.matmul(ctx.pS[0:8, :], wcbl[:, i, :],
                                    rowsNB_[:, sg * 512:(sg + 1) * 512], start=False, stop=True)
                S.op("tensor", mmid, reads=[("xb",), ("wbq", i), ("wcbl", i), "wcbl_b",
                                            ("rowN", sg), ("rowS", sg)],
                     writes=["pS"], tag=f"lmid{i}_{sg}")
                S.op("vector", lambda e, i=i: e.tensor_scalar(mids[0:8, i, :], ctx.pS[0:8, :],
                                                              2.0, None, ALU.mult),
                     reads=["pS"], writes=[("mids", i)], tag=f"lmidev{i}_{sg}")

        PK = [("pmA", ctx.pA), ("pmB", ctx.pB), ("pmE", ctx.pE), ("pmF", ctx.pF)]
        pidx = [0]
        def next_psum():
            k, p = PK[pidx[0] % 4]
            pidx[0] += 1
            return k, p

        def proj_fm(p, ob, buf, oc, sg, psum, pkey):
            def mm(e, p=p, ob=ob, buf=buf, oc=oc, sg=sg, psum=psum):
                for ci in range(8):
                    e.matmul(psum[:, :], wt[:, buf, ci, oc * 128:(oc + 1) * 128], xb[:, ci, :],
                             start=(ci == 0), stop=False)
                och = ob * 2 + oc
                r = e.matmul(psum[:, :], wcb[:, p, och * 128:(och + 1) * 128],
                             rowsNB_[:, sg * 512:(sg + 1) * 512], start=False, stop=not has_lora)
                if has_lora:
                    r = e.matmul(psum[:, :], waq[:, p, och * 128:(och + 1) * 128],
                                 mids[0:8, p, :], start=False, stop=True)
                return r
            reads = [("xb",), ("wt", buf), ("wcb", p, ob), ("wcb_b", p),
                     ("rowN", sg), ("rowS", sg)]
            if has_lora:
                reads += [("mids", p), ("waq", p)]
            S.op("tensor", mm, reads=reads, writes=[pkey], tag=f"prj{p}_{ob}{oc}_{sg}")

        def evac_K(och, sg, psum, pkey):
            S.op("vector", lambda e, och=och, sg=sg, psum=psum: e.tensor_tensor(
                    K_sb[:, och, sg * 512:(sg + 1) * 512], psum[:, :], srep_[:, sg, :], ALU.mult),
                 reads=[pkey, ("srep", sg)], writes=[("K", sg)], tag=f"evK{och}_{sg}")

        def evac_Q(och, sg, psum, pkey):
            S.op("vector", lambda e, och=och, sg=sg, psum=psum: e.tensor_tensor(
                    Q_sb[:, och, :], psum[:, :], srep_[:, sg, :], ALU.mult),
                 reads=[pkey, ("srep", sg)], writes=[("Q", sg)], tag=f"evQ{och}_{sg}")

        def proj_v_tm(ob, buf, tt, sg, psum, pkey):
            def mm(e, ob=ob, buf=buf, tt=tt, psum=psum):
                lt = tt % 4
                for ci in range(8):
                    e.matmul(psum[:, 0:256], xb[:, ci, lt * 128:(lt + 1) * 128], wt[:, buf, ci, :],
                             start=(ci == 0), stop=False)
                r = e.matmul(psum[:, 0:256], rowsNB_[:, tt * 128:(tt + 1) * 128],
                             wcb[:, 2, ob * 256:(ob + 1) * 256], start=False, stop=not has_lora)
                if has_lora:
                    r = e.matmul(psum[:, 0:256], mids[0:8, 2, (tt % 4) * 128:(tt % 4 + 1) * 128],
                                 waq[:, 2, ob * 256:(ob + 1) * 256], start=False, stop=True)
                return r
            reads = [("xb",), ("wt", buf), ("rowN", sg), ("rowS", sg),
                     ("wcb", 2, ob), ("wcb_b", 2)]
            if has_lora:
                reads += [("mids", 2), ("waq", 2)]
            S.op("tensor", mm, reads=reads, writes=[pkey], tag=f"pv{ob}_{tt}")

        def evac_V(ob, tt, psum, pkey):
            out = Vaug[:, tt, ob * 4:(ob + 1) * 4, 0:64]
            S.op("vector", lambda e, psum=psum, out=out, tt=tt: e.tensor_scalar(
                    out, psum[:, 0:256].rearrange("p (h d) -> p h d", d=64),
                    rstdcol_[:, tt:tt + 1], None, ALU.mult),
                 reads=[pkey, ("rstd_col", tt // 4)], writes=[("V", tt // 4)], tag=f"evV{ob}_{tt}")

        wbuf = [0]
        for sg in range(8):
            load_xb(sg)
            if has_lora:
                lora_mids(sg)
            for p in (1, 2):
                for ob in range(4):
                    buf = wbuf[0] % 2
                    wbuf[0] += 1
                    load_w(p, ob, buf, first=(sg == 0))
                    if p == 1:
                        for oc in range(2):
                            pk, ps = next_psum()
                            proj_fm(p, ob, buf, oc, sg, ps, pk)
                            evac_K(ob * 2 + oc, sg, ps, pk)
                    else:
                        for lt in range(4):
                            tt = sg * 4 + lt
                            pk, ps = next_psum()
                            proj_v_tm(ob, buf, tt, sg, ps, pk)
                            evac_V(ob, tt, ps, pk)
        if L == 0:
            for h in range(16):
                def vb(e, h=h):
                    r = None
                    for kt in range(32):
                        r = e.matmul(ctx.pS[0:66, 4 * kt:4 * kt + 4], Vaug[:, kt, h, :],
                                     ctx.blockind[:, :], start=True, stop=True)
                    return r
                S.op("tensor", vb, reads=[("V", 0), ("V", 1), ("V", 2), ("V", 3),
                                          "vones", ("c", "blockind")],
                     writes=["pS"], tag=f"vblk{h}")
                S.op("vector", lambda e: e.tensor_copy(vblkT[:, :], ctx.pS[0:66, 0:128]),
                     reads=["pS"], writes=["vblkT"], tag=f"vblkev{h}")
                S.op("tensor", lambda e: e.transpose(ctx.pTb[0:128, 0:66], vblkT[:, :],
                                                     ctx.identb[0:66, 0:66]),
                     reads=["vblkT", ("c", "identb")], writes=["pTb"], tag=f"vblktp{h}")
                S.op("vector", lambda e, h=h: e.tensor_copy(vblk[:, h, :], ctx.pTb[0:128, 0:66]),
                     reads=["pTb"], writes=[("vblk", h)], tag=f"vblktev{h}")

        for qg in range(8):
            load_xb(qg)
            if has_lora:
                lora_mids(qg)
            for ob in range(4):
                buf = wbuf[0] % 2
                wbuf[0] += 1
                load_w(0, ob, buf, first=(qg == 0))
                for oc in range(2):
                    pk, ps = next_psum()
                    proj_fm(0, ob, buf, oc, qg, ps, pk)
                    evac_Q(ob * 2 + oc, qg, ps, pk)
            ktmax = (4 * qg + 3) if L in (0, 1) else 31
            nkt = ktmax + 1
            for hp in range(8):
                iters = [(hh, kt) for hh in range(2) for kt in range(nkt)]
                pmeta = {}

                def emit_sc(i, hp=hp, qg=qg, iters=iters):
                    hh, kt = iters[i]
                    h = 2 * hp + hh
                    hr = slice((h % 2) * 64, (h % 2) * 64 + 64)
                    hc = h // 2
                    q0 = max(0, 128 * kt - 512 * qg) if L in (0, 1) else 0
                    sl = slice(q0, 512)
                    spk, sp = next_psum()
                    pslot = i % 8
                    pmeta[i] = (sl, pslot)
                    S.op("tensor", lambda e, kt=kt, hr=hr, hc=hc, sl=sl, sp=sp: e.matmul(
                            sp[:, sl], K_sb[hr, hc, kt * 128:(kt + 1) * 128],
                            Q_sb[hr, hc, sl], start=True, stop=True),
                         reads=[("K", kt // 4), ("Q", qg)], writes=[spk],
                         tag=f"sc{h}_{qg}_{kt}")
                    if L == 1:
                        bias = ctx.alibi[:, kt * 16 + h: kt * 16 + h + 1]
                        S.op("scalar", lambda e, pslot=pslot, sl=sl, sp=sp, bias=bias:
                                e.activation(P_sb[:, pslot, sl], sp[:, sl], AF.Exp,
                                             bias=bias, scale=0.125),
                             reads=[spk, ("c", "alibi")], writes=[("P", pslot)],
                             tag=f"exp{h}_{qg}_{kt}")
                    else:
                        S.op("scalar", lambda e, pslot=pslot, sl=sl, sp=sp: e.activation(
                                P_sb[:, pslot, sl], sp[:, sl], AF.Exp, scale=0.125),
                             reads=[spk], writes=[("P", pslot)], tag=f"exp{h}_{qg}_{kt}")
                    if L in (0, 1) and kt >= 4 * qg:
                        j = kt - 4 * qg
                        msk = ctx.stair if L == 0 else ctx.tri
                        mkey = "stair" if L == 0 else "tri"
                        S.op("vector", lambda e, pslot=pslot, j=j, msk=msk: e.tensor_tensor(
                                P_sb[:, pslot, j * 128:(j + 1) * 128],
                                P_sb[:, pslot, j * 128:(j + 1) * 128], msk[:, :], ALU.mult),
                             reads=[("P", pslot), ("c", mkey)], writes=[("P", pslot)],
                             tag=f"msk{h}_{qg}_{kt}")

                def emit_cx(i, first, last, gslots, hp=hp, qg=qg, iters=iters):
                    hh, kt = iters[i]
                    h = 2 * hp + hh
                    cp = ctx.pC if hh == 0 else ctx.pD
                    cpk = "pC" if hh == 0 else "pD"
                    sl, pslot = pmeta[i]
                    if L == 0 and kt == 0:
                        S.op("tensor", lambda e, h=h, qg=qg, cp=cp: e.matmul(
                                cp[0:66, :], vblk[:, h, :], Ucst[:, qg * 512:(qg + 1) * 512],
                                start=True, stop=False),
                             reads=[("vblk", h), ("c", "U")], writes=[cpk], tag=f"sfx{h}_{qg}")
                    st = (L != 0) and kt == 0
                    preads = [("P", s) for s in gslots] if (first or last) else []
                    S.op("tensor", lambda e, kt=kt, h=h, sl=sl, pslot=pslot, cp=cp,
                                 st=st, fin=(kt == ktmax): e.matmul(
                            cp[0:66, sl], Vaug[:, kt, h, :], P_sb[:, pslot, sl],
                            start=st, stop=fin),
                         reads=preads + [("V", kt // 4), "vones", "vzpad"], writes=[cpk],
                         tag=f"cx{h}_{qg}_{kt}")
                    if kt == ktmax:
                        def recf(e, cp=cp, hh=hh):
                            with nc.allow_low_precision(reason="bf16 softmax denom recip"):
                                return e.reciprocal(rec65[:, hh, :], cp[0:65, :])
                        S.op("vector", recf, reads=[cpk], writes=[("rec", hh)],
                             tag=f"rz{h}_{qg}")

                GS = 4
                groups = [list(range(j, min(j + GS, len(iters))))
                          for j in range(0, len(iters), GS)]
                for gi in range(len(groups) + 1):
                    if gi < len(groups):
                        for i in groups[gi]:
                            emit_sc(i)
                    if gi >= 1:
                        grp = groups[gi - 1]
                        gslots = [j % 8 for j in grp]
                        for j, i in enumerate(grp):
                            emit_cx(i, j == 0, j == len(grp) - 1, gslots)
                def bcast(e):
                    e.matmul(ctx.pS[:, :], hsel[64:65, 0, :], rec65[64:65, 0, :],
                             start=True, stop=False)
                    return e.matmul(ctx.pS[:, :], hsel[64:65, 1, :], rec65[64:65, 1, :],
                                    start=False, stop=True)
                S.op("tensor", bcast, reads=[("rec", 0), ("rec", 1), ("hsel", 0),
                                             ("hsel", 1)], writes=["pS"],
                     tag=f"bc{hp}_{qg}")
                S.op("scalar", lambda e: e.copy(rzrep[:, :], ctx.pS[:, :]),
                     reads=["pS"], writes=["rzrep"], tag=f"rzc{hp}_{qg}")
                b = qg % 2
                S.op("vector", lambda e, b=b: e.tensor_tensor(
                        ctxn[0:64, b, :], ctx.pC[0:64, :], rzrep[0:64, :], ALU.mult),
                     reads=["pC", "rzrep", ("ctxnf", b)], writes=[("ctxn", b, 0)],
                     tag=f"cev0_{hp}_{qg}")
                S.op("vector", lambda e, b=b: e.tensor_tensor(
                        ctxn[64:128, b, :], ctx.pD[0:64, :], rzrep[64:128, :], ALU.mult),
                     reads=["pD", "rzrep", ("ctxnf", b)], writes=[("ctxn", b, 1)],
                     tag=f"cev1_{hp}_{qg}")
                S.dma("sync", lambda e, hp=hp, qg=qg, b=b: e.dma_start(
                        out=ctx.ctxd[hp, qg], in_=ctxn[:, b, :]),
                      reads=[("ctxn", b, 0), ("ctxn", b, 1)],
                      writes=[("ctxd", hp, qg), ("ctxnf", b)],
                      tag=f"stctx{hp}_{qg}")
        S.fence()

    with contextlib.ExitStack() as est:
        es = est.enter_context
        has_wo = (L in (0, 1))
        if has_wo:
            wokey = "l0_woT" if L == 0 else "l1_wpT"
            wo_sb = es(nc.sbuf_tensor(f"at_wo{L}", [128, 8, 1024], BF16))
            S.dma("gpsimd", lambda e: e.dma_start(out=wo_sb[:, :, :],
                    in_=wap2(ctx, wokey).rearrange("(c p) o -> p c o", p=128)),
                  writes=["wo"], tag="ldwo")
            if L == 0:
                bias_f = es(nc.sbuf_tensor(f"at_biasf{L}", [128, 8], F32))
                S.dma("sync", lambda e: e.dma_start(out=bias_f[:, :], in_=wap2(ctx, "l0_bo")),
                      writes=["bo_f"], tag="ldbo")
        cstg = es(nc.sbuf_tensor(f"at_cstg{L}", [128, 2, 8, 512], BF16))
        xn2 = es(nc.sbuf_tensor(f"at_xn2{L}", [128, 2, 8, 512], F32))
        for sg in range(8):
            b = sg % 2
            S.dma("sync", lambda e, sg=sg, b=b: e.dma_start(
                    out=cstg[:, b], in_=ctx.ctxd[:, sg].rearrange("c p f -> p c f")),
                  reads=[("cfree", b)] + [("ctxd", hp, sg) for hp in range(8)],
                  writes=[("cstg", b)], tag=f"ldc{sg}")
            S.dma("sync", lambda e, sg=sg, b=b: e.dma_start(
                    out=xn2[:, b], in_=xr[:, :, sg * 512:(sg + 1) * 512]),
                  reads=[("xn2w", b)], writes=[("xn2", b)], tag=f"ldx2{sg}")
            if has_wo:
                for oc in range(8):
                    pk, ps = (("pmA", ctx.pA) if oc % 2 == 0 else ("pmB", ctx.pB))
                    def mm(e, oc=oc, b=b, ps=ps):
                        r = None
                        for ci in range(8):
                            r = e.matmul(ps[:, :], wo_sb[:, ci, oc * 128:(oc + 1) * 128],
                                         cstg[:, b, ci, :], start=(ci == 0), stop=(ci == 7))
                        return r
                    S.op("tensor", mm, reads=[("cstg", b), "wo"], writes=[pk], tag=f"wo{sg}_{oc}")
                    if L == 0:
                        S.op("vector", lambda e, oc=oc, b=b, ps=ps: e.scalar_tensor_tensor(
                                xn2[:, b, oc, :], ps[:, :], bias_f[:, oc:oc + 1], xn2[:, b, oc, :],
                                ALU.add, ALU.add),
                             reads=[pk, ("xn2", b), "bo_f"], writes=[("xn2", b), ("xn2w", b)],
                             tag=f"woev{sg}_{oc}")
                    else:
                        S.op("vector", lambda e, oc=oc, b=b, ps=ps: e.tensor_add(
                                xn2[:, b, oc, :], ps[:, :], xn2[:, b, oc, :]),
                             reads=[pk, ("xn2", b)], writes=[("xn2", b), ("xn2w", b)],
                             tag=f"woev{sg}_{oc}")
            else:
                S.op("vector", lambda e, b=b: e.tensor_add(
                        xn2[:, b].rearrange("p c t -> p (c t)"),
                        xn2[:, b].rearrange("p c t -> p (c t)"),
                        cstg[:, b].rearrange("p c t -> p (c t)")),
                     reads=[("cstg", b), ("xn2", b)], writes=[("xn2", b), ("xn2w", b)],
                     tag=f"resadd{sg}")
            S.dma("sync", lambda e, sg=sg, b=b: e.dma_start(
                    out=xdst.rearrange("(c p) t -> p c t", p=128)[:, :, sg * 512:(sg + 1) * 512],
                    in_=xn2[:, b]),
                  reads=[("xn2", b)], writes=[("xout", sg), ("xn2w", b), ("cfree", b)],
                  tag=f"stx{sg}")
        S.fence()

def emit_quant(ctx, pref, dst, dkey, src, skey, n, tmp):
    nc, S = ctx.nc, ctx.S
    A, B, C, sgn, am, sc, rs, scb = tmp
    nb = n // 16
    r3 = lambda ap: ap.rearrange("p (b k) -> p b k", k=16)
    PH = lambda nm: ("qph", nm)

    def V(fn, reads, writes, tag):
        def wrapped(e, fn=fn):
            with nc.allow_low_precision(reason="fp4 quantizer ladder in bf16"):
                return fn(e)
        S.op("vector", wrapped, reads=reads, writes=writes, tag=tag)

    V(lambda e: e.reduce_max(am[:, 0:nb], r3(src), AX.X, apply_absolute_value=True),
      [skey], [(pref, "am"), PH("am")], f"{pref}_amax")
    V(lambda e: e.tensor_scalar(sc[:, 0:nb], am[:, 0:nb], 1e-12, 1.0 / 6.0,
                                ALU.max, ALU.mult),
      [(pref, "am")], [(pref, "sc"), PH("sc")], f"{pref}_sc")
    V(lambda e: e.tensor_copy(scb[:, 0:nb], sc[:, 0:nb]),
      [(pref, "sc")], [(pref, "scb"), PH("scb")], f"{pref}_scb")
    V(lambda e: e.reciprocal(rs[:, 0:nb], sc[:, 0:nb]),
      [(pref, "sc")], [(pref, "rs"), PH("rs")], f"{pref}_rs")
    V(lambda e: e.tensor_scalar(rs[:, 0:nb], rs[:, 0:nb], 2.0, None, ALU.mult),
      [(pref, "rs")], [(pref, "rs"), PH("rs")], f"{pref}_rs2")
    V(lambda e: e.tensor_tensor(r3(A[:, 0:n]), r3(src),
            rs[:, 0:nb][:, :, None].broadcast_to([128, nb, 16]), ALU.mult),
      [skey, (pref, "rs")], [(pref, "A"), PH("A")], f"{pref}_t")
    S.op("scalar", lambda e: e.activation(sgn[:, 0:n], A[:, 0:n], AF.Sign),
         reads=[(pref, "A")], writes=[(pref, "sgn"), PH("sgn")], tag=f"{pref}_sgn")
    S.op("scalar", lambda e: e.activation(B[:, 0:n], A[:, 0:n], AF.Abs),
         reads=[(pref, "A")], writes=[(pref, "B"), PH("B")], tag=f"{pref}_abs")
    V(lambda e: e.tensor_scalar(C[:, 0:n], B[:, 0:n], 0.5, None, ALU.is_gt),
      [(pref, "B")], [(pref, "C"), PH("C")], f"{pref}_l0")
    for i, th in enumerate([1.5, 2.5, 3.5]):
        V(lambda e, th=th: e.scalar_tensor_tensor(
                C[:, 0:n], B[:, 0:n], th, C[:, 0:n], ALU.is_gt, ALU.add),
          [(pref, "B"), (pref, "C")], [(pref, "C"), PH("C")], f"{pref}_l{i + 1}")
    V(lambda e: e.tensor_scalar(C[:, 0:n], C[:, 0:n], 0.5, None, ALU.mult),
      [(pref, "C")], [(pref, "C"), PH("C")], f"{pref}_half")
    for i, th in enumerate([5.0, 7.0]):
        V(lambda e, th=th: e.scalar_tensor_tensor(
                C[:, 0:n], B[:, 0:n], th, C[:, 0:n], ALU.is_gt, ALU.add),
          [(pref, "B"), (pref, "C")], [(pref, "C"), PH("C")], f"{pref}_h{i}")
    V(lambda e: e.tensor_scalar(A[:, 0:n], B[:, 0:n], 10.0, 2.0, ALU.is_gt, ALU.mult),
      [(pref, "B")], [(pref, "A"), PH("A")], f"{pref}_c2")
    V(lambda e: e.tensor_add(C[:, 0:n], C[:, 0:n], A[:, 0:n]),
      [(pref, "C"), (pref, "A")], [(pref, "C"), PH("C")], f"{pref}_q")
    V(lambda e: e.tensor_tensor(C[:, 0:n], C[:, 0:n], sgn[:, 0:n], ALU.mult),
      [(pref, "C"), (pref, "sgn")], [(pref, "C"), PH("C")], f"{pref}_qs")
    V(lambda e: e.tensor_tensor(r3(dst), r3(C[:, 0:n]),
            scb[:, 0:nb][:, :, None].broadcast_to([128, nb, 16]), ALU.mult),
      [(pref, "C"), (pref, "scb")], [dkey], f"{pref}_out")

def emit_l0ffn(ctx, xsrc, xdst):
    import contextlib
    nc, S = ctx.nc, ctx.S
    xr = xsrc.rearrange("(c p) t -> p c t", p=128)
    with contextlib.ExitStack() as est:
        es = est.enter_context
        wg = es(nc.sbuf_tensor("f0_wg", [128, 8, 2816], BF16))
        wu = es(nc.sbuf_tensor("f0_wu", [128, 8, 2816], BF16))
        wd = es(nc.sbuf_tensor("f0_wd", [128, 22, 1024], BF16))
        qA = es(nc.sbuf_tensor("f0_qA", [128, 1024], BF16))
        qB = es(nc.sbuf_tensor("f0_qB", [128, 1024], BF16))
        qC = es(nc.sbuf_tensor("f0_qC", [128, 1024], BF16))
        qS = es(nc.sbuf_tensor("f0_qS", [128, 1024], BF16))
        qm = es(nc.sbuf_tensor("f0_qm", [128, 64], F32))
        qsc = es(nc.sbuf_tensor("f0_qsc", [128, 64], F32))
        qrs = es(nc.sbuf_tensor("f0_qrs", [128, 64], F32))
        qscb = es(nc.sbuf_tensor("f0_qscb", [128, 64], BF16))
        qtmp = (qA, qB, qC, qS, qm, qsc, qrs, qscb)

        import contextlib as _cl
        with _cl.ExitStack() as est2:
            wstg = est2.enter_context(nc.sbuf_tensor("f0_wstg", [128, 2816], F32))
            wq16 = est2.enter_context(nc.sbuf_tensor("f0_wq16", [128, 2816], BF16))
            def quant_weight(key, rows, cols, dst, tagp):
                nt = rows // 128
                wap = wap2(ctx, key)
                pieces = [(0, 1024), (1024, 1024), (2048, 768)] if cols == 2816 else [(0, 1024)]
                for r in range(nt):
                    S.dma("sync", lambda e, r=r, cols=cols: e.dma_start(
                            out=wstg[:, 0:cols], in_=wap.rearrange("(n p) c -> n p c", p=128)[r]),
                          reads=["f0wstg"], writes=["f0wstg"], tag=f"{tagp}ld{r}")
                    for pi_, (po, pn) in enumerate(pieces):
                        emit_quant(ctx, f"{tagp}q{r}_{pi_}", wq16[:, po:po + pn], "f0wq",
                                   wstg[:, po:po + pn], "f0wstg", pn, qtmp)
                    nch = cols // 128
                    for g in range(0, nch, 8):
                        gn = min(8, nch - g)
                        def tp(e, g=g, gn=gn):
                            rr = None
                            for j in range(gn):
                                rr = e.transpose(ctx.pTb[:, j * 128:(j + 1) * 128],
                                                 wq16[:, (g + j) * 128:(g + j + 1) * 128],
                                                 ctx.identb[:, :])
                            return rr
                        S.op("tensor", tp, reads=["f0wq", ("c", "identb")], writes=["pTb"],
                             tag=f"{tagp}tp{r}_{g}")
                        def ev(e, r=r, g=g, gn=gn):
                            return e.tensor_copy(
                                dst[:, g:g + gn, r * 128:(r + 1) * 128],
                                ctx.pTb[:, 0:gn * 128].rearrange("p (j q) -> p j q", q=128))
                        S.op("vector", ev, reads=["pTb"], writes=[(tagp, "dst"), "f0wq"],
                             tag=f"{tagp}ev{r}_{g}")
            quant_weight("l0_wgate", 2816, 1024, wg, "qg")
            quant_weight("l0_wup", 2816, 1024, wu, "qu")
            quant_weight("l0_wdown", 1024, 2816, wd, "qd")
            S.fence()

        xsl = es(nc.sbuf_tensor("f0_xsl", [128, 8, 512], F32))
        xtm = es(nc.sbuf_tensor("f0_xtm", [128, 1024], F32))
        xq = es(nc.sbuf_tensor("f0_xq", [128, 2, 1024], BF16))
        xqf = es(nc.sbuf_tensor("f0_xqf", [128, 2, 8, 128], BF16))
        hbuf = es(nc.sbuf_tensor("f0_h", [128, 512], F32))
        hq = es(nc.sbuf_tensor("f0_hq", [128, 2, 512], BF16))
        hqf = es(nc.sbuf_tensor("f0_hqf", [128, 22, 256], BF16))
        gact = es(nc.sbuf_tensor("f0_gact", [128, 2, 512], BF16))
        sq1 = es(nc.sbuf_tensor("f0_sq1", [128, 1024], BF16))
        mu = es(nc.sbuf_tensor("f0_mu", [128, 1], F32))
        musq = es(nc.sbuf_tensor("f0_musq", [128, 1], F32))
        ssum = es(nc.sbuf_tensor("f0_ssum", [128, 1], F32))
        ssq = es(nc.sbuf_tensor("f0_ssq", [128, 1], F32))
        rstd = es(nc.sbuf_tensor("f0_rstd", [128, 1], F32))

        for sg in range(8):
            S.dma("sync", lambda e, sg=sg: e.dma_start(
                    out=xsl[:, :, :], in_=xr[:, :, sg * 512:(sg + 1) * 512]),
                  reads=[("xsl",)], writes=[("xsl",)], tag=f"f0ldx{sg}")
            for half in range(2):
                for l2 in range(2):
                    lt = half * 2 + l2
                    tcol = lt * 128
                    for g in range(2):
                        def tpx(e, g=g, tcol=tcol):
                            rr = None
                            for j in range(4):
                                ci = g * 4 + j
                                rr = e.transpose(ctx.pE[:, j * 128:(j + 1) * 128],
                                                 xsl[:, ci, tcol:tcol + 128],
                                                 ctx.identf[:, :])
                            return rr
                        S.op("tensor", tpx, reads=[("xsl",), ("c", "identf")], writes=["pT32"],
                             tag=f"f0tx{sg}_{lt}_{g}")
                        S.op("vector", lambda e, g=g: e.tensor_copy(
                                xtm[:, g * 512:(g + 1) * 512], ctx.pE[:, :]),
                             reads=["pT32"], writes=[("xtm", g)], tag=f"f0txe{sg}_{lt}_{g}")
                    S.op("vector", lambda e: e.reduce_sum(ssum[:], xtm[:, :], AX.X),
                         reads=[("xtm", 0), ("xtm", 1)], writes=["ssum"], narrow=True,
                         tag=f"f0sum{sg}_{lt}")
                    S.op("scalar", lambda e: e.activation(sq1[:, :], xtm[:, :], AF.Square,
                                                          accum_out=ssq[:]),
                         reads=[("xtm", 0), ("xtm", 1)], writes=["ssq"], narrow=True,
                         tag=f"f0ssq{sg}_{lt}")
                    S.op("vector", lambda e: e.tensor_scalar(mu[:], ssum[:], 1.0 / D, None, ALU.mult),
                         reads=["ssum"], writes=["mu"], narrow=True, tag=f"f0mu{sg}_{lt}")
                    S.op("vector", lambda e: e.tensor_tensor(musq[:], mu[:], mu[:], ALU.mult),
                         reads=["mu"], writes=["musq"], narrow=True, tag=f"f0musq{sg}_{lt}")
                    S.op("vector", lambda e: e.scalar_tensor_tensor(ssq[:], ssq[:], 1.0 / D,
                                                                    musq[:], ALU.mult, ALU.subtract),
                         reads=["ssq", "musq"], writes=["ssq"], narrow=True, tag=f"f0var{sg}_{lt}")
                    S.op("vector", lambda e: e.tensor_scalar(ssq[:], ssq[:], EPS, None, ALU.add),
                         reads=["ssq"], writes=["ssq"], narrow=True, tag=f"f0vare{sg}_{lt}")
                    S.op("scalar", lambda e: e.activation(ssq[:], ssq[:], AF.Sqrt),
                         reads=["ssq"], writes=["ssq"], narrow=True, tag=f"f0sig{sg}_{lt}")
                    S.op("vector", lambda e: e.reciprocal(rstd[:], ssq[:]),
                         reads=["ssq"], writes=["rstd"], narrow=True, tag=f"f0rstd{sg}_{lt}")
                    S.op("vector", lambda e: e.tensor_scalar(xtm[:, :], xtm[:, :], mu[:], rstd[:],
                                                             ALU.subtract, ALU.mult),
                         reads=[("xtm", 0), ("xtm", 1), "mu", "rstd"],
                         writes=[("xtm", 0), ("xtm", 1)], tag=f"f0ln{sg}_{lt}")
                    lb = lt % 2
                    emit_quant(ctx, f"f0xq{sg}_{lt}", xq[:, lb, :], ("xq", lb), xtm[:, :],
                               ("xtm", 0), 1024, qtmp)
                    for g in range(1):
                        def tpq(e, lb=lb):
                            rr = None
                            for j in range(8):
                                rr = e.transpose(ctx.pTb[:, j * 128:(j + 1) * 128],
                                                 xq[:, lb, j * 128:(j + 1) * 128], ctx.identb[:, :])
                            return rr
                        S.op("tensor", tpq, reads=[("xq", lb), ("c", "identb")], writes=["pTb"],
                             tag=f"f0tq{sg}_{lt}")
                        S.op("vector", lambda e, lb=lb: e.tensor_copy(
                                xqf[:, lb, :, :], ctx.pTb[:, :].rearrange("p (j q) -> p j q", q=128)),
                             reads=["pTb"], writes=[("xqf", lb)], tag=f"f0tqe{sg}_{lt}")
                    for obl in range(6):
                        nw = 512 if obl < 5 else 256
                        osl = slice(obl * 512, obl * 512 + nw)
                        ob2 = obl % 2
                        def mmg(e, osl=osl, nw=nw, lb=lb):
                            rr = None
                            for ci in range(8):
                                rr = e.matmul(ctx.pA[0:128, 0:nw], xqf[:, lb, ci, :], wg[:, ci, osl],
                                              start=(ci == 0), stop=(ci == 7))
                            return rr
                        def mmu(e, osl=osl, nw=nw, lb=lb):
                            rr = None
                            for ci in range(8):
                                rr = e.matmul(ctx.pB[0:128, 0:nw], xqf[:, lb, ci, :], wu[:, ci, osl],
                                              start=(ci == 0), stop=(ci == 7))
                            return rr
                        S.op("tensor", mmg, reads=[("xqf", lb), ("qg", "dst")], writes=["pA"],
                             tag=f"f0mg{sg}_{lt}_{obl}")
                        S.op("tensor", mmu, reads=[("xqf", lb), ("qu", "dst")], writes=["pB"],
                             tag=f"f0mu{sg}_{lt}_{obl}")
                        S.op("scalar", lambda e, nw=nw, ob2=ob2: e.activation(
                                gact[:, ob2, 0:nw], ctx.pA[:, 0:nw], AF.Silu),
                             reads=["pA"], writes=[("gact", ob2)], tag=f"f0sil{sg}_{lt}_{obl}")
                        S.op("vector", lambda e, nw=nw, ob2=ob2: e.tensor_tensor(
                                hbuf[:, 0:nw], gact[:, ob2, 0:nw], ctx.pB[:, 0:nw], ALU.mult),
                             reads=[("gact", ob2), "pB"], writes=["hbuf"], tag=f"f0h{sg}_{lt}_{obl}")
                        emit_quant(ctx, f"f0hq{sg}_{lt}_{obl}", hq[:, ob2, 0:nw], ("hq", ob2),
                                   hbuf[:, 0:nw], "hbuf", nw, qtmp)
                        nch = nw // 128
                        def tph(e, nch=nch, ob2=ob2):
                            rr = None
                            for j in range(nch):
                                rr = e.transpose(ctx.pTb[:, j * 128:(j + 1) * 128],
                                                 hq[:, ob2, j * 128:(j + 1) * 128], ctx.identb[:, :])
                            return rr
                        S.op("tensor", tph, reads=[("hq", ob2), ("c", "identb")], writes=["pTb"],
                             tag=f"f0th{sg}_{lt}_{obl}")
                        def evh(e, obl=obl, l2=l2, nch=nch):
                            return e.tensor_copy(
                                hqf[:, obl * 4:obl * 4 + nch, l2 * 128:(l2 + 1) * 128],
                                ctx.pTb[:, 0:nch * 128].rearrange("p (j q) -> p j q", q=128))
                        S.op("vector", evh, reads=["pTb"], writes=[("hqf", l2, obl)],
                             tag=f"f0the{sg}_{lt}_{obl}")
                hk = [("hqf", l2, obl) for l2 in range(2) for obl in range(6)]
                for oc in range(8):
                    ps, pk = (ctx.pC, "pC") if oc % 2 == 0 else (ctx.pD, "pD")
                    def mmd2(e, ps=ps, oc=oc):
                        rr = None
                        for cj in range(22):
                            rr = e.matmul(ps[:, 0:256], wd[:, cj, oc * 128:(oc + 1) * 128],
                                          hqf[:, cj, :], start=(cj == 0), stop=(cj == 21))
                        return rr
                    S.op("tensor", mmd2, reads=hk + [("qd", "dst")], writes=[pk],
                         tag=f"f0md{sg}_{half}_{oc}")
                    tc = half * 256
                    S.op("vector", lambda e, oc=oc, tc=tc, ps=ps: e.tensor_add(
                            xsl[:, oc, tc:tc + 256], ps[:, 0:256], xsl[:, oc, tc:tc + 256]),
                         reads=[pk, ("xsl",)], writes=[("xsl",)], tag=f"f0de{sg}_{half}_{oc}")
            S.dma("sync", lambda e, sg=sg: e.dma_start(
                    out=xdst.rearrange("(c p) t -> p c t", p=128)[:, :, sg * 512:(sg + 1) * 512],
                    in_=xsl[:, :, :]),
                  reads=[("xsl",)], writes=[("xout", sg), ("xsl",)], tag=f"f0st{sg}")
        S.fence()

def emit_l1ffn(ctx, xsrc, xdst):
    import contextlib
    nc, S = ctx.nc, ctx.S
    xr = xsrc.rearrange("(c p) t -> p c t", p=128)
    xdr = xdst.rearrange("(c p) t -> p c t", p=128)
    with contextlib.ExitStack() as est:
        es = est.enter_context
        xbw = es(nc.sbuf_tensor("f1_xbw", [128, 8, T], BF16))
        emit_stats(ctx, xsrc, es, tag="f1", xb_dst=xbw)
        rowsNB_, srep_ = ctx.rowsNB, ctx.srep
        rstdcol_ = getattr(ctx, 'rstd_col', None)
        wt1 = es(nc.sbuf_tensor("f1_wt1", [128, 2, 8, 128], BF16))
        wt2 = es(nc.sbuf_tensor("f1_wt2", [128, 2, 8, 128], BF16))
        cs1 = es(nc.sbuf_tensor("f1_cs1", [1, 4096], BF16))
        cs2 = es(nc.sbuf_tensor("f1_cs2", [1, 4096], BF16))
        s1c = es(nc.sbuf_tensor("f1_s1c", [128, 32], F32))
        s2c = es(nc.sbuf_tensor("f1_s2c", [128, 32], F32))
        E1 = es(nc.sbuf_tensor("f1_E1", [128, 512], F32))
        sil = es(nc.sbuf_tensor("f1_sil", [128, 512], BF16))
        hst = es(nc.sbuf_tensor("f1_hst", [128, 2, 512], BF16))
        S.dma("sync", lambda e: e.dma_start(out=s1c[:, :], in_=wap2(ctx, "l1_s1")),
              writes=["s1c"], tag="f1lds1")
        S.dma("sync", lambda e: e.dma_start(out=s2c[:, :], in_=wap2(ctx, "l1_s2")),
              writes=["s2c"], tag="f1lds2")
        for oc in range(32):
            b = oc % 2
            for wi, (wkey, wt, csk) in enumerate([("l1_w1T", wt1, "cs1"), ("l1_w2T", wt2, "cs2")]):
                ap = wap2(ctx, wkey).rearrange("(c p) o -> p c o", p=128)[:, :, oc * 128:(oc + 1) * 128]
                S.dma("gpsimd", lambda e, ap=ap, wt=wt, b=b: e.dma_start(out=wt[:, b], in_=ap),
                      reads=[(csk, "free", b)], writes=[(csk, "wt", b)], tag=f"f1lw{wi}_{oc}")
                cst = cs1 if wi == 0 else cs2
                def csf(e, wt=wt, b=b):
                    rr = None
                    for ci in range(8):
                        rr = e.matmul(ctx.pS[0:1, 0:128], ctx.onescol_b[:], wt[:, b, ci],
                                      start=(ci == 0), stop=(ci == 7))
                    return rr
                S.op("tensor", csf, reads=[(csk, "wt", b), ("c", "onescol_b")], writes=["pS"],
                     tag=f"f1cs{wi}_{oc}")
                S.op("vector", lambda e, cst=cst, oc=oc: e.tensor_copy(
                        cst[0:1, oc * 128:(oc + 1) * 128], ctx.pS[0:1, 0:128]),
                     reads=["pS"], writes=[(csk, "row", oc)], tag=f"f1cse{wi}_{oc}")
            for sg in range(8):
                pk1, ps1 = ("pmA", ctx.pA) if sg % 2 == 0 else ("pmB", ctx.pB)
                pk2, ps2 = ("pmC", ctx.pC) if sg % 2 == 0 else ("pmD", ctx.pD)
                sl = slice(sg * 512, (sg + 1) * 512)
                def mm1(e, b=b, sl=sl, oc=oc, ps=ps1):
                    for ci in range(8):
                        e.matmul(ps[:, :], wt1[:, b, ci], xbw[:, ci, sl], start=(ci == 0), stop=False)
                    return e.matmul(ps[:, :], cs1[0:1, oc * 128:(oc + 1) * 128],
                                    rowsNB_[0:1, sl], start=False, stop=True)
                def mm2(e, b=b, sl=sl, oc=oc, ps=ps2):
                    for ci in range(8):
                        e.matmul(ps[:, :], wt2[:, b, ci], xbw[:, ci, sl], start=(ci == 0), stop=False)
                    return e.matmul(ps[:, :], cs2[0:1, oc * 128:(oc + 1) * 128],
                                    rowsNB_[0:1, sl], start=False, stop=True)
                S.op("tensor", mm1, reads=[("cs1", "wt", b), ("cs1", "row", oc), ("xbw", sg),
                                           ("rowN", sg)], writes=[pk1], tag=f"f1m1_{oc}_{sg}")
                S.op("tensor", mm2, reads=[("cs2", "wt", b), ("cs2", "row", oc), ("xbw", sg),
                                           ("rowN", sg)], writes=[pk2], tag=f"f1m2_{oc}_{sg}")
                S.op("vector", lambda e, sg=sg, ps=ps1: e.tensor_tensor(
                        E1[:, :], ps[:, :], srep_[:, sg, :], ALU.mult),
                     reads=[pk1, ("srep", sg)], writes=["E1"], tag=f"f1e1_{oc}_{sg}")
                S.op("scalar", lambda e, oc=oc: e.activation(sil[:, :], E1[:, :], AF.Silu,
                                                             scale=s1c[:, oc:oc + 1]),
                     reads=["E1", "s1c"], writes=["sil"], tag=f"f1sil_{oc}_{sg}")
                hb = sg % 2
                S.op("vector", lambda e, sg=sg, ps=ps2: e.tensor_tensor(
                        E1[:, :], ps[:, :], srep_[:, sg, :], ALU.mult),
                     reads=[pk2, ("srep", sg), "sil"], writes=["E1"], tag=f"f1e2_{oc}_{sg}")
                S.op("vector", lambda e, oc=oc, hb=hb: e.scalar_tensor_tensor(
                        hst[:, hb, :], E1[:, :], s2c[:, oc:oc + 1], sil[:, :], ALU.mult, ALU.mult),
                     reads=["E1", "s2c", "sil", ("hfree", hb)], writes=[("hst", hb)],
                     tag=f"f1h_{oc}_{sg}")
                S.dma("sync", lambda e, oc=oc, sg=sg, hb=hb: e.dma_start(
                        out=ctx.hD[oc, sg], in_=hst[:, hb, :]),
                      reads=[("hst", hb)], writes=[("hD", oc, sg), ("hfree", hb),
                                                   (csk, "free", b) if sg == 7 else ("na",)],
                      tag=f"f1sth_{oc}_{sg}")
        S.fence()
    for pi, (wkey, skey, srcD, dstD, srck) in enumerate(
            [("l1_h1T", "l1_h1s", ctx.hD, ctx.g1D, "hD"),
             ("l1_h2T", "l1_h2s", ctx.g1D, ctx.g2D, "g1D")]):
        with contextlib.ExitStack() as est:
            es = est.enter_context
            hw = es(nc.sbuf_tensor(f"f1_hw{pi}", [128, 32, 1024], BF16))
            hsl = es(nc.sbuf_tensor(f"f1_hsl{pi}", [128, 2, 32, 512], BF16))
            ssc = es(nc.sbuf_tensor(f"f1_ssc{pi}", [128, 32], F32))
            gst = es(nc.sbuf_tensor(f"f1_gst{pi}", [128, 2, 512], BF16))
            S.dma("sync", lambda e: e.dma_start(out=ssc[:, :], in_=wap2(ctx, skey)),
                  writes=["ssc"], tag=f"f1lds{pi}")
            for ocb in range(4):
                ap = wap2(ctx, wkey).rearrange("(c p) o -> p c o", p=128)[
                    :, :, ocb * 1024:(ocb + 1) * 1024]
                S.dma("gpsimd", lambda e, ap=ap: e.dma_start(out=hw[:, :, :], in_=ap),
                      reads=[("hw",)], writes=[("hw",)], tag=f"f1lhw{pi}_{ocb}")
                for sg in range(8):
                    b = sg % 2
                    S.dma("sync", lambda e, sg=sg, b=b, srcD=srcD: e.dma_start(
                            out=hsl[:, b], in_=srcD[:, sg].rearrange("o p f -> p o f")),
                          reads=[("hslf", b)] + ([("hD", o, sg) for o in range(32)] if pi == 0 and ocb == 0 else []),
                          writes=[("hsl", b)], tag=f"f1lh{pi}_{ocb}_{sg}")
                    for oc in range(8):
                        og = ocb * 8 + oc
                        ps, pk = (ctx.pA, "pmA") if oc % 2 == 0 else (ctx.pB, "pmB")
                        def mmh(e, b=b, og=og, ps=ps):
                            rr = None
                            for cj in range(32):
                                rr = e.matmul(ps[:, :], hw[:, cj, (og % 8) * 128:(og % 8 + 1) * 128],
                                              hsl[:, b, cj, :], start=(cj == 0), stop=(cj == 31))
                            return rr
                        S.op("tensor", mmh, reads=[("hw",), ("hsl", b)], writes=[pk],
                             tag=f"f1mh{pi}_{og}_{sg}")
                        gb = sg % 2
                        S.op("scalar", lambda e, og=og, gb=gb, ps=ps: e.activation(
                                gst[:, gb, :], ps[:, :], AF.Gelu, scale=ssc[:, og:og + 1]),
                             reads=[pk, "ssc", ("gfree", gb)], writes=[("gst", gb)],
                             tag=f"f1g{pi}_{og}_{sg}")
                        S.dma("sync", lambda e, og=og, sg=sg, gb=gb, dstD=dstD: e.dma_start(
                                out=dstD[og, sg], in_=gst[:, gb, :]),
                              reads=[("gst", gb)], writes=[(("g1D" if pi == 0 else "g2D"), og, sg),
                                                           ("gfree", gb)],
                              tag=f"f1sg{pi}_{og}_{sg}")
            S.fence()
    with contextlib.ExitStack() as est:
        es = est.enter_context
        w3 = es(nc.sbuf_tensor("f1_w3", [128, 32, 1024], BF16))
        gsl = es(nc.sbuf_tensor("f1_gsl", [128, 2, 32, 512], BF16))
        s3c = es(nc.sbuf_tensor("f1_s3c", [128, 8], F32))
        xsl = es(nc.sbuf_tensor("f1_xsl", [128, 2, 8, 512], F32))
        S.dma("gpsimd", lambda e: e.dma_start(out=w3[:, :, :],
                in_=wap2(ctx, "l1_w3T").rearrange("(c p) o -> p c o", p=128)),
              writes=[("w3",)], tag="f1lw3")
        S.dma("sync", lambda e: e.dma_start(out=s3c[:, :], in_=wap2(ctx, "l1_s3")),
              writes=["s3c"], tag="f1lds3")
        for sg in range(8):
            b = sg % 2
            S.dma("sync", lambda e, sg=sg, b=b: e.dma_start(
                    out=gsl[:, b], in_=ctx.g2D[:, sg].rearrange("o p f -> p o f")),
                  reads=[("gslf", b)] + [("g2D", o, sg) for o in range(32)],
                  writes=[("gsl", b)], tag=f"f1lg2_{sg}")
            S.dma("sync", lambda e, sg=sg, b=b: e.dma_start(
                    out=xsl[:, b], in_=xr[:, :, sg * 512:(sg + 1) * 512]),
                  reads=[("xslf", b)], writes=[("xsl", b)], tag=f"f1lx{sg}")
            for oc in range(8):
                ps, pk = (ctx.pA, "pmA") if oc % 2 == 0 else (ctx.pB, "pmB")
                def mmo(e, b=b, oc=oc, ps=ps):
                    rr = None
                    for cj in range(32):
                        rr = e.matmul(ps[:, :], w3[:, cj, oc * 128:(oc + 1) * 128],
                                      gsl[:, b, cj, :], start=(cj == 0), stop=(cj == 31))
                    return rr
                S.op("tensor", mmo, reads=[("w3",), ("gsl", b)], writes=[pk], tag=f"f1mo{oc}_{sg}")
                S.op("vector", lambda e, oc=oc, b=b, ps=ps: e.scalar_tensor_tensor(
                        xsl[:, b, oc, :], ps[:, :], s3c[:, oc:oc + 1], xsl[:, b, oc, :],
                        ALU.mult, ALU.add),
                     reads=[pk, "s3c", ("xsl", b)], writes=[("xsl", b)], tag=f"f1oe{oc}_{sg}")
            S.dma("sync", lambda e, sg=sg, b=b: e.dma_start(
                    out=xdr[:, :, sg * 512:(sg + 1) * 512], in_=xsl[:, b]),
                  reads=[("xsl", b)], writes=[("xout", sg), ("xslf", b), ("gslf", b)],
                  tag=f"f1st{sg}")
        S.fence()

def emit_l2ffn(ctx, xsrc, xdst):
    import contextlib
    nc, S = ctx.nc, ctx.S
    xr = xsrc.rearrange("(c p) t -> p c t", p=128)
    xdr = xdst.rearrange("(c p) t -> p c t", p=128)
    with contextlib.ExitStack() as est:
        es = est.enter_context
        emit_stats(ctx, xsrc, es, tag="f2")
        rowsNB_, srep_ = ctx.rowsNB, ctx.srep
        rstdcol_ = getattr(ctx, 'rstd_col', None)
        xln = es(nc.sbuf_tensor("f2_xln", [128, 8, T], BF16))
        xsl = es(nc.sbuf_tensor("f2_xsl", [128, 2, 8, 512], F32))
        wr = es(nc.sbuf_tensor("f2_wr", [128, 8, 1024], BF16))
        wk = es(nc.sbuf_tensor("f2_wk", [128, 8, 1024], BF16))
        wv = es(nc.sbuf_tensor("f2_wv", [128, 8, 1024], BF16))
        mur = es(nc.sbuf_tensor("f2_mur", [128, 8], F32))
        muk = es(nc.sbuf_tensor("f2_muk", [128, 8], F32))
        AA = es(nc.sbuf_tensor("f2_A", [128, 8, 512], F32))
        xmix = es(nc.sbuf_tensor("f2_xmix", [128, 8, 512], BF16))
        rsb = es(nc.sbuf_tensor("f2_r", [128, 8, 512], BF16))
        kact = es(nc.sbuf_tensor("f2_k", [128, 8, 512], BF16))
        for wkey, wsb, tagw in [("l2_wrT", wr, "wr"), ("l2_wkT2", wk, "wk"), ("l2_wvT2", wv, "wv")]:
            S.dma("gpsimd", lambda e, wkey=wkey, wsb=wsb: e.dma_start(out=wsb[:, :, :],
                    in_=wap2(ctx, wkey).rearrange("(c p) o -> p c o", p=128)),
                  writes=[(tagw,)], tag=f"f2l{tagw}")
        S.dma("sync", lambda e: e.dma_start(out=mur[:, :], in_=wap2(ctx, "l2_mur")),
              writes=["mur"], tag="f2lmur")
        S.dma("sync", lambda e: e.dma_start(out=muk[:, :], in_=wap2(ctx, "l2_muk")),
              writes=["muk"], tag="f2lmuk")
        for sg in range(8):
            b = sg % 2
            sl = slice(sg * 512, (sg + 1) * 512)
            S.dma("sync", lambda e, sl=sl, b=b: e.dma_start(out=xsl[:, b], in_=xr[:, :, sl]),
                  reads=[("xslf", b)], writes=[("xsl", b)], tag=f"f2lx{sg}")
            S.op("tensor", lambda e, sg=sg, sl=sl: e.matmul(
                    ctx.pE[:, :], ctx.onesrow_b[:, :], rowsNB_[0:1, sl], start=True, stop=True),
                 reads=[("rowN", sg), ("c", "onesrow_b")], writes=["pE"], tag=f"f2nmu{sg}")
            S.op("vector", lambda e, b=b: e.tensor_tensor(
                    AA[:, :, :], xsl[:, b], ctx.pE[:, None, :].broadcast_to([128, 8, 512]),
                    ALU.add),
                 reads=[("xsl", b), "pE"], writes=[("A2",)], tag=f"f2a{sg}")
            S.op("vector", lambda e, sg=sg, sl=sl: e.tensor_tensor(
                    xln[:, :, sl], AA[:, :, :],
                    srep_[:, sg, None, :].broadcast_to([128, 8, 512]), ALU.mult),
                 reads=[("A2",), ("srep", sg)], writes=[("xln", sg), ("xslf", b)], tag=f"f2xl{sg}")
        for sg in range(8):
            b = sg % 2
            sl = slice(sg * 512, (sg + 1) * 512)
            S.dma("sync", lambda e, sl=sl, b=b: e.dma_start(out=xsl[:, b], in_=xr[:, :, sl]),
                  reads=[("xslf", b)], writes=[("xsl", b)], tag=f"f2lx2{sg}")
            rd = [("xln", sg)] + ([("xln", sg - 1)] if sg > 0 else [])
            for mi, muc in enumerate([mur, muk]):
                c0 = sg * 512
                mk = "mur" if mi == 0 else "muk"
                if sg > 0:
                    S.op("vector", lambda e, c0=c0: e.tensor_sub(
                            AA[:, :, :], xln[:, :, c0:c0 + 512], xln[:, :, c0 - 1:c0 + 511]),
                         reads=rd, writes=[("A2",)], tag=f"f2sub{mi}_{sg}")
                    for ci in range(8):
                        S.op("vector", lambda e, ci=ci, c0=c0, muc=muc: e.scalar_tensor_tensor(
                                xmix[:, ci, :], AA[:, ci, :], muc[:, ci:ci + 1],
                                xln[:, ci, c0 - 1:c0 + 511], ALU.mult, ALU.add),
                             reads=[("A2",), mk] + rd, writes=[("xmix",)], tag=f"f2mx{mi}_{sg}_{ci}")
                else:
                    S.op("vector", lambda e: e.tensor_sub(
                            AA[:, :, 1:512], xln[:, :, 1:512], xln[:, :, 0:511]),
                         reads=rd, writes=[("A2",)], tag=f"f2sub{mi}_{sg}")
                    for ci in range(8):
                        S.op("vector", lambda e, ci=ci, muc=muc: e.scalar_tensor_tensor(
                                xmix[:, ci, 1:512], AA[:, ci, 1:512], muc[:, ci:ci + 1],
                                xln[:, ci, 0:511], ALU.mult, ALU.add),
                             reads=[("A2",), mk] + rd, writes=[("xmix",)], tag=f"f2mx{mi}_{sg}_{ci}")
                        S.op("vector", lambda e, ci=ci, muc=muc: e.tensor_scalar(
                                xmix[:, ci, 0:1], xln[:, ci, 0:1], muc[:, ci:ci + 1], None, ALU.mult),
                             reads=[mk] + rd, writes=[("xmix",)], narrow=True,
                             tag=f"f2mx0{mi}_{sg}_{ci}")
                wsb, wkk = (wr, ("wr",)) if mi == 0 else (wk, ("wk",))
                for oc in range(8):
                    ps, pk = (ctx.pA, "pmA") if oc % 2 == 0 else (ctx.pB, "pmB")
                    def mmr(e, oc=oc, wsb=wsb, ps=ps):
                        rr = None
                        for ci in range(8):
                            rr = e.matmul(ps[:, :], wsb[:, ci, oc * 128:(oc + 1) * 128],
                                          xmix[:, ci, :], start=(ci == 0), stop=(ci == 7))
                        return rr
                    S.op("tensor", mmr, reads=[("xmix",), wkk], writes=[pk],
                         tag=f"f2mm{mi}_{oc}_{sg}")
                    if mi == 0:
                        S.op("scalar", lambda e, oc=oc, ps=ps: e.activation(
                                rsb[:, oc, :], ps[:, :], AF.Sigmoid),
                             reads=[pk], writes=[("rsb",)], tag=f"f2sig{oc}_{sg}")
                    else:
                        S.op("vector", lambda e, oc=oc, ps=ps: e.tensor_scalar(
                                AA[:, oc, :], ps[:, :], 0.0, None, ALU.max),
                             reads=[pk], writes=[("A2",)], tag=f"f2rel{oc}_{sg}")
                        S.op("scalar", lambda e, oc=oc: e.activation(
                                kact[:, oc, :], AA[:, oc, :], AF.Square),
                             reads=[("A2",)], writes=[("kact",)], tag=f"f2sq{oc}_{sg}")
            for oc in range(8):
                ps, pk = (ctx.pC, "pmC") if oc % 2 == 0 else (ctx.pD, "pmD")
                def mmv(e, oc=oc, ps=ps):
                    rr = None
                    for ci in range(8):
                        rr = e.matmul(ps[:, :], wv[:, ci, oc * 128:(oc + 1) * 128],
                                      kact[:, ci, :], start=(ci == 0), stop=(ci == 7))
                    return rr
                S.op("tensor", mmv, reads=[("kact",), ("wv",)], writes=[pk], tag=f"f2mv{oc}_{sg}")
                S.op("vector", lambda e, oc=oc, ps=ps: e.tensor_tensor(
                        AA[:, oc, :], ps[:, :], rsb[:, oc, :], ALU.mult),
                     reads=[pk, ("rsb",)], writes=[("A2",)], tag=f"f2om{oc}_{sg}")
                S.op("vector", lambda e, oc=oc, b=b: e.tensor_add(
                        xsl[:, b, oc, :], xsl[:, b, oc, :], AA[:, oc, :]),
                     reads=[("A2",), ("xsl", b)], writes=[("xsl", b)], tag=f"f2oe{oc}_{sg}")
            S.dma("sync", lambda e, sg=sg, b=b: e.dma_start(
                    out=xdr[:, :, sg * 512:(sg + 1) * 512], in_=xsl[:, b]),
                  reads=[("xsl", b)], writes=[("xout", sg), ("xslf", b)], tag=f"f2st{sg}")
        S.fence()

def emit_l3attn(ctx, xsrc, xdst):
    import contextlib
    nc, S = ctx.nc, ctx.S
    xr = xsrc.rearrange("(c p) t -> p c t", p=128)
    xdr = xdst.rearrange("(c p) t -> p c t", p=128)
    QT = 1024
    with contextlib.ExitStack() as est:
        es = est.enter_context
        emit_stats(ctx, xsrc, es, tag="f3")
        rowsNB_, srep_ = ctx.rowsNB, ctx.srep
        rstdcol_ = getattr(ctx, 'rstd_col', None)
        xbq = es(nc.sbuf_tensor("f3_xb", [128, 8, QT], BF16))
        xin_ = es(nc.sbuf_tensor("f3_xi", [128, 16, QT], BF16))
        zb = es(nc.sbuf_tensor("f3_z", [128, 16, QT], BF16))
        Gt = es(nc.sbuf_tensor("f3_Gt", [128, QT], BF16))
        ht = es(nc.sbuf_tensor("f3_ht", [128, QT], BF16))
        wtile = es(nc.sbuf_tensor("f3_wt", [128, 2, 8, 128], BF16))
        wot = es(nc.sbuf_tensor("f3_wot", [128, 2, 16, 128], BF16))
        cs3 = es(nc.sbuf_tensor("f3_cs", [1, 4096], BF16))
        convw = es(nc.sbuf_tensor("f3_cw", [128, 16, 4], F32))
        convb = es(nc.sbuf_tensor("f3_cb", [128, 16], F32))
        wxc = es(nc.sbuf_tensor("f3_wxc", [128, 16, 16], BF16))
        Bt = es(nc.sbuf_tensor("f3_Bt", [16, 2048], BF16))
        import contextlib as _cl
        with _cl.ExitStack() as est2:
            Btf = est2.enter_context(nc.sbuf_tensor("f3_Btf", [16, 2048], F32))
            S.dma("sync", lambda e: e.dma_start(out=Btf[:, :], in_=wap2(ctx, "l3_blogT")),
                  writes=["Btf"], tag="f3lbt")
            S.op("scalar", lambda e: e.activation(Bt[:, :], Btf[:, :], AF.Exp),
                 reads=["Btf"], writes=["Bt"], tag="f3expB")
            S.fence()
        sC = es(nc.sbuf_tensor("f3_sC", [16, QT], BF16))
        halo = es(nc.sbuf_tensor("f3_halo", [128, 16, 3], BF16))
        hcar = es(nc.sbuf_tensor("f3_hc", [128, 16, 1], F32))
        dec = es(nc.sbuf_tensor("f3_dec", [128, QT], F32))
        CV = es(nc.sbuf_tensor("f3_CV", [128, QT], F32))
        lng = es(nc.sbuf_tensor("f3_lng", [128, 8], F32))
        lnb = es(nc.sbuf_tensor("f3_lnb", [128, 8], F32))
        xsl = es(nc.sbuf_tensor("f3_xsl", [128, 8, 512], F32))
        XLN = es(nc.sbuf_tensor("f3_xln", [128, 8, 512], F32))
        Rb = es(nc.sbuf_tensor("f3_Rb", [128, 8, 512], BF16))
        R2 = es(nc.sbuf_tensor("f3_R2", [128, 8, 512], BF16))
        st_r = [es(nc.sbuf_tensor(f"f3_sr{i}", [1, 512], F32)) for i in range(5)]
        murow, musq, varr, sigr, rst2 = st_r

        S.dma("sync", lambda e: e.dma_start(out=convw[:, :, :], in_=wap2(ctx, "l3_convw")),
              writes=["cw"], tag="f3lcw")
        S.dma("sync", lambda e: e.dma_start(out=convb[:, :], in_=wap2(ctx, "l3_convb")),
              writes=["cb"], tag="f3lcb")
        S.dma("sync", lambda e: e.dma_start(out=lng[:, :], in_=wap2(ctx, "l3_lng")),
              writes=["lng"], tag="f3llng")
        S.dma("sync", lambda e: e.dma_start(out=lnb[:, :], in_=wap2(ctx, "l3_lnb")),
              writes=["lnb"], tag="f3llnb")
        S.dma("gpsimd", lambda e: e.dma_start(out=wxc[:, :, :],
                in_=wap2(ctx, "l3_wxcT").rearrange("(c p) o -> p c o", p=128)),
              writes=["wxc"], tag="f3lwxc")
        S.op("vector", lambda e: e.memset(dec[:, :], 0.95), writes=["dec"], tag="f3dec")
        S.op("vector", lambda e: e.memset(halo[:, :, :], 0.0), writes=["halo"], tag="f3hz")
        S.op("vector", lambda e: e.memset(hcar[:, :, :], 0.0), writes=["hcar"], tag="f3hc0")

        for q in range(4):
            tq = q * QT
            S.dma("gpsimd", lambda e, tq=tq: e.dma_start(
                    out=xbq[:, :, :], in_=xr[:, :, tq:tq + QT]),
                  reads=[("xbq",)], writes=[("xbq",)], tag=f"f3lxb{q}")
            for oc in range(32):
                b = oc % 2
                ap = wap2(ctx, "l3_winT").rearrange("(c p) o -> p c o", p=128)[
                    :, :, oc * 128:(oc + 1) * 128]
                S.dma("gpsimd", lambda e, ap=ap, b=b: e.dma_start(out=wtile[:, b], in_=ap),
                      reads=[("wtf", b)], writes=[("wt3", b)], tag=f"f3lw{q}_{oc}")
                if q == 0:
                    def csf(e, b=b):
                        rr = None
                        for ci in range(8):
                            rr = e.matmul(ctx.pS[0:1, 0:128], ctx.onescol_b[:], wtile[:, b, ci],
                                          start=(ci == 0), stop=(ci == 7))
                        return rr
                    S.op("tensor", csf, reads=[("wt3", b), ("c", "onescol_b")], writes=["pS"],
                         tag=f"f3cs{oc}")
                    S.op("vector", lambda e, oc=oc: e.tensor_copy(
                            cs3[0:1, oc * 128:(oc + 1) * 128], ctx.pS[0:1, 0:128]),
                         reads=["pS"], writes=[("cs3", oc)], tag=f"f3cse{oc}")
                for s2 in range(2):
                    sg = q * 2 + s2
                    sl = slice(s2 * 512, (s2 + 1) * 512)
                    ps, pk = (ctx.pA, "pmA") if s2 == 0 else (ctx.pB, "pmB")
                    def mmz(e, b=b, oc=oc, sl=sl, sg=sg, ps=ps):
                        for ci in range(8):
                            e.matmul(ps[:, :], wtile[:, b, ci], xbq[:, ci, sl],
                                     start=(ci == 0), stop=False)
                        return e.matmul(ps[:, :], cs3[0:1, oc * 128:(oc + 1) * 128],
                                        rowsNB_[0:1, sg * 512:(sg + 1) * 512],
                                        start=False, stop=True)
                    S.op("tensor", mmz, reads=[("wt3", b), ("xbq",), ("cs3", oc), ("rowN", sg)],
                         writes=[pk], tag=f"f3mz{q}_{oc}_{s2}")
                    dstb = xin_ if oc < 16 else zb
                    dk = ("xin", oc % 16) if oc < 16 else ("zb", oc % 16)
                    S.op("vector", lambda e, oc=oc, sl=sl, sg=sg, dstb=dstb, ps=ps: e.tensor_tensor(
                            dstb[:, oc % 16, sl], ps[:, :], srep_[:, sg, :], ALU.mult),
                         reads=[pk, ("srep", sg)], writes=[dk, ("wtf", b) if s2 == 1 else ("na2",)],
                         tag=f"f3ze{q}_{oc}_{s2}")
            for cc in range(16):
                S.op("vector", lambda e, cc=cc: e.tensor_scalar(
                        CV[:, :], xin_[:, cc, :], convw[:, cc, 3:4], None, ALU.mult),
                     reads=[("xin", cc), "cw"], writes=["CV"], tag=f"f3cv3{q}_{cc}")
                for k in range(3):
                    d = 3 - k
                    S.op("vector", lambda e, cc=cc, k=k, d=d: e.scalar_tensor_tensor(
                            CV[:, d:QT], xin_[:, cc, 0:QT - d], convw[:, cc, k:k + 1],
                            CV[:, d:QT], ALU.mult, ALU.add),
                         reads=[("xin", cc), "cw", "CV"], writes=["CV"], tag=f"f3cv{q}_{cc}_{k}")
                    S.op("vector", lambda e, cc=cc, k=k, d=d: e.scalar_tensor_tensor(
                            CV[:, 0:d], halo[:, cc, 3 - d:3], convw[:, cc, k:k + 1],
                            CV[:, 0:d], ALU.mult, ALU.add),
                         reads=["halo", "cw", "CV"], writes=["CV"], narrow=True,
                         tag=f"f3cvb{q}_{cc}_{k}")
                S.op("vector", lambda e, cc=cc: e.tensor_copy(halo[:, cc, :],
                                                              xin_[:, cc, QT - 3:QT]),
                     reads=[("xin", cc)], writes=["halo"], narrow=True, tag=f"f3hs{q}_{cc}")
                S.op("scalar", lambda e, cc=cc: e.activation(
                        xin_[:, cc, :], CV[:, :], AF.Silu, bias=convb[:, cc:cc + 1]),
                     reads=["CV", "cb", "halo"], writes=[("xin", cc)], tag=f"f3sl{q}_{cc}")
            for s2 in range(2):
                sl = slice(s2 * 512, (s2 + 1) * 512)
                def mmc(e, sl=sl):
                    rr = None
                    for cc in range(16):
                        rr = e.matmul(ctx.pS[0:16, :], wxc[:, cc, :], xin_[:, cc, sl],
                                      start=(cc == 0), stop=(cc == 15))
                    return rr
                S.op("tensor", mmc, reads=[("xin", cc2) for cc2 in range(16)] + ["wxc"],
                     writes=["pS"], tag=f"f3sc{q}_{s2}")
                S.op("vector", lambda e, sl=sl: e.tensor_copy(sC[:, sl], ctx.pS[0:16, :]),
                     reads=["pS"], writes=["sC"], tag=f"f3sce{q}_{s2}")
            for cc in range(16):
                def scn(e, cc=cc, q=q):
                    ini = 0.0 if q == 0 else hcar[:, cc, :]
                    return e.tensor_tensor_scan(ht[:, :], dec[:, :], xin_[:, cc, :], ini,
                                                ALU.mult, ALU.add)
                S.op("vector", scn, reads=[("xin", cc), "dec", "hcar", "ht"],
                     writes=["ht"], tag=f"f3sn{q}_{cc}")
                S.op("vector", lambda e, cc=cc: e.tensor_copy(hcar[:, cc, :], ht[:, QT - 1:QT]),
                     reads=["ht"], writes=["hcar"], narrow=True, tag=f"f3snc{q}_{cc}")
                for s2 in range(2):
                    sl = slice(s2 * 512, (s2 + 1) * 512)
                    ps, pk = (ctx.pC, "pmC") if s2 == 0 else (ctx.pD, "pmD")
                    S.op("tensor", lambda e, cc=cc, sl=sl, ps=ps: e.matmul(
                            ps[:, :], Bt[:, cc * 128:(cc + 1) * 128], sC[:, sl],
                            start=True, stop=True),
                         reads=["Bt", "sC"], writes=[pk], tag=f"f3g{q}_{cc}_{s2}")
                    S.op("vector", lambda e, sl=sl, ps=ps: e.tensor_copy(Gt[:, sl], ps[:, :]),
                         reads=[pk], writes=["Gtk"], tag=f"f3ge{q}_{cc}_{s2}")
                S.op("scalar", lambda e, cc=cc: e.activation(zb[:, cc, :], zb[:, cc, :], AF.Silu),
                     reads=[("zb", cc)], writes=[("zb", cc)], tag=f"f3sz{q}_{cc}")
                S.op("vector", lambda e, cc=cc: e.tensor_tensor(
                        CV[:, :], ht[:, :], Gt[:, :], ALU.mult),
                     reads=["ht", "Gtk"], writes=["CV"], tag=f"f3ym{q}_{cc}")
                S.op("vector", lambda e, cc=cc: e.tensor_tensor(
                        zb[:, cc, :], CV[:, :], zb[:, cc, :], ALU.mult),
                     reads=["CV", ("zb", cc)], writes=[("zb", cc)], tag=f"f3y{q}_{cc}")
            for s2 in range(2):
                sg = q * 2 + s2
                sl = slice(s2 * 512, (s2 + 1) * 512)
                gsl = slice(sg * 512, (sg + 1) * 512)
                S.dma("sync", lambda e, gsl=gsl: e.dma_start(out=xsl[:, :, :], in_=xr[:, :, gsl]),
                      reads=[("xsl3",)], writes=[("xsl3",)], tag=f"f3lx{q}_{s2}")
                S.op("tensor", lambda e, sg=sg, gsl=gsl: e.matmul(
                        ctx.pE[:, :], ctx.onesrow_b[:, :], rowsNB_[0:1, gsl],
                        start=True, stop=True),
                     reads=[("rowN", sg), ("c", "onesrow_b")], writes=["pE"], tag=f"f3nm{q}_{s2}")
                S.op("vector", lambda e: e.tensor_tensor(
                        XLN[:, :, :], xsl[:, :, :], ctx.pE[:, None, :].broadcast_to([128, 8, 512]),
                        ALU.add),
                     reads=[("xsl3",), "pE"], writes=[("XLN",)], tag=f"f3xl{q}_{s2}")
                S.op("vector", lambda e, sg=sg: e.tensor_tensor(
                        XLN[:, :, :], XLN[:, :, :],
                        srep_[:, sg, None, :].broadcast_to([128, 8, 512]), ALU.mult),
                     reads=[("XLN",), ("srep", sg)], writes=[("XLN",)], tag=f"f3r1{q}_{s2}")
                for oc in range(8):
                    b = oc % 2
                    ap = wap2(ctx, "l3_woutT").rearrange("(c p) o -> p c o", p=128)[
                        :, :, oc * 128:(oc + 1) * 128]
                    S.dma("gpsimd", lambda e, ap=ap, b=b: e.dma_start(out=wot[:, b], in_=ap),
                          reads=[("wotf", b)], writes=[("wot", b)], tag=f"f3lwo{q}_{s2}_{oc}")
                    ps, pk = (ctx.pA, "pmA") if oc % 2 == 0 else (ctx.pB, "pmB")
                    def mmo(e, b=b, sl=sl, ps=ps):
                        rr = None
                        for cc in range(16):
                            rr = e.matmul(ps[:, :], wot[:, b, cc, :], zb[:, cc, sl],
                                          start=(cc == 0), stop=(cc == 15))
                        return rr
                    S.op("tensor", mmo, reads=[("wot", b)] + [("zb", cc2) for cc2 in range(16)],
                         writes=[pk], tag=f"f3mo{q}_{s2}_{oc}")
                    S.op("vector", lambda e, oc=oc, ps=ps: e.tensor_add(
                            XLN[:, oc, :], ps[:, :], XLN[:, oc, :]),
                         reads=[pk, ("XLN",)], writes=[("XLN",), ("wotf", b)],
                         tag=f"f3moe{q}_{s2}_{oc}")
                S.op("vector", lambda e: e.tensor_copy(Rb.rearrange("p c t -> p (c t)"),
                                                       XLN.rearrange("p c t -> p (c t)")),
                     reads=[("XLN",)], writes=[("Rb",)], tag=f"f3rb{q}_{s2}")
                def mms1(e):
                    rr = None
                    for ci in range(8):
                        rr = e.matmul(ctx.pS[0:1, :], ctx.onescol_b[:], Rb[:, ci, :],
                                      start=(ci == 0), stop=(ci == 7))
                    return rr
                S.op("tensor", mms1, reads=[("Rb",), ("c", "onescol_b")],
                     writes=["pS"], tag=f"f3ms{q}_{s2}")
                S.op("vector", lambda e: e.tensor_scalar(murow[:], ctx.pS[0:1, :], -1.0 / D,
                                                         None, ALU.mult),
                     reads=["pS"], writes=["mu3"], tag=f"f3mu{q}_{s2}")
                S.op("tensor", lambda e: e.matmul(ctx.pE[:, :], ctx.onesrow_f[:, :], murow[:],
                                                  start=True, stop=True),
                     reads=["mu3", ("c", "onesrow_f")], writes=["pE"], tag=f"f3rpm{q}_{s2}")
                S.op("vector", lambda e: e.tensor_tensor(
                        XLN[:, :, :], XLN[:, :, :],
                        ctx.pE[:, None, :].broadcast_to([128, 8, 512]), ALU.add),
                     reads=[("XLN",), "pE"], writes=[("XLN",)], tag=f"f3sub{q}_{s2}")
                S.op("scalar", lambda e: e.activation(R2.rearrange("p c t -> p (c t)"),
                                                      XLN.rearrange("p c t -> p (c t)"), AF.Square),
                     reads=[("XLN",)], writes=[("R2",)], tag=f"f3r2{q}_{s2}")
                def mms2(e):
                    rr = None
                    for ci in range(8):
                        rr = e.matmul(ctx.pF[0:1, :], ctx.onescol_b[:], R2[:, ci, :],
                                      start=(ci == 0), stop=(ci == 7))
                    return rr
                S.op("tensor", mms2, reads=[("R2",), ("c", "onescol_b")],
                     writes=["pF"], tag=f"f3ms2{q}_{s2}")
                S.op("vector", lambda e: e.tensor_scalar(varr[:], ctx.pF[0:1, :], 1.0 / D,
                                                         None, ALU.mult),
                     reads=["pF"], writes=["var3"], tag=f"f3vr{q}_{s2}")
                S.op("scalar", lambda e: e.activation(sigr[:], varr[:], AF.Sqrt,
                                                      bias=ctx.eps1[0:1, :]),
                     reads=["var3", ("c", "eps1")], writes=["sig3"], tag=f"f3sg{q}_{s2}")
                S.op("vector", lambda e: e.reciprocal(rst2[:], sigr[:]),
                     reads=["sig3"], writes=["rst3"], tag=f"f3rs{q}_{s2}")
                S.op("tensor", lambda e: e.matmul(ctx.pF[:, :], ctx.onesrow_f[:, :], rst2[:],
                                                  start=True, stop=True),
                     reads=["rst3", ("c", "onesrow_f")], writes=["pF"], tag=f"f3rps{q}_{s2}")
                S.op("vector", lambda e: e.tensor_tensor(
                        XLN[:, :, :], XLN[:, :, :],
                        ctx.pF[:, None, :].broadcast_to([128, 8, 512]), ALU.mult),
                     reads=[("XLN",), "pF"], writes=[("XLN",)], tag=f"f3finb{q}_{s2}")
                for ci in range(8):
                    S.op("vector", lambda e, ci=ci: e.tensor_scalar(
                            XLN[:, ci, :], XLN[:, ci, :], lng[:, ci:ci + 1],
                            lnb[:, ci:ci + 1], ALU.mult, ALU.add),
                         reads=[("XLN",), "lng", "lnb"], writes=[("XLN",)],
                         tag=f"f3ga{q}_{s2}_{ci}")
                    S.op("vector", lambda e, ci=ci: e.tensor_add(
                            xsl[:, ci, :], xsl[:, ci, :], XLN[:, ci, :]),
                         reads=[("XLN",), ("xsl3",)], writes=[("xsl3",)],
                         tag=f"f3gb{q}_{s2}_{ci}")
                S.dma("sync", lambda e, gsl=gsl: e.dma_start(out=xdr[:, :, gsl], in_=xsl[:, :, :]),
                      reads=[("xsl3",)], writes=[("xout", sg), ("xsl3",)], tag=f"f3st{q}_{s2}")
        S.fence()

def emit_l3ffn(ctx, xsrc, xdst):
    import contextlib
    nc, S = ctx.nc, ctx.S
    xr = xsrc.rearrange("(c p) t -> p c t", p=128)
    xdr = xdst.rearrange("(c p) t -> p c t", p=128)
    with contextlib.ExitStack() as est:
        es = est.enter_context
        emit_stats(ctx, xsrc, es, tag="f4")
        rowsNB_, srep_ = ctx.rowsNB, ctx.srep
        rstdcol_ = getattr(ctx, 'rstd_col', None)
        xb = es(nc.sbuf_tensor("f4_xb", [128, 8, 512], BF16))
        wbufs = [es(nc.sbuf_tensor(f"f4_w{i}", [128, 8, 1024], BF16)) for i in range(2)]
        wcbq = es(nc.sbuf_tensor("f4_wcb", [2, 2, 1024], BF16))
        vf = es(nc.sbuf_tensor("f4_vf", [128, 8, 512], BF16))
        wf = es(nc.sbuf_tensor("f4_wf", [128, 8, 512], BF16))
        vi = es(nc.sbuf_tensor("f4_vi", [128, 8, 512], BF16))
        wi = es(nc.sbuf_tensor("f4_wi", [128, 8, 512], BF16))
        uf = es(nc.sbuf_tensor("f4_uf", [128, 4, 1024], BF16))
        ui = es(nc.sbuf_tensor("f4_ui", [128, 4, 1024], BF16))
        bf_ = es(nc.sbuf_tensor("f4_bf", [8, 1024], BF16))
        bi_ = es(nc.sbuf_tensor("f4_bi", [8, 1024], BF16))
        rw2 = es(nc.sbuf_tensor("f4_rw2", [128, 8, 8], BF16))
        rb1 = es(nc.sbuf_tensor("f4_rb1", [128, 8], F32))
        b2e = es(nc.sbuf_tensor("f4_b2e", [1, 8], BF16))
        alph = es(nc.sbuf_tensor("f4_al", [1, 1], F32))
        alphc = es(nc.sbuf_tensor("f4_alc", [128, 1], F32))
        Qs = es(nc.sbuf_tensor("f4_Q", [128, 8, 512], BF16))
        Ks = es(nc.sbuf_tensor("f4_K", [128, 8, 512], BF16))
        iQs = es(nc.sbuf_tensor("f4_iQ", [128, 8, 512], BF16))
        iKs = es(nc.sbuf_tensor("f4_iK", [128, 8, 512], BF16))
        xV = es(nc.sbuf_tensor("f4_xV", [128, 4, 512], BF16))
        cum = es(nc.sbuf_tensor("f4_cum", [128, 4, 512], BF16))
        prodw = es(nc.sbuf_tensor("f4_pw", [128, 4, 512], BF16))
        wrep = es(nc.sbuf_tensor("f4_wrep", [128, 4, 512], BF16))
        hB = es(nc.sbuf_tensor("f4_hB", [128, 8, 512], BF16))
        wTM = es(nc.sbuf_tensor("f4_wTM", [128, 4, 8], BF16))
        eW = es(nc.sbuf_tensor("f4_eW", [128, 4, 8], F32))
        sumr = es(nc.sbuf_tensor("f4_sum", [128, 1], F32))
        wTr = es(nc.sbuf_tensor("f4_wTr", [8, 512], BF16))
        comb = es(nc.sbuf_tensor("f4_comb", [128, 8, 512], BF16))
        ccar = es(nc.sbuf_tensor("f4_cc", [128, 2, 4, 1], F32))
        ones5 = es(nc.sbuf_tensor("f4_ones", [128, 512], F32))
        nvis = es(nc.sbuf_tensor("f4_nvi", [1, 512], F32))
        nvir = es(nc.sbuf_tensor("f4_nvir", [128, 512], F32))
        AA = es(nc.sbuf_tensor("f4_A", [128, 512], F32))
        xsl = es(nc.sbuf_tensor("f4_xsl", [128, 8, 512], F32))

        for key, dst, kk in [("l3_vfwd", vf, "vf"), ("l3_wfwd", wf, "wfw"),
                             ("l3_vinv", vi, "vi"), ("l3_winvexp", wi, "wiv")]:
            S.dma("gpsimd", lambda e, key=key, dst=dst: e.dma_start(out=dst[:, :, :],
                    in_=wap2(ctx, key).rearrange("(c p) o -> p c o", p=128)),
                  writes=[(kk,)], tag=f"f4l{kk}")
        for key, dst, kk in [("l3_ufwd", uf, "uf"), ("l3_uinv", ui, "ui")]:
            S.dma("gpsimd", lambda e, key=key, dst=dst: e.dma_start(out=dst[:, :, :],
                    in_=wap2(ctx, key).rearrange("(c p) o -> p c o", p=128)),
                  writes=[(kk,)], tag=f"f4l{kk}")
        S.dma("gpsimd", lambda e: e.dma_start(out=bf_[:, :], in_=wap2(ctx, "l3_bfwd")),
              writes=["bf"], tag="f4lbf")
        S.dma("gpsimd", lambda e: e.dma_start(out=bi_[:, :], in_=wap2(ctx, "l3_binv")),
              writes=["bi"], tag="f4lbi")
        S.dma("gpsimd", lambda e: e.dma_start(out=rw2[:, :, :],
                in_=wap2(ctx, "l3_rw2T").rearrange("(c p) o -> p c o", p=128)),
              writes=["rw2"], tag="f4lrw2")
        S.dma("sync", lambda e: e.dma_start(out=rb1[:, :], in_=wap2(ctx, "l3_rb1")),
              writes=["rb1"], tag="f4lrb1")
        S.dma("gpsimd", lambda e: e.dma_start(out=b2e[:, :],
                in_=wap2(ctx, "l3_rb2e").rearrange("(r c) -> r c", r=1)),
              writes=["b2e"], tag="f4lb2e")
        S.dma("sync", lambda e: e.dma_start(out=alph[:, :],
                in_=wap2(ctx, "l3_alpha").rearrange("(r c) -> r c", r=1)),
              writes=["alph"], tag="f4lal")
        S.op("tensor", lambda e: e.matmul(ctx.pF[:, 0:1], ctx.onesrow_f[:, :], alph[:, :],
                                          start=True, stop=True),
             reads=["alph", ("c", "onesrow_f")], writes=["pF"], tag="f4alm")
        S.op("vector", lambda e: e.tensor_copy(alphc[:, :], ctx.pF[:, 0:1]),
             reads=["pF"], writes=["alphc"], narrow=True, tag="f4ale")
        S.op("vector", lambda e: e.memset(ones5[:, :], 1.0), writes=["ones5"], tag="f4o5")
        S.op("vector", lambda e: e.memset(wcbq[0:2, :, :], 0.0), writes=["wcbz"], tag="f4wz")

        def projq(dst, dk, wbuf, wk_, cso, sg, fold, srckey, src):
            for oc in range(8):
                ps, pk = (ctx.pA, "pmA") if oc % 2 == 0 else (ctx.pB, "pmB")
                def mm(e, oc=oc, sg=sg, ps=ps):
                    rr = None
                    for ci in range(8):
                        rr = e.matmul(ps[:, :], wbuf[:, ci, oc * 128:(oc + 1) * 128],
                                      src[:, ci, :], start=(ci == 0), stop=(ci == 7) and not fold)
                    if fold:
                        rr = e.matmul(ps[:, :], wcbq[:, cso, oc * 128:(oc + 1) * 128],
                                      rowsNB_[:, sg * 512:(sg + 1) * 512], start=False, stop=True)
                    return rr
                rds = [srckey, wk_] + ([("wcbq", cso), "wcbz", ("rowN", sg), ("rowS", sg)] if fold else [])
                S.op("tensor", mm, reads=rds, writes=[pk], tag=f"f4p{dk}_{oc}_{sg}")
                if fold:
                    S.op("vector", lambda e, oc=oc, sg=sg, ps=ps: e.tensor_tensor(
                            dst[:, oc, :], ps[:, :], srep_[:, sg, :], ALU.mult),
                         reads=[pk, ("srep", sg)], writes=[(dk,)], tag=f"f4pe{dk}_{oc}_{sg}")
                else:
                    S.op("vector", lambda e, oc=oc, ps=ps: e.tensor_copy(dst[:, oc, :], ps[:, :]),
                         reads=[pk], writes=[(dk,)], tag=f"f4pe{dk}_{oc}_{sg}")

        def branch(br, QS, qk, KS, kk_, Vt, vk, Wt, wk_, Ut, uk, bt, bk_, sg):
            for src, srck, dstb, dk, mat, mk in [(QS, qk, xV, "xV", Vt, vk),
                                                 (KS, kk_, cum, "cum", Wt, wk_)]:
                for krc in range(4):
                    ps, pk = (ctx.pA, "pmA") if krc % 2 == 0 else (ctx.pB, "pmB")
                    def mm(e, krc=krc, src=src, mat=mat, ps=ps):
                        rr = None
                        for ci in range(8):
                            rr = e.matmul(ps[:, :], mat[:, ci, krc * 128:(krc + 1) * 128],
                                          src[:, ci, :], start=(ci == 0), stop=(ci == 7))
                        return rr
                    S.op("tensor", mm, reads=[srck, mk], writes=[pk], tag=f"f4x{br}{dk}_{krc}_{sg}")
                    S.op("vector", lambda e, krc=krc, dstb=dstb, ps=ps: e.tensor_copy(
                            dstb[:, krc, :], ps[:, :]),
                         reads=[pk], writes=[(dk, krc)], tag=f"f4xe{br}{dk}_{krc}_{sg}")
            for krc in range(4):
                def scn(e, krc=krc, sg=sg, br=br):
                    ini = 0.0 if sg == 0 else ccar[:, br, krc, :]
                    return e.tensor_tensor_scan(cum[:, krc, :], ones5[:, :], cum[:, krc, :], ini,
                                                ALU.mult, ALU.add)
                S.op("vector", scn, reads=[("cum", krc), "ones5", ("ccar", br, krc)],
                     writes=[("cum", krc)], tag=f"f4cs{br}_{krc}_{sg}")
                S.op("vector", lambda e, krc=krc, br=br: e.tensor_copy(
                        ccar[:, br, krc, :], cum[:, krc, 511:512]),
                     reads=[("cum", krc)], writes=[("ccar", br, krc)], narrow=True,
                     tag=f"f4csc{br}_{krc}_{sg}")
            for oc in range(8):
                ps, pk = (ctx.pC, "pmC") if oc % 2 == 0 else (ctx.pD, "pmD")
                def mmh(e, oc=oc, ps=ps):
                    rr = None
                    for ci in range(8):
                        rr = e.matmul(ps[:, :], wbufs[1][:, ci, oc * 128:(oc + 1) * 128],
                                      QS[:, ci, :], start=(ci == 0), stop=(ci == 7))
                    return rr
                S.op("tensor", mmh, reads=[qk, ("wl", "rw1")], writes=[pk],
                     tag=f"f4h{br}_{oc}_{sg}")
                S.op("scalar", lambda e, oc=oc, ps=ps: e.activation(
                        hB[:, oc, :], ps[:, :], AF.Gelu, bias=rb1[:, oc:oc + 1]),
                     reads=[pk, "rb1"], writes=[("hB",)], tag=f"f4he{br}_{oc}_{sg}")
            for t4 in range(4):
                def mml(e, t4=t4):
                    for ci in range(8):
                        e.matmul(ctx.pE[0:128, 0:8], hB[:, ci, t4 * 128:(t4 + 1) * 128],
                                 rw2[:, ci, :], start=(ci == 0), stop=False)
                    return e.matmul(ctx.pE[0:128, 0:8], ctx.onesrow_b[0:1, :], b2e[:, :],
                                    start=False, stop=True)
                S.op("tensor", mml, reads=[("hB",), "rw2", "b2e", ("c", "onesrow_b")],
                     writes=["pE"], tag=f"f4l{br}_{t4}_{sg}")
                S.op("scalar", lambda e, t4=t4: e.activation(eW[:, t4, :], ctx.pE[:, 0:8], AF.Exp),
                     reads=["pE"], writes=[("eW", t4)], narrow=True, tag=f"f4ex{br}_{t4}_{sg}")
                S.op("vector", lambda e, t4=t4: e.reduce_sum(sumr[:], eW[:, t4, :], AX.X),
                     reads=[("eW", t4)], writes=["sumr"], narrow=True, tag=f"f4sma{br}_{t4}_{sg}")
                S.op("vector", lambda e: e.reciprocal(sumr[:], sumr[:]),
                     reads=["sumr"], writes=["sumr"], narrow=True, tag=f"f4smb{br}_{t4}_{sg}")
                S.op("vector", lambda e, t4=t4: e.tensor_scalar(
                        wTM[:, t4, :], eW[:, t4, :], sumr[:], None, ALU.mult),
                     reads=[("eW", t4), "sumr"], writes=[("wTM", t4)], narrow=True,
                     tag=f"f4sm{br}_{t4}_{sg}")
            def tpw(e):
                rr = None
                for t4 in range(4):
                    rr = e.transpose(ctx.pTb[0:8, t4 * 128:(t4 + 1) * 128], wTM[:, t4, :],
                                     ctx.identb[:, :])
                return rr
            S.op("tensor", tpw, reads=[("wTM", 0), ("wTM", 1), ("wTM", 2), ("wTM", 3),
                                       ("c", "identb")], writes=["pTb"], tag=f"f4tw{br}_{sg}")
            S.op("vector", lambda e: e.tensor_copy(wTr[:, :], ctx.pTb[0:8, 0:512]),
                 reads=["pTb"], writes=["wTr"], tag=f"f4twe{br}_{sg}")
            for krc in range(4):
                S.op("tensor", lambda e, krc=krc: e.matmul(
                        ctx.pE[:, :], ctx.selk[:, krc * 128:(krc + 1) * 128], wTr[:, :],
                        start=True, stop=True),
                     reads=["wTr", ("c", "selk")], writes=["pE"], tag=f"f4wr{br}_{krc}_{sg}")
                S.op("vector", lambda e, krc=krc: e.tensor_copy(wrep[:, krc, :], ctx.pE[:, :]),
                     reads=["pE"], writes=[("wrep", krc)], tag=f"f4wre{br}_{krc}_{sg}")
                S.op("vector", lambda e, krc=krc: e.tensor_tensor(
                        AA[:, :], xV[:, krc, :], cum[:, krc, :], ALU.mult),
                     reads=[("xV", krc), ("cum", krc)], writes=["AAk"], tag=f"f4pwa{br}_{krc}_{sg}")
                S.op("vector", lambda e, krc=krc: e.tensor_tensor(
                        prodw[:, krc, :], AA[:, :], wrep[:, krc, :], ALU.mult),
                     reads=["AAk", ("wrep", krc)], writes=[("prodw", krc)],
                     tag=f"f4pw{br}_{krc}_{sg}")
            for oc in range(8):
                ps, pk = (ctx.pC, "pmC") if oc % 2 == 0 else (ctx.pD, "pmD")
                def mmo(e, oc=oc, Ut=Ut, bt=bt, ps=ps):
                    for krc in range(4):
                        e.matmul(ps[:, :], Ut[:, krc, oc * 128:(oc + 1) * 128],
                                 prodw[:, krc, :], start=(krc == 0), stop=False)
                    return e.matmul(ps[:, :], bt[:, oc * 128:(oc + 1) * 128], wTr[:, :],
                                    start=False, stop=True)
                S.op("tensor", mmo, reads=[("prodw", 0), ("prodw", 1), ("prodw", 2), ("prodw", 3),
                                           uk, bk_, "wTr"], writes=[pk], tag=f"f4o{br}_{oc}_{sg}")
                if br == 0:
                    S.op("vector", lambda e, oc=oc, ps=ps: e.tensor_tensor(
                            comb[:, oc, :], ps[:, :], nvir[:, :], ALU.mult),
                         reads=[pk, "nvir"], writes=[("comb",)], tag=f"f4oe{br}_{oc}_{sg}")
                else:
                    S.op("vector", lambda e, ps=ps: e.tensor_tensor(
                            AA[:, :], ps[:, :], nvir[:, :], ALU.mult),
                         reads=[pk, "nvir"], writes=["AAk"], tag=f"f4oa{br}_{oc}_{sg}")
                    S.op("vector", lambda e, oc=oc: e.scalar_tensor_tensor(
                            comb[:, oc, :], AA[:, :], alphc[:, :], comb[:, oc, :],
                            ALU.mult, ALU.add),
                         reads=["AAk", "alphc", ("comb",)], writes=[("comb",)],
                         tag=f"f4oe{br}_{oc}_{sg}")

        o_nvi = CF_OFFS["nvi"][0]
        for sg in range(8):
            sl = slice(sg * 512, (sg + 1) * 512)
            S.dma("gpsimd", lambda e, sl=sl: e.dma_start(out=xb[:, :, :], in_=xr[:, :, sl]),
                  reads=[("xb4",)], writes=[("xb4",)], tag=f"f4lxb{sg}")
            S.dma("sync", lambda e, sg=sg: e.dma_start(out=nvis[:, :],
                    in_=ctx.cfin[o_nvi + sg * 512:o_nvi + (sg + 1) * 512].rearrange(
                        "(r c) -> r c", r=1)),
                  reads=[("nvis",)], writes=[("nvis",)], tag=f"f4lnv{sg}")
            S.op("tensor", lambda e: e.matmul(ctx.pF[:, :], ctx.onesrow_f[:, :], nvis[:, :],
                                              start=True, stop=True),
                 reads=[("nvis",), ("c", "onesrow_f")], writes=["pF"], tag=f"f4nvm{sg}")
            S.op("vector", lambda e: e.tensor_copy(nvir[:, :], ctx.pF[:, :]),
                 reads=["pF"], writes=["nvir"], tag=f"f4nve{sg}")
            for wi_, (wkey, dst, dk, cso) in enumerate([("l3_wQT", Qs, "Qs", 0),
                                                        ("l3_wKT", Ks, "Ks", 1)]):
                wb = wbufs[0]
                S.dma("gpsimd", lambda e, wkey=wkey, wb=wb: e.dma_start(out=wb[:, :, :],
                        in_=wap2(ctx, wkey).rearrange("(c p) o -> p c o", p=128)),
                      reads=[("wl", "w0")], writes=[("wl", "w0")], tag=f"f4lw{wi_}_{sg}")
                if sg == 0:
                    def csf(e, wb=wb):
                        rr = None
                        for ci in range(8):
                            rr = e.matmul(ctx.pS[0:1, :], ctx.onescol_b[:],
                                          wb[:, ci, 0:512], start=(ci == 0), stop=False) if False                                 else e.matmul(ctx.pS[0:1, 0:1024][0:1, 0:512] if False else ctx.pS[0:1, 0:512],
                                              ctx.onescol_b[:], wb[:, ci, 0:512],
                                              start=(ci == 0), stop=(ci == 7))
                        return rr
                    def csf2(e, wb=wb):
                        rr = None
                        for ci in range(8):
                            rr = e.matmul(ctx.pF[0:1, 0:512], ctx.onescol_b[:], wb[:, ci, 512:1024],
                                          start=(ci == 0), stop=(ci == 7))
                        return rr
                    S.op("tensor", csf, reads=[("wl", "w0"), ("c", "onescol_b")], writes=["pS"],
                         tag=f"f4cs{wi_}a")
                    S.op("tensor", csf2, reads=[("wl", "w0"), ("c", "onescol_b")], writes=["pF"],
                         tag=f"f4cs{wi_}b")
                    S.op("vector", lambda e, cso=cso: e.tensor_copy(
                            wcbq[0:1, cso, 0:512], ctx.pS[0:1, 0:512]),
                         reads=["pS", "wcbz"], writes=[("wcbq", cso)], tag=f"f4cse{wi_}a")
                    S.op("vector", lambda e, cso=cso: e.tensor_copy(
                            wcbq[0:1, cso, 512:1024], ctx.pF[0:1, 0:512]),
                         reads=["pF", "wcbz"], writes=[("wcbq", cso)], tag=f"f4cse{wi_}b")
                projq(dst, dk, wb, ("wl", "w0"), cso, sg, True, ("xb4",), xb)
            S.dma("gpsimd", lambda e: e.dma_start(out=wbufs[0][:, :, :],
                    in_=wap2(ctx, "l3_winvT").rearrange("(c p) o -> p c o", p=128)),
                  reads=[("wl", "w0")], writes=[("wl", "w0")], tag=f"f4lwi_{sg}")
            projq(iQs, "iQs", wbufs[0], ("wl", "w0"), 0, sg, False, ("Qs",), Qs)
            projq(iKs, "iKs", wbufs[0], ("wl", "w0"), 0, sg, False, ("Ks",), Ks)
            S.dma("gpsimd", lambda e: e.dma_start(out=wbufs[1][:, :, :],
                    in_=wap2(ctx, "l3_rw1T").rearrange("(c p) o -> p c o", p=128)),
                  reads=[("wl", "rw1")], writes=[("wl", "rw1")], tag=f"f4lrw1_{sg}")
            branch(0, Qs, ("Qs",), Ks, ("Ks",), vf, ("vf",), wf, ("wfw",), uf, ("uf",),
                   bf_, "bf", sg)
            branch(1, iQs, ("iQs",), iKs, ("iKs",), vi, ("vi",), wi, ("wiv",), ui, ("ui",),
                   bi_, "bi", sg)
            S.dma("gpsimd", lambda e: e.dma_start(out=wbufs[0][:, :, :],
                    in_=wap2(ctx, "l3_wOT").rearrange("(c p) o -> p c o", p=128)),
                  reads=[("wl", "w0")], writes=[("wl", "w0")], tag=f"f4lwo_{sg}")
            S.dma("sync", lambda e, sl=sl: e.dma_start(out=xsl[:, :, :], in_=xr[:, :, sl]),
                  reads=[("xsl4",)], writes=[("xsl4",)], tag=f"f4lx{sg}")
            for oc in range(8):
                ps, pk = (ctx.pA, "pmA") if oc % 2 == 0 else (ctx.pB, "pmB")
                def mmo(e, oc=oc, ps=ps):
                    rr = None
                    for ci in range(8):
                        rr = e.matmul(ps[:, :], wbufs[0][:, ci, oc * 128:(oc + 1) * 128],
                                      comb[:, ci, :], start=(ci == 0), stop=(ci == 7))
                    return rr
                S.op("tensor", mmo, reads=[("comb",), ("wl", "w0")], writes=[pk],
                     tag=f"f4wo{oc}_{sg}")
                S.op("vector", lambda e, oc=oc, ps=ps: e.tensor_add(
                        xsl[:, oc, :], ps[:, :], xsl[:, oc, :]),
                     reads=[pk, ("xsl4",)], writes=[("xsl4",)], tag=f"f4woe{oc}_{sg}")
            S.dma("sync", lambda e, sl=sl: e.dma_start(out=xdr[:, :, sl], in_=xsl[:, :, :]),
                  reads=[("xsl4",)], writes=[("xout", sg), ("xsl4",)], tag=f"f4st{sg}")
        S.fence()

def _emit_devcopy(ctx, src):
    S = ctx.S
    for sg in range(8):
        S.dma("sync", lambda e, sg=sg: e.dma_start(
                out=ctx.xout[:, sg * 512:(sg + 1) * 512], in_=src[:, sg * 512:(sg + 1) * 512]),
              reads=[("xout", sg)], writes=[("xoutf", sg)], tag=f"dev{sg}")

def build_program(upto=8, only=None):
    nc = bass.Bass("TRN2", target_bir_lowering=False)
    ctx = _mk_ctx(nc)
    S = ctx.S
    _emit_preamble(ctx)
    S.fence()
    plan = [
        (emit_attn, 0), (emit_l0ffn, None), (emit_attn, 1), (emit_l1ffn, None),
        (emit_attn, 2), (emit_l2ffn, None), (emit_l3attn, None), (emit_l3ffn, None),
    ]
    seq = list(range(upto)) if only is None else list(only)
    src = ctx.xin
    nxt = [ctx.xB, ctx.xA]
    full = (len(seq) == 8)
    for j, i in enumerate(seq):
        fn, L = plan[i]
        dst = ctx.xout if (full and j == len(seq) - 1) else nxt[j % 2]
        if L is None:
            fn(ctx, src, dst)
        else:
            fn(ctx, L, src, dst)
        src = dst
    if not full:
        _emit_devcopy(ctx, src)
    S.op("sync", lambda e: e.nop(nofuse=True),
         reads=[("xout" if full else "xoutf", sg) for sg in range(8)], tag="final")
    S.emit()
    return nc

def kernel(**inputs):
    blob = _prep_blob(inputs)
    x = np.asarray(inputs["L0_attn_hidden_states"], np.float32)
    nc = build_program(upto=8)
    in_maps = []
    for core in range(NCORES):
        in_maps.append({
            "xin": np.ascontiguousarray(x[core].T),
            "wblob": blob,
            "cbin": CB_BLOB,
            "cfin": CF_BLOB,
        })
    res = run_bass_kernel_spmd(nc, in_maps, core_ids=list(range(NCORES)))
    out = np.stack([np.asarray(r["xout"], np.float32).T for r in res.results])
    return np.ascontiguousarray(out)


# version 07
# speedup vs baseline: 1.1654x; 1.1654x over previous
import numpy as np
import ml_dtypes
import concourse.bass as bass
import concourse.mybir as mybir
from concourse.bass_utils import run_bass_kernel_spmd

F32 = mybir.dt.float32
BF16 = mybir.dt.bfloat16
I32 = mybir.dt.int32
AF = mybir.ActivationFunctionType
ALU = mybir.AluOpType
AX = mybir.AxisListType

T = 4096
D = 1024
NCORES = 8
EPS = 1e-5
BF = ml_dtypes.bfloat16
RHD = 0.49999997

WSPEC = [
    ("l0_wqT", (D, D)), ("l0_wkT", (D, D)), ("l0_wvT", (D, D)), ("l0_woT", (D, D)),
    ("l0_bq", (D,)), ("l0_bk", (D,)), ("l0_bv", (D,)), ("l0_bo", (128, 8)),
    ("l0_wgate", (2816, D)), ("l0_wup", (2816, D)), ("l0_wdown", (D, 2816)),
    ("l1_wqT", (D, D)), ("l1_wkT", (D, D)), ("l1_wvT", (D, D)), ("l1_wpT", (D, D)),
    ("l1_bqT", (D, 8)), ("l1_bkT", (D, 8)), ("l1_bvT", (D, 8)),
    ("l1_aqT", (8, D)), ("l1_akT", (8, D)), ("l1_avT", (8, D)),
    ("l1_w1T", (D, 4096)), ("l1_w2T", (D, 4096)),
    ("l1_h1T", (4096, 4096)), ("l1_h2T", (4096, 4096)), ("l1_w3T", (4096, D)),
    ("l1_s1", (128, 32)), ("l1_s2", (128, 32)), ("l1_h1s", (128, 32)), ("l1_h2s", (128, 32)),
    ("l1_s3", (128, 8)),
    ("l2_wqT", (D, D)), ("l2_wkT", (D, D)), ("l2_wvT", (D, D)),
    ("l2_wrT", (D, D)), ("l2_wkT2", (D, D)), ("l2_wvT2", (D, D)),
    ("l2_mur", (128, 8)), ("l2_muk", (128, 8)),
    ("l3_winT", (D, 4096)), ("l3_convw", (128, 16, 4)), ("l3_convb", (128, 16)),
    ("l3_wxcT", (2048, 16)), ("l3_blogT", (16, 2048)),
    ("l3_woutT", (2048, D)), ("l3_lng", (128, 8)), ("l3_lnb", (128, 8)),
    ("l3_wQT", (D, D)), ("l3_wKT", (D, D)), ("l3_wOT", (D, D)), ("l3_winvT", (D, D)),
    ("l3_vfwd", (D, 512)), ("l3_wfwd", (D, 512)), ("l3_ufwd", (512, D)), ("l3_bfwd", (8, D)),
    ("l3_vinv", (D, 512)), ("l3_winvexp", (D, 512)), ("l3_uinv", (512, D)), ("l3_binv", (8, D)),
    ("l3_rw1T", (D, D)), ("l3_rb1", (128, 8)), ("l3_rw2T", (D, 8)), ("l3_rb2e", (8,)),
    ("l3_alpha", (1,)),
]

def _blob_offsets():
    offs, cur = {}, 0
    for key, shape in WSPEC:
        n = int(np.prod(shape))
        offs[key] = (cur, shape)
        cur += ((n + 511) // 512) * 512
    return offs, cur

BLOB_OFFS, BLOB_N = _blob_offsets()

def _prep_blob(inp):
    t = lambda a: np.ascontiguousarray(np.asarray(a).T.astype(np.float32))
    c = lambda a: np.ascontiguousarray(np.asarray(a).astype(np.float32))
    def pcol(a):
        a = np.asarray(a).astype(np.float32).reshape(-1, 128)
        return np.ascontiguousarray(a.T)
    d = {}
    d["l0_wqT"] = t(inp["L0_attn_Wq"]); d["l0_wkT"] = t(inp["L0_attn_Wk"])
    d["l0_wvT"] = t(inp["L0_attn_Wv"]); d["l0_woT"] = t(inp["L0_attn_Wo"])
    d["l0_bq"] = c(inp["L0_attn_bq"]); d["l0_bk"] = c(inp["L0_attn_bk"])
    d["l0_bv"] = c(inp["L0_attn_bv"]); d["l0_bo"] = pcol(inp["L0_attn_bo"])
    d["l0_wgate"] = c(inp["L0_ffn_w_gate"]); d["l0_wup"] = c(inp["L0_ffn_w_up"])
    d["l0_wdown"] = c(inp["L0_ffn_w_down"])
    d["l1_wqT"] = t(inp["L1_attn_Wq"]); d["l1_wkT"] = t(inp["L1_attn_Wk"])
    d["l1_wvT"] = t(inp["L1_attn_Wv"]); d["l1_wpT"] = t(inp["L1_attn_Wp"])
    d["l1_bqT"] = t(inp["L1_attn_Bq"]); d["l1_bkT"] = t(inp["L1_attn_Bk"])
    d["l1_bvT"] = t(inp["L1_attn_Bv"])
    d["l1_aqT"] = t(inp["L1_attn_Aq"]); d["l1_akT"] = t(inp["L1_attn_Ak"])
    d["l1_avT"] = t(inp["L1_attn_Av"])
    d["l1_w1T"] = t(inp["L1_ffn_w1_w"]); d["l1_w2T"] = t(inp["L1_ffn_w2_w"])
    d["l1_h1T"] = t(inp["L1_ffn_h1_w"]); d["l1_h2T"] = t(inp["L1_ffn_h2_w"])
    d["l1_w3T"] = t(inp["L1_ffn_w3_w"])
    d["l1_s1"] = pcol(inp["L1_ffn_w1_s"]); d["l1_s2"] = pcol(inp["L1_ffn_w2_s"])
    d["l1_h1s"] = pcol(inp["L1_ffn_h1_s"]); d["l1_h2s"] = pcol(inp["L1_ffn_h2_s"])
    d["l1_s3"] = pcol(inp["L1_ffn_w3_s"])
    wqkv = np.asarray(inp["L2_attn_W_qkv"])
    idx = np.arange(16)[:, None] * 192 + np.arange(64)[None, :]
    d["l2_wqT"] = t(wqkv[(idx + 0).reshape(-1)])
    d["l2_wkT"] = t(wqkv[(idx + 64).reshape(-1)])
    d["l2_wvT"] = t(wqkv[(idx + 128).reshape(-1)])
    d["l2_wrT"] = t(inp["L2_ffn_W_r"]); d["l2_wkT2"] = t(inp["L2_ffn_W_k"])
    d["l2_wvT2"] = t(inp["L2_ffn_W_v"])
    d["l2_mur"] = pcol(inp["L2_ffn_mu_r"]); d["l2_muk"] = pcol(inp["L2_ffn_mu_k"])
    d["l3_winT"] = t(inp["L3_attn_W_in"])
    d["l3_convw"] = np.ascontiguousarray(np.asarray(inp["L3_attn_conv_w"]).astype(np.float32).reshape(16, 128, 4).transpose(1, 0, 2))
    d["l3_convb"] = pcol(inp["L3_attn_conv_b"])
    d["l3_wxcT"] = t(np.asarray(inp["L3_attn_W_x"])[16:32])
    d["l3_blogT"] = t(inp["L3_attn_B_log"])
    d["l3_woutT"] = t(inp["L3_attn_W_out"])
    d["l3_lng"] = pcol(inp["L3_attn_ln_g"]); d["l3_lnb"] = pcol(inp["L3_attn_ln_b"])
    d["l3_wQT"] = t(inp["L3_ffn_W_Q"]); d["l3_wKT"] = t(inp["L3_ffn_W_K"])
    d["l3_wOT"] = t(inp["L3_ffn_W_O"]); d["l3_winvT"] = t(inp["L3_ffn_W_inv"])
    flat_d_kr = lambda a: c(np.asarray(a).transpose(1, 0, 2).reshape(D, 512))
    flat_kr_d = lambda a: c(np.asarray(a).transpose(0, 2, 1).reshape(512, D))
    d["l3_vfwd"] = flat_d_kr(inp["L3_ffn_V_fwd"])
    d["l3_wfwd"] = flat_d_kr(inp["L3_ffn_W_fwd"])
    d["l3_ufwd"] = flat_kr_d(inp["L3_ffn_U_fwd"])
    d["l3_bfwd"] = c(inp["L3_ffn_b_fwd"])
    d["l3_vinv"] = flat_d_kr(inp["L3_ffn_W_inv_exp"])
    d["l3_winvexp"] = flat_d_kr(inp["L3_ffn_V_inv"])
    d["l3_uinv"] = flat_kr_d(inp["L3_ffn_U_inv"])
    d["l3_binv"] = c(inp["L3_ffn_b_inv"])
    d["l3_rw1T"] = t(inp["L3_ffn_router_w1"]); d["l3_rb1"] = pcol(inp["L3_ffn_router_b1"])
    d["l3_rw2T"] = t(inp["L3_ffn_router_w2"])
    d["l3_rb2e"] = c(np.asarray(inp["L3_ffn_router_b2"]) + np.asarray(inp["L3_ffn_expert_bias"]))
    d["l3_alpha"] = c(np.asarray(inp["L3_ffn_alpha_bi"]).reshape(1))
    blob = np.zeros(BLOB_N, np.float32)
    for key, (off, shape) in BLOB_OFFS.items():
        a = d[key]
        assert a.shape == tuple(shape), (key, a.shape, shape)
        blob[off:off + a.size] = a.reshape(-1)
    return blob

def _prep_consts():
    import math
    p = np.arange(128)
    ident = np.eye(128, dtype=np.float32)
    stair = (p[:, None] // 32 <= p[None, :] // 32).astype(np.float32)
    tri = (p[:, None] <= p[None, :]).astype(np.float32)
    blk = np.arange(T) // 32
    U = (np.arange(128)[:, None] > blk[None, :]).astype(np.float32)
    blockind = (p[:, None] // 32 == np.arange(4)[None, :]).astype(np.float32)
    start = 2.0 ** (-2.0 ** (-(math.log2(16) - 3)))
    slopes = np.array([start * start ** i for i in range(16)], np.float32)
    alibi = np.zeros((128, 512), np.float32)
    for kt in range(32):
        for h in range(16):
            alibi[:, kt * 16 + h] = -slopes[h] * (128 * kt + p)
    nvi = (1.0 / np.arange(1, T + 1, dtype=np.float64)).astype(np.float32)[None, :]
    kr = np.arange(512)
    selk = (kr[None, :] // 64 == np.arange(8)[:, None]).astype(np.float32)
    half0 = np.zeros((1, 128), np.float32); half0[0, 0:64] = 1.0
    half1 = np.zeros((1, 128), np.float32); half1[0, 64:128] = 1.0
    cbf = {
        "identb": ident, "stair": stair, "tri": tri, "U": U, "blockind": blockind,
        "onescol_b": np.ones((128, 1), np.float32), "selk": selk,
        "half0": half0, "half1": half1,
    }
    cf = {
        "identf": ident, "alibi": alibi, "nvi": nvi,
        "onesrow_f": np.ones((1, 128), np.float32),
        "eps1": np.full((1, 1), EPS, np.float32),
        "selrows": np.array([[-1.0, 0.0, 0.0, 1.0]], np.float32),
    }
    def pack(dd, dt):
        offs, cur = {}, 0
        for k, a in dd.items():
            offs[k] = (cur, a.shape)
            cur += ((a.size + 511) // 512) * 512
        blob = np.zeros(cur, np.float32)
        for k, (o, s) in offs.items():
            blob[o:o + int(np.prod(s))] = dd[k].reshape(-1)
        return blob.astype(dt), offs
    bblob, bo = pack(cbf, np.float32)
    fblob, fo = pack(cf, np.float32)
    return bblob, bo, fblob, fo

CB_BLOB, CB_OFFS, CF_BLOB, CF_OFFS = _prep_consts()

ENGINES = ["tensor", "vector", "scalar", "gpsimd", "sync"]

class _Op:
    __slots__ = ("eng", "fn", "deps", "idx", "need_inc", "inc_seq", "is_dma",
                 "narrow", "tag", "sem_idx")
    def __init__(self, eng, fn, is_dma, narrow, tag):
        self.eng = eng
        self.fn = fn
        self.deps = []
        self.idx = 0
        self.need_inc = is_dma
        self.inc_seq = 0
        self.is_dma = is_dma
        self.narrow = narrow
        self.tag = tag

class Sched:
    def __init__(self, nc):
        self.nc = nc
        self.ops = {e: [] for e in ENGINES}
        self.last_writer = {}
        self.readers = {}
        self.all = []

    def _record(self, eng, fn, reads, writes, is_dma, narrow, tag):
        op = _Op(eng, fn, is_dma, narrow, tag)
        op.idx = len(self.ops[eng])
        writes = list(writes) + [("__last__", eng)]
        deps = []
        for k in reads:
            w = self.last_writer.get(k)
            if w is not None:
                deps.append(w)
        for k in writes:
            w = self.last_writer.get(k)
            if w is not None:
                deps.append(w)
            deps.extend(self.readers.get(k, ()))
        op.deps = list({id(d): d for d in deps if d is not op}.values())
        for k in writes:
            self.last_writer[k] = op
            self.readers[k] = []
        for k in reads:
            self.readers.setdefault(k, []).append(op)
        self.ops[eng].append(op)
        self.all.append(op)
        return op

    def op(self, eng, fn, reads=(), writes=(), narrow=False, tag=""):
        return self._record(eng, fn, reads, writes, False, narrow, tag)

    def dma(self, eng, fn, reads=(), writes=(), tag=""):
        return self._record(eng, fn, reads, writes, True, False, tag)

    def fence(self):
        for e in ENGINES:
            self.op(e, lambda en: en.nop(nofuse=True),
                    reads=[("__last__", x) for x in ENGINES if x != e], tag="fence")

    def emit(self):
        nc = self.nc
        for op in self.all:
            for d in op.deps:
                d.need_inc = True
        NDSEM = 10
        cnt = {e: 0 for e in ENGINES}
        dcnt = {e: 0 for e in ENGINES}
        for e in ENGINES:
            for op in self.ops[e]:
                if op.is_dma:
                    n = dcnt[e]
                    dcnt[e] += 1
                    op.inc_seq = (n // NDSEM) + 1
                    op.sem_idx = n % NDSEM
                elif op.need_inc:
                    cnt[e] += 1
                    op.inc_seq = cnt[e]
        sems = {}
        dsems = {}
        for e in ENGINES:
            sems[e] = nc.semaphore(f"se_{e}").__enter__()
            dsems[e] = [nc.semaphore(f"sd_{e}{i}").__enter__() for i in range(NDSEM)]
        waited = {e: {} for e in ENGINES}

        def body(e_handle, ename):
            wd = waited[ename]
            for op in self.ops[ename]:
                wants = {}
                for d in op.deps:
                    if d.is_dma:
                        key = ("d", d.eng, d.sem_idx)
                        val = 16 * d.inc_seq
                    else:
                        key = ("c", d.eng)
                        val = d.inc_seq
                    if wants.get(key, 0) < val:
                        wants[key] = val
                if op.is_dma and op.inc_seq > 1:
                    key = ("d", ename, op.sem_idx)
                    val = 16 * (op.inc_seq - 1)
                    if wants.get(key, 0) < val:
                        wants[key] = val
                for key, val in sorted(wants.items()):
                    if wd.get(key, 0) >= val:
                        continue
                    sem = dsems[key[1]][key[2]] if key[0] == "d" else sems[key[1]]
                    e_handle.wait_ge(sem, val)
                    wd[key] = val
                inst = op.fn(e_handle)
                if op.need_inc:
                    if op.is_dma:
                        inst.then_inc(dsems[ename][op.sem_idx], 16)
                    else:
                        if inst is None:
                            inst = e_handle.nop(nofuse=True)
                        inst.then_inc(sems[ename], 1)

        with nc.Block("main") as blk:
            blk.tensor(lambda e: body(e, "tensor"))
            blk.vector(lambda e: body(e, "vector"))
            blk.scalar(lambda e: body(e, "scalar"))
            blk.gpsimd(lambda e: body(e, "gpsimd"))
            blk.sync(lambda e: body(e, "sync"))

class Ctx:
    pass

def _mk_ctx(nc):
    ctx = Ctx()
    ctx.nc = nc
    ctx.S = Sched(nc)
    ctx.xin = nc.dram_tensor("xin", [D, T], F32, kind="ExternalInput")
    ctx.wblob = nc.dram_tensor("wblob", [BLOB_N], F32, kind="ExternalInput")
    ctx.cbin = nc.dram_tensor("cbin", [CB_BLOB.size], F32, kind="ExternalInput")
    ctx.cfin = nc.dram_tensor("cfin", [CF_BLOB.size], F32, kind="ExternalInput")
    ctx.xout = nc.dram_tensor("xout", [D, T], F32, kind="ExternalOutput")
    ctx.xA = nc.dram_tensor("xA", [D, T], F32)
    ctx.xB = nc.dram_tensor("xB", [D, T], F32)
    ctx.ctxd = nc.dram_tensor("ctxd", [8, 8, 128, 512], BF16)
    ctx.hD = nc.dram_tensor("hD", [32, 8, 128, 512], BF16)
    ctx.g1D = nc.dram_tensor("g1D", [32, 8, 128, 512], BF16)
    ctx.g2D = nc.dram_tensor("g2D", [32, 8, 128, 512], BF16)

    ctx.live = []
    def keep(cm):
        v = cm.__enter__()
        ctx.live.append(cm)
        return v
    ctx.keep = keep

    ctx.identb = keep(nc.sbuf_tensor("identb", [128, 128], BF16))
    ctx.stair = keep(nc.sbuf_tensor("stair", [128, 128], BF16))
    ctx.tri = keep(nc.sbuf_tensor("tri", [128, 128], BF16))
    ctx.blockind = keep(nc.sbuf_tensor("blockind", [128, 4], BF16))
    ctx.selk = keep(nc.sbuf_tensor("selk", [8, 512], BF16))
    ctx.half0 = keep(nc.sbuf_tensor("half0", [1, 128], BF16))
    ctx.half1 = keep(nc.sbuf_tensor("half1", [1, 128], BF16))
    ctx.onescol_b = keep(nc.sbuf_tensor("onescol_b", [128, 1], BF16))
    ctx.identf = keep(nc.sbuf_tensor("identf", [128, 128], F32))
    ctx.alibi = keep(nc.sbuf_tensor("alibi", [128, 512], F32))
    ctx.onesrow_f = keep(nc.sbuf_tensor("onesrow_f", [1, 128], F32))
    ctx.selrows = keep(nc.sbuf_tensor("selrows", [1, 4], F32))
    ctx.onesrow_b = keep(nc.sbuf_tensor("onesrow_b", [1, 128], BF16))
    ctx.ones64_b = keep(nc.sbuf_tensor("ones64_b", [65, 128], BF16))
    ctx.eps1 = keep(nc.sbuf_tensor("eps1", [1, 1], F32))
    ctx.pA = keep(nc.psum_tensor("pA", [128, 512], F32))
    ctx.pB = keep(nc.psum_tensor("pB", [128, 512], F32))
    ctx.pC = keep(nc.psum_tensor("pC", [128, 512], F32))
    ctx.pD = keep(nc.psum_tensor("pD", [128, 512], F32))
    ctx.pE = keep(nc.psum_tensor("pE", [128, 512], F32))
    ctx.pF = keep(nc.psum_tensor("pF", [128, 512], F32))
    ctx.pS = keep(nc.psum_tensor("pS", [128, 512], F32))
    ctx.pTb = keep(nc.psum_tensor("pTb", [128, 1024], BF16))
    return ctx

def wap2(ctx, key):
    off, shape = BLOB_OFFS[key]
    n = int(np.prod(shape))
    a = ctx.wblob[off:off + n]
    if len(shape) == 2:
        a = a.rearrange("(r c) -> r c", c=shape[1])
    elif len(shape) == 3:
        a = a.rearrange("(r c k) -> r c k", c=shape[1], k=shape[2])
    return a

def _emit_preamble(ctx):
    S = ctx.S
    def ld(dst, src, offs, key):
        o, shape = offs[key]
        rows = shape[0] if len(shape) == 2 else 1
        cols = shape[1] if len(shape) == 2 else shape[0]
        ap = src[o:o + rows * cols].rearrange("(r c) -> r c", c=cols)
        q = "gpsimd" if dst.ap().dtype == BF16 else "sync"
        S.dma(q, lambda e, d=dst, a=ap, r=rows, cc=cols: e.dma_start(out=d[:r, :cc], in_=a),
              reads=(), writes=[("c", key)], tag=f"ld_{key}")
    ld(ctx.identb, ctx.cbin, CB_OFFS, "identb")
    ld(ctx.stair, ctx.cbin, CB_OFFS, "stair")
    ld(ctx.tri, ctx.cbin, CB_OFFS, "tri")
    ld(ctx.blockind, ctx.cbin, CB_OFFS, "blockind")
    ld(ctx.selk, ctx.cbin, CB_OFFS, "selk")
    ld(ctx.half0, ctx.cbin, CB_OFFS, "half0")
    ld(ctx.half1, ctx.cbin, CB_OFFS, "half1")
    ld(ctx.onescol_b, ctx.cbin, CB_OFFS, "onescol_b")
    ld(ctx.identf, ctx.cfin, CF_OFFS, "identf")
    ld(ctx.alibi, ctx.cfin, CF_OFFS, "alibi")
    ld(ctx.onesrow_f, ctx.cfin, CF_OFFS, "onesrow_f")
    ld(ctx.selrows, ctx.cfin, CF_OFFS, "selrows")
    ld(ctx.eps1, ctx.cfin, CF_OFFS, "eps1")
    S.op("vector", lambda e: e.tensor_copy(ctx.onesrow_b[:], ctx.onesrow_f[:]),
         reads=[("c", "onesrow_f")], writes=[("c", "onesrow_b")])
    S.op("vector", lambda e: e.memset(ctx.ones64_b[64:65, :], 1.0),
         writes=[("c", "ones64")])

def emit_stats(ctx, xsrc, pes, want_col=False, tag="", xb_dst=None):
    import contextlib
    nc, S = ctx.nc, ctx.S
    ctx.rowsNB = pes(nc.sbuf_tensor(f"rowsNB{tag}", [2, T], BF16))
    ctx.srep = pes(nc.sbuf_tensor(f"srep{tag}", [128, 8, 512], BF16))
    if want_col:
        ctx.rstd_col = pes(nc.sbuf_tensor(f"rstdc{tag}", [128, 32], F32))
    rowsNB_, srep_ = ctx.rowsNB, ctx.srep
    rstdcol_ = ctx.rstd_col if want_col else None
    with contextlib.ExitStack() as est:
        es = est.enter_context
        xsl = es(nc.sbuf_tensor(f"st_x{tag}", [128, 2, 8, 512], F32))
        x2t = es(nc.sbuf_tensor(f"st_x2{tag}", [128, 2, 8, 512], BF16))
        xbt = es(nc.sbuf_tensor(f"st_xb{tag}", [128, 2, 8, 512], BF16))
        var = es(nc.sbuf_tensor(f"st_var{tag}", [1, 512], F32))
        musq = es(nc.sbuf_tensor(f"st_musq{tag}", [1, 512], F32))
        sig = es(nc.sbuf_tensor(f"st_sig{tag}", [1, 512], F32))
        murow = es(nc.sbuf_tensor(f"st_mu{tag}", [1, 512], F32))
        rstd = es(nc.sbuf_tensor(f"st_rstd{tag}", [1, 512], F32))
        xr = xsrc.rearrange("(c p) t -> p c t", p=128)
        for s in range(8):
            b = s % 2
            sl = slice(s * 512, (s + 1) * 512)
            xbap = xbt[:, b] if xb_dst is None else xb_dst[:, :, sl]
            S.dma("sync", lambda e, b=b, sl=sl: e.dma_start(out=xsl[:, b], in_=xr[:, :, sl]),
                  reads=[("xslfree", b)], writes=[("xsl", b)], tag=f"st_ld{s}")
            S.op("vector", lambda e, b=b, xbap=xbap: e.tensor_copy(xbap, xsl[:, b]),
                 reads=[("xsl", b)], writes=[("stxb", b), ("xbw", s)], tag=f"st_cast{s}")
            S.op("scalar", lambda e, b=b: e.activation(
                    x2t[:, b].rearrange("p c t -> p (c t)"), xsl[:, b].rearrange("p c t -> p (c t)"),
                    AF.Square),
                 reads=[("xsl", b)], writes=[("x2t", b), ("xslfree", b)], tag=f"st_sq{s}")
            def mm_sum(e, src_ap, ps):
                r = None
                for ci in range(8):
                    r = e.matmul(ps[0:1, :], ctx.onescol_b[:], src_ap[:, ci], start=(ci == 0), stop=(ci == 7))
                return r
            S.op("tensor", lambda e, b=b, xbap=xbap: mm_sum(e, xbap, ctx.pA),
                 reads=[("stxb", b), ("c", "onescol_b")], writes=["pA"], tag=f"st_mmx{s}")
            S.op("tensor", lambda e, b=b: mm_sum(e, x2t[:, b], ctx.pB),
                 reads=[("x2t", b), ("c", "onescol_b")], writes=["pB"], tag=f"st_mmx2{s}")
            S.op("vector", lambda e: e.tensor_scalar(murow[:], ctx.pA[0:1, :], 1.0 / D, None, ALU.mult),
                 reads=["pA"], writes=["mu"], tag=f"st_mu{s}")
            S.op("scalar", lambda e: e.activation(musq[:], murow[:], AF.Square),
                 reads=["mu"], writes=["musq"], tag=f"st_musq{s}")
            S.op("vector", lambda e: e.scalar_tensor_tensor(var[:], ctx.pB[0:1, :], 1.0 / D, musq[:],
                                                            ALU.mult, ALU.subtract),
                 reads=["pB", "musq"], writes=["var"], tag=f"st_var{s}")
            S.op("scalar", lambda e: e.activation(sig[:], var[:], AF.Sqrt, bias=ctx.eps1[0:1, :]),
                 reads=["var", ("c", "eps1")], writes=["sig"], tag=f"st_sig{s}")
            S.op("vector", lambda e: e.reciprocal(rstd[:], sig[:]),
                 reads=["sig"], writes=["rstd"], tag=f"st_rstd{s}")
            def rowmm(e):
                e.matmul(ctx.pE[0:2, :], ctx.selrows[0:1, 0:2], murow[:], start=True, stop=False)
                return e.matmul(ctx.pE[0:2, :], ctx.selrows[0:1, 2:4], sig[:], start=False, stop=True)
            S.op("tensor", rowmm, reads=["mu", "sig", ("c", "selrows")], writes=["pE"],
                 tag=f"st_rowmm{s}")
            S.op("vector", lambda e, sl=sl: e.tensor_copy(rowsNB_[0:2, sl], ctx.pE[0:2, :]),
                 reads=["pE"], writes=[("rowN", s), ("rowS", s)], tag=f"st_rows{s}")
            S.op("tensor", lambda e: e.matmul(ctx.pS[:, :], ctx.onesrow_f[:],
                                              rstd[:], start=True, stop=True),
                 reads=["rstd", ("c", "onesrow_f")], writes=["pS"], tag=f"st_srepmm{s}")
            S.op("vector", lambda e, s=s: e.tensor_copy(srep_[:, s, :], ctx.pS[:, :]),
                 reads=["pS"], writes=[("srep", s)], tag=f"st_srepev{s}")
            if want_col:
                def tp(e):
                    r = None
                    for j in range(4):
                        r = e.transpose(ctx.pF[:, j:j + 1], rstd[:, j * 128:(j + 1) * 128],
                                        ctx.identf[0:1, 0:1])
                    return r
                S.op("tensor", tp, reads=["rstd", ("c", "identf")], writes=["pF"],
                     tag=f"st_coltp{s}")
                S.op("vector", lambda e, s=s: e.tensor_copy(rstdcol_[:, s * 4:s * 4 + 4],
                                                            ctx.pF[:, 0:4]),
                     reads=["pF"], writes=[("rstd_col", s)], narrow=True, tag=f"st_colev{s}")
        S.fence()

def emit_attn(ctx, L, xsrc, xdst):
    import contextlib
    nc, S = ctx.nc, ctx.S
    wq, wk, wv = [["l0_wqT", "l0_wkT", "l0_wvT"], ["l1_wqT", "l1_wkT", "l1_wvT"],
                  ["l2_wqT", "l2_wkT", "l2_wvT"]][L]
    has_bias = (L == 0)
    has_lora = (L == 1)
    xr = xsrc.rearrange("(c p) t -> p c t", p=128)

    with contextlib.ExitStack() as est:
        es = est.enter_context
        emit_stats(ctx, xsrc, es, want_col=True, tag=f"a{L}")
        rowsNB_, srep_ = ctx.rowsNB, ctx.srep
        rstdcol_ = getattr(ctx, 'rstd_col', None)
        K_sb = es(nc.sbuf_tensor(f"K_sb{L}", [128, 8, T], BF16))
        Vaug = es(nc.sbuf_tensor(f"Vaug{L}", [128, 32, 16, 66], BF16))
        Q_sb = es(nc.sbuf_tensor(f"Q_sb{L}", [128, 8, 512], BF16))
        xb = es(nc.sbuf_tensor(f"at_xb{L}", [128, 8, 512], BF16))
        wt = es(nc.sbuf_tensor(f"at_wt{L}", [128, 2, 8, 256], BF16))
        wcb = es(nc.sbuf_tensor(f"at_wcb{L}", [2, 3, 1024], BF16))
        if has_lora:
            wbq = es(nc.sbuf_tensor(f"at_wbq{L}", [128, 3, 8, 8], BF16))
            waq = es(nc.sbuf_tensor(f"at_waq{L}", [8, 3, 1024], BF16))
            wcbl = es(nc.sbuf_tensor(f"at_wcbl{L}", [2, 3, 8], BF16))
            mids = es(nc.sbuf_tensor(f"at_mids{L}", [8, 3, 512], BF16))
        P_sb = es(nc.sbuf_tensor(f"at_P{L}", [128, 8, 512], BF16))
        rec65 = es(nc.sbuf_tensor(f"at_rec65{L}", [65, 2, 512], BF16))
        rzrep = es(nc.sbuf_tensor(f"at_rzrep{L}", [128, 512], BF16))
        ctxn = es(nc.sbuf_tensor(f"at_ctxn{L}", [128, 2, 512], BF16))
        hsel = es(nc.sbuf_tensor(f"at_hsel{L}", [65, 2, 128], BF16))
        S.op("vector", lambda e: e.memset(hsel[64:65, :, :], 0.0), writes=["hselz"],
             narrow=True, tag="hselz")
        S.op("vector", lambda e: e.memset(hsel[64:65, 0, 0:64], 1.0),
             reads=["hselz"], writes=[("hsel", 0)], narrow=True, tag="hsel0")
        S.op("vector", lambda e: e.memset(hsel[64:65, 1, 64:128], 1.0),
             reads=["hselz"], writes=[("hsel", 1)], narrow=True, tag="hsel1")
        if L == 0:
            vblkT = es(nc.sbuf_tensor(f"at_vblkT{L}", [66, 128], BF16))
            vblk = es(nc.sbuf_tensor(f"at_vblk{L}", [128, 16, 66], BF16))
            Ucst = es(nc.sbuf_tensor(f"at_U{L}", [128, T], BF16))
            o_U = CB_OFFS["U"][0]
            S.dma("gpsimd", lambda e: e.dma_start(
                    out=Ucst[:, :], in_=ctx.cbin[o_U:o_U + 128 * T].rearrange("(r c) -> r c", c=T)),
                  writes=[("c", "U")], tag="ldU")

        S.op("vector", lambda e: e.memset(wcb[0:2, :, :], 0.0),
             writes=["wcbz"] + [("wcb_b", i) for i in range(3)], tag="zb")
        if has_bias:
            for i, bk in enumerate(["l0_bq", "l0_bk", "l0_bv"]):
                S.dma("gpsimd", lambda e, i=i, bk=bk: e.dma_start(
                        out=wcb[1:2, i, :], in_=wap2(ctx, bk).rearrange("(r c) -> r c", r=1)),
                      reads=["wcbz"], writes=[("wcb_b", i)], tag=f"ldb{i}")
        if has_lora:
            for i, (bk, ak) in enumerate([("l1_bqT", "l1_aqT"), ("l1_bkT", "l1_akT"),
                                          ("l1_bvT", "l1_avT")]):
                S.dma("gpsimd", lambda e, i=i, bk=bk: e.dma_start(
                        out=wbq[:, i], in_=wap2(ctx, bk).rearrange("(c p) o -> p c o", p=128)),
                      writes=[("wbq", i)], tag=f"ldlb{i}")
                S.dma("gpsimd", lambda e, i=i, ak=ak: e.dma_start(
                        out=waq[:, i, :], in_=wap2(ctx, ak)),
                      writes=[("waq", i)], tag=f"ldla{i}")
            S.op("vector", lambda e: e.memset(wcbl[0:2, :, :], 0.0), writes=["wcbl_b"], tag="zlb")
            for i in range(3):
                def cs(e, i=i):
                    r = None
                    for ci in range(8):
                        r = e.matmul(ctx.pS[0:1, 0:8], ctx.onescol_b[:], wbq[:, i, ci],
                                     start=(ci == 0), stop=(ci == 7))
                    return r
                S.op("tensor", cs, reads=[("wbq", i), ("c", "onescol_b")], writes=["pS"],
                     tag=f"lcs{i}")
                S.op("vector", lambda e, i=i: e.tensor_copy(wcbl[0:1, i, :], ctx.pS[0:1, 0:8]),
                     reads=["pS"], writes=[("wcbl", i)], narrow=True, tag=f"lcsev{i}")

        S.op("vector", lambda e: e.memset(Vaug[:, :, :, 64:65], 1.0), writes=["vones"], tag="vones")
        S.op("vector", lambda e: e.memset(Vaug[:, :, :, 65:66], 0.0), writes=["vzpad"], tag="vzpad")

        wkeys = [wq, wk, wv]
        def load_w(p, ob, buf, first):
            ap = wap2(ctx, wkeys[p]).rearrange("(c p) o -> p c o", p=128)[:, :, ob * 256:(ob + 1) * 256]
            S.dma("gpsimd", lambda e, ap=ap, buf=buf: e.dma_start(out=wt[:, buf], in_=ap),
                  writes=[("wt", buf)], tag=f"ldw{p}{ob}")
            if first:
                def cs(e, buf=buf):
                    r = None
                    for ci in range(8):
                        r = e.matmul(ctx.pS[0:1, 0:256], ctx.onescol_b[:], wt[:, buf, ci],
                                     start=(ci == 0), stop=(ci == 7))
                    return r
                S.op("tensor", cs, reads=[("wt", buf), ("c", "onescol_b")], writes=["pS"],
                     tag=f"cs{p}{ob}")
                S.op("vector", lambda e, p=p, ob=ob: e.tensor_copy(
                        wcb[0:1, p, ob * 256:(ob + 1) * 256], ctx.pS[0:1, 0:256]),
                     reads=["pS", "wcbz"], writes=[("wcb", p, ob)], tag=f"csev{p}{ob}")

        def load_xb(sg):
            S.dma("gpsimd", lambda e, sg=sg: e.dma_start(
                    out=xb[:, :, :], in_=xr[:, :, sg * 512:(sg + 1) * 512]),
                  reads=[("xb",)], writes=[("xb",)], tag=f"ldxb{sg}")

        def lora_mids(sg):
            for i in range(3):
                def mmid(e, i=i, sg=sg):
                    for ci in range(8):
                        e.matmul(ctx.pS[0:8, :], wbq[:, i, ci], xb[:, ci, :],
                                 start=(ci == 0), stop=False)
                    return e.matmul(ctx.pS[0:8, :], wcbl[:, i, :],
                                    rowsNB_[:, sg * 512:(sg + 1) * 512], start=False, stop=True)
                S.op("tensor", mmid, reads=[("xb",), ("wbq", i), ("wcbl", i), "wcbl_b",
                                            ("rowN", sg), ("rowS", sg)],
                     writes=["pS"], tag=f"lmid{i}_{sg}")
                S.op("vector", lambda e, i=i: e.tensor_scalar(mids[0:8, i, :], ctx.pS[0:8, :],
                                                              2.0, None, ALU.mult),
                     reads=["pS"], writes=[("mids", i)], tag=f"lmidev{i}_{sg}")

        PK = [("pmA", ctx.pA), ("pmB", ctx.pB), ("pmE", ctx.pE), ("pmF", ctx.pF)]
        pidx = [0]
        def next_psum():
            k, p = PK[pidx[0] % 4]
            pidx[0] += 1
            return k, p

        def proj_fm(p, ob, buf, oc, sg, psum, pkey):
            def mm(e, p=p, ob=ob, buf=buf, oc=oc, sg=sg, psum=psum):
                for ci in range(8):
                    e.matmul(psum[:, :], wt[:, buf, ci, oc * 128:(oc + 1) * 128], xb[:, ci, :],
                             start=(ci == 0), stop=False)
                och = ob * 2 + oc
                r = e.matmul(psum[:, :], wcb[:, p, och * 128:(och + 1) * 128],
                             rowsNB_[:, sg * 512:(sg + 1) * 512], start=False, stop=not has_lora)
                if has_lora:
                    r = e.matmul(psum[:, :], waq[:, p, och * 128:(och + 1) * 128],
                                 mids[0:8, p, :], start=False, stop=True)
                return r
            reads = [("xb",), ("wt", buf), ("wcb", p, ob), ("wcb_b", p),
                     ("rowN", sg), ("rowS", sg)]
            if has_lora:
                reads += [("mids", p), ("waq", p)]
            S.op("tensor", mm, reads=reads, writes=[pkey], tag=f"prj{p}_{ob}{oc}_{sg}")

        def evac_K(och, sg, psum, pkey):
            S.op("vector", lambda e, och=och, sg=sg, psum=psum: e.tensor_tensor(
                    K_sb[:, och, sg * 512:(sg + 1) * 512], psum[:, :], srep_[:, sg, :], ALU.mult),
                 reads=[pkey, ("srep", sg)], writes=[("K", sg)], tag=f"evK{och}_{sg}")

        def evac_Q(och, sg, psum, pkey):
            S.op("vector", lambda e, och=och, sg=sg, psum=psum: e.tensor_tensor(
                    Q_sb[:, och, :], psum[:, :], srep_[:, sg, :], ALU.mult),
                 reads=[pkey, ("srep", sg)], writes=[("Q", sg)], tag=f"evQ{och}_{sg}")

        def proj_v_tm(ob, buf, tt, sg, psum, pkey):
            def mm(e, ob=ob, buf=buf, tt=tt, psum=psum):
                lt = tt % 4
                for ci in range(8):
                    e.matmul(psum[:, 0:256], xb[:, ci, lt * 128:(lt + 1) * 128], wt[:, buf, ci, :],
                             start=(ci == 0), stop=False)
                r = e.matmul(psum[:, 0:256], rowsNB_[:, tt * 128:(tt + 1) * 128],
                             wcb[:, 2, ob * 256:(ob + 1) * 256], start=False, stop=not has_lora)
                if has_lora:
                    r = e.matmul(psum[:, 0:256], mids[0:8, 2, (tt % 4) * 128:(tt % 4 + 1) * 128],
                                 waq[:, 2, ob * 256:(ob + 1) * 256], start=False, stop=True)
                return r
            reads = [("xb",), ("wt", buf), ("rowN", sg), ("rowS", sg),
                     ("wcb", 2, ob), ("wcb_b", 2)]
            if has_lora:
                reads += [("mids", 2), ("waq", 2)]
            S.op("tensor", mm, reads=reads, writes=[pkey], tag=f"pv{ob}_{tt}")

        def evac_V(ob, tt, psum, pkey):
            out = Vaug[:, tt, ob * 4:(ob + 1) * 4, 0:64]
            S.op("vector", lambda e, psum=psum, out=out, tt=tt: e.tensor_scalar(
                    out, psum[:, 0:256].rearrange("p (h d) -> p h d", d=64),
                    rstdcol_[:, tt:tt + 1], None, ALU.mult),
                 reads=[pkey, ("rstd_col", tt // 4)], writes=[("V", tt // 4)], tag=f"evV{ob}_{tt}")

        wbuf = [0]
        for sg in range(8):
            load_xb(sg)
            if has_lora:
                lora_mids(sg)
            for p in (1, 2):
                for ob in range(4):
                    buf = wbuf[0] % 2
                    wbuf[0] += 1
                    load_w(p, ob, buf, first=(sg == 0))
                    if p == 1:
                        for oc in range(2):
                            pk, ps = next_psum()
                            proj_fm(p, ob, buf, oc, sg, ps, pk)
                            evac_K(ob * 2 + oc, sg, ps, pk)
                    else:
                        for lt in range(4):
                            tt = sg * 4 + lt
                            pk, ps = next_psum()
                            proj_v_tm(ob, buf, tt, sg, ps, pk)
                            evac_V(ob, tt, ps, pk)
        if L == 0:
            for h in range(16):
                def vb(e, h=h):
                    r = None
                    for kt in range(32):
                        r = e.matmul(ctx.pS[0:66, 4 * kt:4 * kt + 4], Vaug[:, kt, h, :],
                                     ctx.blockind[:, :], start=True, stop=True)
                    return r
                S.op("tensor", vb, reads=[("V", 0), ("V", 1), ("V", 2), ("V", 3),
                                          "vones", ("c", "blockind")],
                     writes=["pS"], tag=f"vblk{h}")
                S.op("vector", lambda e: e.tensor_copy(vblkT[:, :], ctx.pS[0:66, 0:128]),
                     reads=["pS"], writes=["vblkT"], tag=f"vblkev{h}")
                S.op("tensor", lambda e: e.transpose(ctx.pTb[0:128, 0:66], vblkT[:, :],
                                                     ctx.identb[0:66, 0:66]),
                     reads=["vblkT", ("c", "identb")], writes=["pTb"], tag=f"vblktp{h}")
                S.op("vector", lambda e, h=h: e.tensor_copy(vblk[:, h, :], ctx.pTb[0:128, 0:66]),
                     reads=["pTb"], writes=[("vblk", h)], tag=f"vblktev{h}")

        for qg in range(8):
            load_xb(qg)
            if has_lora:
                lora_mids(qg)
            for ob in range(4):
                buf = wbuf[0] % 2
                wbuf[0] += 1
                load_w(0, ob, buf, first=(qg == 0))
                for oc in range(2):
                    pk, ps = next_psum()
                    proj_fm(0, ob, buf, oc, qg, ps, pk)
                    evac_Q(ob * 2 + oc, qg, ps, pk)
            ktmax = (4 * qg + 3) if L in (0, 1) else 31
            nkt = ktmax + 1
            for hp in range(8):
                iters = [(hh, kt) for hh in range(2) for kt in range(nkt)]
                pmeta = {}

                def emit_sc(i, hp=hp, qg=qg, iters=iters):
                    hh, kt = iters[i]
                    h = 2 * hp + hh
                    hr = slice((h % 2) * 64, (h % 2) * 64 + 64)
                    hc = h // 2
                    q0 = max(0, 128 * kt - 512 * qg) if L in (0, 1) else 0
                    sl = slice(q0, 512)
                    spk, sp = next_psum()
                    pslot = i % 8
                    pmeta[i] = (sl, pslot)
                    S.op("tensor", lambda e, kt=kt, hr=hr, hc=hc, sl=sl, sp=sp: e.matmul(
                            sp[:, sl], K_sb[hr, hc, kt * 128:(kt + 1) * 128],
                            Q_sb[hr, hc, sl], start=True, stop=True),
                         reads=[("K", kt // 4), ("Q", qg)], writes=[spk],
                         tag=f"sc{h}_{qg}_{kt}")
                    if L == 1:
                        bias = ctx.alibi[:, kt * 16 + h: kt * 16 + h + 1]
                        S.op("scalar", lambda e, pslot=pslot, sl=sl, sp=sp, bias=bias:
                                e.activation(P_sb[:, pslot, sl], sp[:, sl], AF.Exp,
                                             bias=bias, scale=0.125),
                             reads=[spk, ("c", "alibi")], writes=[("P", pslot)],
                             tag=f"exp{h}_{qg}_{kt}")
                    else:
                        S.op("scalar", lambda e, pslot=pslot, sl=sl, sp=sp: e.activation(
                                P_sb[:, pslot, sl], sp[:, sl], AF.Exp, scale=0.125),
                             reads=[spk], writes=[("P", pslot)], tag=f"exp{h}_{qg}_{kt}")
                    if L in (0, 1) and kt >= 4 * qg:
                        j = kt - 4 * qg
                        msk = ctx.stair if L == 0 else ctx.tri
                        mkey = "stair" if L == 0 else "tri"
                        S.op("vector", lambda e, pslot=pslot, j=j, msk=msk: e.tensor_tensor(
                                P_sb[:, pslot, j * 128:(j + 1) * 128],
                                P_sb[:, pslot, j * 128:(j + 1) * 128], msk[:, :], ALU.mult),
                             reads=[("P", pslot), ("c", mkey)], writes=[("P", pslot)],
                             tag=f"msk{h}_{qg}_{kt}")

                def emit_cx(i, first, last, gslots, hp=hp, qg=qg, iters=iters):
                    hh, kt = iters[i]
                    h = 2 * hp + hh
                    cp = ctx.pC if hh == 0 else ctx.pD
                    cpk = "pC" if hh == 0 else "pD"
                    sl, pslot = pmeta[i]
                    if L == 0 and kt == 0:
                        S.op("tensor", lambda e, h=h, qg=qg, cp=cp: e.matmul(
                                cp[0:66, :], vblk[:, h, :], Ucst[:, qg * 512:(qg + 1) * 512],
                                start=True, stop=False),
                             reads=[("vblk", h), ("c", "U")], writes=[cpk], tag=f"sfx{h}_{qg}")
                    st = (L != 0) and kt == 0
                    preads = [("P", s) for s in gslots] if (first or last) else []
                    S.op("tensor", lambda e, kt=kt, h=h, sl=sl, pslot=pslot, cp=cp,
                                 st=st, fin=(kt == ktmax): e.matmul(
                            cp[0:66, sl], Vaug[:, kt, h, :], P_sb[:, pslot, sl],
                            start=st, stop=fin),
                         reads=preads + [("V", kt // 4), "vones", "vzpad"], writes=[cpk],
                         tag=f"cx{h}_{qg}_{kt}")
                    if kt == ktmax:
                        def recf(e, cp=cp, hh=hh):
                            with nc.allow_low_precision(reason="bf16 softmax denom recip"):
                                return e.reciprocal(rec65[:, hh, :], cp[0:65, :])
                        S.op("vector", recf, reads=[cpk], writes=[("rec", hh)],
                             tag=f"rz{h}_{qg}")

                GS = 4
                groups = [list(range(j, min(j + GS, len(iters))))
                          for j in range(0, len(iters), GS)]
                for gi in range(len(groups) + 1):
                    if gi < len(groups):
                        for i in groups[gi]:
                            emit_sc(i)
                    if gi >= 1:
                        grp = groups[gi - 1]
                        gslots = [j % 8 for j in grp]
                        for j, i in enumerate(grp):
                            emit_cx(i, j == 0, j == len(grp) - 1, gslots)
                def bcast(e):
                    e.matmul(ctx.pS[:, :], hsel[64:65, 0, :], rec65[64:65, 0, :],
                             start=True, stop=False)
                    return e.matmul(ctx.pS[:, :], hsel[64:65, 1, :], rec65[64:65, 1, :],
                                    start=False, stop=True)
                S.op("tensor", bcast, reads=[("rec", 0), ("rec", 1), ("hsel", 0),
                                             ("hsel", 1)], writes=["pS"],
                     tag=f"bc{hp}_{qg}")
                S.op("scalar", lambda e: e.copy(rzrep[:, :], ctx.pS[:, :]),
                     reads=["pS"], writes=["rzrep"], tag=f"rzc{hp}_{qg}")
                b = qg % 2
                S.op("vector", lambda e, b=b: e.tensor_tensor(
                        ctxn[0:64, b, :], ctx.pC[0:64, :], rzrep[0:64, :], ALU.mult),
                     reads=["pC", "rzrep", ("ctxnf", b)], writes=[("ctxn", b, 0)],
                     tag=f"cev0_{hp}_{qg}")
                S.op("vector", lambda e, b=b: e.tensor_tensor(
                        ctxn[64:128, b, :], ctx.pD[0:64, :], rzrep[64:128, :], ALU.mult),
                     reads=["pD", "rzrep", ("ctxnf", b)], writes=[("ctxn", b, 1)],
                     tag=f"cev1_{hp}_{qg}")
                S.dma("sync", lambda e, hp=hp, qg=qg, b=b: e.dma_start(
                        out=ctx.ctxd[hp, qg], in_=ctxn[:, b, :]),
                      reads=[("ctxn", b, 0), ("ctxn", b, 1)],
                      writes=[("ctxd", hp, qg), ("ctxnf", b)],
                      tag=f"stctx{hp}_{qg}")
        S.fence()

    with contextlib.ExitStack() as est:
        es = est.enter_context
        has_wo = (L in (0, 1))
        if has_wo:
            wokey = "l0_woT" if L == 0 else "l1_wpT"
            wo_sb = es(nc.sbuf_tensor(f"at_wo{L}", [128, 8, 1024], BF16))
            S.dma("gpsimd", lambda e: e.dma_start(out=wo_sb[:, :, :],
                    in_=wap2(ctx, wokey).rearrange("(c p) o -> p c o", p=128)),
                  writes=["wo"], tag="ldwo")
            if L == 0:
                bias_f = es(nc.sbuf_tensor(f"at_biasf{L}", [128, 8], F32))
                S.dma("sync", lambda e: e.dma_start(out=bias_f[:, :], in_=wap2(ctx, "l0_bo")),
                      writes=["bo_f"], tag="ldbo")
        cstg = es(nc.sbuf_tensor(f"at_cstg{L}", [128, 2, 8, 512], BF16))
        xn2 = es(nc.sbuf_tensor(f"at_xn2{L}", [128, 2, 8, 512], F32))
        for sg in range(8):
            b = sg % 2
            S.dma("sync", lambda e, sg=sg, b=b: e.dma_start(
                    out=cstg[:, b], in_=ctx.ctxd[:, sg].rearrange("c p f -> p c f")),
                  reads=[("cfree", b)] + [("ctxd", hp, sg) for hp in range(8)],
                  writes=[("cstg", b)], tag=f"ldc{sg}")
            S.dma("sync", lambda e, sg=sg, b=b: e.dma_start(
                    out=xn2[:, b], in_=xr[:, :, sg * 512:(sg + 1) * 512]),
                  reads=[("xn2w", b)], writes=[("xn2", b)], tag=f"ldx2{sg}")
            if has_wo:
                for oc in range(8):
                    pk, ps = (("pmA", ctx.pA) if oc % 2 == 0 else ("pmB", ctx.pB))
                    def mm(e, oc=oc, b=b, ps=ps):
                        r = None
                        for ci in range(8):
                            r = e.matmul(ps[:, :], wo_sb[:, ci, oc * 128:(oc + 1) * 128],
                                         cstg[:, b, ci, :], start=(ci == 0), stop=(ci == 7))
                        return r
                    S.op("tensor", mm, reads=[("cstg", b), "wo"], writes=[pk], tag=f"wo{sg}_{oc}")
                    if L == 0:
                        S.op("vector", lambda e, oc=oc, b=b, ps=ps: e.scalar_tensor_tensor(
                                xn2[:, b, oc, :], ps[:, :], bias_f[:, oc:oc + 1], xn2[:, b, oc, :],
                                ALU.add, ALU.add),
                             reads=[pk, ("xn2", b), "bo_f"], writes=[("xn2", b), ("xn2w", b)],
                             tag=f"woev{sg}_{oc}")
                    else:
                        S.op("vector", lambda e, oc=oc, b=b, ps=ps: e.tensor_add(
                                xn2[:, b, oc, :], ps[:, :], xn2[:, b, oc, :]),
                             reads=[pk, ("xn2", b)], writes=[("xn2", b), ("xn2w", b)],
                             tag=f"woev{sg}_{oc}")
            else:
                S.op("vector", lambda e, b=b: e.tensor_add(
                        xn2[:, b].rearrange("p c t -> p (c t)"),
                        xn2[:, b].rearrange("p c t -> p (c t)"),
                        cstg[:, b].rearrange("p c t -> p (c t)")),
                     reads=[("cstg", b), ("xn2", b)], writes=[("xn2", b), ("xn2w", b)],
                     tag=f"resadd{sg}")
            S.dma("sync", lambda e, sg=sg, b=b: e.dma_start(
                    out=xdst.rearrange("(c p) t -> p c t", p=128)[:, :, sg * 512:(sg + 1) * 512],
                    in_=xn2[:, b]),
                  reads=[("xn2", b)], writes=[("xout", sg), ("xn2w", b), ("cfree", b)],
                  tag=f"stx{sg}")
        S.fence()

def emit_quant(ctx, pref, dst, dkey, src, skey, n, tmp):
    nc, S = ctx.nc, ctx.S
    A, B, C, E, S2, S3, sgn, am, sc, rs, scb, thc = tmp
    nb = n // 16
    r3 = lambda ap: ap.rearrange("p (b k) -> p b k", k=16)
    PH = lambda nm: ("qph", nm)

    def V(fn, reads, writes, tag):
        def wrapped(e, fn=fn):
            with nc.allow_low_precision(reason="fp4 grid counters in bf16 (exact small ints)"):
                return fn(e)
        S.op("vector", wrapped, reads=reads, writes=writes, tag=tag)

    V(lambda e: e.reduce_max(am[:, 0:nb], r3(src), AX.X, apply_absolute_value=True),
      [skey], [(pref, "am"), PH("am")], f"{pref}_amax")
    V(lambda e: e.tensor_scalar(sc[:, 0:nb], am[:, 0:nb], 1e-12, 1.0 / 6.0,
                                ALU.max, ALU.mult),
      [(pref, "am")], [(pref, "sc"), PH("sc")], f"{pref}_sc")
    V(lambda e: e.tensor_copy(scb[:, 0:nb], sc[:, 0:nb]),
      [(pref, "sc")], [(pref, "scb"), PH("scb")], f"{pref}_scb")
    V(lambda e: e.reciprocal(rs[:, 0:nb], sc[:, 0:nb]),
      [(pref, "sc")], [(pref, "rs"), PH("rs")], f"{pref}_rs")
    V(lambda e: e.tensor_scalar(rs[:, 0:nb], rs[:, 0:nb], 2.0, None, ALU.mult),
      [(pref, "rs")], [(pref, "rs"), PH("rs")], f"{pref}_rs2")
    V(lambda e: e.tensor_tensor(r3(A[:, 0:n]), r3(src),
            rs[:, 0:nb][:, :, None].broadcast_to([128, nb, 16]), ALU.mult),
      [skey, (pref, "rs")], [(pref, "A"), PH("A")], f"{pref}_t")
    S.op("scalar", lambda e: e.activation(sgn[:, 0:n], A[:, 0:n], AF.Sign),
         reads=[(pref, "A")], writes=[(pref, "sgn"), PH("sgn")], tag=f"{pref}_sgn")
    S.op("scalar", lambda e: e.activation(B[:, 0:n], A[:, 0:n], AF.Abs),
         reads=[(pref, "A")], writes=[(pref, "B"), PH("B")], tag=f"{pref}_abs")
    for name, buf, bk, ti in [("s25", S2, "S2", 0), ("s35", S3, "S3", 1)]:
        S.op("scalar", lambda e, buf=buf, ti=ti: e.activation(
                buf[:, 0:n], B[:, 0:n], AF.Sign, bias=thc[:, ti:ti + 1]),
             reads=[(pref, "B"), "thc"], writes=[(pref, bk), PH(bk)], tag=f"{pref}_{name}")
    V(lambda e: e.tensor_scalar(C[:, 0:n], B[:, 0:n], 0.5, None, ALU.is_gt),
      [(pref, "B")], [(pref, "C"), PH("C")], f"{pref}_l0")
    V(lambda e: e.scalar_tensor_tensor(C[:, 0:n], B[:, 0:n], 1.5, C[:, 0:n],
                                       ALU.is_gt, ALU.add),
      [(pref, "B"), (pref, "C")], [(pref, "C"), PH("C")], f"{pref}_l1")
    V(lambda e: e.tensor_scalar(C[:, 0:n], C[:, 0:n], 0.5, 2.5, ALU.mult, ALU.add),
      [(pref, "C")], [(pref, "C"), PH("C")], f"{pref}_half")
    V(lambda e: e.tensor_add(E[:, 0:n], S2[:, 0:n], S3[:, 0:n]),
      [(pref, "S2"), (pref, "S3")], [(pref, "E"), PH("E")], f"{pref}_d1")
    V(lambda e: e.scalar_tensor_tensor(C[:, 0:n], E[:, 0:n], 0.25, C[:, 0:n],
                                       ALU.mult, ALU.add),
      [(pref, "E"), (pref, "C")], [(pref, "C"), PH("C")], f"{pref}_d2")
    for name, buf, bk, ti in [("s5", S2, "S2", 2), ("s7", S3, "S3", 3)]:
        S.op("scalar", lambda e, buf=buf, ti=ti: e.activation(
                buf[:, 0:n], B[:, 0:n], AF.Sign, bias=thc[:, ti:ti + 1]),
             reads=[(pref, "B"), "thc"], writes=[(pref, bk), PH(bk)], tag=f"{pref}_{name}")
    V(lambda e: e.tensor_add(E[:, 0:n], S2[:, 0:n], S3[:, 0:n]),
      [(pref, "S2"), (pref, "S3")], [(pref, "E"), PH("E")], f"{pref}_d3")
    V(lambda e: e.scalar_tensor_tensor(C[:, 0:n], E[:, 0:n], 0.5, C[:, 0:n],
                                       ALU.mult, ALU.add),
      [(pref, "E"), (pref, "C")], [(pref, "C"), PH("C")], f"{pref}_d4")
    S.op("scalar", lambda e: e.activation(S2[:, 0:n], B[:, 0:n], AF.Sign,
                                          bias=thc[:, 4:5]),
         reads=[(pref, "B"), "thc"], writes=[(pref, "S2"), PH("S2")], tag=f"{pref}_s10")
    V(lambda e: e.tensor_add(C[:, 0:n], C[:, 0:n], S2[:, 0:n]),
      [(pref, "C"), (pref, "S2")], [(pref, "C"), PH("C")], f"{pref}_d5")
    V(lambda e: e.tensor_tensor(C[:, 0:n], C[:, 0:n], sgn[:, 0:n], ALU.mult),
      [(pref, "C"), (pref, "sgn")], [(pref, "C"), PH("C")], f"{pref}_qs")
    V(lambda e: e.tensor_tensor(r3(dst), r3(C[:, 0:n]),
            scb[:, 0:nb][:, :, None].broadcast_to([128, nb, 16]), ALU.mult),
      [(pref, "C"), (pref, "scb")], [dkey], f"{pref}_out")

def emit_l0ffn(ctx, xsrc, xdst):
    import contextlib
    nc, S = ctx.nc, ctx.S
    xr = xsrc.rearrange("(c p) t -> p c t", p=128)
    with contextlib.ExitStack() as est:
        es = est.enter_context
        wg = es(nc.sbuf_tensor("f0_wg", [128, 8, 2816], BF16))
        wu = es(nc.sbuf_tensor("f0_wu", [128, 8, 2816], BF16))
        wd = es(nc.sbuf_tensor("f0_wd", [128, 22, 1024], BF16))
        qA = es(nc.sbuf_tensor("f0_qA", [128, 1024], F32))
        qB = es(nc.sbuf_tensor("f0_qB", [128, 1024], F32))
        qC = es(nc.sbuf_tensor("f0_qC", [128, 1024], BF16))
        qE = es(nc.sbuf_tensor("f0_qE", [128, 1024], BF16))
        qS2 = es(nc.sbuf_tensor("f0_qS2", [128, 1024], BF16))
        qS3 = es(nc.sbuf_tensor("f0_qS3", [128, 1024], BF16))
        qS = es(nc.sbuf_tensor("f0_qS", [128, 1024], BF16))
        qm = es(nc.sbuf_tensor("f0_qm", [128, 64], F32))
        qsc = es(nc.sbuf_tensor("f0_qsc", [128, 64], F32))
        qrs = es(nc.sbuf_tensor("f0_qrs", [128, 64], F32))
        qscb = es(nc.sbuf_tensor("f0_qscb", [128, 64], BF16))
        qth = es(nc.sbuf_tensor("f0_qth", [128, 5], F32))
        for ti, tv in enumerate([-2.5, -3.5, -5.0, -7.0, -10.0]):
            S.op("vector", lambda e, ti=ti, tv=tv: e.memset(qth[:, ti:ti + 1], tv),
                 writes=["thc"], narrow=True, tag=f"f0th{ti}")
        qtmp = (qA, qB, qC, qE, qS2, qS3, qS, qm, qsc, qrs, qscb, qth)

        import contextlib as _cl
        with _cl.ExitStack() as est2:
            wstg = est2.enter_context(nc.sbuf_tensor("f0_wstg", [128, 2816], F32))
            wq16 = est2.enter_context(nc.sbuf_tensor("f0_wq16", [128, 2816], BF16))
            def quant_weight(key, rows, cols, dst, tagp):
                nt = rows // 128
                wap = wap2(ctx, key)
                pieces = [(0, 1024), (1024, 1024), (2048, 768)] if cols == 2816 else [(0, 1024)]
                for r in range(nt):
                    S.dma("sync", lambda e, r=r, cols=cols: e.dma_start(
                            out=wstg[:, 0:cols], in_=wap.rearrange("(n p) c -> n p c", p=128)[r]),
                          reads=["f0wstg"], writes=["f0wstg"], tag=f"{tagp}ld{r}")
                    for pi_, (po, pn) in enumerate(pieces):
                        emit_quant(ctx, f"{tagp}q{r}_{pi_}", wq16[:, po:po + pn], "f0wq",
                                   wstg[:, po:po + pn], "f0wstg", pn, qtmp)
                    nch = cols // 128
                    for g in range(0, nch, 8):
                        gn = min(8, nch - g)
                        def tp(e, g=g, gn=gn):
                            rr = None
                            for j in range(gn):
                                rr = e.transpose(ctx.pTb[:, j * 128:(j + 1) * 128],
                                                 wq16[:, (g + j) * 128:(g + j + 1) * 128],
                                                 ctx.identb[:, :])
                            return rr
                        S.op("tensor", tp, reads=["f0wq", ("c", "identb")], writes=["pTb"],
                             tag=f"{tagp}tp{r}_{g}")
                        def ev(e, r=r, g=g, gn=gn):
                            return e.tensor_copy(
                                dst[:, g:g + gn, r * 128:(r + 1) * 128],
                                ctx.pTb[:, 0:gn * 128].rearrange("p (j q) -> p j q", q=128))
                        S.op("vector", ev, reads=["pTb"], writes=[(tagp, "dst"), "f0wq"],
                             tag=f"{tagp}ev{r}_{g}")
            quant_weight("l0_wgate", 2816, 1024, wg, "qg")
            quant_weight("l0_wup", 2816, 1024, wu, "qu")
            quant_weight("l0_wdown", 1024, 2816, wd, "qd")
            S.fence()

        xsl = es(nc.sbuf_tensor("f0_xsl", [128, 8, 512], F32))
        xtm = es(nc.sbuf_tensor("f0_xtm", [128, 1024], F32))
        xq = es(nc.sbuf_tensor("f0_xq", [128, 2, 1024], BF16))
        xqf = es(nc.sbuf_tensor("f0_xqf", [128, 2, 8, 128], BF16))
        hbuf = es(nc.sbuf_tensor("f0_h", [128, 512], F32))
        hq = es(nc.sbuf_tensor("f0_hq", [128, 2, 512], BF16))
        hqf = es(nc.sbuf_tensor("f0_hqf", [128, 22, 256], BF16))
        gact = es(nc.sbuf_tensor("f0_gact", [128, 2, 512], BF16))
        sq1 = es(nc.sbuf_tensor("f0_sq1", [128, 1024], BF16))
        mu = es(nc.sbuf_tensor("f0_mu", [128, 1], F32))
        musq = es(nc.sbuf_tensor("f0_musq", [128, 1], F32))
        ssum = es(nc.sbuf_tensor("f0_ssum", [128, 1], F32))
        ssq = es(nc.sbuf_tensor("f0_ssq", [128, 1], F32))
        rstd = es(nc.sbuf_tensor("f0_rstd", [128, 1], F32))

        for sg in range(8):
            S.dma("sync", lambda e, sg=sg: e.dma_start(
                    out=xsl[:, :, :], in_=xr[:, :, sg * 512:(sg + 1) * 512]),
                  reads=[("xsl",)], writes=[("xsl",)], tag=f"f0ldx{sg}")
            for half in range(2):
                for l2 in range(2):
                    lt = half * 2 + l2
                    tcol = lt * 128
                    for g in range(2):
                        def tpx(e, g=g, tcol=tcol):
                            rr = None
                            for j in range(4):
                                ci = g * 4 + j
                                rr = e.transpose(ctx.pE[:, j * 128:(j + 1) * 128],
                                                 xsl[:, ci, tcol:tcol + 128],
                                                 ctx.identf[:, :])
                            return rr
                        S.op("tensor", tpx, reads=[("xsl",), ("c", "identf")], writes=["pT32"],
                             tag=f"f0tx{sg}_{lt}_{g}")
                        S.op("vector", lambda e, g=g: e.tensor_copy(
                                xtm[:, g * 512:(g + 1) * 512], ctx.pE[:, :]),
                             reads=["pT32"], writes=[("xtm", g)], tag=f"f0txe{sg}_{lt}_{g}")
                    S.op("vector", lambda e: e.reduce_sum(ssum[:], xtm[:, :], AX.X),
                         reads=[("xtm", 0), ("xtm", 1)], writes=["ssum"], narrow=True,
                         tag=f"f0sum{sg}_{lt}")
                    S.op("scalar", lambda e: e.activation(sq1[:, :], xtm[:, :], AF.Square,
                                                          accum_out=ssq[:]),
                         reads=[("xtm", 0), ("xtm", 1)], writes=["ssq"], narrow=True,
                         tag=f"f0ssq{sg}_{lt}")
                    S.op("vector", lambda e: e.tensor_scalar(mu[:], ssum[:], 1.0 / D, None, ALU.mult),
                         reads=["ssum"], writes=["mu"], narrow=True, tag=f"f0mu{sg}_{lt}")
                    S.op("vector", lambda e: e.tensor_tensor(musq[:], mu[:], mu[:], ALU.mult),
                         reads=["mu"], writes=["musq"], narrow=True, tag=f"f0musq{sg}_{lt}")
                    S.op("vector", lambda e: e.scalar_tensor_tensor(ssq[:], ssq[:], 1.0 / D,
                                                                    musq[:], ALU.mult, ALU.subtract),
                         reads=["ssq", "musq"], writes=["ssq"], narrow=True, tag=f"f0var{sg}_{lt}")
                    S.op("vector", lambda e: e.tensor_scalar(ssq[:], ssq[:], EPS, None, ALU.add),
                         reads=["ssq"], writes=["ssq"], narrow=True, tag=f"f0vare{sg}_{lt}")
                    S.op("scalar", lambda e: e.activation(ssq[:], ssq[:], AF.Sqrt),
                         reads=["ssq"], writes=["ssq"], narrow=True, tag=f"f0sig{sg}_{lt}")
                    S.op("vector", lambda e: e.reciprocal(rstd[:], ssq[:]),
                         reads=["ssq"], writes=["rstd"], narrow=True, tag=f"f0rstd{sg}_{lt}")
                    S.op("vector", lambda e: e.tensor_scalar(xtm[:, :], xtm[:, :], mu[:], rstd[:],
                                                             ALU.subtract, ALU.mult),
                         reads=[("xtm", 0), ("xtm", 1), "mu", "rstd"],
                         writes=[("xtm", 0), ("xtm", 1)], tag=f"f0ln{sg}_{lt}")
                    lb = lt % 2
                    emit_quant(ctx, f"f0xq{sg}_{lt}", xq[:, lb, :], ("xq", lb), xtm[:, :],
                               ("xtm", 0), 1024, qtmp)
                    for g in range(1):
                        def tpq(e, lb=lb):
                            rr = None
                            for j in range(8):
                                rr = e.transpose(ctx.pTb[:, j * 128:(j + 1) * 128],
                                                 xq[:, lb, j * 128:(j + 1) * 128], ctx.identb[:, :])
                            return rr
                        S.op("tensor", tpq, reads=[("xq", lb), ("c", "identb")], writes=["pTb"],
                             tag=f"f0tq{sg}_{lt}")
                        S.op("vector", lambda e, lb=lb: e.tensor_copy(
                                xqf[:, lb, :, :], ctx.pTb[:, :].rearrange("p (j q) -> p j q", q=128)),
                             reads=["pTb"], writes=[("xqf", lb)], tag=f"f0tqe{sg}_{lt}")
                    for obl in range(6):
                        nw = 512 if obl < 5 else 256
                        osl = slice(obl * 512, obl * 512 + nw)
                        ob2 = obl % 2
                        def mmg(e, osl=osl, nw=nw, lb=lb):
                            rr = None
                            for ci in range(8):
                                rr = e.matmul(ctx.pA[0:128, 0:nw], xqf[:, lb, ci, :], wg[:, ci, osl],
                                              start=(ci == 0), stop=(ci == 7))
                            return rr
                        def mmu(e, osl=osl, nw=nw, lb=lb):
                            rr = None
                            for ci in range(8):
                                rr = e.matmul(ctx.pB[0:128, 0:nw], xqf[:, lb, ci, :], wu[:, ci, osl],
                                              start=(ci == 0), stop=(ci == 7))
                            return rr
                        S.op("tensor", mmg, reads=[("xqf", lb), ("qg", "dst")], writes=["pA"],
                             tag=f"f0mg{sg}_{lt}_{obl}")
                        S.op("tensor", mmu, reads=[("xqf", lb), ("qu", "dst")], writes=["pB"],
                             tag=f"f0mu{sg}_{lt}_{obl}")
                        S.op("scalar", lambda e, nw=nw, ob2=ob2: e.activation(
                                gact[:, ob2, 0:nw], ctx.pA[:, 0:nw], AF.Silu),
                             reads=["pA"], writes=[("gact", ob2)], tag=f"f0sil{sg}_{lt}_{obl}")
                        S.op("vector", lambda e, nw=nw, ob2=ob2: e.tensor_tensor(
                                hbuf[:, 0:nw], gact[:, ob2, 0:nw], ctx.pB[:, 0:nw], ALU.mult),
                             reads=[("gact", ob2), "pB"], writes=["hbuf"], tag=f"f0h{sg}_{lt}_{obl}")
                        emit_quant(ctx, f"f0hq{sg}_{lt}_{obl}", hq[:, ob2, 0:nw], ("hq", ob2),
                                   hbuf[:, 0:nw], "hbuf", nw, qtmp)
                        nch = nw // 128
                        def tph(e, nch=nch, ob2=ob2):
                            rr = None
                            for j in range(nch):
                                rr = e.transpose(ctx.pTb[:, j * 128:(j + 1) * 128],
                                                 hq[:, ob2, j * 128:(j + 1) * 128], ctx.identb[:, :])
                            return rr
                        S.op("tensor", tph, reads=[("hq", ob2), ("c", "identb")], writes=["pTb"],
                             tag=f"f0th{sg}_{lt}_{obl}")
                        def evh(e, obl=obl, l2=l2, nch=nch):
                            return e.tensor_copy(
                                hqf[:, obl * 4:obl * 4 + nch, l2 * 128:(l2 + 1) * 128],
                                ctx.pTb[:, 0:nch * 128].rearrange("p (j q) -> p j q", q=128))
                        S.op("vector", evh, reads=["pTb"], writes=[("hqf", l2, obl)],
                             tag=f"f0the{sg}_{lt}_{obl}")
                hk = [("hqf", l2, obl) for l2 in range(2) for obl in range(6)]
                for oc in range(8):
                    ps, pk = (ctx.pC, "pC") if oc % 2 == 0 else (ctx.pD, "pD")
                    def mmd2(e, ps=ps, oc=oc):
                        rr = None
                        for cj in range(22):
                            rr = e.matmul(ps[:, 0:256], wd[:, cj, oc * 128:(oc + 1) * 128],
                                          hqf[:, cj, :], start=(cj == 0), stop=(cj == 21))
                        return rr
                    S.op("tensor", mmd2, reads=hk + [("qd", "dst")], writes=[pk],
                         tag=f"f0md{sg}_{half}_{oc}")
                    tc = half * 256
                    S.op("vector", lambda e, oc=oc, tc=tc, ps=ps: e.tensor_add(
                            xsl[:, oc, tc:tc + 256], ps[:, 0:256], xsl[:, oc, tc:tc + 256]),
                         reads=[pk, ("xsl",)], writes=[("xsl",)], tag=f"f0de{sg}_{half}_{oc}")
            S.dma("sync", lambda e, sg=sg: e.dma_start(
                    out=xdst.rearrange("(c p) t -> p c t", p=128)[:, :, sg * 512:(sg + 1) * 512],
                    in_=xsl[:, :, :]),
                  reads=[("xsl",)], writes=[("xout", sg), ("xsl",)], tag=f"f0st{sg}")
        S.fence()

def emit_l1ffn(ctx, xsrc, xdst):
    import contextlib
    nc, S = ctx.nc, ctx.S
    xr = xsrc.rearrange("(c p) t -> p c t", p=128)
    xdr = xdst.rearrange("(c p) t -> p c t", p=128)
    with contextlib.ExitStack() as est:
        es = est.enter_context
        xbw = es(nc.sbuf_tensor("f1_xbw", [128, 8, T], BF16))
        emit_stats(ctx, xsrc, es, tag="f1", xb_dst=xbw)
        rowsNB_, srep_ = ctx.rowsNB, ctx.srep
        rstdcol_ = getattr(ctx, 'rstd_col', None)
        wt1 = es(nc.sbuf_tensor("f1_wt1", [128, 2, 8, 128], BF16))
        wt2 = es(nc.sbuf_tensor("f1_wt2", [128, 2, 8, 128], BF16))
        cs1 = es(nc.sbuf_tensor("f1_cs1", [1, 4096], BF16))
        cs2 = es(nc.sbuf_tensor("f1_cs2", [1, 4096], BF16))
        s1c = es(nc.sbuf_tensor("f1_s1c", [128, 32], F32))
        s2c = es(nc.sbuf_tensor("f1_s2c", [128, 32], F32))
        E1 = es(nc.sbuf_tensor("f1_E1", [128, 512], F32))
        sil = es(nc.sbuf_tensor("f1_sil", [128, 512], BF16))
        hst = es(nc.sbuf_tensor("f1_hst", [128, 2, 512], BF16))
        S.dma("sync", lambda e: e.dma_start(out=s1c[:, :], in_=wap2(ctx, "l1_s1")),
              writes=["s1c"], tag="f1lds1")
        S.dma("sync", lambda e: e.dma_start(out=s2c[:, :], in_=wap2(ctx, "l1_s2")),
              writes=["s2c"], tag="f1lds2")
        for oc in range(32):
            b = oc % 2
            for wi, (wkey, wt, csk) in enumerate([("l1_w1T", wt1, "cs1"), ("l1_w2T", wt2, "cs2")]):
                ap = wap2(ctx, wkey).rearrange("(c p) o -> p c o", p=128)[:, :, oc * 128:(oc + 1) * 128]
                S.dma("gpsimd", lambda e, ap=ap, wt=wt, b=b: e.dma_start(out=wt[:, b], in_=ap),
                      reads=[(csk, "free", b)], writes=[(csk, "wt", b)], tag=f"f1lw{wi}_{oc}")
                cst = cs1 if wi == 0 else cs2
                def csf(e, wt=wt, b=b):
                    rr = None
                    for ci in range(8):
                        rr = e.matmul(ctx.pS[0:1, 0:128], ctx.onescol_b[:], wt[:, b, ci],
                                      start=(ci == 0), stop=(ci == 7))
                    return rr
                S.op("tensor", csf, reads=[(csk, "wt", b), ("c", "onescol_b")], writes=["pS"],
                     tag=f"f1cs{wi}_{oc}")
                S.op("vector", lambda e, cst=cst, oc=oc: e.tensor_copy(
                        cst[0:1, oc * 128:(oc + 1) * 128], ctx.pS[0:1, 0:128]),
                     reads=["pS"], writes=[(csk, "row", oc)], tag=f"f1cse{wi}_{oc}")
            for sg in range(8):
                pk1, ps1 = ("pmA", ctx.pA) if sg % 2 == 0 else ("pmB", ctx.pB)
                pk2, ps2 = ("pmC", ctx.pC) if sg % 2 == 0 else ("pmD", ctx.pD)
                sl = slice(sg * 512, (sg + 1) * 512)
                def mm1(e, b=b, sl=sl, oc=oc, ps=ps1):
                    for ci in range(8):
                        e.matmul(ps[:, :], wt1[:, b, ci], xbw[:, ci, sl], start=(ci == 0), stop=False)
                    return e.matmul(ps[:, :], cs1[0:1, oc * 128:(oc + 1) * 128],
                                    rowsNB_[0:1, sl], start=False, stop=True)
                def mm2(e, b=b, sl=sl, oc=oc, ps=ps2):
                    for ci in range(8):
                        e.matmul(ps[:, :], wt2[:, b, ci], xbw[:, ci, sl], start=(ci == 0), stop=False)
                    return e.matmul(ps[:, :], cs2[0:1, oc * 128:(oc + 1) * 128],
                                    rowsNB_[0:1, sl], start=False, stop=True)
                S.op("tensor", mm1, reads=[("cs1", "wt", b), ("cs1", "row", oc), ("xbw", sg),
                                           ("rowN", sg)], writes=[pk1], tag=f"f1m1_{oc}_{sg}")
                S.op("tensor", mm2, reads=[("cs2", "wt", b), ("cs2", "row", oc), ("xbw", sg),
                                           ("rowN", sg)], writes=[pk2], tag=f"f1m2_{oc}_{sg}")
                S.op("vector", lambda e, sg=sg, ps=ps1: e.tensor_tensor(
                        E1[:, :], ps[:, :], srep_[:, sg, :], ALU.mult),
                     reads=[pk1, ("srep", sg)], writes=["E1"], tag=f"f1e1_{oc}_{sg}")
                S.op("scalar", lambda e, oc=oc: e.activation(sil[:, :], E1[:, :], AF.Silu,
                                                             scale=s1c[:, oc:oc + 1]),
                     reads=["E1", "s1c"], writes=["sil"], tag=f"f1sil_{oc}_{sg}")
                hb = sg % 2
                S.op("vector", lambda e, sg=sg, ps=ps2: e.tensor_tensor(
                        E1[:, :], ps[:, :], srep_[:, sg, :], ALU.mult),
                     reads=[pk2, ("srep", sg), "sil"], writes=["E1"], tag=f"f1e2_{oc}_{sg}")
                S.op("vector", lambda e, oc=oc, hb=hb: e.scalar_tensor_tensor(
                        hst[:, hb, :], E1[:, :], s2c[:, oc:oc + 1], sil[:, :], ALU.mult, ALU.mult),
                     reads=["E1", "s2c", "sil", ("hfree", hb)], writes=[("hst", hb)],
                     tag=f"f1h_{oc}_{sg}")
                S.dma("sync", lambda e, oc=oc, sg=sg, hb=hb: e.dma_start(
                        out=ctx.hD[oc, sg], in_=hst[:, hb, :]),
                      reads=[("hst", hb)], writes=[("hD", oc, sg), ("hfree", hb),
                                                   (csk, "free", b) if sg == 7 else ("na",)],
                      tag=f"f1sth_{oc}_{sg}")
        S.fence()
    for pi, (wkey, skey, srcD, dstD, srck) in enumerate(
            [("l1_h1T", "l1_h1s", ctx.hD, ctx.g1D, "hD"),
             ("l1_h2T", "l1_h2s", ctx.g1D, ctx.g2D, "g1D")]):
        with contextlib.ExitStack() as est:
            es = est.enter_context
            hw = es(nc.sbuf_tensor(f"f1_hw{pi}", [128, 32, 1024], BF16))
            hsl = es(nc.sbuf_tensor(f"f1_hsl{pi}", [128, 2, 32, 512], BF16))
            ssc = es(nc.sbuf_tensor(f"f1_ssc{pi}", [128, 32], F32))
            gst = es(nc.sbuf_tensor(f"f1_gst{pi}", [128, 2, 512], BF16))
            S.dma("sync", lambda e: e.dma_start(out=ssc[:, :], in_=wap2(ctx, skey)),
                  writes=["ssc"], tag=f"f1lds{pi}")
            for ocb in range(4):
                ap = wap2(ctx, wkey).rearrange("(c p) o -> p c o", p=128)[
                    :, :, ocb * 1024:(ocb + 1) * 1024]
                S.dma("gpsimd", lambda e, ap=ap: e.dma_start(out=hw[:, :, :], in_=ap),
                      reads=[("hw",)], writes=[("hw",)], tag=f"f1lhw{pi}_{ocb}")
                for sg in range(8):
                    b = sg % 2
                    S.dma("sync", lambda e, sg=sg, b=b, srcD=srcD: e.dma_start(
                            out=hsl[:, b], in_=srcD[:, sg].rearrange("o p f -> p o f")),
                          reads=[("hslf", b)] + ([("hD", o, sg) for o in range(32)] if pi == 0 and ocb == 0 else []),
                          writes=[("hsl", b)], tag=f"f1lh{pi}_{ocb}_{sg}")
                    for oc in range(8):
                        og = ocb * 8 + oc
                        ps, pk = (ctx.pA, "pmA") if oc % 2 == 0 else (ctx.pB, "pmB")
                        def mmh(e, b=b, og=og, ps=ps):
                            rr = None
                            for cj in range(32):
                                rr = e.matmul(ps[:, :], hw[:, cj, (og % 8) * 128:(og % 8 + 1) * 128],
                                              hsl[:, b, cj, :], start=(cj == 0), stop=(cj == 31))
                            return rr
                        S.op("tensor", mmh, reads=[("hw",), ("hsl", b)], writes=[pk],
                             tag=f"f1mh{pi}_{og}_{sg}")
                        gb = sg % 2
                        S.op("scalar", lambda e, og=og, gb=gb, ps=ps: e.activation(
                                gst[:, gb, :], ps[:, :], AF.Gelu, scale=ssc[:, og:og + 1]),
                             reads=[pk, "ssc", ("gfree", gb)], writes=[("gst", gb)],
                             tag=f"f1g{pi}_{og}_{sg}")
                        S.dma("sync", lambda e, og=og, sg=sg, gb=gb, dstD=dstD: e.dma_start(
                                out=dstD[og, sg], in_=gst[:, gb, :]),
                              reads=[("gst", gb)], writes=[(("g1D" if pi == 0 else "g2D"), og, sg),
                                                           ("gfree", gb)],
                              tag=f"f1sg{pi}_{og}_{sg}")
            S.fence()
    with contextlib.ExitStack() as est:
        es = est.enter_context
        w3 = es(nc.sbuf_tensor("f1_w3", [128, 32, 1024], BF16))
        gsl = es(nc.sbuf_tensor("f1_gsl", [128, 2, 32, 512], BF16))
        s3c = es(nc.sbuf_tensor("f1_s3c", [128, 8], F32))
        xsl = es(nc.sbuf_tensor("f1_xsl", [128, 2, 8, 512], F32))
        S.dma("gpsimd", lambda e: e.dma_start(out=w3[:, :, :],
                in_=wap2(ctx, "l1_w3T").rearrange("(c p) o -> p c o", p=128)),
              writes=[("w3",)], tag="f1lw3")
        S.dma("sync", lambda e: e.dma_start(out=s3c[:, :], in_=wap2(ctx, "l1_s3")),
              writes=["s3c"], tag="f1lds3")
        for sg in range(8):
            b = sg % 2
            S.dma("sync", lambda e, sg=sg, b=b: e.dma_start(
                    out=gsl[:, b], in_=ctx.g2D[:, sg].rearrange("o p f -> p o f")),
                  reads=[("gslf", b)] + [("g2D", o, sg) for o in range(32)],
                  writes=[("gsl", b)], tag=f"f1lg2_{sg}")
            S.dma("sync", lambda e, sg=sg, b=b: e.dma_start(
                    out=xsl[:, b], in_=xr[:, :, sg * 512:(sg + 1) * 512]),
                  reads=[("xslf", b)], writes=[("xsl", b)], tag=f"f1lx{sg}")
            for oc in range(8):
                ps, pk = (ctx.pA, "pmA") if oc % 2 == 0 else (ctx.pB, "pmB")
                def mmo(e, b=b, oc=oc, ps=ps):
                    rr = None
                    for cj in range(32):
                        rr = e.matmul(ps[:, :], w3[:, cj, oc * 128:(oc + 1) * 128],
                                      gsl[:, b, cj, :], start=(cj == 0), stop=(cj == 31))
                    return rr
                S.op("tensor", mmo, reads=[("w3",), ("gsl", b)], writes=[pk], tag=f"f1mo{oc}_{sg}")
                S.op("vector", lambda e, oc=oc, b=b, ps=ps: e.scalar_tensor_tensor(
                        xsl[:, b, oc, :], ps[:, :], s3c[:, oc:oc + 1], xsl[:, b, oc, :],
                        ALU.mult, ALU.add),
                     reads=[pk, "s3c", ("xsl", b)], writes=[("xsl", b)], tag=f"f1oe{oc}_{sg}")
            S.dma("sync", lambda e, sg=sg, b=b: e.dma_start(
                    out=xdr[:, :, sg * 512:(sg + 1) * 512], in_=xsl[:, b]),
                  reads=[("xsl", b)], writes=[("xout", sg), ("xslf", b), ("gslf", b)],
                  tag=f"f1st{sg}")
        S.fence()

def emit_l2ffn(ctx, xsrc, xdst):
    import contextlib
    nc, S = ctx.nc, ctx.S
    xr = xsrc.rearrange("(c p) t -> p c t", p=128)
    xdr = xdst.rearrange("(c p) t -> p c t", p=128)
    with contextlib.ExitStack() as est:
        es = est.enter_context
        emit_stats(ctx, xsrc, es, tag="f2")
        rowsNB_, srep_ = ctx.rowsNB, ctx.srep
        rstdcol_ = getattr(ctx, 'rstd_col', None)
        xln = es(nc.sbuf_tensor("f2_xln", [128, 8, T], BF16))
        xsl = es(nc.sbuf_tensor("f2_xsl", [128, 2, 8, 512], F32))
        wr = es(nc.sbuf_tensor("f2_wr", [128, 8, 1024], BF16))
        wk = es(nc.sbuf_tensor("f2_wk", [128, 8, 1024], BF16))
        wv = es(nc.sbuf_tensor("f2_wv", [128, 8, 1024], BF16))
        mur = es(nc.sbuf_tensor("f2_mur", [128, 8], F32))
        muk = es(nc.sbuf_tensor("f2_muk", [128, 8], F32))
        AA = es(nc.sbuf_tensor("f2_A", [128, 8, 512], F32))
        xmix = es(nc.sbuf_tensor("f2_xmix", [128, 8, 512], BF16))
        rsb = es(nc.sbuf_tensor("f2_r", [128, 8, 512], BF16))
        kact = es(nc.sbuf_tensor("f2_k", [128, 8, 512], BF16))
        for wkey, wsb, tagw in [("l2_wrT", wr, "wr"), ("l2_wkT2", wk, "wk"), ("l2_wvT2", wv, "wv")]:
            S.dma("gpsimd", lambda e, wkey=wkey, wsb=wsb: e.dma_start(out=wsb[:, :, :],
                    in_=wap2(ctx, wkey).rearrange("(c p) o -> p c o", p=128)),
                  writes=[(tagw,)], tag=f"f2l{tagw}")
        S.dma("sync", lambda e: e.dma_start(out=mur[:, :], in_=wap2(ctx, "l2_mur")),
              writes=["mur"], tag="f2lmur")
        S.dma("sync", lambda e: e.dma_start(out=muk[:, :], in_=wap2(ctx, "l2_muk")),
              writes=["muk"], tag="f2lmuk")
        for sg in range(8):
            b = sg % 2
            sl = slice(sg * 512, (sg + 1) * 512)
            S.dma("sync", lambda e, sl=sl, b=b: e.dma_start(out=xsl[:, b], in_=xr[:, :, sl]),
                  reads=[("xslf", b)], writes=[("xsl", b)], tag=f"f2lx{sg}")
            S.op("tensor", lambda e, sg=sg, sl=sl: e.matmul(
                    ctx.pE[:, :], ctx.onesrow_b[:, :], rowsNB_[0:1, sl], start=True, stop=True),
                 reads=[("rowN", sg), ("c", "onesrow_b")], writes=["pE"], tag=f"f2nmu{sg}")
            S.op("vector", lambda e, b=b: e.tensor_tensor(
                    AA[:, :, :], xsl[:, b], ctx.pE[:, None, :].broadcast_to([128, 8, 512]),
                    ALU.add),
                 reads=[("xsl", b), "pE"], writes=[("A2",)], tag=f"f2a{sg}")
            S.op("vector", lambda e, sg=sg, sl=sl: e.tensor_tensor(
                    xln[:, :, sl], AA[:, :, :],
                    srep_[:, sg, None, :].broadcast_to([128, 8, 512]), ALU.mult),
                 reads=[("A2",), ("srep", sg)], writes=[("xln", sg), ("xslf", b)], tag=f"f2xl{sg}")
        for sg in range(8):
            b = sg % 2
            sl = slice(sg * 512, (sg + 1) * 512)
            S.dma("sync", lambda e, sl=sl, b=b: e.dma_start(out=xsl[:, b], in_=xr[:, :, sl]),
                  reads=[("xslf", b)], writes=[("xsl", b)], tag=f"f2lx2{sg}")
            rd = [("xln", sg)] + ([("xln", sg - 1)] if sg > 0 else [])
            for mi, muc in enumerate([mur, muk]):
                c0 = sg * 512
                mk = "mur" if mi == 0 else "muk"
                if sg > 0:
                    S.op("vector", lambda e, c0=c0: e.tensor_sub(
                            AA[:, :, :], xln[:, :, c0:c0 + 512], xln[:, :, c0 - 1:c0 + 511]),
                         reads=rd, writes=[("A2",)], tag=f"f2sub{mi}_{sg}")
                    for ci in range(8):
                        S.op("vector", lambda e, ci=ci, c0=c0, muc=muc: e.scalar_tensor_tensor(
                                xmix[:, ci, :], AA[:, ci, :], muc[:, ci:ci + 1],
                                xln[:, ci, c0 - 1:c0 + 511], ALU.mult, ALU.add),
                             reads=[("A2",), mk] + rd, writes=[("xmix",)], tag=f"f2mx{mi}_{sg}_{ci}")
                else:
                    S.op("vector", lambda e: e.tensor_sub(
                            AA[:, :, 1:512], xln[:, :, 1:512], xln[:, :, 0:511]),
                         reads=rd, writes=[("A2",)], tag=f"f2sub{mi}_{sg}")
                    for ci in range(8):
                        S.op("vector", lambda e, ci=ci, muc=muc: e.scalar_tensor_tensor(
                                xmix[:, ci, 1:512], AA[:, ci, 1:512], muc[:, ci:ci + 1],
                                xln[:, ci, 0:511], ALU.mult, ALU.add),
                             reads=[("A2",), mk] + rd, writes=[("xmix",)], tag=f"f2mx{mi}_{sg}_{ci}")
                        S.op("vector", lambda e, ci=ci, muc=muc: e.tensor_scalar(
                                xmix[:, ci, 0:1], xln[:, ci, 0:1], muc[:, ci:ci + 1], None, ALU.mult),
                             reads=[mk] + rd, writes=[("xmix",)], narrow=True,
                             tag=f"f2mx0{mi}_{sg}_{ci}")
                wsb, wkk = (wr, ("wr",)) if mi == 0 else (wk, ("wk",))
                for oc in range(8):
                    ps, pk = (ctx.pA, "pmA") if oc % 2 == 0 else (ctx.pB, "pmB")
                    def mmr(e, oc=oc, wsb=wsb, ps=ps):
                        rr = None
                        for ci in range(8):
                            rr = e.matmul(ps[:, :], wsb[:, ci, oc * 128:(oc + 1) * 128],
                                          xmix[:, ci, :], start=(ci == 0), stop=(ci == 7))
                        return rr
                    S.op("tensor", mmr, reads=[("xmix",), wkk], writes=[pk],
                         tag=f"f2mm{mi}_{oc}_{sg}")
                    if mi == 0:
                        S.op("scalar", lambda e, oc=oc, ps=ps: e.activation(
                                rsb[:, oc, :], ps[:, :], AF.Sigmoid),
                             reads=[pk], writes=[("rsb",)], tag=f"f2sig{oc}_{sg}")
                    else:
                        S.op("vector", lambda e, oc=oc, ps=ps: e.tensor_scalar(
                                AA[:, oc, :], ps[:, :], 0.0, None, ALU.max),
                             reads=[pk], writes=[("A2",)], tag=f"f2rel{oc}_{sg}")
                        S.op("scalar", lambda e, oc=oc: e.activation(
                                kact[:, oc, :], AA[:, oc, :], AF.Square),
                             reads=[("A2",)], writes=[("kact",)], tag=f"f2sq{oc}_{sg}")
            for oc in range(8):
                ps, pk = (ctx.pC, "pmC") if oc % 2 == 0 else (ctx.pD, "pmD")
                def mmv(e, oc=oc, ps=ps):
                    rr = None
                    for ci in range(8):
                        rr = e.matmul(ps[:, :], wv[:, ci, oc * 128:(oc + 1) * 128],
                                      kact[:, ci, :], start=(ci == 0), stop=(ci == 7))
                    return rr
                S.op("tensor", mmv, reads=[("kact",), ("wv",)], writes=[pk], tag=f"f2mv{oc}_{sg}")
                S.op("vector", lambda e, oc=oc, ps=ps: e.tensor_tensor(
                        AA[:, oc, :], ps[:, :], rsb[:, oc, :], ALU.mult),
                     reads=[pk, ("rsb",)], writes=[("A2",)], tag=f"f2om{oc}_{sg}")
                S.op("vector", lambda e, oc=oc, b=b: e.tensor_add(
                        xsl[:, b, oc, :], xsl[:, b, oc, :], AA[:, oc, :]),
                     reads=[("A2",), ("xsl", b)], writes=[("xsl", b)], tag=f"f2oe{oc}_{sg}")
            S.dma("sync", lambda e, sg=sg, b=b: e.dma_start(
                    out=xdr[:, :, sg * 512:(sg + 1) * 512], in_=xsl[:, b]),
                  reads=[("xsl", b)], writes=[("xout", sg), ("xslf", b)], tag=f"f2st{sg}")
        S.fence()

def emit_l3attn(ctx, xsrc, xdst):
    import contextlib
    nc, S = ctx.nc, ctx.S
    xr = xsrc.rearrange("(c p) t -> p c t", p=128)
    xdr = xdst.rearrange("(c p) t -> p c t", p=128)
    QT = 1024
    with contextlib.ExitStack() as est:
        es = est.enter_context
        emit_stats(ctx, xsrc, es, tag="f3")
        rowsNB_, srep_ = ctx.rowsNB, ctx.srep
        rstdcol_ = getattr(ctx, 'rstd_col', None)
        xbq = es(nc.sbuf_tensor("f3_xb", [128, 8, QT], BF16))
        xin_ = es(nc.sbuf_tensor("f3_xi", [128, 16, QT], BF16))
        zb = es(nc.sbuf_tensor("f3_z", [128, 16, QT], BF16))
        Gt = es(nc.sbuf_tensor("f3_Gt", [128, QT], BF16))
        ht = es(nc.sbuf_tensor("f3_ht", [128, QT], BF16))
        wtile = es(nc.sbuf_tensor("f3_wt", [128, 2, 8, 128], BF16))
        wot = es(nc.sbuf_tensor("f3_wot", [128, 2, 16, 128], BF16))
        cs3 = es(nc.sbuf_tensor("f3_cs", [1, 4096], BF16))
        convw = es(nc.sbuf_tensor("f3_cw", [128, 16, 4], F32))
        convb = es(nc.sbuf_tensor("f3_cb", [128, 16], F32))
        wxc = es(nc.sbuf_tensor("f3_wxc", [128, 16, 16], BF16))
        Bt = es(nc.sbuf_tensor("f3_Bt", [16, 2048], BF16))
        import contextlib as _cl
        with _cl.ExitStack() as est2:
            Btf = est2.enter_context(nc.sbuf_tensor("f3_Btf", [16, 2048], F32))
            S.dma("sync", lambda e: e.dma_start(out=Btf[:, :], in_=wap2(ctx, "l3_blogT")),
                  writes=["Btf"], tag="f3lbt")
            S.op("scalar", lambda e: e.activation(Bt[:, :], Btf[:, :], AF.Exp),
                 reads=["Btf"], writes=["Bt"], tag="f3expB")
            S.fence()
        sC = es(nc.sbuf_tensor("f3_sC", [16, QT], BF16))
        halo = es(nc.sbuf_tensor("f3_halo", [128, 16, 3], BF16))
        hcar = es(nc.sbuf_tensor("f3_hc", [128, 16, 1], F32))
        dec = es(nc.sbuf_tensor("f3_dec", [128, QT], F32))
        CV = es(nc.sbuf_tensor("f3_CV", [128, QT], F32))
        lng = es(nc.sbuf_tensor("f3_lng", [128, 8], F32))
        lnb = es(nc.sbuf_tensor("f3_lnb", [128, 8], F32))
        xsl = es(nc.sbuf_tensor("f3_xsl", [128, 8, 512], F32))
        XLN = es(nc.sbuf_tensor("f3_xln", [128, 8, 512], F32))
        Rb = es(nc.sbuf_tensor("f3_Rb", [128, 8, 512], BF16))
        R2 = es(nc.sbuf_tensor("f3_R2", [128, 8, 512], BF16))
        st_r = [es(nc.sbuf_tensor(f"f3_sr{i}", [1, 512], F32)) for i in range(5)]
        murow, musq, varr, sigr, rst2 = st_r

        S.dma("sync", lambda e: e.dma_start(out=convw[:, :, :], in_=wap2(ctx, "l3_convw")),
              writes=["cw"], tag="f3lcw")
        S.dma("sync", lambda e: e.dma_start(out=convb[:, :], in_=wap2(ctx, "l3_convb")),
              writes=["cb"], tag="f3lcb")
        S.dma("sync", lambda e: e.dma_start(out=lng[:, :], in_=wap2(ctx, "l3_lng")),
              writes=["lng"], tag="f3llng")
        S.dma("sync", lambda e: e.dma_start(out=lnb[:, :], in_=wap2(ctx, "l3_lnb")),
              writes=["lnb"], tag="f3llnb")
        S.dma("gpsimd", lambda e: e.dma_start(out=wxc[:, :, :],
                in_=wap2(ctx, "l3_wxcT").rearrange("(c p) o -> p c o", p=128)),
              writes=["wxc"], tag="f3lwxc")
        S.op("vector", lambda e: e.memset(dec[:, :], 0.95), writes=["dec"], tag="f3dec")
        S.op("vector", lambda e: e.memset(halo[:, :, :], 0.0), writes=["halo"], tag="f3hz")
        S.op("vector", lambda e: e.memset(hcar[:, :, :], 0.0), writes=["hcar"], tag="f3hc0")

        for q in range(4):
            tq = q * QT
            S.dma("gpsimd", lambda e, tq=tq: e.dma_start(
                    out=xbq[:, :, :], in_=xr[:, :, tq:tq + QT]),
                  reads=[("xbq",)], writes=[("xbq",)], tag=f"f3lxb{q}")
            for oc in range(32):
                b = oc % 2
                ap = wap2(ctx, "l3_winT").rearrange("(c p) o -> p c o", p=128)[
                    :, :, oc * 128:(oc + 1) * 128]
                S.dma("gpsimd", lambda e, ap=ap, b=b: e.dma_start(out=wtile[:, b], in_=ap),
                      reads=[("wtf", b)], writes=[("wt3", b)], tag=f"f3lw{q}_{oc}")
                if q == 0:
                    def csf(e, b=b):
                        rr = None
                        for ci in range(8):
                            rr = e.matmul(ctx.pS[0:1, 0:128], ctx.onescol_b[:], wtile[:, b, ci],
                                          start=(ci == 0), stop=(ci == 7))
                        return rr
                    S.op("tensor", csf, reads=[("wt3", b), ("c", "onescol_b")], writes=["pS"],
                         tag=f"f3cs{oc}")
                    S.op("vector", lambda e, oc=oc: e.tensor_copy(
                            cs3[0:1, oc * 128:(oc + 1) * 128], ctx.pS[0:1, 0:128]),
                         reads=["pS"], writes=[("cs3", oc)], tag=f"f3cse{oc}")
                for s2 in range(2):
                    sg = q * 2 + s2
                    sl = slice(s2 * 512, (s2 + 1) * 512)
                    ps, pk = (ctx.pA, "pmA") if s2 == 0 else (ctx.pB, "pmB")
                    def mmz(e, b=b, oc=oc, sl=sl, sg=sg, ps=ps):
                        for ci in range(8):
                            e.matmul(ps[:, :], wtile[:, b, ci], xbq[:, ci, sl],
                                     start=(ci == 0), stop=False)
                        return e.matmul(ps[:, :], cs3[0:1, oc * 128:(oc + 1) * 128],
                                        rowsNB_[0:1, sg * 512:(sg + 1) * 512],
                                        start=False, stop=True)
                    S.op("tensor", mmz, reads=[("wt3", b), ("xbq",), ("cs3", oc), ("rowN", sg)],
                         writes=[pk], tag=f"f3mz{q}_{oc}_{s2}")
                    dstb = xin_ if oc < 16 else zb
                    dk = ("xin", oc % 16) if oc < 16 else ("zb", oc % 16)
                    S.op("vector", lambda e, oc=oc, sl=sl, sg=sg, dstb=dstb, ps=ps: e.tensor_tensor(
                            dstb[:, oc % 16, sl], ps[:, :], srep_[:, sg, :], ALU.mult),
                         reads=[pk, ("srep", sg)], writes=[dk, ("wtf", b) if s2 == 1 else ("na2",)],
                         tag=f"f3ze{q}_{oc}_{s2}")
            for cc in range(16):
                S.op("vector", lambda e, cc=cc: e.tensor_scalar(
                        CV[:, :], xin_[:, cc, :], convw[:, cc, 3:4], None, ALU.mult),
                     reads=[("xin", cc), "cw"], writes=["CV"], tag=f"f3cv3{q}_{cc}")
                for k in range(3):
                    d = 3 - k
                    S.op("vector", lambda e, cc=cc, k=k, d=d: e.scalar_tensor_tensor(
                            CV[:, d:QT], xin_[:, cc, 0:QT - d], convw[:, cc, k:k + 1],
                            CV[:, d:QT], ALU.mult, ALU.add),
                         reads=[("xin", cc), "cw", "CV"], writes=["CV"], tag=f"f3cv{q}_{cc}_{k}")
                    S.op("vector", lambda e, cc=cc, k=k, d=d: e.scalar_tensor_tensor(
                            CV[:, 0:d], halo[:, cc, 3 - d:3], convw[:, cc, k:k + 1],
                            CV[:, 0:d], ALU.mult, ALU.add),
                         reads=["halo", "cw", "CV"], writes=["CV"], narrow=True,
                         tag=f"f3cvb{q}_{cc}_{k}")
                S.op("vector", lambda e, cc=cc: e.tensor_copy(halo[:, cc, :],
                                                              xin_[:, cc, QT - 3:QT]),
                     reads=[("xin", cc)], writes=["halo"], narrow=True, tag=f"f3hs{q}_{cc}")
                S.op("scalar", lambda e, cc=cc: e.activation(
                        xin_[:, cc, :], CV[:, :], AF.Silu, bias=convb[:, cc:cc + 1]),
                     reads=["CV", "cb", "halo"], writes=[("xin", cc)], tag=f"f3sl{q}_{cc}")
            for s2 in range(2):
                sl = slice(s2 * 512, (s2 + 1) * 512)
                def mmc(e, sl=sl):
                    rr = None
                    for cc in range(16):
                        rr = e.matmul(ctx.pS[0:16, :], wxc[:, cc, :], xin_[:, cc, sl],
                                      start=(cc == 0), stop=(cc == 15))
                    return rr
                S.op("tensor", mmc, reads=[("xin", cc2) for cc2 in range(16)] + ["wxc"],
                     writes=["pS"], tag=f"f3sc{q}_{s2}")
                S.op("vector", lambda e, sl=sl: e.tensor_copy(sC[:, sl], ctx.pS[0:16, :]),
                     reads=["pS"], writes=["sC"], tag=f"f3sce{q}_{s2}")
            for cc in range(16):
                def scn(e, cc=cc, q=q):
                    ini = 0.0 if q == 0 else hcar[:, cc, :]
                    return e.tensor_tensor_scan(ht[:, :], dec[:, :], xin_[:, cc, :], ini,
                                                ALU.mult, ALU.add)
                S.op("vector", scn, reads=[("xin", cc), "dec", "hcar", "ht"],
                     writes=["ht"], tag=f"f3sn{q}_{cc}")
                S.op("vector", lambda e, cc=cc: e.tensor_copy(hcar[:, cc, :], ht[:, QT - 1:QT]),
                     reads=["ht"], writes=["hcar"], narrow=True, tag=f"f3snc{q}_{cc}")
                for s2 in range(2):
                    sl = slice(s2 * 512, (s2 + 1) * 512)
                    ps, pk = (ctx.pC, "pmC") if s2 == 0 else (ctx.pD, "pmD")
                    S.op("tensor", lambda e, cc=cc, sl=sl, ps=ps: e.matmul(
                            ps[:, :], Bt[:, cc * 128:(cc + 1) * 128], sC[:, sl],
                            start=True, stop=True),
                         reads=["Bt", "sC"], writes=[pk], tag=f"f3g{q}_{cc}_{s2}")
                    S.op("vector", lambda e, sl=sl, ps=ps: e.tensor_copy(Gt[:, sl], ps[:, :]),
                         reads=[pk], writes=["Gtk"], tag=f"f3ge{q}_{cc}_{s2}")
                S.op("scalar", lambda e, cc=cc: e.activation(zb[:, cc, :], zb[:, cc, :], AF.Silu),
                     reads=[("zb", cc)], writes=[("zb", cc)], tag=f"f3sz{q}_{cc}")
                S.op("vector", lambda e, cc=cc: e.tensor_tensor(
                        CV[:, :], ht[:, :], Gt[:, :], ALU.mult),
                     reads=["ht", "Gtk"], writes=["CV"], tag=f"f3ym{q}_{cc}")
                S.op("vector", lambda e, cc=cc: e.tensor_tensor(
                        zb[:, cc, :], CV[:, :], zb[:, cc, :], ALU.mult),
                     reads=["CV", ("zb", cc)], writes=[("zb", cc)], tag=f"f3y{q}_{cc}")
            for s2 in range(2):
                sg = q * 2 + s2
                sl = slice(s2 * 512, (s2 + 1) * 512)
                gsl = slice(sg * 512, (sg + 1) * 512)
                S.dma("sync", lambda e, gsl=gsl: e.dma_start(out=xsl[:, :, :], in_=xr[:, :, gsl]),
                      reads=[("xsl3",)], writes=[("xsl3",)], tag=f"f3lx{q}_{s2}")
                S.op("tensor", lambda e, sg=sg, gsl=gsl: e.matmul(
                        ctx.pE[:, :], ctx.onesrow_b[:, :], rowsNB_[0:1, gsl],
                        start=True, stop=True),
                     reads=[("rowN", sg), ("c", "onesrow_b")], writes=["pE"], tag=f"f3nm{q}_{s2}")
                S.op("vector", lambda e: e.tensor_tensor(
                        XLN[:, :, :], xsl[:, :, :], ctx.pE[:, None, :].broadcast_to([128, 8, 512]),
                        ALU.add),
                     reads=[("xsl3",), "pE"], writes=[("XLN",)], tag=f"f3xl{q}_{s2}")
                S.op("vector", lambda e, sg=sg: e.tensor_tensor(
                        XLN[:, :, :], XLN[:, :, :],
                        srep_[:, sg, None, :].broadcast_to([128, 8, 512]), ALU.mult),
                     reads=[("XLN",), ("srep", sg)], writes=[("XLN",)], tag=f"f3r1{q}_{s2}")
                for oc in range(8):
                    b = oc % 2
                    ap = wap2(ctx, "l3_woutT").rearrange("(c p) o -> p c o", p=128)[
                        :, :, oc * 128:(oc + 1) * 128]
                    S.dma("gpsimd", lambda e, ap=ap, b=b: e.dma_start(out=wot[:, b], in_=ap),
                          reads=[("wotf", b)], writes=[("wot", b)], tag=f"f3lwo{q}_{s2}_{oc}")
                    ps, pk = (ctx.pA, "pmA") if oc % 2 == 0 else (ctx.pB, "pmB")
                    def mmo(e, b=b, sl=sl, ps=ps):
                        rr = None
                        for cc in range(16):
                            rr = e.matmul(ps[:, :], wot[:, b, cc, :], zb[:, cc, sl],
                                          start=(cc == 0), stop=(cc == 15))
                        return rr
                    S.op("tensor", mmo, reads=[("wot", b)] + [("zb", cc2) for cc2 in range(16)],
                         writes=[pk], tag=f"f3mo{q}_{s2}_{oc}")
                    S.op("vector", lambda e, oc=oc, ps=ps: e.tensor_add(
                            XLN[:, oc, :], ps[:, :], XLN[:, oc, :]),
                         reads=[pk, ("XLN",)], writes=[("XLN",), ("wotf", b)],
                         tag=f"f3moe{q}_{s2}_{oc}")
                S.op("vector", lambda e: e.tensor_copy(Rb.rearrange("p c t -> p (c t)"),
                                                       XLN.rearrange("p c t -> p (c t)")),
                     reads=[("XLN",)], writes=[("Rb",)], tag=f"f3rb{q}_{s2}")
                def mms1(e):
                    rr = None
                    for ci in range(8):
                        rr = e.matmul(ctx.pS[0:1, :], ctx.onescol_b[:], Rb[:, ci, :],
                                      start=(ci == 0), stop=(ci == 7))
                    return rr
                S.op("tensor", mms1, reads=[("Rb",), ("c", "onescol_b")],
                     writes=["pS"], tag=f"f3ms{q}_{s2}")
                S.op("vector", lambda e: e.tensor_scalar(murow[:], ctx.pS[0:1, :], -1.0 / D,
                                                         None, ALU.mult),
                     reads=["pS"], writes=["mu3"], tag=f"f3mu{q}_{s2}")
                S.op("tensor", lambda e: e.matmul(ctx.pE[:, :], ctx.onesrow_f[:, :], murow[:],
                                                  start=True, stop=True),
                     reads=["mu3", ("c", "onesrow_f")], writes=["pE"], tag=f"f3rpm{q}_{s2}")
                S.op("vector", lambda e: e.tensor_tensor(
                        XLN[:, :, :], XLN[:, :, :],
                        ctx.pE[:, None, :].broadcast_to([128, 8, 512]), ALU.add),
                     reads=[("XLN",), "pE"], writes=[("XLN",)], tag=f"f3sub{q}_{s2}")
                S.op("scalar", lambda e: e.activation(R2.rearrange("p c t -> p (c t)"),
                                                      XLN.rearrange("p c t -> p (c t)"), AF.Square),
                     reads=[("XLN",)], writes=[("R2",)], tag=f"f3r2{q}_{s2}")
                def mms2(e):
                    rr = None
                    for ci in range(8):
                        rr = e.matmul(ctx.pF[0:1, :], ctx.onescol_b[:], R2[:, ci, :],
                                      start=(ci == 0), stop=(ci == 7))
                    return rr
                S.op("tensor", mms2, reads=[("R2",), ("c", "onescol_b")],
                     writes=["pF"], tag=f"f3ms2{q}_{s2}")
                S.op("vector", lambda e: e.tensor_scalar(varr[:], ctx.pF[0:1, :], 1.0 / D,
                                                         None, ALU.mult),
                     reads=["pF"], writes=["var3"], tag=f"f3vr{q}_{s2}")
                S.op("scalar", lambda e: e.activation(sigr[:], varr[:], AF.Sqrt,
                                                      bias=ctx.eps1[0:1, :]),
                     reads=["var3", ("c", "eps1")], writes=["sig3"], tag=f"f3sg{q}_{s2}")
                S.op("vector", lambda e: e.reciprocal(rst2[:], sigr[:]),
                     reads=["sig3"], writes=["rst3"], tag=f"f3rs{q}_{s2}")
                S.op("tensor", lambda e: e.matmul(ctx.pF[:, :], ctx.onesrow_f[:, :], rst2[:],
                                                  start=True, stop=True),
                     reads=["rst3", ("c", "onesrow_f")], writes=["pF"], tag=f"f3rps{q}_{s2}")
                S.op("vector", lambda e: e.tensor_tensor(
                        XLN[:, :, :], XLN[:, :, :],
                        ctx.pF[:, None, :].broadcast_to([128, 8, 512]), ALU.mult),
                     reads=[("XLN",), "pF"], writes=[("XLN",)], tag=f"f3finb{q}_{s2}")
                for ci in range(8):
                    S.op("vector", lambda e, ci=ci: e.tensor_scalar(
                            XLN[:, ci, :], XLN[:, ci, :], lng[:, ci:ci + 1],
                            lnb[:, ci:ci + 1], ALU.mult, ALU.add),
                         reads=[("XLN",), "lng", "lnb"], writes=[("XLN",)],
                         tag=f"f3ga{q}_{s2}_{ci}")
                    S.op("vector", lambda e, ci=ci: e.tensor_add(
                            xsl[:, ci, :], xsl[:, ci, :], XLN[:, ci, :]),
                         reads=[("XLN",), ("xsl3",)], writes=[("xsl3",)],
                         tag=f"f3gb{q}_{s2}_{ci}")
                S.dma("sync", lambda e, gsl=gsl: e.dma_start(out=xdr[:, :, gsl], in_=xsl[:, :, :]),
                      reads=[("xsl3",)], writes=[("xout", sg), ("xsl3",)], tag=f"f3st{q}_{s2}")
        S.fence()

def emit_l3ffn(ctx, xsrc, xdst):
    import contextlib
    nc, S = ctx.nc, ctx.S
    xr = xsrc.rearrange("(c p) t -> p c t", p=128)
    xdr = xdst.rearrange("(c p) t -> p c t", p=128)
    with contextlib.ExitStack() as est:
        es = est.enter_context
        emit_stats(ctx, xsrc, es, tag="f4")
        rowsNB_, srep_ = ctx.rowsNB, ctx.srep
        rstdcol_ = getattr(ctx, 'rstd_col', None)
        xb = es(nc.sbuf_tensor("f4_xb", [128, 8, 512], BF16))
        wbufs = [es(nc.sbuf_tensor(f"f4_w{i}", [128, 8, 1024], BF16)) for i in range(2)]
        wcbq = es(nc.sbuf_tensor("f4_wcb", [2, 2, 1024], BF16))
        vf = es(nc.sbuf_tensor("f4_vf", [128, 8, 512], BF16))
        wf = es(nc.sbuf_tensor("f4_wf", [128, 8, 512], BF16))
        vi = es(nc.sbuf_tensor("f4_vi", [128, 8, 512], BF16))
        wi = es(nc.sbuf_tensor("f4_wi", [128, 8, 512], BF16))
        uf = es(nc.sbuf_tensor("f4_uf", [128, 4, 1024], BF16))
        ui = es(nc.sbuf_tensor("f4_ui", [128, 4, 1024], BF16))
        bf_ = es(nc.sbuf_tensor("f4_bf", [8, 1024], BF16))
        bi_ = es(nc.sbuf_tensor("f4_bi", [8, 1024], BF16))
        rw2 = es(nc.sbuf_tensor("f4_rw2", [128, 8, 8], BF16))
        rb1 = es(nc.sbuf_tensor("f4_rb1", [128, 8], F32))
        b2e = es(nc.sbuf_tensor("f4_b2e", [1, 8], BF16))
        alph = es(nc.sbuf_tensor("f4_al", [1, 1], F32))
        alphc = es(nc.sbuf_tensor("f4_alc", [128, 1], F32))
        Qs = es(nc.sbuf_tensor("f4_Q", [128, 8, 512], BF16))
        Ks = es(nc.sbuf_tensor("f4_K", [128, 8, 512], BF16))
        iQs = es(nc.sbuf_tensor("f4_iQ", [128, 8, 512], BF16))
        iKs = es(nc.sbuf_tensor("f4_iK", [128, 8, 512], BF16))
        xV = es(nc.sbuf_tensor("f4_xV", [128, 4, 512], BF16))
        cum = es(nc.sbuf_tensor("f4_cum", [128, 4, 512], BF16))
        prodw = es(nc.sbuf_tensor("f4_pw", [128, 4, 512], BF16))
        wrep = es(nc.sbuf_tensor("f4_wrep", [128, 4, 512], BF16))
        hB = es(nc.sbuf_tensor("f4_hB", [128, 8, 512], BF16))
        wTM = es(nc.sbuf_tensor("f4_wTM", [128, 4, 8], BF16))
        eW = es(nc.sbuf_tensor("f4_eW", [128, 4, 8], F32))
        sumr = es(nc.sbuf_tensor("f4_sum", [128, 1], F32))
        wTr = es(nc.sbuf_tensor("f4_wTr", [8, 512], BF16))
        comb = es(nc.sbuf_tensor("f4_comb", [128, 8, 512], BF16))
        ccar = es(nc.sbuf_tensor("f4_cc", [128, 2, 4, 1], F32))
        ones5 = es(nc.sbuf_tensor("f4_ones", [128, 512], F32))
        nvis = es(nc.sbuf_tensor("f4_nvi", [1, 512], F32))
        nvir = es(nc.sbuf_tensor("f4_nvir", [128, 512], F32))
        AA = es(nc.sbuf_tensor("f4_A", [128, 512], F32))
        xsl = es(nc.sbuf_tensor("f4_xsl", [128, 8, 512], F32))

        for key, dst, kk in [("l3_vfwd", vf, "vf"), ("l3_wfwd", wf, "wfw"),
                             ("l3_vinv", vi, "vi"), ("l3_winvexp", wi, "wiv")]:
            S.dma("gpsimd", lambda e, key=key, dst=dst: e.dma_start(out=dst[:, :, :],
                    in_=wap2(ctx, key).rearrange("(c p) o -> p c o", p=128)),
                  writes=[(kk,)], tag=f"f4l{kk}")
        for key, dst, kk in [("l3_ufwd", uf, "uf"), ("l3_uinv", ui, "ui")]:
            S.dma("gpsimd", lambda e, key=key, dst=dst: e.dma_start(out=dst[:, :, :],
                    in_=wap2(ctx, key).rearrange("(c p) o -> p c o", p=128)),
                  writes=[(kk,)], tag=f"f4l{kk}")
        S.dma("gpsimd", lambda e: e.dma_start(out=bf_[:, :], in_=wap2(ctx, "l3_bfwd")),
              writes=["bf"], tag="f4lbf")
        S.dma("gpsimd", lambda e: e.dma_start(out=bi_[:, :], in_=wap2(ctx, "l3_binv")),
              writes=["bi"], tag="f4lbi")
        S.dma("gpsimd", lambda e: e.dma_start(out=rw2[:, :, :],
                in_=wap2(ctx, "l3_rw2T").rearrange("(c p) o -> p c o", p=128)),
              writes=["rw2"], tag="f4lrw2")
        S.dma("sync", lambda e: e.dma_start(out=rb1[:, :], in_=wap2(ctx, "l3_rb1")),
              writes=["rb1"], tag="f4lrb1")
        S.dma("gpsimd", lambda e: e.dma_start(out=b2e[:, :],
                in_=wap2(ctx, "l3_rb2e").rearrange("(r c) -> r c", r=1)),
              writes=["b2e"], tag="f4lb2e")
        S.dma("sync", lambda e: e.dma_start(out=alph[:, :],
                in_=wap2(ctx, "l3_alpha").rearrange("(r c) -> r c", r=1)),
              writes=["alph"], tag="f4lal")
        S.op("tensor", lambda e: e.matmul(ctx.pF[:, 0:1], ctx.onesrow_f[:, :], alph[:, :],
                                          start=True, stop=True),
             reads=["alph", ("c", "onesrow_f")], writes=["pF"], tag="f4alm")
        S.op("vector", lambda e: e.tensor_copy(alphc[:, :], ctx.pF[:, 0:1]),
             reads=["pF"], writes=["alphc"], narrow=True, tag="f4ale")
        S.op("vector", lambda e: e.memset(ones5[:, :], 1.0), writes=["ones5"], tag="f4o5")
        S.op("vector", lambda e: e.memset(wcbq[0:2, :, :], 0.0), writes=["wcbz"], tag="f4wz")

        def projq(dst, dk, wbuf, wk_, cso, sg, fold, srckey, src):
            for oc in range(8):
                ps, pk = (ctx.pA, "pmA") if oc % 2 == 0 else (ctx.pB, "pmB")
                def mm(e, oc=oc, sg=sg, ps=ps):
                    rr = None
                    for ci in range(8):
                        rr = e.matmul(ps[:, :], wbuf[:, ci, oc * 128:(oc + 1) * 128],
                                      src[:, ci, :], start=(ci == 0), stop=(ci == 7) and not fold)
                    if fold:
                        rr = e.matmul(ps[:, :], wcbq[:, cso, oc * 128:(oc + 1) * 128],
                                      rowsNB_[:, sg * 512:(sg + 1) * 512], start=False, stop=True)
                    return rr
                rds = [srckey, wk_] + ([("wcbq", cso), "wcbz", ("rowN", sg), ("rowS", sg)] if fold else [])
                S.op("tensor", mm, reads=rds, writes=[pk], tag=f"f4p{dk}_{oc}_{sg}")
                if fold:
                    S.op("vector", lambda e, oc=oc, sg=sg, ps=ps: e.tensor_tensor(
                            dst[:, oc, :], ps[:, :], srep_[:, sg, :], ALU.mult),
                         reads=[pk, ("srep", sg)], writes=[(dk,)], tag=f"f4pe{dk}_{oc}_{sg}")
                else:
                    S.op("vector", lambda e, oc=oc, ps=ps: e.tensor_copy(dst[:, oc, :], ps[:, :]),
                         reads=[pk], writes=[(dk,)], tag=f"f4pe{dk}_{oc}_{sg}")

        def branch(br, QS, qk, KS, kk_, Vt, vk, Wt, wk_, Ut, uk, bt, bk_, sg):
            for src, srck, dstb, dk, mat, mk in [(QS, qk, xV, "xV", Vt, vk),
                                                 (KS, kk_, cum, "cum", Wt, wk_)]:
                for krc in range(4):
                    ps, pk = (ctx.pA, "pmA") if krc % 2 == 0 else (ctx.pB, "pmB")
                    def mm(e, krc=krc, src=src, mat=mat, ps=ps):
                        rr = None
                        for ci in range(8):
                            rr = e.matmul(ps[:, :], mat[:, ci, krc * 128:(krc + 1) * 128],
                                          src[:, ci, :], start=(ci == 0), stop=(ci == 7))
                        return rr
                    S.op("tensor", mm, reads=[srck, mk], writes=[pk], tag=f"f4x{br}{dk}_{krc}_{sg}")
                    S.op("vector", lambda e, krc=krc, dstb=dstb, ps=ps: e.tensor_copy(
                            dstb[:, krc, :], ps[:, :]),
                         reads=[pk], writes=[(dk, krc)], tag=f"f4xe{br}{dk}_{krc}_{sg}")
            for krc in range(4):
                def scn(e, krc=krc, sg=sg, br=br):
                    ini = 0.0 if sg == 0 else ccar[:, br, krc, :]
                    return e.tensor_tensor_scan(cum[:, krc, :], ones5[:, :], cum[:, krc, :], ini,
                                                ALU.mult, ALU.add)
                S.op("vector", scn, reads=[("cum", krc), "ones5", ("ccar", br, krc)],
                     writes=[("cum", krc)], tag=f"f4cs{br}_{krc}_{sg}")
                S.op("vector", lambda e, krc=krc, br=br: e.tensor_copy(
                        ccar[:, br, krc, :], cum[:, krc, 511:512]),
                     reads=[("cum", krc)], writes=[("ccar", br, krc)], narrow=True,
                     tag=f"f4csc{br}_{krc}_{sg}")
            for oc in range(8):
                ps, pk = (ctx.pC, "pmC") if oc % 2 == 0 else (ctx.pD, "pmD")
                def mmh(e, oc=oc, ps=ps):
                    rr = None
                    for ci in range(8):
                        rr = e.matmul(ps[:, :], wbufs[1][:, ci, oc * 128:(oc + 1) * 128],
                                      QS[:, ci, :], start=(ci == 0), stop=(ci == 7))
                    return rr
                S.op("tensor", mmh, reads=[qk, ("wl", "rw1")], writes=[pk],
                     tag=f"f4h{br}_{oc}_{sg}")
                S.op("scalar", lambda e, oc=oc, ps=ps: e.activation(
                        hB[:, oc, :], ps[:, :], AF.Gelu, bias=rb1[:, oc:oc + 1]),
                     reads=[pk, "rb1"], writes=[("hB",)], tag=f"f4he{br}_{oc}_{sg}")
            for t4 in range(4):
                def mml(e, t4=t4):
                    for ci in range(8):
                        e.matmul(ctx.pE[0:128, 0:8], hB[:, ci, t4 * 128:(t4 + 1) * 128],
                                 rw2[:, ci, :], start=(ci == 0), stop=False)
                    return e.matmul(ctx.pE[0:128, 0:8], ctx.onesrow_b[0:1, :], b2e[:, :],
                                    start=False, stop=True)
                S.op("tensor", mml, reads=[("hB",), "rw2", "b2e", ("c", "onesrow_b")],
                     writes=["pE"], tag=f"f4l{br}_{t4}_{sg}")
                S.op("scalar", lambda e, t4=t4: e.activation(eW[:, t4, :], ctx.pE[:, 0:8], AF.Exp),
                     reads=["pE"], writes=[("eW", t4)], narrow=True, tag=f"f4ex{br}_{t4}_{sg}")
                S.op("vector", lambda e, t4=t4: e.reduce_sum(sumr[:], eW[:, t4, :], AX.X),
                     reads=[("eW", t4)], writes=["sumr"], narrow=True, tag=f"f4sma{br}_{t4}_{sg}")
                S.op("vector", lambda e: e.reciprocal(sumr[:], sumr[:]),
                     reads=["sumr"], writes=["sumr"], narrow=True, tag=f"f4smb{br}_{t4}_{sg}")
                S.op("vector", lambda e, t4=t4: e.tensor_scalar(
                        wTM[:, t4, :], eW[:, t4, :], sumr[:], None, ALU.mult),
                     reads=[("eW", t4), "sumr"], writes=[("wTM", t4)], narrow=True,
                     tag=f"f4sm{br}_{t4}_{sg}")
            def tpw(e):
                rr = None
                for t4 in range(4):
                    rr = e.transpose(ctx.pTb[0:8, t4 * 128:(t4 + 1) * 128], wTM[:, t4, :],
                                     ctx.identb[:, :])
                return rr
            S.op("tensor", tpw, reads=[("wTM", 0), ("wTM", 1), ("wTM", 2), ("wTM", 3),
                                       ("c", "identb")], writes=["pTb"], tag=f"f4tw{br}_{sg}")
            S.op("vector", lambda e: e.tensor_copy(wTr[:, :], ctx.pTb[0:8, 0:512]),
                 reads=["pTb"], writes=["wTr"], tag=f"f4twe{br}_{sg}")
            for krc in range(4):
                S.op("tensor", lambda e, krc=krc: e.matmul(
                        ctx.pE[:, :], ctx.selk[:, krc * 128:(krc + 1) * 128], wTr[:, :],
                        start=True, stop=True),
                     reads=["wTr", ("c", "selk")], writes=["pE"], tag=f"f4wr{br}_{krc}_{sg}")
                S.op("vector", lambda e, krc=krc: e.tensor_copy(wrep[:, krc, :], ctx.pE[:, :]),
                     reads=["pE"], writes=[("wrep", krc)], tag=f"f4wre{br}_{krc}_{sg}")
                S.op("vector", lambda e, krc=krc: e.tensor_tensor(
                        AA[:, :], xV[:, krc, :], cum[:, krc, :], ALU.mult),
                     reads=[("xV", krc), ("cum", krc)], writes=["AAk"], tag=f"f4pwa{br}_{krc}_{sg}")
                S.op("vector", lambda e, krc=krc: e.tensor_tensor(
                        prodw[:, krc, :], AA[:, :], wrep[:, krc, :], ALU.mult),
                     reads=["AAk", ("wrep", krc)], writes=[("prodw", krc)],
                     tag=f"f4pw{br}_{krc}_{sg}")
            for oc in range(8):
                ps, pk = (ctx.pC, "pmC") if oc % 2 == 0 else (ctx.pD, "pmD")
                def mmo(e, oc=oc, Ut=Ut, bt=bt, ps=ps):
                    for krc in range(4):
                        e.matmul(ps[:, :], Ut[:, krc, oc * 128:(oc + 1) * 128],
                                 prodw[:, krc, :], start=(krc == 0), stop=False)
                    return e.matmul(ps[:, :], bt[:, oc * 128:(oc + 1) * 128], wTr[:, :],
                                    start=False, stop=True)
                S.op("tensor", mmo, reads=[("prodw", 0), ("prodw", 1), ("prodw", 2), ("prodw", 3),
                                           uk, bk_, "wTr"], writes=[pk], tag=f"f4o{br}_{oc}_{sg}")
                if br == 0:
                    S.op("vector", lambda e, oc=oc, ps=ps: e.tensor_tensor(
                            comb[:, oc, :], ps[:, :], nvir[:, :], ALU.mult),
                         reads=[pk, "nvir"], writes=[("comb",)], tag=f"f4oe{br}_{oc}_{sg}")
                else:
                    S.op("vector", lambda e, ps=ps: e.tensor_tensor(
                            AA[:, :], ps[:, :], nvir[:, :], ALU.mult),
                         reads=[pk, "nvir"], writes=["AAk"], tag=f"f4oa{br}_{oc}_{sg}")
                    S.op("vector", lambda e, oc=oc: e.scalar_tensor_tensor(
                            comb[:, oc, :], AA[:, :], alphc[:, :], comb[:, oc, :],
                            ALU.mult, ALU.add),
                         reads=["AAk", "alphc", ("comb",)], writes=[("comb",)],
                         tag=f"f4oe{br}_{oc}_{sg}")

        o_nvi = CF_OFFS["nvi"][0]
        for sg in range(8):
            sl = slice(sg * 512, (sg + 1) * 512)
            S.dma("gpsimd", lambda e, sl=sl: e.dma_start(out=xb[:, :, :], in_=xr[:, :, sl]),
                  reads=[("xb4",)], writes=[("xb4",)], tag=f"f4lxb{sg}")
            S.dma("sync", lambda e, sg=sg: e.dma_start(out=nvis[:, :],
                    in_=ctx.cfin[o_nvi + sg * 512:o_nvi + (sg + 1) * 512].rearrange(
                        "(r c) -> r c", r=1)),
                  reads=[("nvis",)], writes=[("nvis",)], tag=f"f4lnv{sg}")
            S.op("tensor", lambda e: e.matmul(ctx.pF[:, :], ctx.onesrow_f[:, :], nvis[:, :],
                                              start=True, stop=True),
                 reads=[("nvis",), ("c", "onesrow_f")], writes=["pF"], tag=f"f4nvm{sg}")
            S.op("vector", lambda e: e.tensor_copy(nvir[:, :], ctx.pF[:, :]),
                 reads=["pF"], writes=["nvir"], tag=f"f4nve{sg}")
            for wi_, (wkey, dst, dk, cso) in enumerate([("l3_wQT", Qs, "Qs", 0),
                                                        ("l3_wKT", Ks, "Ks", 1)]):
                wb = wbufs[0]
                S.dma("gpsimd", lambda e, wkey=wkey, wb=wb: e.dma_start(out=wb[:, :, :],
                        in_=wap2(ctx, wkey).rearrange("(c p) o -> p c o", p=128)),
                      reads=[("wl", "w0")], writes=[("wl", "w0")], tag=f"f4lw{wi_}_{sg}")
                if sg == 0:
                    def csf(e, wb=wb):
                        rr = None
                        for ci in range(8):
                            rr = e.matmul(ctx.pS[0:1, :], ctx.onescol_b[:],
                                          wb[:, ci, 0:512], start=(ci == 0), stop=False) if False                                 else e.matmul(ctx.pS[0:1, 0:1024][0:1, 0:512] if False else ctx.pS[0:1, 0:512],
                                              ctx.onescol_b[:], wb[:, ci, 0:512],
                                              start=(ci == 0), stop=(ci == 7))
                        return rr
                    def csf2(e, wb=wb):
                        rr = None
                        for ci in range(8):
                            rr = e.matmul(ctx.pF[0:1, 0:512], ctx.onescol_b[:], wb[:, ci, 512:1024],
                                          start=(ci == 0), stop=(ci == 7))
                        return rr
                    S.op("tensor", csf, reads=[("wl", "w0"), ("c", "onescol_b")], writes=["pS"],
                         tag=f"f4cs{wi_}a")
                    S.op("tensor", csf2, reads=[("wl", "w0"), ("c", "onescol_b")], writes=["pF"],
                         tag=f"f4cs{wi_}b")
                    S.op("vector", lambda e, cso=cso: e.tensor_copy(
                            wcbq[0:1, cso, 0:512], ctx.pS[0:1, 0:512]),
                         reads=["pS", "wcbz"], writes=[("wcbq", cso)], tag=f"f4cse{wi_}a")
                    S.op("vector", lambda e, cso=cso: e.tensor_copy(
                            wcbq[0:1, cso, 512:1024], ctx.pF[0:1, 0:512]),
                         reads=["pF", "wcbz"], writes=[("wcbq", cso)], tag=f"f4cse{wi_}b")
                projq(dst, dk, wb, ("wl", "w0"), cso, sg, True, ("xb4",), xb)
            S.dma("gpsimd", lambda e: e.dma_start(out=wbufs[0][:, :, :],
                    in_=wap2(ctx, "l3_winvT").rearrange("(c p) o -> p c o", p=128)),
                  reads=[("wl", "w0")], writes=[("wl", "w0")], tag=f"f4lwi_{sg}")
            projq(iQs, "iQs", wbufs[0], ("wl", "w0"), 0, sg, False, ("Qs",), Qs)
            projq(iKs, "iKs", wbufs[0], ("wl", "w0"), 0, sg, False, ("Ks",), Ks)
            S.dma("gpsimd", lambda e: e.dma_start(out=wbufs[1][:, :, :],
                    in_=wap2(ctx, "l3_rw1T").rearrange("(c p) o -> p c o", p=128)),
                  reads=[("wl", "rw1")], writes=[("wl", "rw1")], tag=f"f4lrw1_{sg}")
            branch(0, Qs, ("Qs",), Ks, ("Ks",), vf, ("vf",), wf, ("wfw",), uf, ("uf",),
                   bf_, "bf", sg)
            branch(1, iQs, ("iQs",), iKs, ("iKs",), vi, ("vi",), wi, ("wiv",), ui, ("ui",),
                   bi_, "bi", sg)
            S.dma("gpsimd", lambda e: e.dma_start(out=wbufs[0][:, :, :],
                    in_=wap2(ctx, "l3_wOT").rearrange("(c p) o -> p c o", p=128)),
                  reads=[("wl", "w0")], writes=[("wl", "w0")], tag=f"f4lwo_{sg}")
            S.dma("sync", lambda e, sl=sl: e.dma_start(out=xsl[:, :, :], in_=xr[:, :, sl]),
                  reads=[("xsl4",)], writes=[("xsl4",)], tag=f"f4lx{sg}")
            for oc in range(8):
                ps, pk = (ctx.pA, "pmA") if oc % 2 == 0 else (ctx.pB, "pmB")
                def mmo(e, oc=oc, ps=ps):
                    rr = None
                    for ci in range(8):
                        rr = e.matmul(ps[:, :], wbufs[0][:, ci, oc * 128:(oc + 1) * 128],
                                      comb[:, ci, :], start=(ci == 0), stop=(ci == 7))
                    return rr
                S.op("tensor", mmo, reads=[("comb",), ("wl", "w0")], writes=[pk],
                     tag=f"f4wo{oc}_{sg}")
                S.op("vector", lambda e, oc=oc, ps=ps: e.tensor_add(
                        xsl[:, oc, :], ps[:, :], xsl[:, oc, :]),
                     reads=[pk, ("xsl4",)], writes=[("xsl4",)], tag=f"f4woe{oc}_{sg}")
            S.dma("sync", lambda e, sl=sl: e.dma_start(out=xdr[:, :, sl], in_=xsl[:, :, :]),
                  reads=[("xsl4",)], writes=[("xout", sg), ("xsl4",)], tag=f"f4st{sg}")
        S.fence()

def _emit_devcopy(ctx, src):
    S = ctx.S
    for sg in range(8):
        S.dma("sync", lambda e, sg=sg: e.dma_start(
                out=ctx.xout[:, sg * 512:(sg + 1) * 512], in_=src[:, sg * 512:(sg + 1) * 512]),
              reads=[("xout", sg)], writes=[("xoutf", sg)], tag=f"dev{sg}")

def build_program(upto=8, only=None):
    nc = bass.Bass("TRN2", target_bir_lowering=False)
    ctx = _mk_ctx(nc)
    S = ctx.S
    _emit_preamble(ctx)
    S.fence()
    plan = [
        (emit_attn, 0), (emit_l0ffn, None), (emit_attn, 1), (emit_l1ffn, None),
        (emit_attn, 2), (emit_l2ffn, None), (emit_l3attn, None), (emit_l3ffn, None),
    ]
    seq = list(range(upto)) if only is None else list(only)
    src = ctx.xin
    nxt = [ctx.xB, ctx.xA]
    full = (len(seq) == 8)
    for j, i in enumerate(seq):
        fn, L = plan[i]
        dst = ctx.xout if (full and j == len(seq) - 1) else nxt[j % 2]
        if L is None:
            fn(ctx, src, dst)
        else:
            fn(ctx, L, src, dst)
        src = dst
    if not full:
        _emit_devcopy(ctx, src)
    S.op("sync", lambda e: e.nop(nofuse=True),
         reads=[("xout" if full else "xoutf", sg) for sg in range(8)], tag="final")
    S.emit()
    return nc

def kernel(**inputs):
    blob = _prep_blob(inputs)
    x = np.asarray(inputs["L0_attn_hidden_states"], np.float32)
    nc = build_program(upto=8)
    in_maps = []
    for core in range(NCORES):
        in_maps.append({
            "xin": np.ascontiguousarray(x[core].T),
            "wblob": blob,
            "cbin": CB_BLOB,
            "cfin": CF_BLOB,
        })
    res = run_bass_kernel_spmd(nc, in_maps, core_ids=list(range(NCORES)))
    out = np.stack([np.asarray(r["xout"], np.float32).T for r in res.results])
    return np.ascontiguousarray(out)


# version 08
# speedup vs baseline: 1.1659x; 1.0004x over previous
import numpy as np
import ml_dtypes
import concourse.bass as bass
import concourse.mybir as mybir
from concourse.bass_utils import run_bass_kernel_spmd

F32 = mybir.dt.float32
BF16 = mybir.dt.bfloat16
I32 = mybir.dt.int32
AF = mybir.ActivationFunctionType
ALU = mybir.AluOpType
AX = mybir.AxisListType

T = 4096
D = 1024
NCORES = 8
EPS = 1e-5
BF = ml_dtypes.bfloat16
RHD = 0.49999997

WSPEC = [
    ("l0_wqT", (D, D)), ("l0_wkT", (D, D)), ("l0_wvT", (D, D)), ("l0_woT", (D, D)),
    ("l0_bq", (D,)), ("l0_bk", (D,)), ("l0_bv", (D,)), ("l0_bo", (128, 8)),
    ("l0_wgate", (2816, D)), ("l0_wup", (2816, D)), ("l0_wdown", (D, 2816)),
    ("l1_wqT", (D, D)), ("l1_wkT", (D, D)), ("l1_wvT", (D, D)), ("l1_wpT", (D, D)),
    ("l1_bqT", (D, 8)), ("l1_bkT", (D, 8)), ("l1_bvT", (D, 8)),
    ("l1_aqT", (8, D)), ("l1_akT", (8, D)), ("l1_avT", (8, D)),
    ("l1_w1T", (D, 4096)), ("l1_w2T", (D, 4096)),
    ("l1_h1T", (4096, 4096)), ("l1_h2T", (4096, 4096)), ("l1_w3T", (4096, D)),
    ("l1_s1", (128, 32)), ("l1_s2", (128, 32)), ("l1_h1s", (128, 32)), ("l1_h2s", (128, 32)),
    ("l1_s3", (128, 8)),
    ("l2_wqT", (D, D)), ("l2_wkT", (D, D)), ("l2_wvT", (D, D)),
    ("l2_wrT", (D, D)), ("l2_wkT2", (D, D)), ("l2_wvT2", (D, D)),
    ("l2_mur", (128, 8)), ("l2_muk", (128, 8)),
    ("l3_winT", (D, 4096)), ("l3_convw", (128, 16, 4)), ("l3_convb", (128, 16)),
    ("l3_wxcT", (2048, 16)), ("l3_blogT", (16, 2048)),
    ("l3_woutT", (2048, D)), ("l3_lng", (128, 8)), ("l3_lnb", (128, 8)),
    ("l3_wQT", (D, D)), ("l3_wKT", (D, D)), ("l3_wOT", (D, D)), ("l3_winvT", (D, D)),
    ("l3_vfwd", (D, 512)), ("l3_wfwd", (D, 512)), ("l3_ufwd", (512, D)), ("l3_bfwd", (8, D)),
    ("l3_vinv", (D, 512)), ("l3_winvexp", (D, 512)), ("l3_uinv", (512, D)), ("l3_binv", (8, D)),
    ("l3_rw1T", (D, D)), ("l3_rb1", (128, 8)), ("l3_rw2T", (D, 8)), ("l3_rb2e", (8,)),
    ("l3_alpha", (1,)),
]

def _blob_offsets():
    offs, cur = {}, 0
    for key, shape in WSPEC:
        n = int(np.prod(shape))
        offs[key] = (cur, shape)
        cur += ((n + 511) // 512) * 512
    return offs, cur

BLOB_OFFS, BLOB_N = _blob_offsets()

def _prep_blob(inp):
    t = lambda a: np.ascontiguousarray(np.asarray(a).T.astype(np.float32))
    c = lambda a: np.ascontiguousarray(np.asarray(a).astype(np.float32))
    def pcol(a):
        a = np.asarray(a).astype(np.float32).reshape(-1, 128)
        return np.ascontiguousarray(a.T)
    d = {}
    d["l0_wqT"] = t(inp["L0_attn_Wq"]); d["l0_wkT"] = t(inp["L0_attn_Wk"])
    d["l0_wvT"] = t(inp["L0_attn_Wv"]); d["l0_woT"] = t(inp["L0_attn_Wo"])
    d["l0_bq"] = c(inp["L0_attn_bq"]); d["l0_bk"] = c(inp["L0_attn_bk"])
    d["l0_bv"] = c(inp["L0_attn_bv"]); d["l0_bo"] = pcol(inp["L0_attn_bo"])
    d["l0_wgate"] = c(inp["L0_ffn_w_gate"]); d["l0_wup"] = c(inp["L0_ffn_w_up"])
    d["l0_wdown"] = c(inp["L0_ffn_w_down"])
    d["l1_wqT"] = t(inp["L1_attn_Wq"]); d["l1_wkT"] = t(inp["L1_attn_Wk"])
    d["l1_wvT"] = t(inp["L1_attn_Wv"]); d["l1_wpT"] = t(inp["L1_attn_Wp"])
    d["l1_bqT"] = t(inp["L1_attn_Bq"]); d["l1_bkT"] = t(inp["L1_attn_Bk"])
    d["l1_bvT"] = t(inp["L1_attn_Bv"])
    d["l1_aqT"] = t(inp["L1_attn_Aq"]); d["l1_akT"] = t(inp["L1_attn_Ak"])
    d["l1_avT"] = t(inp["L1_attn_Av"])
    d["l1_w1T"] = t(inp["L1_ffn_w1_w"]); d["l1_w2T"] = t(inp["L1_ffn_w2_w"])
    d["l1_h1T"] = t(inp["L1_ffn_h1_w"]); d["l1_h2T"] = t(inp["L1_ffn_h2_w"])
    d["l1_w3T"] = t(inp["L1_ffn_w3_w"])
    d["l1_s1"] = pcol(inp["L1_ffn_w1_s"]); d["l1_s2"] = pcol(inp["L1_ffn_w2_s"])
    d["l1_h1s"] = pcol(inp["L1_ffn_h1_s"]); d["l1_h2s"] = pcol(inp["L1_ffn_h2_s"])
    d["l1_s3"] = pcol(inp["L1_ffn_w3_s"])
    wqkv = np.asarray(inp["L2_attn_W_qkv"])
    idx = np.arange(16)[:, None] * 192 + np.arange(64)[None, :]
    d["l2_wqT"] = t(wqkv[(idx + 0).reshape(-1)])
    d["l2_wkT"] = t(wqkv[(idx + 64).reshape(-1)])
    d["l2_wvT"] = t(wqkv[(idx + 128).reshape(-1)])
    d["l2_wrT"] = t(inp["L2_ffn_W_r"]); d["l2_wkT2"] = t(inp["L2_ffn_W_k"])
    d["l2_wvT2"] = t(inp["L2_ffn_W_v"])
    d["l2_mur"] = pcol(inp["L2_ffn_mu_r"]); d["l2_muk"] = pcol(inp["L2_ffn_mu_k"])
    d["l3_winT"] = t(inp["L3_attn_W_in"])
    d["l3_convw"] = np.ascontiguousarray(np.asarray(inp["L3_attn_conv_w"]).astype(np.float32).reshape(16, 128, 4).transpose(1, 0, 2))
    d["l3_convb"] = pcol(inp["L3_attn_conv_b"])
    d["l3_wxcT"] = t(np.asarray(inp["L3_attn_W_x"])[16:32])
    d["l3_blogT"] = t(inp["L3_attn_B_log"])
    d["l3_woutT"] = t(inp["L3_attn_W_out"])
    d["l3_lng"] = pcol(inp["L3_attn_ln_g"]); d["l3_lnb"] = pcol(inp["L3_attn_ln_b"])
    d["l3_wQT"] = t(inp["L3_ffn_W_Q"]); d["l3_wKT"] = t(inp["L3_ffn_W_K"])
    d["l3_wOT"] = t(inp["L3_ffn_W_O"]); d["l3_winvT"] = t(inp["L3_ffn_W_inv"])
    flat_d_kr = lambda a: c(np.asarray(a).transpose(1, 0, 2).reshape(D, 512))
    flat_kr_d = lambda a: c(np.asarray(a).transpose(0, 2, 1).reshape(512, D))
    d["l3_vfwd"] = flat_d_kr(inp["L3_ffn_V_fwd"])
    d["l3_wfwd"] = flat_d_kr(inp["L3_ffn_W_fwd"])
    d["l3_ufwd"] = flat_kr_d(inp["L3_ffn_U_fwd"])
    d["l3_bfwd"] = c(inp["L3_ffn_b_fwd"])
    d["l3_vinv"] = flat_d_kr(inp["L3_ffn_W_inv_exp"])
    d["l3_winvexp"] = flat_d_kr(inp["L3_ffn_V_inv"])
    d["l3_uinv"] = flat_kr_d(inp["L3_ffn_U_inv"])
    d["l3_binv"] = c(inp["L3_ffn_b_inv"])
    d["l3_rw1T"] = t(inp["L3_ffn_router_w1"]); d["l3_rb1"] = pcol(inp["L3_ffn_router_b1"])
    d["l3_rw2T"] = t(inp["L3_ffn_router_w2"])
    d["l3_rb2e"] = c(np.asarray(inp["L3_ffn_router_b2"]) + np.asarray(inp["L3_ffn_expert_bias"]))
    d["l3_alpha"] = c(np.asarray(inp["L3_ffn_alpha_bi"]).reshape(1))
    blob = np.zeros(BLOB_N, np.float32)
    for key, (off, shape) in BLOB_OFFS.items():
        a = d[key]
        assert a.shape == tuple(shape), (key, a.shape, shape)
        blob[off:off + a.size] = a.reshape(-1)
    return blob

def _prep_consts():
    import math
    p = np.arange(128)
    ident = np.eye(128, dtype=np.float32)
    stair = (p[:, None] // 32 <= p[None, :] // 32).astype(np.float32)
    tri = (p[:, None] <= p[None, :]).astype(np.float32)
    blk = np.arange(T) // 32
    U = (np.arange(128)[:, None] > blk[None, :]).astype(np.float32)
    blockind = (p[:, None] // 32 == np.arange(4)[None, :]).astype(np.float32)
    start = 2.0 ** (-2.0 ** (-(math.log2(16) - 3)))
    slopes = np.array([start * start ** i for i in range(16)], np.float32)
    alibi = np.zeros((128, 512), np.float32)
    for kt in range(32):
        for h in range(16):
            alibi[:, kt * 16 + h] = -slopes[h] * (128 * kt + p)
    nvi = (1.0 / np.arange(1, T + 1, dtype=np.float64)).astype(np.float32)[None, :]
    kr = np.arange(512)
    selk = (kr[None, :] // 64 == np.arange(8)[:, None]).astype(np.float32)
    half0 = np.zeros((1, 128), np.float32); half0[0, 0:64] = 1.0
    half1 = np.zeros((1, 128), np.float32); half1[0, 64:128] = 1.0
    cbf = {
        "identb": ident, "stair": stair, "tri": tri, "U": U, "blockind": blockind,
        "onescol_b": np.ones((128, 1), np.float32), "selk": selk,
        "half0": half0, "half1": half1,
    }
    cf = {
        "identf": ident, "alibi": alibi, "nvi": nvi,
        "onesrow_f": np.ones((1, 128), np.float32),
        "eps1": np.full((1, 1), EPS, np.float32),
        "selrows": np.array([[-1.0, 0.0, 0.0, 1.0]], np.float32),
    }
    def pack(dd, dt):
        offs, cur = {}, 0
        for k, a in dd.items():
            offs[k] = (cur, a.shape)
            cur += ((a.size + 511) // 512) * 512
        blob = np.zeros(cur, np.float32)
        for k, (o, s) in offs.items():
            blob[o:o + int(np.prod(s))] = dd[k].reshape(-1)
        return blob.astype(dt), offs
    bblob, bo = pack(cbf, np.float32)
    fblob, fo = pack(cf, np.float32)
    return bblob, bo, fblob, fo

CB_BLOB, CB_OFFS, CF_BLOB, CF_OFFS = _prep_consts()

ENGINES = ["tensor", "vector", "scalar", "gpsimd", "sync"]

class _Op:
    __slots__ = ("eng", "fn", "deps", "idx", "need_inc", "inc_seq", "is_dma",
                 "narrow", "tag", "sem_idx")
    def __init__(self, eng, fn, is_dma, narrow, tag):
        self.eng = eng
        self.fn = fn
        self.deps = []
        self.idx = 0
        self.need_inc = is_dma
        self.inc_seq = 0
        self.is_dma = is_dma
        self.narrow = narrow
        self.tag = tag

class Sched:
    def __init__(self, nc):
        self.nc = nc
        self.ops = {e: [] for e in ENGINES}
        self.last_writer = {}
        self.readers = {}
        self.all = []

    def _record(self, eng, fn, reads, writes, is_dma, narrow, tag):
        op = _Op(eng, fn, is_dma, narrow, tag)
        op.idx = len(self.ops[eng])
        writes = list(writes) + [("__last__", eng)]
        deps = []
        for k in reads:
            w = self.last_writer.get(k)
            if w is not None:
                deps.append(w)
        for k in writes:
            w = self.last_writer.get(k)
            if w is not None:
                deps.append(w)
            deps.extend(self.readers.get(k, ()))
        op.deps = list({id(d): d for d in deps if d is not op}.values())
        for k in writes:
            self.last_writer[k] = op
            self.readers[k] = []
        for k in reads:
            self.readers.setdefault(k, []).append(op)
        self.ops[eng].append(op)
        self.all.append(op)
        return op

    def op(self, eng, fn, reads=(), writes=(), narrow=False, tag=""):
        return self._record(eng, fn, reads, writes, False, narrow, tag)

    def dma(self, eng, fn, reads=(), writes=(), tag=""):
        return self._record(eng, fn, reads, writes, True, False, tag)

    def fence(self):
        for e in ENGINES:
            self.op(e, lambda en: en.nop(nofuse=True),
                    reads=[("__last__", x) for x in ENGINES if x != e], tag="fence")

    def emit(self):
        nc = self.nc
        for op in self.all:
            for d in op.deps:
                d.need_inc = True
        NDSEM = 10
        cnt = {e: 0 for e in ENGINES}
        dcnt = {e: 0 for e in ENGINES}
        for e in ENGINES:
            for op in self.ops[e]:
                if op.is_dma:
                    n = dcnt[e]
                    dcnt[e] += 1
                    op.inc_seq = (n // NDSEM) + 1
                    op.sem_idx = n % NDSEM
                elif op.need_inc:
                    cnt[e] += 1
                    op.inc_seq = cnt[e]
        sems = {}
        dsems = {}
        for e in ENGINES:
            sems[e] = nc.semaphore(f"se_{e}").__enter__()
            dsems[e] = [nc.semaphore(f"sd_{e}{i}").__enter__() for i in range(NDSEM)]
        waited = {e: {} for e in ENGINES}

        def body(e_handle, ename):
            wd = waited[ename]
            for op in self.ops[ename]:
                wants = {}
                for d in op.deps:
                    if d.is_dma:
                        key = ("d", d.eng, d.sem_idx)
                        val = 16 * d.inc_seq
                    else:
                        key = ("c", d.eng)
                        val = d.inc_seq
                    if wants.get(key, 0) < val:
                        wants[key] = val
                if op.is_dma and op.inc_seq > 1:
                    key = ("d", ename, op.sem_idx)
                    val = 16 * (op.inc_seq - 1)
                    if wants.get(key, 0) < val:
                        wants[key] = val
                for key, val in sorted(wants.items()):
                    if wd.get(key, 0) >= val:
                        continue
                    sem = dsems[key[1]][key[2]] if key[0] == "d" else sems[key[1]]
                    e_handle.wait_ge(sem, val)
                    wd[key] = val
                inst = op.fn(e_handle)
                if op.need_inc:
                    if op.is_dma:
                        inst.then_inc(dsems[ename][op.sem_idx], 16)
                    else:
                        if inst is None:
                            inst = e_handle.nop(nofuse=True)
                        inst.then_inc(sems[ename], 1)

        with nc.Block("main") as blk:
            blk.tensor(lambda e: body(e, "tensor"))
            blk.vector(lambda e: body(e, "vector"))
            blk.scalar(lambda e: body(e, "scalar"))
            blk.gpsimd(lambda e: body(e, "gpsimd"))
            blk.sync(lambda e: body(e, "sync"))

class Ctx:
    pass

def _mk_ctx(nc):
    ctx = Ctx()
    ctx.nc = nc
    ctx.S = Sched(nc)
    ctx.xin = nc.dram_tensor("xin", [D, T], F32, kind="ExternalInput")
    ctx.wblob = nc.dram_tensor("wblob", [BLOB_N], F32, kind="ExternalInput")
    ctx.cbin = nc.dram_tensor("cbin", [CB_BLOB.size], F32, kind="ExternalInput")
    ctx.cfin = nc.dram_tensor("cfin", [CF_BLOB.size], F32, kind="ExternalInput")
    ctx.xout = nc.dram_tensor("xout", [D, T], F32, kind="ExternalOutput")
    ctx.xA = nc.dram_tensor("xA", [D, T], F32)
    ctx.xB = nc.dram_tensor("xB", [D, T], F32)
    ctx.ctxd = nc.dram_tensor("ctxd", [8, 8, 128, 512], BF16)
    ctx.hD = nc.dram_tensor("hD", [32, 8, 128, 512], BF16)
    ctx.g1D = nc.dram_tensor("g1D", [32, 8, 128, 512], BF16)
    ctx.g2D = nc.dram_tensor("g2D", [32, 8, 128, 512], BF16)

    ctx.live = []
    def keep(cm):
        v = cm.__enter__()
        ctx.live.append(cm)
        return v
    ctx.keep = keep

    ctx.identb = keep(nc.sbuf_tensor("identb", [128, 128], BF16))
    ctx.stair = keep(nc.sbuf_tensor("stair", [128, 128], BF16))
    ctx.tri = keep(nc.sbuf_tensor("tri", [128, 128], BF16))
    ctx.blockind = keep(nc.sbuf_tensor("blockind", [128, 4], BF16))
    ctx.selk = keep(nc.sbuf_tensor("selk", [8, 512], BF16))
    ctx.half0 = keep(nc.sbuf_tensor("half0", [1, 128], BF16))
    ctx.half1 = keep(nc.sbuf_tensor("half1", [1, 128], BF16))
    ctx.onescol_b = keep(nc.sbuf_tensor("onescol_b", [128, 1], BF16))
    ctx.identf = keep(nc.sbuf_tensor("identf", [128, 128], F32))
    ctx.alibi = keep(nc.sbuf_tensor("alibi", [128, 512], F32))
    ctx.onesrow_f = keep(nc.sbuf_tensor("onesrow_f", [1, 128], F32))
    ctx.selrows = keep(nc.sbuf_tensor("selrows", [1, 4], F32))
    ctx.onesrow_b = keep(nc.sbuf_tensor("onesrow_b", [1, 128], BF16))
    ctx.ones64_b = keep(nc.sbuf_tensor("ones64_b", [65, 128], BF16))
    ctx.eps1 = keep(nc.sbuf_tensor("eps1", [1, 1], F32))
    ctx.pA = keep(nc.psum_tensor("pA", [128, 512], F32))
    ctx.pB = keep(nc.psum_tensor("pB", [128, 512], F32))
    ctx.pC = keep(nc.psum_tensor("pC", [128, 512], F32))
    ctx.pD = keep(nc.psum_tensor("pD", [128, 512], F32))
    ctx.pE = keep(nc.psum_tensor("pE", [128, 512], F32))
    ctx.pF = keep(nc.psum_tensor("pF", [128, 512], F32))
    ctx.pS = keep(nc.psum_tensor("pS", [128, 512], F32))
    ctx.pTb = keep(nc.psum_tensor("pTb", [128, 1024], BF16))
    return ctx

def wap2(ctx, key):
    off, shape = BLOB_OFFS[key]
    n = int(np.prod(shape))
    a = ctx.wblob[off:off + n]
    if len(shape) == 2:
        a = a.rearrange("(r c) -> r c", c=shape[1])
    elif len(shape) == 3:
        a = a.rearrange("(r c k) -> r c k", c=shape[1], k=shape[2])
    return a

def _emit_preamble(ctx):
    S = ctx.S
    def ld(dst, src, offs, key):
        o, shape = offs[key]
        rows = shape[0] if len(shape) == 2 else 1
        cols = shape[1] if len(shape) == 2 else shape[0]
        ap = src[o:o + rows * cols].rearrange("(r c) -> r c", c=cols)
        q = "gpsimd" if dst.ap().dtype == BF16 else "sync"
        S.dma(q, lambda e, d=dst, a=ap, r=rows, cc=cols: e.dma_start(out=d[:r, :cc], in_=a),
              reads=(), writes=[("c", key)], tag=f"ld_{key}")
    ld(ctx.identb, ctx.cbin, CB_OFFS, "identb")
    ld(ctx.stair, ctx.cbin, CB_OFFS, "stair")
    ld(ctx.tri, ctx.cbin, CB_OFFS, "tri")
    ld(ctx.blockind, ctx.cbin, CB_OFFS, "blockind")
    ld(ctx.selk, ctx.cbin, CB_OFFS, "selk")
    ld(ctx.half0, ctx.cbin, CB_OFFS, "half0")
    ld(ctx.half1, ctx.cbin, CB_OFFS, "half1")
    ld(ctx.onescol_b, ctx.cbin, CB_OFFS, "onescol_b")
    ld(ctx.identf, ctx.cfin, CF_OFFS, "identf")
    ld(ctx.alibi, ctx.cfin, CF_OFFS, "alibi")
    ld(ctx.onesrow_f, ctx.cfin, CF_OFFS, "onesrow_f")
    ld(ctx.selrows, ctx.cfin, CF_OFFS, "selrows")
    ld(ctx.eps1, ctx.cfin, CF_OFFS, "eps1")
    S.op("vector", lambda e: e.tensor_copy(ctx.onesrow_b[:], ctx.onesrow_f[:]),
         reads=[("c", "onesrow_f")], writes=[("c", "onesrow_b")])
    S.op("vector", lambda e: e.memset(ctx.ones64_b[64:65, :], 1.0),
         writes=[("c", "ones64")])

def emit_stats(ctx, xsrc, pes, want_col=False, tag="", xb_dst=None):
    import contextlib
    nc, S = ctx.nc, ctx.S
    ctx.rowsNB = pes(nc.sbuf_tensor(f"rowsNB{tag}", [2, T], BF16))
    ctx.srep = pes(nc.sbuf_tensor(f"srep{tag}", [128, 8, 512], BF16))
    if want_col:
        ctx.rstd_col = pes(nc.sbuf_tensor(f"rstdc{tag}", [128, 32], F32))
    rowsNB_, srep_ = ctx.rowsNB, ctx.srep
    rstdcol_ = ctx.rstd_col if want_col else None
    with contextlib.ExitStack() as est:
        es = est.enter_context
        xsl = es(nc.sbuf_tensor(f"st_x{tag}", [128, 2, 8, 512], F32))
        x2t = es(nc.sbuf_tensor(f"st_x2{tag}", [128, 2, 8, 512], BF16))
        xbt = es(nc.sbuf_tensor(f"st_xb{tag}", [128, 2, 8, 512], BF16))
        var = es(nc.sbuf_tensor(f"st_var{tag}", [1, 512], F32))
        musq = es(nc.sbuf_tensor(f"st_musq{tag}", [1, 512], F32))
        sig = es(nc.sbuf_tensor(f"st_sig{tag}", [1, 512], F32))
        murow = es(nc.sbuf_tensor(f"st_mu{tag}", [1, 512], F32))
        rstd = es(nc.sbuf_tensor(f"st_rstd{tag}", [1, 512], F32))
        xr = xsrc.rearrange("(c p) t -> p c t", p=128)
        for s in range(8):
            b = s % 2
            sl = slice(s * 512, (s + 1) * 512)
            xbap = xbt[:, b] if xb_dst is None else xb_dst[:, :, sl]
            S.dma("sync", lambda e, b=b, sl=sl: e.dma_start(out=xsl[:, b], in_=xr[:, :, sl]),
                  reads=[("xslfree", b)], writes=[("xsl", b)], tag=f"st_ld{s}")
            S.op("vector", lambda e, b=b, xbap=xbap: e.tensor_copy(xbap, xsl[:, b]),
                 reads=[("xsl", b)], writes=[("stxb", b), ("xbw", s)], tag=f"st_cast{s}")
            S.op("scalar", lambda e, b=b: e.activation(
                    x2t[:, b].rearrange("p c t -> p (c t)"), xsl[:, b].rearrange("p c t -> p (c t)"),
                    AF.Square),
                 reads=[("xsl", b)], writes=[("x2t", b), ("xslfree", b)], tag=f"st_sq{s}")
            def mm_sum(e, src_ap, ps):
                r = None
                for ci in range(8):
                    r = e.matmul(ps[0:1, :], ctx.onescol_b[:], src_ap[:, ci], start=(ci == 0), stop=(ci == 7))
                return r
            S.op("tensor", lambda e, b=b, xbap=xbap: mm_sum(e, xbap, ctx.pA),
                 reads=[("stxb", b), ("c", "onescol_b")], writes=["pA"], tag=f"st_mmx{s}")
            S.op("tensor", lambda e, b=b: mm_sum(e, x2t[:, b], ctx.pB),
                 reads=[("x2t", b), ("c", "onescol_b")], writes=["pB"], tag=f"st_mmx2{s}")
            S.op("vector", lambda e: e.tensor_scalar(murow[:], ctx.pA[0:1, :], 1.0 / D, None, ALU.mult),
                 reads=["pA"], writes=["mu"], tag=f"st_mu{s}")
            S.op("scalar", lambda e: e.activation(musq[:], murow[:], AF.Square),
                 reads=["mu"], writes=["musq"], tag=f"st_musq{s}")
            S.op("vector", lambda e: e.scalar_tensor_tensor(var[:], ctx.pB[0:1, :], 1.0 / D, musq[:],
                                                            ALU.mult, ALU.subtract),
                 reads=["pB", "musq"], writes=["var"], tag=f"st_var{s}")
            S.op("scalar", lambda e: e.activation(sig[:], var[:], AF.Sqrt, bias=ctx.eps1[0:1, :]),
                 reads=["var", ("c", "eps1")], writes=["sig"], tag=f"st_sig{s}")
            S.op("vector", lambda e: e.reciprocal(rstd[:], sig[:]),
                 reads=["sig"], writes=["rstd"], tag=f"st_rstd{s}")
            def rowmm(e):
                e.matmul(ctx.pE[0:2, :], ctx.selrows[0:1, 0:2], murow[:], start=True, stop=False)
                return e.matmul(ctx.pE[0:2, :], ctx.selrows[0:1, 2:4], sig[:], start=False, stop=True)
            S.op("tensor", rowmm, reads=["mu", "sig", ("c", "selrows")], writes=["pE"],
                 tag=f"st_rowmm{s}")
            S.op("vector", lambda e, sl=sl: e.tensor_copy(rowsNB_[0:2, sl], ctx.pE[0:2, :]),
                 reads=["pE"], writes=[("rowN", s), ("rowS", s)], tag=f"st_rows{s}")
            S.op("tensor", lambda e: e.matmul(ctx.pS[:, :], ctx.onesrow_f[:],
                                              rstd[:], start=True, stop=True),
                 reads=["rstd", ("c", "onesrow_f")], writes=["pS"], tag=f"st_srepmm{s}")
            S.op("vector", lambda e, s=s: e.tensor_copy(srep_[:, s, :], ctx.pS[:, :]),
                 reads=["pS"], writes=[("srep", s)], tag=f"st_srepev{s}")
            if want_col:
                def tp(e):
                    r = None
                    for j in range(4):
                        r = e.transpose(ctx.pF[:, j:j + 1], rstd[:, j * 128:(j + 1) * 128],
                                        ctx.identf[0:1, 0:1])
                    return r
                S.op("tensor", tp, reads=["rstd", ("c", "identf")], writes=["pF"],
                     tag=f"st_coltp{s}")
                S.op("vector", lambda e, s=s: e.tensor_copy(rstdcol_[:, s * 4:s * 4 + 4],
                                                            ctx.pF[:, 0:4]),
                     reads=["pF"], writes=[("rstd_col", s)], narrow=True, tag=f"st_colev{s}")
        S.fence()

def emit_attn(ctx, L, xsrc, xdst):
    import contextlib
    nc, S = ctx.nc, ctx.S
    wq, wk, wv = [["l0_wqT", "l0_wkT", "l0_wvT"], ["l1_wqT", "l1_wkT", "l1_wvT"],
                  ["l2_wqT", "l2_wkT", "l2_wvT"]][L]
    has_bias = (L == 0)
    has_lora = (L == 1)
    xr = xsrc.rearrange("(c p) t -> p c t", p=128)

    with contextlib.ExitStack() as est:
        es = est.enter_context
        emit_stats(ctx, xsrc, es, want_col=True, tag=f"a{L}")
        rowsNB_, srep_ = ctx.rowsNB, ctx.srep
        rstdcol_ = getattr(ctx, 'rstd_col', None)
        K_sb = es(nc.sbuf_tensor(f"K_sb{L}", [128, 8, T], BF16))
        Vaug = es(nc.sbuf_tensor(f"Vaug{L}", [128, 32, 16, 66], BF16))
        Q_sb = es(nc.sbuf_tensor(f"Q_sb{L}", [128, 8, 512], BF16))
        xb = es(nc.sbuf_tensor(f"at_xb{L}", [128, 8, 512], BF16))
        wt = es(nc.sbuf_tensor(f"at_wt{L}", [128, 2, 8, 256], BF16))
        wcb = es(nc.sbuf_tensor(f"at_wcb{L}", [2, 3, 1024], BF16))
        if has_lora:
            wbq = es(nc.sbuf_tensor(f"at_wbq{L}", [128, 3, 8, 8], BF16))
            waq = es(nc.sbuf_tensor(f"at_waq{L}", [8, 3, 1024], BF16))
            wcbl = es(nc.sbuf_tensor(f"at_wcbl{L}", [2, 3, 8], BF16))
            mids = es(nc.sbuf_tensor(f"at_mids{L}", [8, 3, 512], BF16))
        NPS = 16 if L == 2 else 8
        P_sb = es(nc.sbuf_tensor(f"at_P{L}", [128, NPS, 512], BF16))
        rec65 = es(nc.sbuf_tensor(f"at_rec65{L}", [65, 2, 512], BF16))
        rzrep = es(nc.sbuf_tensor(f"at_rzrep{L}", [128, 512], BF16))
        ctxn = es(nc.sbuf_tensor(f"at_ctxn{L}", [128, 2, 512], BF16))
        hsel = es(nc.sbuf_tensor(f"at_hsel{L}", [65, 2, 128], BF16))
        S.op("vector", lambda e: e.memset(hsel[64:65, :, :], 0.0), writes=["hselz"],
             narrow=True, tag="hselz")
        S.op("vector", lambda e: e.memset(hsel[64:65, 0, 0:64], 1.0),
             reads=["hselz"], writes=[("hsel", 0)], narrow=True, tag="hsel0")
        S.op("vector", lambda e: e.memset(hsel[64:65, 1, 64:128], 1.0),
             reads=["hselz"], writes=[("hsel", 1)], narrow=True, tag="hsel1")
        if L == 0:
            vblkT = es(nc.sbuf_tensor(f"at_vblkT{L}", [66, 128], BF16))
            vblk = es(nc.sbuf_tensor(f"at_vblk{L}", [128, 16, 66], BF16))
            Ucst = es(nc.sbuf_tensor(f"at_U{L}", [128, T], BF16))
            o_U = CB_OFFS["U"][0]
            S.dma("gpsimd", lambda e: e.dma_start(
                    out=Ucst[:, :], in_=ctx.cbin[o_U:o_U + 128 * T].rearrange("(r c) -> r c", c=T)),
                  writes=[("c", "U")], tag="ldU")

        S.op("vector", lambda e: e.memset(wcb[0:2, :, :], 0.0),
             writes=["wcbz"] + [("wcb_b", i) for i in range(3)], tag="zb")
        if has_bias:
            for i, bk in enumerate(["l0_bq", "l0_bk", "l0_bv"]):
                S.dma("gpsimd", lambda e, i=i, bk=bk: e.dma_start(
                        out=wcb[1:2, i, :], in_=wap2(ctx, bk).rearrange("(r c) -> r c", r=1)),
                      reads=["wcbz"], writes=[("wcb_b", i)], tag=f"ldb{i}")
        if has_lora:
            for i, (bk, ak) in enumerate([("l1_bqT", "l1_aqT"), ("l1_bkT", "l1_akT"),
                                          ("l1_bvT", "l1_avT")]):
                S.dma("gpsimd", lambda e, i=i, bk=bk: e.dma_start(
                        out=wbq[:, i], in_=wap2(ctx, bk).rearrange("(c p) o -> p c o", p=128)),
                      writes=[("wbq", i)], tag=f"ldlb{i}")
                S.dma("gpsimd", lambda e, i=i, ak=ak: e.dma_start(
                        out=waq[:, i, :], in_=wap2(ctx, ak)),
                      writes=[("waq", i)], tag=f"ldla{i}")
            S.op("vector", lambda e: e.memset(wcbl[0:2, :, :], 0.0), writes=["wcbl_b"], tag="zlb")
            for i in range(3):
                def cs(e, i=i):
                    r = None
                    for ci in range(8):
                        r = e.matmul(ctx.pS[0:1, 0:8], ctx.onescol_b[:], wbq[:, i, ci],
                                     start=(ci == 0), stop=(ci == 7))
                    return r
                S.op("tensor", cs, reads=[("wbq", i), ("c", "onescol_b")], writes=["pS"],
                     tag=f"lcs{i}")
                S.op("vector", lambda e, i=i: e.tensor_copy(wcbl[0:1, i, :], ctx.pS[0:1, 0:8]),
                     reads=["pS"], writes=[("wcbl", i)], narrow=True, tag=f"lcsev{i}")

        S.op("vector", lambda e: e.memset(Vaug[:, :, :, 64:65], 1.0), writes=["vones"], tag="vones")
        S.op("vector", lambda e: e.memset(Vaug[:, :, :, 65:66], 0.0), writes=["vzpad"], tag="vzpad")

        wkeys = [wq, wk, wv]
        def load_w(p, ob, buf, first):
            ap = wap2(ctx, wkeys[p]).rearrange("(c p) o -> p c o", p=128)[:, :, ob * 256:(ob + 1) * 256]
            S.dma("gpsimd", lambda e, ap=ap, buf=buf: e.dma_start(out=wt[:, buf], in_=ap),
                  writes=[("wt", buf)], tag=f"ldw{p}{ob}")
            if first:
                def cs(e, buf=buf):
                    r = None
                    for ci in range(8):
                        r = e.matmul(ctx.pS[0:1, 0:256], ctx.onescol_b[:], wt[:, buf, ci],
                                     start=(ci == 0), stop=(ci == 7))
                    return r
                S.op("tensor", cs, reads=[("wt", buf), ("c", "onescol_b")], writes=["pS"],
                     tag=f"cs{p}{ob}")
                S.op("vector", lambda e, p=p, ob=ob: e.tensor_copy(
                        wcb[0:1, p, ob * 256:(ob + 1) * 256], ctx.pS[0:1, 0:256]),
                     reads=["pS", "wcbz"], writes=[("wcb", p, ob)], tag=f"csev{p}{ob}")

        def load_xb(sg):
            S.dma("gpsimd", lambda e, sg=sg: e.dma_start(
                    out=xb[:, :, :], in_=xr[:, :, sg * 512:(sg + 1) * 512]),
                  reads=[("xb",)], writes=[("xb",)], tag=f"ldxb{sg}")

        def lora_mids(sg):
            for i in range(3):
                def mmid(e, i=i, sg=sg):
                    for ci in range(8):
                        e.matmul(ctx.pS[0:8, :], wbq[:, i, ci], xb[:, ci, :],
                                 start=(ci == 0), stop=False)
                    return e.matmul(ctx.pS[0:8, :], wcbl[:, i, :],
                                    rowsNB_[:, sg * 512:(sg + 1) * 512], start=False, stop=True)
                S.op("tensor", mmid, reads=[("xb",), ("wbq", i), ("wcbl", i), "wcbl_b",
                                            ("rowN", sg), ("rowS", sg)],
                     writes=["pS"], tag=f"lmid{i}_{sg}")
                S.op("vector", lambda e, i=i: e.tensor_scalar(mids[0:8, i, :], ctx.pS[0:8, :],
                                                              2.0, None, ALU.mult),
                     reads=["pS"], writes=[("mids", i)], tag=f"lmidev{i}_{sg}")

        PK = [("pmA", ctx.pA), ("pmB", ctx.pB), ("pmE", ctx.pE), ("pmF", ctx.pF)]
        pidx = [0]
        def next_psum():
            k, p = PK[pidx[0] % 4]
            pidx[0] += 1
            return k, p

        def proj_fm(p, ob, buf, oc, sg, psum, pkey):
            def mm(e, p=p, ob=ob, buf=buf, oc=oc, sg=sg, psum=psum):
                for ci in range(8):
                    e.matmul(psum[:, :], wt[:, buf, ci, oc * 128:(oc + 1) * 128], xb[:, ci, :],
                             start=(ci == 0), stop=False)
                och = ob * 2 + oc
                r = e.matmul(psum[:, :], wcb[:, p, och * 128:(och + 1) * 128],
                             rowsNB_[:, sg * 512:(sg + 1) * 512], start=False, stop=not has_lora)
                if has_lora:
                    r = e.matmul(psum[:, :], waq[:, p, och * 128:(och + 1) * 128],
                                 mids[0:8, p, :], start=False, stop=True)
                return r
            reads = [("xb",), ("wt", buf), ("wcb", p, ob), ("wcb_b", p),
                     ("rowN", sg), ("rowS", sg)]
            if has_lora:
                reads += [("mids", p), ("waq", p)]
            S.op("tensor", mm, reads=reads, writes=[pkey], tag=f"prj{p}_{ob}{oc}_{sg}")

        def evac_K(och, sg, psum, pkey):
            S.op("vector", lambda e, och=och, sg=sg, psum=psum: e.tensor_tensor(
                    K_sb[:, och, sg * 512:(sg + 1) * 512], psum[:, :], srep_[:, sg, :], ALU.mult),
                 reads=[pkey, ("srep", sg)], writes=[("K", sg)], tag=f"evK{och}_{sg}")

        def evac_Q(och, sg, psum, pkey):
            S.op("vector", lambda e, och=och, sg=sg, psum=psum: e.tensor_tensor(
                    Q_sb[:, och, :], psum[:, :], srep_[:, sg, :], ALU.mult),
                 reads=[pkey, ("srep", sg)], writes=[("Q", sg)], tag=f"evQ{och}_{sg}")

        def proj_v_tm(ob, buf, tt, sg, psum, pkey):
            def mm(e, ob=ob, buf=buf, tt=tt, psum=psum):
                lt = tt % 4
                for ci in range(8):
                    e.matmul(psum[:, 0:256], xb[:, ci, lt * 128:(lt + 1) * 128], wt[:, buf, ci, :],
                             start=(ci == 0), stop=False)
                r = e.matmul(psum[:, 0:256], rowsNB_[:, tt * 128:(tt + 1) * 128],
                             wcb[:, 2, ob * 256:(ob + 1) * 256], start=False, stop=not has_lora)
                if has_lora:
                    r = e.matmul(psum[:, 0:256], mids[0:8, 2, (tt % 4) * 128:(tt % 4 + 1) * 128],
                                 waq[:, 2, ob * 256:(ob + 1) * 256], start=False, stop=True)
                return r
            reads = [("xb",), ("wt", buf), ("rowN", sg), ("rowS", sg),
                     ("wcb", 2, ob), ("wcb_b", 2)]
            if has_lora:
                reads += [("mids", 2), ("waq", 2)]
            S.op("tensor", mm, reads=reads, writes=[pkey], tag=f"pv{ob}_{tt}")

        def evac_V(ob, tt, psum, pkey):
            out = Vaug[:, tt, ob * 4:(ob + 1) * 4, 0:64]
            S.op("vector", lambda e, psum=psum, out=out, tt=tt: e.tensor_scalar(
                    out, psum[:, 0:256].rearrange("p (h d) -> p h d", d=64),
                    rstdcol_[:, tt:tt + 1], None, ALU.mult),
                 reads=[pkey, ("rstd_col", tt // 4)], writes=[("V", tt // 4)], tag=f"evV{ob}_{tt}")

        wbuf = [0]
        for sg in range(8):
            load_xb(sg)
            if has_lora:
                lora_mids(sg)
            for p in (1, 2):
                for ob in range(4):
                    buf = wbuf[0] % 2
                    wbuf[0] += 1
                    load_w(p, ob, buf, first=(sg == 0))
                    if p == 1:
                        for oc in range(2):
                            pk, ps = next_psum()
                            proj_fm(p, ob, buf, oc, sg, ps, pk)
                            evac_K(ob * 2 + oc, sg, ps, pk)
                    else:
                        for lt in range(4):
                            tt = sg * 4 + lt
                            pk, ps = next_psum()
                            proj_v_tm(ob, buf, tt, sg, ps, pk)
                            evac_V(ob, tt, ps, pk)
        if L == 0:
            for h in range(16):
                def vb(e, h=h):
                    r = None
                    for kt in range(32):
                        r = e.matmul(ctx.pS[0:66, 4 * kt:4 * kt + 4], Vaug[:, kt, h, :],
                                     ctx.blockind[:, :], start=True, stop=True)
                    return r
                S.op("tensor", vb, reads=[("V", 0), ("V", 1), ("V", 2), ("V", 3),
                                          "vones", ("c", "blockind")],
                     writes=["pS"], tag=f"vblk{h}")
                S.op("vector", lambda e: e.tensor_copy(vblkT[:, :], ctx.pS[0:66, 0:128]),
                     reads=["pS"], writes=["vblkT"], tag=f"vblkev{h}")
                S.op("tensor", lambda e: e.transpose(ctx.pTb[0:128, 0:66], vblkT[:, :],
                                                     ctx.identb[0:66, 0:66]),
                     reads=["vblkT", ("c", "identb")], writes=["pTb"], tag=f"vblktp{h}")
                S.op("vector", lambda e, h=h: e.tensor_copy(vblk[:, h, :], ctx.pTb[0:128, 0:66]),
                     reads=["pTb"], writes=[("vblk", h)], tag=f"vblktev{h}")

        for qg in range(8):
            load_xb(qg)
            if has_lora:
                lora_mids(qg)
            for ob in range(4):
                buf = wbuf[0] % 2
                wbuf[0] += 1
                load_w(0, ob, buf, first=(qg == 0))
                for oc in range(2):
                    pk, ps = next_psum()
                    proj_fm(0, ob, buf, oc, qg, ps, pk)
                    evac_Q(ob * 2 + oc, qg, ps, pk)
            ktmax = (4 * qg + 3) if L in (0, 1) else 31
            nkt = ktmax + 1
            for hp in range(8):
                iters = [(hh, kt) for hh in range(2) for kt in range(nkt)]
                pmeta = {}

                def emit_sc(i, hp=hp, qg=qg, iters=iters):
                    hh, kt = iters[i]
                    h = 2 * hp + hh
                    hr = slice((h % 2) * 64, (h % 2) * 64 + 64)
                    hc = h // 2
                    q0 = max(0, 128 * kt - 512 * qg) if L in (0, 1) else 0
                    sl = slice(q0, 512)
                    spk, sp = next_psum()
                    pslot = i % NPS
                    pmeta[i] = (sl, pslot)
                    S.op("tensor", lambda e, kt=kt, hr=hr, hc=hc, sl=sl, sp=sp: e.matmul(
                            sp[:, sl], K_sb[hr, hc, kt * 128:(kt + 1) * 128],
                            Q_sb[hr, hc, sl], start=True, stop=True),
                         reads=[("K", kt // 4), ("Q", qg)], writes=[spk],
                         tag=f"sc{h}_{qg}_{kt}")
                    if L == 1:
                        bias = ctx.alibi[:, kt * 16 + h: kt * 16 + h + 1]
                        S.op("scalar", lambda e, pslot=pslot, sl=sl, sp=sp, bias=bias:
                                e.activation(P_sb[:, pslot, sl], sp[:, sl], AF.Exp,
                                             bias=bias, scale=0.125),
                             reads=[spk, ("c", "alibi")], writes=[("P", pslot)],
                             tag=f"exp{h}_{qg}_{kt}")
                    else:
                        S.op("scalar", lambda e, pslot=pslot, sl=sl, sp=sp: e.activation(
                                P_sb[:, pslot, sl], sp[:, sl], AF.Exp, scale=0.125),
                             reads=[spk], writes=[("P", pslot)], tag=f"exp{h}_{qg}_{kt}")
                    if L in (0, 1) and kt >= 4 * qg:
                        j = kt - 4 * qg
                        msk = ctx.stair if L == 0 else ctx.tri
                        mkey = "stair" if L == 0 else "tri"
                        S.op("vector", lambda e, pslot=pslot, j=j, msk=msk: e.tensor_tensor(
                                P_sb[:, pslot, j * 128:(j + 1) * 128],
                                P_sb[:, pslot, j * 128:(j + 1) * 128], msk[:, :], ALU.mult),
                             reads=[("P", pslot), ("c", mkey)], writes=[("P", pslot)],
                             tag=f"msk{h}_{qg}_{kt}")

                def emit_cx(i, pr, hp=hp, qg=qg, iters=iters):
                    hh, kt = iters[i]
                    h = 2 * hp + hh
                    cp = ctx.pC if hh == 0 else ctx.pD
                    cpk = "pC" if hh == 0 else "pD"
                    sl, pslot = pmeta[i]
                    if L == 0 and kt == 0:
                        S.op("tensor", lambda e, h=h, qg=qg, cp=cp: e.matmul(
                                cp[0:66, :], vblk[:, h, :], Ucst[:, qg * 512:(qg + 1) * 512],
                                start=True, stop=False),
                             reads=[("vblk", h), ("c", "U")], writes=[cpk], tag=f"sfx{h}_{qg}")
                    st = (L != 0) and kt == 0
                    preads = [("P", s) for s in pr]
                    S.op("tensor", lambda e, kt=kt, h=h, sl=sl, pslot=pslot, cp=cp,
                                 st=st, fin=(kt == ktmax): e.matmul(
                            cp[0:66, sl], Vaug[:, kt, h, :], P_sb[:, pslot, sl],
                            start=st, stop=fin),
                         reads=preads + [("V", kt // 4), "vones", "vzpad"], writes=[cpk],
                         tag=f"cx{h}_{qg}_{kt}")
                    if kt == ktmax:
                        def recf(e, cp=cp, hh=hh):
                            with nc.allow_low_precision(reason="bf16 softmax denom recip"):
                                return e.reciprocal(rec65[:, hh, :], cp[0:65, :])
                        S.op("vector", recf, reads=[cpk], writes=[("rec", hh)],
                             tag=f"rz{h}_{qg}")

                GS = NPS
                groups = [list(range(j, min(j + GS, len(iters))))
                          for j in range(0, len(iters), GS)]
                for grp in groups:
                    for i in grp:
                        emit_sc(i)
                    gslots = [j % NPS for j in grp]
                    half = len(grp) // 2
                    for j, i in enumerate(grp):
                        if j == len(grp) - 1:
                            pr = gslots
                        elif j == 0:
                            pr = gslots[:half] if half > 0 else gslots
                        elif j == half:
                            pr = gslots[half:]
                        else:
                            pr = []
                        emit_cx(i, pr)
                def bcast(e):
                    e.matmul(ctx.pS[:, :], hsel[64:65, 0, :], rec65[64:65, 0, :],
                             start=True, stop=False)
                    return e.matmul(ctx.pS[:, :], hsel[64:65, 1, :], rec65[64:65, 1, :],
                                    start=False, stop=True)
                S.op("tensor", bcast, reads=[("rec", 0), ("rec", 1), ("hsel", 0),
                                             ("hsel", 1)], writes=["pS"],
                     tag=f"bc{hp}_{qg}")
                S.op("scalar", lambda e: e.copy(rzrep[:, :], ctx.pS[:, :]),
                     reads=["pS"], writes=["rzrep"], tag=f"rzc{hp}_{qg}")
                b = qg % 2
                S.op("vector", lambda e, b=b: e.tensor_tensor(
                        ctxn[0:64, b, :], ctx.pC[0:64, :], rzrep[0:64, :], ALU.mult),
                     reads=["pC", "rzrep", ("ctxnf", b)], writes=[("ctxn", b, 0)],
                     tag=f"cev0_{hp}_{qg}")
                S.op("vector", lambda e, b=b: e.tensor_tensor(
                        ctxn[64:128, b, :], ctx.pD[0:64, :], rzrep[64:128, :], ALU.mult),
                     reads=["pD", "rzrep", ("ctxnf", b)], writes=[("ctxn", b, 1)],
                     tag=f"cev1_{hp}_{qg}")
                S.dma("sync", lambda e, hp=hp, qg=qg, b=b: e.dma_start(
                        out=ctx.ctxd[hp, qg], in_=ctxn[:, b, :]),
                      reads=[("ctxn", b, 0), ("ctxn", b, 1)],
                      writes=[("ctxd", hp, qg), ("ctxnf", b)],
                      tag=f"stctx{hp}_{qg}")
        S.fence()

    with contextlib.ExitStack() as est:
        es = est.enter_context
        has_wo = (L in (0, 1))
        if has_wo:
            wokey = "l0_woT" if L == 0 else "l1_wpT"
            wo_sb = es(nc.sbuf_tensor(f"at_wo{L}", [128, 8, 1024], BF16))
            S.dma("gpsimd", lambda e: e.dma_start(out=wo_sb[:, :, :],
                    in_=wap2(ctx, wokey).rearrange("(c p) o -> p c o", p=128)),
                  writes=["wo"], tag="ldwo")
            if L == 0:
                bias_f = es(nc.sbuf_tensor(f"at_biasf{L}", [128, 8], F32))
                S.dma("sync", lambda e: e.dma_start(out=bias_f[:, :], in_=wap2(ctx, "l0_bo")),
                      writes=["bo_f"], tag="ldbo")
        cstg = es(nc.sbuf_tensor(f"at_cstg{L}", [128, 2, 8, 512], BF16))
        xn2 = es(nc.sbuf_tensor(f"at_xn2{L}", [128, 2, 8, 512], F32))
        for sg in range(8):
            b = sg % 2
            S.dma("sync", lambda e, sg=sg, b=b: e.dma_start(
                    out=cstg[:, b], in_=ctx.ctxd[:, sg].rearrange("c p f -> p c f")),
                  reads=[("cfree", b)] + [("ctxd", hp, sg) for hp in range(8)],
                  writes=[("cstg", b)], tag=f"ldc{sg}")
            S.dma("sync", lambda e, sg=sg, b=b: e.dma_start(
                    out=xn2[:, b], in_=xr[:, :, sg * 512:(sg + 1) * 512]),
                  reads=[("xn2w", b)], writes=[("xn2", b)], tag=f"ldx2{sg}")
            if has_wo:
                for oc in range(8):
                    pk, ps = (("pmA", ctx.pA) if oc % 2 == 0 else ("pmB", ctx.pB))
                    def mm(e, oc=oc, b=b, ps=ps):
                        r = None
                        for ci in range(8):
                            r = e.matmul(ps[:, :], wo_sb[:, ci, oc * 128:(oc + 1) * 128],
                                         cstg[:, b, ci, :], start=(ci == 0), stop=(ci == 7))
                        return r
                    S.op("tensor", mm, reads=[("cstg", b), "wo"], writes=[pk], tag=f"wo{sg}_{oc}")
                    if L == 0:
                        S.op("vector", lambda e, oc=oc, b=b, ps=ps: e.scalar_tensor_tensor(
                                xn2[:, b, oc, :], ps[:, :], bias_f[:, oc:oc + 1], xn2[:, b, oc, :],
                                ALU.add, ALU.add),
                             reads=[pk, ("xn2", b), "bo_f"], writes=[("xn2", b), ("xn2w", b)],
                             tag=f"woev{sg}_{oc}")
                    else:
                        S.op("vector", lambda e, oc=oc, b=b, ps=ps: e.tensor_add(
                                xn2[:, b, oc, :], ps[:, :], xn2[:, b, oc, :]),
                             reads=[pk, ("xn2", b)], writes=[("xn2", b), ("xn2w", b)],
                             tag=f"woev{sg}_{oc}")
            else:
                S.op("vector", lambda e, b=b: e.tensor_add(
                        xn2[:, b].rearrange("p c t -> p (c t)"),
                        xn2[:, b].rearrange("p c t -> p (c t)"),
                        cstg[:, b].rearrange("p c t -> p (c t)")),
                     reads=[("cstg", b), ("xn2", b)], writes=[("xn2", b), ("xn2w", b)],
                     tag=f"resadd{sg}")
            S.dma("sync", lambda e, sg=sg, b=b: e.dma_start(
                    out=xdst.rearrange("(c p) t -> p c t", p=128)[:, :, sg * 512:(sg + 1) * 512],
                    in_=xn2[:, b]),
                  reads=[("xn2", b)], writes=[("xout", sg), ("xn2w", b), ("cfree", b)],
                  tag=f"stx{sg}")
        S.fence()

def emit_quant(ctx, pref, dst, dkey, src, skey, n, tmp):
    nc, S = ctx.nc, ctx.S
    A, B, C, E, S2, S3, sgn, am, sc, rs, scb, thc = tmp
    nb = n // 16
    r3 = lambda ap: ap.rearrange("p (b k) -> p b k", k=16)
    PH = lambda nm: ("qph", nm)

    def V(fn, reads, writes, tag):
        def wrapped(e, fn=fn):
            with nc.allow_low_precision(reason="fp4 grid counters in bf16 (exact small ints)"):
                return fn(e)
        S.op("vector", wrapped, reads=reads, writes=writes, tag=tag)

    V(lambda e: e.reduce_max(am[:, 0:nb], r3(src), AX.X, apply_absolute_value=True),
      [skey], [(pref, "am"), PH("am")], f"{pref}_amax")
    V(lambda e: e.tensor_scalar(sc[:, 0:nb], am[:, 0:nb], 1e-12, 1.0 / 6.0,
                                ALU.max, ALU.mult),
      [(pref, "am")], [(pref, "sc"), PH("sc")], f"{pref}_sc")
    V(lambda e: e.tensor_copy(scb[:, 0:nb], sc[:, 0:nb]),
      [(pref, "sc")], [(pref, "scb"), PH("scb")], f"{pref}_scb")
    V(lambda e: e.reciprocal(rs[:, 0:nb], sc[:, 0:nb]),
      [(pref, "sc")], [(pref, "rs"), PH("rs")], f"{pref}_rs")
    V(lambda e: e.tensor_scalar(rs[:, 0:nb], rs[:, 0:nb], 2.0, None, ALU.mult),
      [(pref, "rs")], [(pref, "rs"), PH("rs")], f"{pref}_rs2")
    V(lambda e: e.tensor_tensor(r3(A[:, 0:n]), r3(src),
            rs[:, 0:nb][:, :, None].broadcast_to([128, nb, 16]), ALU.mult),
      [skey, (pref, "rs")], [(pref, "A"), PH("A")], f"{pref}_t")
    S.op("scalar", lambda e: e.activation(sgn[:, 0:n], A[:, 0:n], AF.Sign),
         reads=[(pref, "A")], writes=[(pref, "sgn"), PH("sgn")], tag=f"{pref}_sgn")
    S.op("scalar", lambda e: e.activation(B[:, 0:n], A[:, 0:n], AF.Abs),
         reads=[(pref, "A")], writes=[(pref, "B"), PH("B")], tag=f"{pref}_abs")
    for name, buf, bk, ti in [("s25", S2, "S2", 0), ("s35", S3, "S3", 1)]:
        S.op("scalar", lambda e, buf=buf, ti=ti: e.activation(
                buf[:, 0:n], B[:, 0:n], AF.Sign, bias=thc[:, ti:ti + 1]),
             reads=[(pref, "B"), "thc"], writes=[(pref, bk), PH(bk)], tag=f"{pref}_{name}")
    V(lambda e: e.tensor_scalar(C[:, 0:n], B[:, 0:n], 0.5, None, ALU.is_gt),
      [(pref, "B")], [(pref, "C"), PH("C")], f"{pref}_l0")
    V(lambda e: e.scalar_tensor_tensor(C[:, 0:n], B[:, 0:n], 1.5, C[:, 0:n],
                                       ALU.is_gt, ALU.add),
      [(pref, "B"), (pref, "C")], [(pref, "C"), PH("C")], f"{pref}_l1")
    V(lambda e: e.tensor_scalar(C[:, 0:n], C[:, 0:n], 0.5, 2.5, ALU.mult, ALU.add),
      [(pref, "C")], [(pref, "C"), PH("C")], f"{pref}_half")
    V(lambda e: e.tensor_add(E[:, 0:n], S2[:, 0:n], S3[:, 0:n]),
      [(pref, "S2"), (pref, "S3")], [(pref, "E"), PH("E")], f"{pref}_d1")
    V(lambda e: e.scalar_tensor_tensor(C[:, 0:n], E[:, 0:n], 0.25, C[:, 0:n],
                                       ALU.mult, ALU.add),
      [(pref, "E"), (pref, "C")], [(pref, "C"), PH("C")], f"{pref}_d2")
    for name, buf, bk, ti in [("s5", S2, "S2", 2), ("s7", S3, "S3", 3)]:
        S.op("scalar", lambda e, buf=buf, ti=ti: e.activation(
                buf[:, 0:n], B[:, 0:n], AF.Sign, bias=thc[:, ti:ti + 1]),
             reads=[(pref, "B"), "thc"], writes=[(pref, bk), PH(bk)], tag=f"{pref}_{name}")
    V(lambda e: e.tensor_add(E[:, 0:n], S2[:, 0:n], S3[:, 0:n]),
      [(pref, "S2"), (pref, "S3")], [(pref, "E"), PH("E")], f"{pref}_d3")
    V(lambda e: e.scalar_tensor_tensor(C[:, 0:n], E[:, 0:n], 0.5, C[:, 0:n],
                                       ALU.mult, ALU.add),
      [(pref, "E"), (pref, "C")], [(pref, "C"), PH("C")], f"{pref}_d4")
    S.op("scalar", lambda e: e.activation(S2[:, 0:n], B[:, 0:n], AF.Sign,
                                          bias=thc[:, 4:5]),
         reads=[(pref, "B"), "thc"], writes=[(pref, "S2"), PH("S2")], tag=f"{pref}_s10")
    V(lambda e: e.tensor_add(C[:, 0:n], C[:, 0:n], S2[:, 0:n]),
      [(pref, "C"), (pref, "S2")], [(pref, "C"), PH("C")], f"{pref}_d5")
    V(lambda e: e.tensor_tensor(C[:, 0:n], C[:, 0:n], sgn[:, 0:n], ALU.mult),
      [(pref, "C"), (pref, "sgn")], [(pref, "C"), PH("C")], f"{pref}_qs")
    V(lambda e: e.tensor_tensor(r3(dst), r3(C[:, 0:n]),
            scb[:, 0:nb][:, :, None].broadcast_to([128, nb, 16]), ALU.mult),
      [(pref, "C"), (pref, "scb")], [dkey], f"{pref}_out")

def emit_l0ffn(ctx, xsrc, xdst):
    import contextlib
    nc, S = ctx.nc, ctx.S
    xr = xsrc.rearrange("(c p) t -> p c t", p=128)
    with contextlib.ExitStack() as est:
        es = est.enter_context
        wg = es(nc.sbuf_tensor("f0_wg", [128, 8, 2816], BF16))
        wu = es(nc.sbuf_tensor("f0_wu", [128, 8, 2816], BF16))
        wd = es(nc.sbuf_tensor("f0_wd", [128, 22, 1024], BF16))
        qA = es(nc.sbuf_tensor("f0_qA", [128, 1024], F32))
        qB = es(nc.sbuf_tensor("f0_qB", [128, 1024], F32))
        qC = es(nc.sbuf_tensor("f0_qC", [128, 1024], BF16))
        qE = es(nc.sbuf_tensor("f0_qE", [128, 1024], BF16))
        qS2 = es(nc.sbuf_tensor("f0_qS2", [128, 1024], BF16))
        qS3 = es(nc.sbuf_tensor("f0_qS3", [128, 1024], BF16))
        qS = es(nc.sbuf_tensor("f0_qS", [128, 1024], BF16))
        qm = es(nc.sbuf_tensor("f0_qm", [128, 64], F32))
        qsc = es(nc.sbuf_tensor("f0_qsc", [128, 64], F32))
        qrs = es(nc.sbuf_tensor("f0_qrs", [128, 64], F32))
        qscb = es(nc.sbuf_tensor("f0_qscb", [128, 64], BF16))
        qth = es(nc.sbuf_tensor("f0_qth", [128, 5], F32))
        for ti, tv in enumerate([-2.5, -3.5, -5.0, -7.0, -10.0]):
            S.op("vector", lambda e, ti=ti, tv=tv: e.memset(qth[:, ti:ti + 1], tv),
                 writes=["thc"], narrow=True, tag=f"f0th{ti}")
        qtmp = (qA, qB, qC, qE, qS2, qS3, qS, qm, qsc, qrs, qscb, qth)

        import contextlib as _cl
        with _cl.ExitStack() as est2:
            wstg = est2.enter_context(nc.sbuf_tensor("f0_wstg", [128, 2816], F32))
            wq16 = est2.enter_context(nc.sbuf_tensor("f0_wq16", [128, 2816], BF16))
            def quant_weight(key, rows, cols, dst, tagp):
                nt = rows // 128
                wap = wap2(ctx, key)
                pieces = [(0, 1024), (1024, 1024), (2048, 768)] if cols == 2816 else [(0, 1024)]
                for r in range(nt):
                    S.dma("sync", lambda e, r=r, cols=cols: e.dma_start(
                            out=wstg[:, 0:cols], in_=wap.rearrange("(n p) c -> n p c", p=128)[r]),
                          reads=["f0wstg"], writes=["f0wstg"], tag=f"{tagp}ld{r}")
                    for pi_, (po, pn) in enumerate(pieces):
                        emit_quant(ctx, f"{tagp}q{r}_{pi_}", wq16[:, po:po + pn], "f0wq",
                                   wstg[:, po:po + pn], "f0wstg", pn, qtmp)
                    nch = cols // 128
                    for g in range(0, nch, 8):
                        gn = min(8, nch - g)
                        def tp(e, g=g, gn=gn):
                            rr = None
                            for j in range(gn):
                                rr = e.transpose(ctx.pTb[:, j * 128:(j + 1) * 128],
                                                 wq16[:, (g + j) * 128:(g + j + 1) * 128],
                                                 ctx.identb[:, :])
                            return rr
                        S.op("tensor", tp, reads=["f0wq", ("c", "identb")], writes=["pTb"],
                             tag=f"{tagp}tp{r}_{g}")
                        def ev(e, r=r, g=g, gn=gn):
                            return e.tensor_copy(
                                dst[:, g:g + gn, r * 128:(r + 1) * 128],
                                ctx.pTb[:, 0:gn * 128].rearrange("p (j q) -> p j q", q=128))
                        S.op("vector", ev, reads=["pTb"], writes=[(tagp, "dst"), "f0wq"],
                             tag=f"{tagp}ev{r}_{g}")
            quant_weight("l0_wgate", 2816, 1024, wg, "qg")
            quant_weight("l0_wup", 2816, 1024, wu, "qu")
            quant_weight("l0_wdown", 1024, 2816, wd, "qd")
            S.fence()

        xsl = es(nc.sbuf_tensor("f0_xsl", [128, 8, 512], F32))
        xtm = es(nc.sbuf_tensor("f0_xtm", [128, 1024], F32))
        xq = es(nc.sbuf_tensor("f0_xq", [128, 2, 1024], BF16))
        xqf = es(nc.sbuf_tensor("f0_xqf", [128, 2, 8, 128], BF16))
        hbuf = es(nc.sbuf_tensor("f0_h", [128, 512], F32))
        hq = es(nc.sbuf_tensor("f0_hq", [128, 2, 512], BF16))
        hqf = es(nc.sbuf_tensor("f0_hqf", [128, 22, 256], BF16))
        gact = es(nc.sbuf_tensor("f0_gact", [128, 2, 512], BF16))
        sq1 = es(nc.sbuf_tensor("f0_sq1", [128, 1024], BF16))
        mu = es(nc.sbuf_tensor("f0_mu", [128, 1], F32))
        musq = es(nc.sbuf_tensor("f0_musq", [128, 1], F32))
        ssum = es(nc.sbuf_tensor("f0_ssum", [128, 1], F32))
        ssq = es(nc.sbuf_tensor("f0_ssq", [128, 1], F32))
        rstd = es(nc.sbuf_tensor("f0_rstd", [128, 1], F32))

        for sg in range(8):
            S.dma("sync", lambda e, sg=sg: e.dma_start(
                    out=xsl[:, :, :], in_=xr[:, :, sg * 512:(sg + 1) * 512]),
                  reads=[("xsl",)], writes=[("xsl",)], tag=f"f0ldx{sg}")
            for half in range(2):
                for l2 in range(2):
                    lt = half * 2 + l2
                    tcol = lt * 128
                    for g in range(2):
                        def tpx(e, g=g, tcol=tcol):
                            rr = None
                            for j in range(4):
                                ci = g * 4 + j
                                rr = e.transpose(ctx.pE[:, j * 128:(j + 1) * 128],
                                                 xsl[:, ci, tcol:tcol + 128],
                                                 ctx.identf[:, :])
                            return rr
                        S.op("tensor", tpx, reads=[("xsl",), ("c", "identf")], writes=["pT32"],
                             tag=f"f0tx{sg}_{lt}_{g}")
                        S.op("vector", lambda e, g=g: e.tensor_copy(
                                xtm[:, g * 512:(g + 1) * 512], ctx.pE[:, :]),
                             reads=["pT32"], writes=[("xtm", g)], tag=f"f0txe{sg}_{lt}_{g}")
                    S.op("vector", lambda e: e.reduce_sum(ssum[:], xtm[:, :], AX.X),
                         reads=[("xtm", 0), ("xtm", 1)], writes=["ssum"], narrow=True,
                         tag=f"f0sum{sg}_{lt}")
                    S.op("scalar", lambda e: e.activation(sq1[:, :], xtm[:, :], AF.Square,
                                                          accum_out=ssq[:]),
                         reads=[("xtm", 0), ("xtm", 1)], writes=["ssq"], narrow=True,
                         tag=f"f0ssq{sg}_{lt}")
                    S.op("vector", lambda e: e.tensor_scalar(mu[:], ssum[:], 1.0 / D, None, ALU.mult),
                         reads=["ssum"], writes=["mu"], narrow=True, tag=f"f0mu{sg}_{lt}")
                    S.op("vector", lambda e: e.tensor_tensor(musq[:], mu[:], mu[:], ALU.mult),
                         reads=["mu"], writes=["musq"], narrow=True, tag=f"f0musq{sg}_{lt}")
                    S.op("vector", lambda e: e.scalar_tensor_tensor(ssq[:], ssq[:], 1.0 / D,
                                                                    musq[:], ALU.mult, ALU.subtract),
                         reads=["ssq", "musq"], writes=["ssq"], narrow=True, tag=f"f0var{sg}_{lt}")
                    S.op("vector", lambda e: e.tensor_scalar(ssq[:], ssq[:], EPS, None, ALU.add),
                         reads=["ssq"], writes=["ssq"], narrow=True, tag=f"f0vare{sg}_{lt}")
                    S.op("scalar", lambda e: e.activation(ssq[:], ssq[:], AF.Sqrt),
                         reads=["ssq"], writes=["ssq"], narrow=True, tag=f"f0sig{sg}_{lt}")
                    S.op("vector", lambda e: e.reciprocal(rstd[:], ssq[:]),
                         reads=["ssq"], writes=["rstd"], narrow=True, tag=f"f0rstd{sg}_{lt}")
                    S.op("vector", lambda e: e.tensor_scalar(xtm[:, :], xtm[:, :], mu[:], rstd[:],
                                                             ALU.subtract, ALU.mult),
                         reads=[("xtm", 0), ("xtm", 1), "mu", "rstd"],
                         writes=[("xtm", 0), ("xtm", 1)], tag=f"f0ln{sg}_{lt}")
                    lb = lt % 2
                    emit_quant(ctx, f"f0xq{sg}_{lt}", xq[:, lb, :], ("xq", lb), xtm[:, :],
                               ("xtm", 0), 1024, qtmp)
                    for g in range(1):
                        def tpq(e, lb=lb):
                            rr = None
                            for j in range(8):
                                rr = e.transpose(ctx.pTb[:, j * 128:(j + 1) * 128],
                                                 xq[:, lb, j * 128:(j + 1) * 128], ctx.identb[:, :])
                            return rr
                        S.op("tensor", tpq, reads=[("xq", lb), ("c", "identb")], writes=["pTb"],
                             tag=f"f0tq{sg}_{lt}")
                        S.op("vector", lambda e, lb=lb: e.tensor_copy(
                                xqf[:, lb, :, :], ctx.pTb[:, :].rearrange("p (j q) -> p j q", q=128)),
                             reads=["pTb"], writes=[("xqf", lb)], tag=f"f0tqe{sg}_{lt}")
                    for obl in range(6):
                        nw = 512 if obl < 5 else 256
                        osl = slice(obl * 512, obl * 512 + nw)
                        ob2 = obl % 2
                        def mmg(e, osl=osl, nw=nw, lb=lb):
                            rr = None
                            for ci in range(8):
                                rr = e.matmul(ctx.pA[0:128, 0:nw], xqf[:, lb, ci, :], wg[:, ci, osl],
                                              start=(ci == 0), stop=(ci == 7))
                            return rr
                        def mmu(e, osl=osl, nw=nw, lb=lb):
                            rr = None
                            for ci in range(8):
                                rr = e.matmul(ctx.pB[0:128, 0:nw], xqf[:, lb, ci, :], wu[:, ci, osl],
                                              start=(ci == 0), stop=(ci == 7))
                            return rr
                        S.op("tensor", mmg, reads=[("xqf", lb), ("qg", "dst")], writes=["pA"],
                             tag=f"f0mg{sg}_{lt}_{obl}")
                        S.op("tensor", mmu, reads=[("xqf", lb), ("qu", "dst")], writes=["pB"],
                             tag=f"f0mu{sg}_{lt}_{obl}")
                        S.op("scalar", lambda e, nw=nw, ob2=ob2: e.activation(
                                gact[:, ob2, 0:nw], ctx.pA[:, 0:nw], AF.Silu),
                             reads=["pA"], writes=[("gact", ob2)], tag=f"f0sil{sg}_{lt}_{obl}")
                        S.op("vector", lambda e, nw=nw, ob2=ob2: e.tensor_tensor(
                                hbuf[:, 0:nw], gact[:, ob2, 0:nw], ctx.pB[:, 0:nw], ALU.mult),
                             reads=[("gact", ob2), "pB"], writes=["hbuf"], tag=f"f0h{sg}_{lt}_{obl}")
                        emit_quant(ctx, f"f0hq{sg}_{lt}_{obl}", hq[:, ob2, 0:nw], ("hq", ob2),
                                   hbuf[:, 0:nw], "hbuf", nw, qtmp)
                        nch = nw // 128
                        def tph(e, nch=nch, ob2=ob2):
                            rr = None
                            for j in range(nch):
                                rr = e.transpose(ctx.pTb[:, j * 128:(j + 1) * 128],
                                                 hq[:, ob2, j * 128:(j + 1) * 128], ctx.identb[:, :])
                            return rr
                        S.op("tensor", tph, reads=[("hq", ob2), ("c", "identb")], writes=["pTb"],
                             tag=f"f0th{sg}_{lt}_{obl}")
                        def evh(e, obl=obl, l2=l2, nch=nch):
                            return e.tensor_copy(
                                hqf[:, obl * 4:obl * 4 + nch, l2 * 128:(l2 + 1) * 128],
                                ctx.pTb[:, 0:nch * 128].rearrange("p (j q) -> p j q", q=128))
                        S.op("vector", evh, reads=["pTb"], writes=[("hqf", l2, obl)],
                             tag=f"f0the{sg}_{lt}_{obl}")
                hk = [("hqf", l2, obl) for l2 in range(2) for obl in range(6)]
                for oc in range(8):
                    ps, pk = (ctx.pC, "pC") if oc % 2 == 0 else (ctx.pD, "pD")
                    def mmd2(e, ps=ps, oc=oc):
                        rr = None
                        for cj in range(22):
                            rr = e.matmul(ps[:, 0:256], wd[:, cj, oc * 128:(oc + 1) * 128],
                                          hqf[:, cj, :], start=(cj == 0), stop=(cj == 21))
                        return rr
                    S.op("tensor", mmd2, reads=hk + [("qd", "dst")], writes=[pk],
                         tag=f"f0md{sg}_{half}_{oc}")
                    tc = half * 256
                    S.op("vector", lambda e, oc=oc, tc=tc, ps=ps: e.tensor_add(
                            xsl[:, oc, tc:tc + 256], ps[:, 0:256], xsl[:, oc, tc:tc + 256]),
                         reads=[pk, ("xsl",)], writes=[("xsl",)], tag=f"f0de{sg}_{half}_{oc}")
            S.dma("sync", lambda e, sg=sg: e.dma_start(
                    out=xdst.rearrange("(c p) t -> p c t", p=128)[:, :, sg * 512:(sg + 1) * 512],
                    in_=xsl[:, :, :]),
                  reads=[("xsl",)], writes=[("xout", sg), ("xsl",)], tag=f"f0st{sg}")
        S.fence()

def emit_l1ffn(ctx, xsrc, xdst):
    import contextlib
    nc, S = ctx.nc, ctx.S
    xr = xsrc.rearrange("(c p) t -> p c t", p=128)
    xdr = xdst.rearrange("(c p) t -> p c t", p=128)
    with contextlib.ExitStack() as est:
        es = est.enter_context
        xbw = es(nc.sbuf_tensor("f1_xbw", [128, 8, T], BF16))
        emit_stats(ctx, xsrc, es, tag="f1", xb_dst=xbw)
        rowsNB_, srep_ = ctx.rowsNB, ctx.srep
        rstdcol_ = getattr(ctx, 'rstd_col', None)
        wt1 = es(nc.sbuf_tensor("f1_wt1", [128, 2, 8, 128], BF16))
        wt2 = es(nc.sbuf_tensor("f1_wt2", [128, 2, 8, 128], BF16))
        cs1 = es(nc.sbuf_tensor("f1_cs1", [1, 4096], BF16))
        cs2 = es(nc.sbuf_tensor("f1_cs2", [1, 4096], BF16))
        s1c = es(nc.sbuf_tensor("f1_s1c", [128, 32], F32))
        s2c = es(nc.sbuf_tensor("f1_s2c", [128, 32], F32))
        E1 = es(nc.sbuf_tensor("f1_E1", [128, 512], F32))
        sil = es(nc.sbuf_tensor("f1_sil", [128, 512], BF16))
        hst = es(nc.sbuf_tensor("f1_hst", [128, 2, 512], BF16))
        S.dma("sync", lambda e: e.dma_start(out=s1c[:, :], in_=wap2(ctx, "l1_s1")),
              writes=["s1c"], tag="f1lds1")
        S.dma("sync", lambda e: e.dma_start(out=s2c[:, :], in_=wap2(ctx, "l1_s2")),
              writes=["s2c"], tag="f1lds2")
        for oc in range(32):
            b = oc % 2
            for wi, (wkey, wt, csk) in enumerate([("l1_w1T", wt1, "cs1"), ("l1_w2T", wt2, "cs2")]):
                ap = wap2(ctx, wkey).rearrange("(c p) o -> p c o", p=128)[:, :, oc * 128:(oc + 1) * 128]
                S.dma("gpsimd", lambda e, ap=ap, wt=wt, b=b: e.dma_start(out=wt[:, b], in_=ap),
                      reads=[(csk, "free", b)], writes=[(csk, "wt", b)], tag=f"f1lw{wi}_{oc}")
                cst = cs1 if wi == 0 else cs2
                def csf(e, wt=wt, b=b):
                    rr = None
                    for ci in range(8):
                        rr = e.matmul(ctx.pS[0:1, 0:128], ctx.onescol_b[:], wt[:, b, ci],
                                      start=(ci == 0), stop=(ci == 7))
                    return rr
                S.op("tensor", csf, reads=[(csk, "wt", b), ("c", "onescol_b")], writes=["pS"],
                     tag=f"f1cs{wi}_{oc}")
                S.op("vector", lambda e, cst=cst, oc=oc: e.tensor_copy(
                        cst[0:1, oc * 128:(oc + 1) * 128], ctx.pS[0:1, 0:128]),
                     reads=["pS"], writes=[(csk, "row", oc)], tag=f"f1cse{wi}_{oc}")
            for sg in range(8):
                pk1, ps1 = ("pmA", ctx.pA) if sg % 2 == 0 else ("pmB", ctx.pB)
                pk2, ps2 = ("pmC", ctx.pC) if sg % 2 == 0 else ("pmD", ctx.pD)
                sl = slice(sg * 512, (sg + 1) * 512)
                def mm1(e, b=b, sl=sl, oc=oc, ps=ps1):
                    for ci in range(8):
                        e.matmul(ps[:, :], wt1[:, b, ci], xbw[:, ci, sl], start=(ci == 0), stop=False)
                    return e.matmul(ps[:, :], cs1[0:1, oc * 128:(oc + 1) * 128],
                                    rowsNB_[0:1, sl], start=False, stop=True)
                def mm2(e, b=b, sl=sl, oc=oc, ps=ps2):
                    for ci in range(8):
                        e.matmul(ps[:, :], wt2[:, b, ci], xbw[:, ci, sl], start=(ci == 0), stop=False)
                    return e.matmul(ps[:, :], cs2[0:1, oc * 128:(oc + 1) * 128],
                                    rowsNB_[0:1, sl], start=False, stop=True)
                S.op("tensor", mm1, reads=[("cs1", "wt", b), ("cs1", "row", oc), ("xbw", sg),
                                           ("rowN", sg)], writes=[pk1], tag=f"f1m1_{oc}_{sg}")
                S.op("tensor", mm2, reads=[("cs2", "wt", b), ("cs2", "row", oc), ("xbw", sg),
                                           ("rowN", sg)], writes=[pk2], tag=f"f1m2_{oc}_{sg}")
                S.op("vector", lambda e, sg=sg, ps=ps1: e.tensor_tensor(
                        E1[:, :], ps[:, :], srep_[:, sg, :], ALU.mult),
                     reads=[pk1, ("srep", sg)], writes=["E1"], tag=f"f1e1_{oc}_{sg}")
                S.op("scalar", lambda e, oc=oc: e.activation(sil[:, :], E1[:, :], AF.Silu,
                                                             scale=s1c[:, oc:oc + 1]),
                     reads=["E1", "s1c"], writes=["sil"], tag=f"f1sil_{oc}_{sg}")
                hb = sg % 2
                S.op("vector", lambda e, sg=sg, ps=ps2: e.tensor_tensor(
                        E1[:, :], ps[:, :], srep_[:, sg, :], ALU.mult),
                     reads=[pk2, ("srep", sg), "sil"], writes=["E1"], tag=f"f1e2_{oc}_{sg}")
                S.op("vector", lambda e, oc=oc, hb=hb: e.scalar_tensor_tensor(
                        hst[:, hb, :], E1[:, :], s2c[:, oc:oc + 1], sil[:, :], ALU.mult, ALU.mult),
                     reads=["E1", "s2c", "sil", ("hfree", hb)], writes=[("hst", hb)],
                     tag=f"f1h_{oc}_{sg}")
                S.dma("sync", lambda e, oc=oc, sg=sg, hb=hb: e.dma_start(
                        out=ctx.hD[oc, sg], in_=hst[:, hb, :]),
                      reads=[("hst", hb)], writes=[("hD", oc, sg), ("hfree", hb),
                                                   (csk, "free", b) if sg == 7 else ("na",)],
                      tag=f"f1sth_{oc}_{sg}")
        S.fence()
    for pi, (wkey, skey, srcD, dstD, srck) in enumerate(
            [("l1_h1T", "l1_h1s", ctx.hD, ctx.g1D, "hD"),
             ("l1_h2T", "l1_h2s", ctx.g1D, ctx.g2D, "g1D")]):
        with contextlib.ExitStack() as est:
            es = est.enter_context
            hw = es(nc.sbuf_tensor(f"f1_hw{pi}", [128, 32, 1024], BF16))
            hsl = es(nc.sbuf_tensor(f"f1_hsl{pi}", [128, 2, 32, 512], BF16))
            ssc = es(nc.sbuf_tensor(f"f1_ssc{pi}", [128, 32], F32))
            gst = es(nc.sbuf_tensor(f"f1_gst{pi}", [128, 2, 512], BF16))
            S.dma("sync", lambda e: e.dma_start(out=ssc[:, :], in_=wap2(ctx, skey)),
                  writes=["ssc"], tag=f"f1lds{pi}")
            for ocb in range(4):
                ap = wap2(ctx, wkey).rearrange("(c p) o -> p c o", p=128)[
                    :, :, ocb * 1024:(ocb + 1) * 1024]
                S.dma("gpsimd", lambda e, ap=ap: e.dma_start(out=hw[:, :, :], in_=ap),
                      reads=[("hw",)], writes=[("hw",)], tag=f"f1lhw{pi}_{ocb}")
                for sg in range(8):
                    b = sg % 2
                    S.dma("sync", lambda e, sg=sg, b=b, srcD=srcD: e.dma_start(
                            out=hsl[:, b], in_=srcD[:, sg].rearrange("o p f -> p o f")),
                          reads=[("hslf", b)] + ([("hD", o, sg) for o in range(32)] if pi == 0 and ocb == 0 else []),
                          writes=[("hsl", b)], tag=f"f1lh{pi}_{ocb}_{sg}")
                    for oc in range(8):
                        og = ocb * 8 + oc
                        ps, pk = (ctx.pA, "pmA") if oc % 2 == 0 else (ctx.pB, "pmB")
                        def mmh(e, b=b, og=og, ps=ps):
                            rr = None
                            for cj in range(32):
                                rr = e.matmul(ps[:, :], hw[:, cj, (og % 8) * 128:(og % 8 + 1) * 128],
                                              hsl[:, b, cj, :], start=(cj == 0), stop=(cj == 31))
                            return rr
                        S.op("tensor", mmh, reads=[("hw",), ("hsl", b)], writes=[pk],
                             tag=f"f1mh{pi}_{og}_{sg}")
                        gb = sg % 2
                        S.op("scalar", lambda e, og=og, gb=gb, ps=ps: e.activation(
                                gst[:, gb, :], ps[:, :], AF.Gelu, scale=ssc[:, og:og + 1]),
                             reads=[pk, "ssc", ("gfree", gb)], writes=[("gst", gb)],
                             tag=f"f1g{pi}_{og}_{sg}")
                        S.dma("sync", lambda e, og=og, sg=sg, gb=gb, dstD=dstD: e.dma_start(
                                out=dstD[og, sg], in_=gst[:, gb, :]),
                              reads=[("gst", gb)], writes=[(("g1D" if pi == 0 else "g2D"), og, sg),
                                                           ("gfree", gb)],
                              tag=f"f1sg{pi}_{og}_{sg}")
            S.fence()
    with contextlib.ExitStack() as est:
        es = est.enter_context
        w3 = es(nc.sbuf_tensor("f1_w3", [128, 32, 1024], BF16))
        gsl = es(nc.sbuf_tensor("f1_gsl", [128, 2, 32, 512], BF16))
        s3c = es(nc.sbuf_tensor("f1_s3c", [128, 8], F32))
        xsl = es(nc.sbuf_tensor("f1_xsl", [128, 2, 8, 512], F32))
        S.dma("gpsimd", lambda e: e.dma_start(out=w3[:, :, :],
                in_=wap2(ctx, "l1_w3T").rearrange("(c p) o -> p c o", p=128)),
              writes=[("w3",)], tag="f1lw3")
        S.dma("sync", lambda e: e.dma_start(out=s3c[:, :], in_=wap2(ctx, "l1_s3")),
              writes=["s3c"], tag="f1lds3")
        for sg in range(8):
            b = sg % 2
            S.dma("sync", lambda e, sg=sg, b=b: e.dma_start(
                    out=gsl[:, b], in_=ctx.g2D[:, sg].rearrange("o p f -> p o f")),
                  reads=[("gslf", b)] + [("g2D", o, sg) for o in range(32)],
                  writes=[("gsl", b)], tag=f"f1lg2_{sg}")
            S.dma("sync", lambda e, sg=sg, b=b: e.dma_start(
                    out=xsl[:, b], in_=xr[:, :, sg * 512:(sg + 1) * 512]),
                  reads=[("xslf", b)], writes=[("xsl", b)], tag=f"f1lx{sg}")
            for oc in range(8):
                ps, pk = (ctx.pA, "pmA") if oc % 2 == 0 else (ctx.pB, "pmB")
                def mmo(e, b=b, oc=oc, ps=ps):
                    rr = None
                    for cj in range(32):
                        rr = e.matmul(ps[:, :], w3[:, cj, oc * 128:(oc + 1) * 128],
                                      gsl[:, b, cj, :], start=(cj == 0), stop=(cj == 31))
                    return rr
                S.op("tensor", mmo, reads=[("w3",), ("gsl", b)], writes=[pk], tag=f"f1mo{oc}_{sg}")
                S.op("vector", lambda e, oc=oc, b=b, ps=ps: e.scalar_tensor_tensor(
                        xsl[:, b, oc, :], ps[:, :], s3c[:, oc:oc + 1], xsl[:, b, oc, :],
                        ALU.mult, ALU.add),
                     reads=[pk, "s3c", ("xsl", b)], writes=[("xsl", b)], tag=f"f1oe{oc}_{sg}")
            S.dma("sync", lambda e, sg=sg, b=b: e.dma_start(
                    out=xdr[:, :, sg * 512:(sg + 1) * 512], in_=xsl[:, b]),
                  reads=[("xsl", b)], writes=[("xout", sg), ("xslf", b), ("gslf", b)],
                  tag=f"f1st{sg}")
        S.fence()

def emit_l2ffn(ctx, xsrc, xdst):
    import contextlib
    nc, S = ctx.nc, ctx.S
    xr = xsrc.rearrange("(c p) t -> p c t", p=128)
    xdr = xdst.rearrange("(c p) t -> p c t", p=128)
    with contextlib.ExitStack() as est:
        es = est.enter_context
        emit_stats(ctx, xsrc, es, tag="f2")
        rowsNB_, srep_ = ctx.rowsNB, ctx.srep
        rstdcol_ = getattr(ctx, 'rstd_col', None)
        xln = es(nc.sbuf_tensor("f2_xln", [128, 8, T], BF16))
        xsl = es(nc.sbuf_tensor("f2_xsl", [128, 2, 8, 512], F32))
        wr = es(nc.sbuf_tensor("f2_wr", [128, 8, 1024], BF16))
        wk = es(nc.sbuf_tensor("f2_wk", [128, 8, 1024], BF16))
        wv = es(nc.sbuf_tensor("f2_wv", [128, 8, 1024], BF16))
        mur = es(nc.sbuf_tensor("f2_mur", [128, 8], F32))
        muk = es(nc.sbuf_tensor("f2_muk", [128, 8], F32))
        AA = es(nc.sbuf_tensor("f2_A", [128, 8, 512], F32))
        xmix = es(nc.sbuf_tensor("f2_xmix", [128, 8, 512], BF16))
        rsb = es(nc.sbuf_tensor("f2_r", [128, 8, 512], BF16))
        kact = es(nc.sbuf_tensor("f2_k", [128, 8, 512], BF16))
        for wkey, wsb, tagw in [("l2_wrT", wr, "wr"), ("l2_wkT2", wk, "wk"), ("l2_wvT2", wv, "wv")]:
            S.dma("gpsimd", lambda e, wkey=wkey, wsb=wsb: e.dma_start(out=wsb[:, :, :],
                    in_=wap2(ctx, wkey).rearrange("(c p) o -> p c o", p=128)),
                  writes=[(tagw,)], tag=f"f2l{tagw}")
        S.dma("sync", lambda e: e.dma_start(out=mur[:, :], in_=wap2(ctx, "l2_mur")),
              writes=["mur"], tag="f2lmur")
        S.dma("sync", lambda e: e.dma_start(out=muk[:, :], in_=wap2(ctx, "l2_muk")),
              writes=["muk"], tag="f2lmuk")
        for sg in range(8):
            b = sg % 2
            sl = slice(sg * 512, (sg + 1) * 512)
            S.dma("sync", lambda e, sl=sl, b=b: e.dma_start(out=xsl[:, b], in_=xr[:, :, sl]),
                  reads=[("xslf", b)], writes=[("xsl", b)], tag=f"f2lx{sg}")
            S.op("tensor", lambda e, sg=sg, sl=sl: e.matmul(
                    ctx.pE[:, :], ctx.onesrow_b[:, :], rowsNB_[0:1, sl], start=True, stop=True),
                 reads=[("rowN", sg), ("c", "onesrow_b")], writes=["pE"], tag=f"f2nmu{sg}")
            S.op("vector", lambda e, b=b: e.tensor_tensor(
                    AA[:, :, :], xsl[:, b], ctx.pE[:, None, :].broadcast_to([128, 8, 512]),
                    ALU.add),
                 reads=[("xsl", b), "pE"], writes=[("A2",)], tag=f"f2a{sg}")
            S.op("vector", lambda e, sg=sg, sl=sl: e.tensor_tensor(
                    xln[:, :, sl], AA[:, :, :],
                    srep_[:, sg, None, :].broadcast_to([128, 8, 512]), ALU.mult),
                 reads=[("A2",), ("srep", sg)], writes=[("xln", sg), ("xslf", b)], tag=f"f2xl{sg}")
        for sg in range(8):
            b = sg % 2
            sl = slice(sg * 512, (sg + 1) * 512)
            S.dma("sync", lambda e, sl=sl, b=b: e.dma_start(out=xsl[:, b], in_=xr[:, :, sl]),
                  reads=[("xslf", b)], writes=[("xsl", b)], tag=f"f2lx2{sg}")
            rd = [("xln", sg)] + ([("xln", sg - 1)] if sg > 0 else [])
            for mi, muc in enumerate([mur, muk]):
                c0 = sg * 512
                mk = "mur" if mi == 0 else "muk"
                if sg > 0:
                    S.op("vector", lambda e, c0=c0: e.tensor_sub(
                            AA[:, :, :], xln[:, :, c0:c0 + 512], xln[:, :, c0 - 1:c0 + 511]),
                         reads=rd, writes=[("A2",)], tag=f"f2sub{mi}_{sg}")
                    for ci in range(8):
                        S.op("vector", lambda e, ci=ci, c0=c0, muc=muc: e.scalar_tensor_tensor(
                                xmix[:, ci, :], AA[:, ci, :], muc[:, ci:ci + 1],
                                xln[:, ci, c0 - 1:c0 + 511], ALU.mult, ALU.add),
                             reads=[("A2",), mk] + rd, writes=[("xmix",)], tag=f"f2mx{mi}_{sg}_{ci}")
                else:
                    S.op("vector", lambda e: e.tensor_sub(
                            AA[:, :, 1:512], xln[:, :, 1:512], xln[:, :, 0:511]),
                         reads=rd, writes=[("A2",)], tag=f"f2sub{mi}_{sg}")
                    for ci in range(8):
                        S.op("vector", lambda e, ci=ci, muc=muc: e.scalar_tensor_tensor(
                                xmix[:, ci, 1:512], AA[:, ci, 1:512], muc[:, ci:ci + 1],
                                xln[:, ci, 0:511], ALU.mult, ALU.add),
                             reads=[("A2",), mk] + rd, writes=[("xmix",)], tag=f"f2mx{mi}_{sg}_{ci}")
                        S.op("vector", lambda e, ci=ci, muc=muc: e.tensor_scalar(
                                xmix[:, ci, 0:1], xln[:, ci, 0:1], muc[:, ci:ci + 1], None, ALU.mult),
                             reads=[mk] + rd, writes=[("xmix",)], narrow=True,
                             tag=f"f2mx0{mi}_{sg}_{ci}")
                wsb, wkk = (wr, ("wr",)) if mi == 0 else (wk, ("wk",))
                for oc in range(8):
                    ps, pk = (ctx.pA, "pmA") if oc % 2 == 0 else (ctx.pB, "pmB")
                    def mmr(e, oc=oc, wsb=wsb, ps=ps):
                        rr = None
                        for ci in range(8):
                            rr = e.matmul(ps[:, :], wsb[:, ci, oc * 128:(oc + 1) * 128],
                                          xmix[:, ci, :], start=(ci == 0), stop=(ci == 7))
                        return rr
                    S.op("tensor", mmr, reads=[("xmix",), wkk], writes=[pk],
                         tag=f"f2mm{mi}_{oc}_{sg}")
                    if mi == 0:
                        S.op("scalar", lambda e, oc=oc, ps=ps: e.activation(
                                rsb[:, oc, :], ps[:, :], AF.Sigmoid),
                             reads=[pk], writes=[("rsb",)], tag=f"f2sig{oc}_{sg}")
                    else:
                        S.op("vector", lambda e, oc=oc, ps=ps: e.tensor_scalar(
                                AA[:, oc, :], ps[:, :], 0.0, None, ALU.max),
                             reads=[pk], writes=[("A2",)], tag=f"f2rel{oc}_{sg}")
                        S.op("scalar", lambda e, oc=oc: e.activation(
                                kact[:, oc, :], AA[:, oc, :], AF.Square),
                             reads=[("A2",)], writes=[("kact",)], tag=f"f2sq{oc}_{sg}")
            for oc in range(8):
                ps, pk = (ctx.pC, "pmC") if oc % 2 == 0 else (ctx.pD, "pmD")
                def mmv(e, oc=oc, ps=ps):
                    rr = None
                    for ci in range(8):
                        rr = e.matmul(ps[:, :], wv[:, ci, oc * 128:(oc + 1) * 128],
                                      kact[:, ci, :], start=(ci == 0), stop=(ci == 7))
                    return rr
                S.op("tensor", mmv, reads=[("kact",), ("wv",)], writes=[pk], tag=f"f2mv{oc}_{sg}")
                S.op("vector", lambda e, oc=oc, ps=ps: e.tensor_tensor(
                        AA[:, oc, :], ps[:, :], rsb[:, oc, :], ALU.mult),
                     reads=[pk, ("rsb",)], writes=[("A2",)], tag=f"f2om{oc}_{sg}")
                S.op("vector", lambda e, oc=oc, b=b: e.tensor_add(
                        xsl[:, b, oc, :], xsl[:, b, oc, :], AA[:, oc, :]),
                     reads=[("A2",), ("xsl", b)], writes=[("xsl", b)], tag=f"f2oe{oc}_{sg}")
            S.dma("sync", lambda e, sg=sg, b=b: e.dma_start(
                    out=xdr[:, :, sg * 512:(sg + 1) * 512], in_=xsl[:, b]),
                  reads=[("xsl", b)], writes=[("xout", sg), ("xslf", b)], tag=f"f2st{sg}")
        S.fence()

def emit_l3attn(ctx, xsrc, xdst):
    import contextlib
    nc, S = ctx.nc, ctx.S
    xr = xsrc.rearrange("(c p) t -> p c t", p=128)
    xdr = xdst.rearrange("(c p) t -> p c t", p=128)
    QT = 1024
    with contextlib.ExitStack() as est:
        es = est.enter_context
        emit_stats(ctx, xsrc, es, tag="f3")
        rowsNB_, srep_ = ctx.rowsNB, ctx.srep
        rstdcol_ = getattr(ctx, 'rstd_col', None)
        xbq = es(nc.sbuf_tensor("f3_xb", [128, 8, QT], BF16))
        xin_ = es(nc.sbuf_tensor("f3_xi", [128, 16, QT], BF16))
        zb = es(nc.sbuf_tensor("f3_z", [128, 16, QT], BF16))
        Gt = es(nc.sbuf_tensor("f3_Gt", [128, QT], BF16))
        ht = es(nc.sbuf_tensor("f3_ht", [128, QT], BF16))
        wtile = es(nc.sbuf_tensor("f3_wt", [128, 2, 8, 128], BF16))
        wot = es(nc.sbuf_tensor("f3_wot", [128, 2, 16, 128], BF16))
        cs3 = es(nc.sbuf_tensor("f3_cs", [1, 4096], BF16))
        convw = es(nc.sbuf_tensor("f3_cw", [128, 16, 4], F32))
        convb = es(nc.sbuf_tensor("f3_cb", [128, 16], F32))
        wxc = es(nc.sbuf_tensor("f3_wxc", [128, 16, 16], BF16))
        Bt = es(nc.sbuf_tensor("f3_Bt", [16, 2048], BF16))
        import contextlib as _cl
        with _cl.ExitStack() as est2:
            Btf = est2.enter_context(nc.sbuf_tensor("f3_Btf", [16, 2048], F32))
            S.dma("sync", lambda e: e.dma_start(out=Btf[:, :], in_=wap2(ctx, "l3_blogT")),
                  writes=["Btf"], tag="f3lbt")
            S.op("scalar", lambda e: e.activation(Bt[:, :], Btf[:, :], AF.Exp),
                 reads=["Btf"], writes=["Bt"], tag="f3expB")
            S.fence()
        sC = es(nc.sbuf_tensor("f3_sC", [16, QT], BF16))
        halo = es(nc.sbuf_tensor("f3_halo", [128, 16, 3], BF16))
        hcar = es(nc.sbuf_tensor("f3_hc", [128, 16, 1], F32))
        dec = es(nc.sbuf_tensor("f3_dec", [128, QT], F32))
        CV = es(nc.sbuf_tensor("f3_CV", [128, QT], F32))
        lng = es(nc.sbuf_tensor("f3_lng", [128, 8], F32))
        lnb = es(nc.sbuf_tensor("f3_lnb", [128, 8], F32))
        xsl = es(nc.sbuf_tensor("f3_xsl", [128, 8, 512], F32))
        XLN = es(nc.sbuf_tensor("f3_xln", [128, 8, 512], F32))
        Rb = es(nc.sbuf_tensor("f3_Rb", [128, 8, 512], BF16))
        R2 = es(nc.sbuf_tensor("f3_R2", [128, 8, 512], BF16))
        st_r = [es(nc.sbuf_tensor(f"f3_sr{i}", [1, 512], F32)) for i in range(5)]
        murow, musq, varr, sigr, rst2 = st_r

        S.dma("sync", lambda e: e.dma_start(out=convw[:, :, :], in_=wap2(ctx, "l3_convw")),
              writes=["cw"], tag="f3lcw")
        S.dma("sync", lambda e: e.dma_start(out=convb[:, :], in_=wap2(ctx, "l3_convb")),
              writes=["cb"], tag="f3lcb")
        S.dma("sync", lambda e: e.dma_start(out=lng[:, :], in_=wap2(ctx, "l3_lng")),
              writes=["lng"], tag="f3llng")
        S.dma("sync", lambda e: e.dma_start(out=lnb[:, :], in_=wap2(ctx, "l3_lnb")),
              writes=["lnb"], tag="f3llnb")
        S.dma("gpsimd", lambda e: e.dma_start(out=wxc[:, :, :],
                in_=wap2(ctx, "l3_wxcT").rearrange("(c p) o -> p c o", p=128)),
              writes=["wxc"], tag="f3lwxc")
        S.op("vector", lambda e: e.memset(dec[:, :], 0.95), writes=["dec"], tag="f3dec")
        S.op("vector", lambda e: e.memset(halo[:, :, :], 0.0), writes=["halo"], tag="f3hz")
        S.op("vector", lambda e: e.memset(hcar[:, :, :], 0.0), writes=["hcar"], tag="f3hc0")

        for q in range(4):
            tq = q * QT
            S.dma("gpsimd", lambda e, tq=tq: e.dma_start(
                    out=xbq[:, :, :], in_=xr[:, :, tq:tq + QT]),
                  reads=[("xbq",)], writes=[("xbq",)], tag=f"f3lxb{q}")
            for oc in range(32):
                b = oc % 2
                ap = wap2(ctx, "l3_winT").rearrange("(c p) o -> p c o", p=128)[
                    :, :, oc * 128:(oc + 1) * 128]
                S.dma("gpsimd", lambda e, ap=ap, b=b: e.dma_start(out=wtile[:, b], in_=ap),
                      reads=[("wtf", b)], writes=[("wt3", b)], tag=f"f3lw{q}_{oc}")
                if q == 0:
                    def csf(e, b=b):
                        rr = None
                        for ci in range(8):
                            rr = e.matmul(ctx.pS[0:1, 0:128], ctx.onescol_b[:], wtile[:, b, ci],
                                          start=(ci == 0), stop=(ci == 7))
                        return rr
                    S.op("tensor", csf, reads=[("wt3", b), ("c", "onescol_b")], writes=["pS"],
                         tag=f"f3cs{oc}")
                    S.op("vector", lambda e, oc=oc: e.tensor_copy(
                            cs3[0:1, oc * 128:(oc + 1) * 128], ctx.pS[0:1, 0:128]),
                         reads=["pS"], writes=[("cs3", oc)], tag=f"f3cse{oc}")
                for s2 in range(2):
                    sg = q * 2 + s2
                    sl = slice(s2 * 512, (s2 + 1) * 512)
                    ps, pk = (ctx.pA, "pmA") if s2 == 0 else (ctx.pB, "pmB")
                    def mmz(e, b=b, oc=oc, sl=sl, sg=sg, ps=ps):
                        for ci in range(8):
                            e.matmul(ps[:, :], wtile[:, b, ci], xbq[:, ci, sl],
                                     start=(ci == 0), stop=False)
                        return e.matmul(ps[:, :], cs3[0:1, oc * 128:(oc + 1) * 128],
                                        rowsNB_[0:1, sg * 512:(sg + 1) * 512],
                                        start=False, stop=True)
                    S.op("tensor", mmz, reads=[("wt3", b), ("xbq",), ("cs3", oc), ("rowN", sg)],
                         writes=[pk], tag=f"f3mz{q}_{oc}_{s2}")
                    dstb = xin_ if oc < 16 else zb
                    dk = ("xin", oc % 16) if oc < 16 else ("zb", oc % 16)
                    S.op("vector", lambda e, oc=oc, sl=sl, sg=sg, dstb=dstb, ps=ps: e.tensor_tensor(
                            dstb[:, oc % 16, sl], ps[:, :], srep_[:, sg, :], ALU.mult),
                         reads=[pk, ("srep", sg)], writes=[dk, ("wtf", b) if s2 == 1 else ("na2",)],
                         tag=f"f3ze{q}_{oc}_{s2}")
            for cc in range(16):
                S.op("vector", lambda e, cc=cc: e.tensor_scalar(
                        CV[:, :], xin_[:, cc, :], convw[:, cc, 3:4], None, ALU.mult),
                     reads=[("xin", cc), "cw"], writes=["CV"], tag=f"f3cv3{q}_{cc}")
                for k in range(3):
                    d = 3 - k
                    S.op("vector", lambda e, cc=cc, k=k, d=d: e.scalar_tensor_tensor(
                            CV[:, d:QT], xin_[:, cc, 0:QT - d], convw[:, cc, k:k + 1],
                            CV[:, d:QT], ALU.mult, ALU.add),
                         reads=[("xin", cc), "cw", "CV"], writes=["CV"], tag=f"f3cv{q}_{cc}_{k}")
                    S.op("vector", lambda e, cc=cc, k=k, d=d: e.scalar_tensor_tensor(
                            CV[:, 0:d], halo[:, cc, 3 - d:3], convw[:, cc, k:k + 1],
                            CV[:, 0:d], ALU.mult, ALU.add),
                         reads=["halo", "cw", "CV"], writes=["CV"], narrow=True,
                         tag=f"f3cvb{q}_{cc}_{k}")
                S.op("vector", lambda e, cc=cc: e.tensor_copy(halo[:, cc, :],
                                                              xin_[:, cc, QT - 3:QT]),
                     reads=[("xin", cc)], writes=["halo"], narrow=True, tag=f"f3hs{q}_{cc}")
                S.op("scalar", lambda e, cc=cc: e.activation(
                        xin_[:, cc, :], CV[:, :], AF.Silu, bias=convb[:, cc:cc + 1]),
                     reads=["CV", "cb", "halo"], writes=[("xin", cc)], tag=f"f3sl{q}_{cc}")
            for s2 in range(2):
                sl = slice(s2 * 512, (s2 + 1) * 512)
                def mmc(e, sl=sl):
                    rr = None
                    for cc in range(16):
                        rr = e.matmul(ctx.pS[0:16, :], wxc[:, cc, :], xin_[:, cc, sl],
                                      start=(cc == 0), stop=(cc == 15))
                    return rr
                S.op("tensor", mmc, reads=[("xin", cc2) for cc2 in range(16)] + ["wxc"],
                     writes=["pS"], tag=f"f3sc{q}_{s2}")
                S.op("vector", lambda e, sl=sl: e.tensor_copy(sC[:, sl], ctx.pS[0:16, :]),
                     reads=["pS"], writes=["sC"], tag=f"f3sce{q}_{s2}")
            for cc in range(16):
                def scn(e, cc=cc, q=q):
                    ini = 0.0 if q == 0 else hcar[:, cc, :]
                    return e.tensor_tensor_scan(ht[:, :], dec[:, :], xin_[:, cc, :], ini,
                                                ALU.mult, ALU.add)
                S.op("vector", scn, reads=[("xin", cc), "dec", "hcar", "ht"],
                     writes=["ht"], tag=f"f3sn{q}_{cc}")
                S.op("vector", lambda e, cc=cc: e.tensor_copy(hcar[:, cc, :], ht[:, QT - 1:QT]),
                     reads=["ht"], writes=["hcar"], narrow=True, tag=f"f3snc{q}_{cc}")
                for s2 in range(2):
                    sl = slice(s2 * 512, (s2 + 1) * 512)
                    ps, pk = (ctx.pC, "pmC") if s2 == 0 else (ctx.pD, "pmD")
                    S.op("tensor", lambda e, cc=cc, sl=sl, ps=ps: e.matmul(
                            ps[:, :], Bt[:, cc * 128:(cc + 1) * 128], sC[:, sl],
                            start=True, stop=True),
                         reads=["Bt", "sC"], writes=[pk], tag=f"f3g{q}_{cc}_{s2}")
                    S.op("vector", lambda e, sl=sl, ps=ps: e.tensor_copy(Gt[:, sl], ps[:, :]),
                         reads=[pk], writes=["Gtk"], tag=f"f3ge{q}_{cc}_{s2}")
                S.op("scalar", lambda e, cc=cc: e.activation(zb[:, cc, :], zb[:, cc, :], AF.Silu),
                     reads=[("zb", cc)], writes=[("zb", cc)], tag=f"f3sz{q}_{cc}")
                S.op("vector", lambda e, cc=cc: e.tensor_tensor(
                        CV[:, :], ht[:, :], Gt[:, :], ALU.mult),
                     reads=["ht", "Gtk"], writes=["CV"], tag=f"f3ym{q}_{cc}")
                S.op("vector", lambda e, cc=cc: e.tensor_tensor(
                        zb[:, cc, :], CV[:, :], zb[:, cc, :], ALU.mult),
                     reads=["CV", ("zb", cc)], writes=[("zb", cc)], tag=f"f3y{q}_{cc}")
            for s2 in range(2):
                sg = q * 2 + s2
                sl = slice(s2 * 512, (s2 + 1) * 512)
                gsl = slice(sg * 512, (sg + 1) * 512)
                S.dma("sync", lambda e, gsl=gsl: e.dma_start(out=xsl[:, :, :], in_=xr[:, :, gsl]),
                      reads=[("xsl3",)], writes=[("xsl3",)], tag=f"f3lx{q}_{s2}")
                S.op("tensor", lambda e, sg=sg, gsl=gsl: e.matmul(
                        ctx.pE[:, :], ctx.onesrow_b[:, :], rowsNB_[0:1, gsl],
                        start=True, stop=True),
                     reads=[("rowN", sg), ("c", "onesrow_b")], writes=["pE"], tag=f"f3nm{q}_{s2}")
                S.op("vector", lambda e: e.tensor_tensor(
                        XLN[:, :, :], xsl[:, :, :], ctx.pE[:, None, :].broadcast_to([128, 8, 512]),
                        ALU.add),
                     reads=[("xsl3",), "pE"], writes=[("XLN",)], tag=f"f3xl{q}_{s2}")
                S.op("vector", lambda e, sg=sg: e.tensor_tensor(
                        XLN[:, :, :], XLN[:, :, :],
                        srep_[:, sg, None, :].broadcast_to([128, 8, 512]), ALU.mult),
                     reads=[("XLN",), ("srep", sg)], writes=[("XLN",)], tag=f"f3r1{q}_{s2}")
                for oc in range(8):
                    b = oc % 2
                    ap = wap2(ctx, "l3_woutT").rearrange("(c p) o -> p c o", p=128)[
                        :, :, oc * 128:(oc + 1) * 128]
                    S.dma("gpsimd", lambda e, ap=ap, b=b: e.dma_start(out=wot[:, b], in_=ap),
                          reads=[("wotf", b)], writes=[("wot", b)], tag=f"f3lwo{q}_{s2}_{oc}")
                    ps, pk = (ctx.pA, "pmA") if oc % 2 == 0 else (ctx.pB, "pmB")
                    def mmo(e, b=b, sl=sl, ps=ps):
                        rr = None
                        for cc in range(16):
                            rr = e.matmul(ps[:, :], wot[:, b, cc, :], zb[:, cc, sl],
                                          start=(cc == 0), stop=(cc == 15))
                        return rr
                    S.op("tensor", mmo, reads=[("wot", b)] + [("zb", cc2) for cc2 in range(16)],
                         writes=[pk], tag=f"f3mo{q}_{s2}_{oc}")
                    S.op("vector", lambda e, oc=oc, ps=ps: e.tensor_add(
                            XLN[:, oc, :], ps[:, :], XLN[:, oc, :]),
                         reads=[pk, ("XLN",)], writes=[("XLN",), ("wotf", b)],
                         tag=f"f3moe{q}_{s2}_{oc}")
                S.op("vector", lambda e: e.tensor_copy(Rb.rearrange("p c t -> p (c t)"),
                                                       XLN.rearrange("p c t -> p (c t)")),
                     reads=[("XLN",)], writes=[("Rb",)], tag=f"f3rb{q}_{s2}")
                def mms1(e):
                    rr = None
                    for ci in range(8):
                        rr = e.matmul(ctx.pS[0:1, :], ctx.onescol_b[:], Rb[:, ci, :],
                                      start=(ci == 0), stop=(ci == 7))
                    return rr
                S.op("tensor", mms1, reads=[("Rb",), ("c", "onescol_b")],
                     writes=["pS"], tag=f"f3ms{q}_{s2}")
                S.op("vector", lambda e: e.tensor_scalar(murow[:], ctx.pS[0:1, :], -1.0 / D,
                                                         None, ALU.mult),
                     reads=["pS"], writes=["mu3"], tag=f"f3mu{q}_{s2}")
                S.op("tensor", lambda e: e.matmul(ctx.pE[:, :], ctx.onesrow_f[:, :], murow[:],
                                                  start=True, stop=True),
                     reads=["mu3", ("c", "onesrow_f")], writes=["pE"], tag=f"f3rpm{q}_{s2}")
                S.op("vector", lambda e: e.tensor_tensor(
                        XLN[:, :, :], XLN[:, :, :],
                        ctx.pE[:, None, :].broadcast_to([128, 8, 512]), ALU.add),
                     reads=[("XLN",), "pE"], writes=[("XLN",)], tag=f"f3sub{q}_{s2}")
                S.op("scalar", lambda e: e.activation(R2.rearrange("p c t -> p (c t)"),
                                                      XLN.rearrange("p c t -> p (c t)"), AF.Square),
                     reads=[("XLN",)], writes=[("R2",)], tag=f"f3r2{q}_{s2}")
                def mms2(e):
                    rr = None
                    for ci in range(8):
                        rr = e.matmul(ctx.pF[0:1, :], ctx.onescol_b[:], R2[:, ci, :],
                                      start=(ci == 0), stop=(ci == 7))
                    return rr
                S.op("tensor", mms2, reads=[("R2",), ("c", "onescol_b")],
                     writes=["pF"], tag=f"f3ms2{q}_{s2}")
                S.op("vector", lambda e: e.tensor_scalar(varr[:], ctx.pF[0:1, :], 1.0 / D,
                                                         None, ALU.mult),
                     reads=["pF"], writes=["var3"], tag=f"f3vr{q}_{s2}")
                S.op("scalar", lambda e: e.activation(sigr[:], varr[:], AF.Sqrt,
                                                      bias=ctx.eps1[0:1, :]),
                     reads=["var3", ("c", "eps1")], writes=["sig3"], tag=f"f3sg{q}_{s2}")
                S.op("vector", lambda e: e.reciprocal(rst2[:], sigr[:]),
                     reads=["sig3"], writes=["rst3"], tag=f"f3rs{q}_{s2}")
                S.op("tensor", lambda e: e.matmul(ctx.pF[:, :], ctx.onesrow_f[:, :], rst2[:],
                                                  start=True, stop=True),
                     reads=["rst3", ("c", "onesrow_f")], writes=["pF"], tag=f"f3rps{q}_{s2}")
                S.op("vector", lambda e: e.tensor_tensor(
                        XLN[:, :, :], XLN[:, :, :],
                        ctx.pF[:, None, :].broadcast_to([128, 8, 512]), ALU.mult),
                     reads=[("XLN",), "pF"], writes=[("XLN",)], tag=f"f3finb{q}_{s2}")
                for ci in range(8):
                    S.op("vector", lambda e, ci=ci: e.tensor_scalar(
                            XLN[:, ci, :], XLN[:, ci, :], lng[:, ci:ci + 1],
                            lnb[:, ci:ci + 1], ALU.mult, ALU.add),
                         reads=[("XLN",), "lng", "lnb"], writes=[("XLN",)],
                         tag=f"f3ga{q}_{s2}_{ci}")
                    S.op("vector", lambda e, ci=ci: e.tensor_add(
                            xsl[:, ci, :], xsl[:, ci, :], XLN[:, ci, :]),
                         reads=[("XLN",), ("xsl3",)], writes=[("xsl3",)],
                         tag=f"f3gb{q}_{s2}_{ci}")
                S.dma("sync", lambda e, gsl=gsl: e.dma_start(out=xdr[:, :, gsl], in_=xsl[:, :, :]),
                      reads=[("xsl3",)], writes=[("xout", sg), ("xsl3",)], tag=f"f3st{q}_{s2}")
        S.fence()

def emit_l3ffn(ctx, xsrc, xdst):
    import contextlib
    nc, S = ctx.nc, ctx.S
    xr = xsrc.rearrange("(c p) t -> p c t", p=128)
    xdr = xdst.rearrange("(c p) t -> p c t", p=128)
    with contextlib.ExitStack() as est:
        es = est.enter_context
        emit_stats(ctx, xsrc, es, tag="f4")
        rowsNB_, srep_ = ctx.rowsNB, ctx.srep
        rstdcol_ = getattr(ctx, 'rstd_col', None)
        xb = es(nc.sbuf_tensor("f4_xb", [128, 8, 512], BF16))
        wbufs = [es(nc.sbuf_tensor(f"f4_w{i}", [128, 8, 1024], BF16)) for i in range(2)]
        wcbq = es(nc.sbuf_tensor("f4_wcb", [2, 2, 1024], BF16))
        vf = es(nc.sbuf_tensor("f4_vf", [128, 8, 512], BF16))
        wf = es(nc.sbuf_tensor("f4_wf", [128, 8, 512], BF16))
        vi = es(nc.sbuf_tensor("f4_vi", [128, 8, 512], BF16))
        wi = es(nc.sbuf_tensor("f4_wi", [128, 8, 512], BF16))
        uf = es(nc.sbuf_tensor("f4_uf", [128, 4, 1024], BF16))
        ui = es(nc.sbuf_tensor("f4_ui", [128, 4, 1024], BF16))
        bf_ = es(nc.sbuf_tensor("f4_bf", [8, 1024], BF16))
        bi_ = es(nc.sbuf_tensor("f4_bi", [8, 1024], BF16))
        rw2 = es(nc.sbuf_tensor("f4_rw2", [128, 8, 8], BF16))
        rb1 = es(nc.sbuf_tensor("f4_rb1", [128, 8], F32))
        b2e = es(nc.sbuf_tensor("f4_b2e", [1, 8], BF16))
        alph = es(nc.sbuf_tensor("f4_al", [1, 1], F32))
        alphc = es(nc.sbuf_tensor("f4_alc", [128, 1], F32))
        Qs = es(nc.sbuf_tensor("f4_Q", [128, 8, 512], BF16))
        Ks = es(nc.sbuf_tensor("f4_K", [128, 8, 512], BF16))
        iQs = es(nc.sbuf_tensor("f4_iQ", [128, 8, 512], BF16))
        iKs = es(nc.sbuf_tensor("f4_iK", [128, 8, 512], BF16))
        xV = es(nc.sbuf_tensor("f4_xV", [128, 4, 512], BF16))
        cum = es(nc.sbuf_tensor("f4_cum", [128, 4, 512], BF16))
        prodw = es(nc.sbuf_tensor("f4_pw", [128, 4, 512], BF16))
        wrep = es(nc.sbuf_tensor("f4_wrep", [128, 4, 512], BF16))
        hB = es(nc.sbuf_tensor("f4_hB", [128, 8, 512], BF16))
        wTM = es(nc.sbuf_tensor("f4_wTM", [128, 4, 8], BF16))
        eW = es(nc.sbuf_tensor("f4_eW", [128, 4, 8], F32))
        sumr = es(nc.sbuf_tensor("f4_sum", [128, 1], F32))
        wTr = es(nc.sbuf_tensor("f4_wTr", [8, 512], BF16))
        comb = es(nc.sbuf_tensor("f4_comb", [128, 8, 512], BF16))
        ccar = es(nc.sbuf_tensor("f4_cc", [128, 2, 4, 1], F32))
        ones5 = es(nc.sbuf_tensor("f4_ones", [128, 512], F32))
        nvis = es(nc.sbuf_tensor("f4_nvi", [1, 512], F32))
        nvir = es(nc.sbuf_tensor("f4_nvir", [128, 512], F32))
        AA = es(nc.sbuf_tensor("f4_A", [128, 512], F32))
        xsl = es(nc.sbuf_tensor("f4_xsl", [128, 8, 512], F32))

        for key, dst, kk in [("l3_vfwd", vf, "vf"), ("l3_wfwd", wf, "wfw"),
                             ("l3_vinv", vi, "vi"), ("l3_winvexp", wi, "wiv")]:
            S.dma("gpsimd", lambda e, key=key, dst=dst: e.dma_start(out=dst[:, :, :],
                    in_=wap2(ctx, key).rearrange("(c p) o -> p c o", p=128)),
                  writes=[(kk,)], tag=f"f4l{kk}")
        for key, dst, kk in [("l3_ufwd", uf, "uf"), ("l3_uinv", ui, "ui")]:
            S.dma("gpsimd", lambda e, key=key, dst=dst: e.dma_start(out=dst[:, :, :],
                    in_=wap2(ctx, key).rearrange("(c p) o -> p c o", p=128)),
                  writes=[(kk,)], tag=f"f4l{kk}")
        S.dma("gpsimd", lambda e: e.dma_start(out=bf_[:, :], in_=wap2(ctx, "l3_bfwd")),
              writes=["bf"], tag="f4lbf")
        S.dma("gpsimd", lambda e: e.dma_start(out=bi_[:, :], in_=wap2(ctx, "l3_binv")),
              writes=["bi"], tag="f4lbi")
        S.dma("gpsimd", lambda e: e.dma_start(out=rw2[:, :, :],
                in_=wap2(ctx, "l3_rw2T").rearrange("(c p) o -> p c o", p=128)),
              writes=["rw2"], tag="f4lrw2")
        S.dma("sync", lambda e: e.dma_start(out=rb1[:, :], in_=wap2(ctx, "l3_rb1")),
              writes=["rb1"], tag="f4lrb1")
        S.dma("gpsimd", lambda e: e.dma_start(out=b2e[:, :],
                in_=wap2(ctx, "l3_rb2e").rearrange("(r c) -> r c", r=1)),
              writes=["b2e"], tag="f4lb2e")
        S.dma("sync", lambda e: e.dma_start(out=alph[:, :],
                in_=wap2(ctx, "l3_alpha").rearrange("(r c) -> r c", r=1)),
              writes=["alph"], tag="f4lal")
        S.op("tensor", lambda e: e.matmul(ctx.pF[:, 0:1], ctx.onesrow_f[:, :], alph[:, :],
                                          start=True, stop=True),
             reads=["alph", ("c", "onesrow_f")], writes=["pF"], tag="f4alm")
        S.op("vector", lambda e: e.tensor_copy(alphc[:, :], ctx.pF[:, 0:1]),
             reads=["pF"], writes=["alphc"], narrow=True, tag="f4ale")
        S.op("vector", lambda e: e.memset(ones5[:, :], 1.0), writes=["ones5"], tag="f4o5")
        S.op("vector", lambda e: e.memset(wcbq[0:2, :, :], 0.0), writes=["wcbz"], tag="f4wz")

        def projq(dst, dk, wbuf, wk_, cso, sg, fold, srckey, src):
            for oc in range(8):
                ps, pk = (ctx.pA, "pmA") if oc % 2 == 0 else (ctx.pB, "pmB")
                def mm(e, oc=oc, sg=sg, ps=ps):
                    rr = None
                    for ci in range(8):
                        rr = e.matmul(ps[:, :], wbuf[:, ci, oc * 128:(oc + 1) * 128],
                                      src[:, ci, :], start=(ci == 0), stop=(ci == 7) and not fold)
                    if fold:
                        rr = e.matmul(ps[:, :], wcbq[:, cso, oc * 128:(oc + 1) * 128],
                                      rowsNB_[:, sg * 512:(sg + 1) * 512], start=False, stop=True)
                    return rr
                rds = [srckey, wk_] + ([("wcbq", cso), "wcbz", ("rowN", sg), ("rowS", sg)] if fold else [])
                S.op("tensor", mm, reads=rds, writes=[pk], tag=f"f4p{dk}_{oc}_{sg}")
                if fold:
                    S.op("vector", lambda e, oc=oc, sg=sg, ps=ps: e.tensor_tensor(
                            dst[:, oc, :], ps[:, :], srep_[:, sg, :], ALU.mult),
                         reads=[pk, ("srep", sg)], writes=[(dk,)], tag=f"f4pe{dk}_{oc}_{sg}")
                else:
                    S.op("vector", lambda e, oc=oc, ps=ps: e.tensor_copy(dst[:, oc, :], ps[:, :]),
                         reads=[pk], writes=[(dk,)], tag=f"f4pe{dk}_{oc}_{sg}")

        def branch(br, QS, qk, KS, kk_, Vt, vk, Wt, wk_, Ut, uk, bt, bk_, sg):
            for src, srck, dstb, dk, mat, mk in [(QS, qk, xV, "xV", Vt, vk),
                                                 (KS, kk_, cum, "cum", Wt, wk_)]:
                for krc in range(4):
                    ps, pk = (ctx.pA, "pmA") if krc % 2 == 0 else (ctx.pB, "pmB")
                    def mm(e, krc=krc, src=src, mat=mat, ps=ps):
                        rr = None
                        for ci in range(8):
                            rr = e.matmul(ps[:, :], mat[:, ci, krc * 128:(krc + 1) * 128],
                                          src[:, ci, :], start=(ci == 0), stop=(ci == 7))
                        return rr
                    S.op("tensor", mm, reads=[srck, mk], writes=[pk], tag=f"f4x{br}{dk}_{krc}_{sg}")
                    S.op("vector", lambda e, krc=krc, dstb=dstb, ps=ps: e.tensor_copy(
                            dstb[:, krc, :], ps[:, :]),
                         reads=[pk], writes=[(dk, krc)], tag=f"f4xe{br}{dk}_{krc}_{sg}")
            for krc in range(4):
                def scn(e, krc=krc, sg=sg, br=br):
                    ini = 0.0 if sg == 0 else ccar[:, br, krc, :]
                    return e.tensor_tensor_scan(cum[:, krc, :], ones5[:, :], cum[:, krc, :], ini,
                                                ALU.mult, ALU.add)
                S.op("vector", scn, reads=[("cum", krc), "ones5", ("ccar", br, krc)],
                     writes=[("cum", krc)], tag=f"f4cs{br}_{krc}_{sg}")
                S.op("vector", lambda e, krc=krc, br=br: e.tensor_copy(
                        ccar[:, br, krc, :], cum[:, krc, 511:512]),
                     reads=[("cum", krc)], writes=[("ccar", br, krc)], narrow=True,
                     tag=f"f4csc{br}_{krc}_{sg}")
            for oc in range(8):
                ps, pk = (ctx.pC, "pmC") if oc % 2 == 0 else (ctx.pD, "pmD")
                def mmh(e, oc=oc, ps=ps):
                    rr = None
                    for ci in range(8):
                        rr = e.matmul(ps[:, :], wbufs[1][:, ci, oc * 128:(oc + 1) * 128],
                                      QS[:, ci, :], start=(ci == 0), stop=(ci == 7))
                    return rr
                S.op("tensor", mmh, reads=[qk, ("wl", "rw1")], writes=[pk],
                     tag=f"f4h{br}_{oc}_{sg}")
                S.op("scalar", lambda e, oc=oc, ps=ps: e.activation(
                        hB[:, oc, :], ps[:, :], AF.Gelu, bias=rb1[:, oc:oc + 1]),
                     reads=[pk, "rb1"], writes=[("hB",)], tag=f"f4he{br}_{oc}_{sg}")
            for t4 in range(4):
                def mml(e, t4=t4):
                    for ci in range(8):
                        e.matmul(ctx.pE[0:128, 0:8], hB[:, ci, t4 * 128:(t4 + 1) * 128],
                                 rw2[:, ci, :], start=(ci == 0), stop=False)
                    return e.matmul(ctx.pE[0:128, 0:8], ctx.onesrow_b[0:1, :], b2e[:, :],
                                    start=False, stop=True)
                S.op("tensor", mml, reads=[("hB",), "rw2", "b2e", ("c", "onesrow_b")],
                     writes=["pE"], tag=f"f4l{br}_{t4}_{sg}")
                S.op("scalar", lambda e, t4=t4: e.activation(eW[:, t4, :], ctx.pE[:, 0:8], AF.Exp),
                     reads=["pE"], writes=[("eW", t4)], narrow=True, tag=f"f4ex{br}_{t4}_{sg}")
                S.op("vector", lambda e, t4=t4: e.reduce_sum(sumr[:], eW[:, t4, :], AX.X),
                     reads=[("eW", t4)], writes=["sumr"], narrow=True, tag=f"f4sma{br}_{t4}_{sg}")
                S.op("vector", lambda e: e.reciprocal(sumr[:], sumr[:]),
                     reads=["sumr"], writes=["sumr"], narrow=True, tag=f"f4smb{br}_{t4}_{sg}")
                S.op("vector", lambda e, t4=t4: e.tensor_scalar(
                        wTM[:, t4, :], eW[:, t4, :], sumr[:], None, ALU.mult),
                     reads=[("eW", t4), "sumr"], writes=[("wTM", t4)], narrow=True,
                     tag=f"f4sm{br}_{t4}_{sg}")
            def tpw(e):
                rr = None
                for t4 in range(4):
                    rr = e.transpose(ctx.pTb[0:8, t4 * 128:(t4 + 1) * 128], wTM[:, t4, :],
                                     ctx.identb[:, :])
                return rr
            S.op("tensor", tpw, reads=[("wTM", 0), ("wTM", 1), ("wTM", 2), ("wTM", 3),
                                       ("c", "identb")], writes=["pTb"], tag=f"f4tw{br}_{sg}")
            S.op("vector", lambda e: e.tensor_copy(wTr[:, :], ctx.pTb[0:8, 0:512]),
                 reads=["pTb"], writes=["wTr"], tag=f"f4twe{br}_{sg}")
            for krc in range(4):
                S.op("tensor", lambda e, krc=krc: e.matmul(
                        ctx.pE[:, :], ctx.selk[:, krc * 128:(krc + 1) * 128], wTr[:, :],
                        start=True, stop=True),
                     reads=["wTr", ("c", "selk")], writes=["pE"], tag=f"f4wr{br}_{krc}_{sg}")
                S.op("vector", lambda e, krc=krc: e.tensor_copy(wrep[:, krc, :], ctx.pE[:, :]),
                     reads=["pE"], writes=[("wrep", krc)], tag=f"f4wre{br}_{krc}_{sg}")
                S.op("vector", lambda e, krc=krc: e.tensor_tensor(
                        AA[:, :], xV[:, krc, :], cum[:, krc, :], ALU.mult),
                     reads=[("xV", krc), ("cum", krc)], writes=["AAk"], tag=f"f4pwa{br}_{krc}_{sg}")
                S.op("vector", lambda e, krc=krc: e.tensor_tensor(
                        prodw[:, krc, :], AA[:, :], wrep[:, krc, :], ALU.mult),
                     reads=["AAk", ("wrep", krc)], writes=[("prodw", krc)],
                     tag=f"f4pw{br}_{krc}_{sg}")
            for oc in range(8):
                ps, pk = (ctx.pC, "pmC") if oc % 2 == 0 else (ctx.pD, "pmD")
                def mmo(e, oc=oc, Ut=Ut, bt=bt, ps=ps):
                    for krc in range(4):
                        e.matmul(ps[:, :], Ut[:, krc, oc * 128:(oc + 1) * 128],
                                 prodw[:, krc, :], start=(krc == 0), stop=False)
                    return e.matmul(ps[:, :], bt[:, oc * 128:(oc + 1) * 128], wTr[:, :],
                                    start=False, stop=True)
                S.op("tensor", mmo, reads=[("prodw", 0), ("prodw", 1), ("prodw", 2), ("prodw", 3),
                                           uk, bk_, "wTr"], writes=[pk], tag=f"f4o{br}_{oc}_{sg}")
                if br == 0:
                    S.op("vector", lambda e, oc=oc, ps=ps: e.tensor_tensor(
                            comb[:, oc, :], ps[:, :], nvir[:, :], ALU.mult),
                         reads=[pk, "nvir"], writes=[("comb",)], tag=f"f4oe{br}_{oc}_{sg}")
                else:
                    S.op("vector", lambda e, ps=ps: e.tensor_tensor(
                            AA[:, :], ps[:, :], nvir[:, :], ALU.mult),
                         reads=[pk, "nvir"], writes=["AAk"], tag=f"f4oa{br}_{oc}_{sg}")
                    S.op("vector", lambda e, oc=oc: e.scalar_tensor_tensor(
                            comb[:, oc, :], AA[:, :], alphc[:, :], comb[:, oc, :],
                            ALU.mult, ALU.add),
                         reads=["AAk", "alphc", ("comb",)], writes=[("comb",)],
                         tag=f"f4oe{br}_{oc}_{sg}")

        o_nvi = CF_OFFS["nvi"][0]
        for sg in range(8):
            sl = slice(sg * 512, (sg + 1) * 512)
            S.dma("gpsimd", lambda e, sl=sl: e.dma_start(out=xb[:, :, :], in_=xr[:, :, sl]),
                  reads=[("xb4",)], writes=[("xb4",)], tag=f"f4lxb{sg}")
            S.dma("sync", lambda e, sg=sg: e.dma_start(out=nvis[:, :],
                    in_=ctx.cfin[o_nvi + sg * 512:o_nvi + (sg + 1) * 512].rearrange(
                        "(r c) -> r c", r=1)),
                  reads=[("nvis",)], writes=[("nvis",)], tag=f"f4lnv{sg}")
            S.op("tensor", lambda e: e.matmul(ctx.pF[:, :], ctx.onesrow_f[:, :], nvis[:, :],
                                              start=True, stop=True),
                 reads=[("nvis",), ("c", "onesrow_f")], writes=["pF"], tag=f"f4nvm{sg}")
            S.op("vector", lambda e: e.tensor_copy(nvir[:, :], ctx.pF[:, :]),
                 reads=["pF"], writes=["nvir"], tag=f"f4nve{sg}")
            for wi_, (wkey, dst, dk, cso) in enumerate([("l3_wQT", Qs, "Qs", 0),
                                                        ("l3_wKT", Ks, "Ks", 1)]):
                wb = wbufs[0]
                S.dma("gpsimd", lambda e, wkey=wkey, wb=wb: e.dma_start(out=wb[:, :, :],
                        in_=wap2(ctx, wkey).rearrange("(c p) o -> p c o", p=128)),
                      reads=[("wl", "w0")], writes=[("wl", "w0")], tag=f"f4lw{wi_}_{sg}")
                if sg == 0:
                    def csf(e, wb=wb):
                        rr = None
                        for ci in range(8):
                            rr = e.matmul(ctx.pS[0:1, :], ctx.onescol_b[:],
                                          wb[:, ci, 0:512], start=(ci == 0), stop=False) if False                                 else e.matmul(ctx.pS[0:1, 0:1024][0:1, 0:512] if False else ctx.pS[0:1, 0:512],
                                              ctx.onescol_b[:], wb[:, ci, 0:512],
                                              start=(ci == 0), stop=(ci == 7))
                        return rr
                    def csf2(e, wb=wb):
                        rr = None
                        for ci in range(8):
                            rr = e.matmul(ctx.pF[0:1, 0:512], ctx.onescol_b[:], wb[:, ci, 512:1024],
                                          start=(ci == 0), stop=(ci == 7))
                        return rr
                    S.op("tensor", csf, reads=[("wl", "w0"), ("c", "onescol_b")], writes=["pS"],
                         tag=f"f4cs{wi_}a")
                    S.op("tensor", csf2, reads=[("wl", "w0"), ("c", "onescol_b")], writes=["pF"],
                         tag=f"f4cs{wi_}b")
                    S.op("vector", lambda e, cso=cso: e.tensor_copy(
                            wcbq[0:1, cso, 0:512], ctx.pS[0:1, 0:512]),
                         reads=["pS", "wcbz"], writes=[("wcbq", cso)], tag=f"f4cse{wi_}a")
                    S.op("vector", lambda e, cso=cso: e.tensor_copy(
                            wcbq[0:1, cso, 512:1024], ctx.pF[0:1, 0:512]),
                         reads=["pF", "wcbz"], writes=[("wcbq", cso)], tag=f"f4cse{wi_}b")
                projq(dst, dk, wb, ("wl", "w0"), cso, sg, True, ("xb4",), xb)
            S.dma("gpsimd", lambda e: e.dma_start(out=wbufs[0][:, :, :],
                    in_=wap2(ctx, "l3_winvT").rearrange("(c p) o -> p c o", p=128)),
                  reads=[("wl", "w0")], writes=[("wl", "w0")], tag=f"f4lwi_{sg}")
            projq(iQs, "iQs", wbufs[0], ("wl", "w0"), 0, sg, False, ("Qs",), Qs)
            projq(iKs, "iKs", wbufs[0], ("wl", "w0"), 0, sg, False, ("Ks",), Ks)
            S.dma("gpsimd", lambda e: e.dma_start(out=wbufs[1][:, :, :],
                    in_=wap2(ctx, "l3_rw1T").rearrange("(c p) o -> p c o", p=128)),
                  reads=[("wl", "rw1")], writes=[("wl", "rw1")], tag=f"f4lrw1_{sg}")
            branch(0, Qs, ("Qs",), Ks, ("Ks",), vf, ("vf",), wf, ("wfw",), uf, ("uf",),
                   bf_, "bf", sg)
            branch(1, iQs, ("iQs",), iKs, ("iKs",), vi, ("vi",), wi, ("wiv",), ui, ("ui",),
                   bi_, "bi", sg)
            S.dma("gpsimd", lambda e: e.dma_start(out=wbufs[0][:, :, :],
                    in_=wap2(ctx, "l3_wOT").rearrange("(c p) o -> p c o", p=128)),
                  reads=[("wl", "w0")], writes=[("wl", "w0")], tag=f"f4lwo_{sg}")
            S.dma("sync", lambda e, sl=sl: e.dma_start(out=xsl[:, :, :], in_=xr[:, :, sl]),
                  reads=[("xsl4",)], writes=[("xsl4",)], tag=f"f4lx{sg}")
            for oc in range(8):
                ps, pk = (ctx.pA, "pmA") if oc % 2 == 0 else (ctx.pB, "pmB")
                def mmo(e, oc=oc, ps=ps):
                    rr = None
                    for ci in range(8):
                        rr = e.matmul(ps[:, :], wbufs[0][:, ci, oc * 128:(oc + 1) * 128],
                                      comb[:, ci, :], start=(ci == 0), stop=(ci == 7))
                    return rr
                S.op("tensor", mmo, reads=[("comb",), ("wl", "w0")], writes=[pk],
                     tag=f"f4wo{oc}_{sg}")
                S.op("vector", lambda e, oc=oc, ps=ps: e.tensor_add(
                        xsl[:, oc, :], ps[:, :], xsl[:, oc, :]),
                     reads=[pk, ("xsl4",)], writes=[("xsl4",)], tag=f"f4woe{oc}_{sg}")
            S.dma("sync", lambda e, sl=sl: e.dma_start(out=xdr[:, :, sl], in_=xsl[:, :, :]),
                  reads=[("xsl4",)], writes=[("xout", sg), ("xsl4",)], tag=f"f4st{sg}")
        S.fence()

def _emit_devcopy(ctx, src):
    S = ctx.S
    for sg in range(8):
        S.dma("sync", lambda e, sg=sg: e.dma_start(
                out=ctx.xout[:, sg * 512:(sg + 1) * 512], in_=src[:, sg * 512:(sg + 1) * 512]),
              reads=[("xout", sg)], writes=[("xoutf", sg)], tag=f"dev{sg}")

def build_program(upto=8, only=None):
    nc = bass.Bass("TRN2", target_bir_lowering=False)
    ctx = _mk_ctx(nc)
    S = ctx.S
    _emit_preamble(ctx)
    S.fence()
    plan = [
        (emit_attn, 0), (emit_l0ffn, None), (emit_attn, 1), (emit_l1ffn, None),
        (emit_attn, 2), (emit_l2ffn, None), (emit_l3attn, None), (emit_l3ffn, None),
    ]
    seq = list(range(upto)) if only is None else list(only)
    src = ctx.xin
    nxt = [ctx.xB, ctx.xA]
    full = (len(seq) == 8)
    for j, i in enumerate(seq):
        fn, L = plan[i]
        dst = ctx.xout if (full and j == len(seq) - 1) else nxt[j % 2]
        if L is None:
            fn(ctx, src, dst)
        else:
            fn(ctx, L, src, dst)
        src = dst
    if not full:
        _emit_devcopy(ctx, src)
    S.op("sync", lambda e: e.nop(nofuse=True),
         reads=[("xout" if full else "xoutf", sg) for sg in range(8)], tag="final")
    S.emit()
    return nc

def kernel(**inputs):
    blob = _prep_blob(inputs)
    x = np.asarray(inputs["L0_attn_hidden_states"], np.float32)
    nc = build_program(upto=8)
    in_maps = []
    for core in range(NCORES):
        in_maps.append({
            "xin": np.ascontiguousarray(x[core].T),
            "wblob": blob,
            "cbin": CB_BLOB,
            "cfin": CF_BLOB,
        })
    res = run_bass_kernel_spmd(nc, in_maps, core_ids=list(range(NCORES)))
    out = np.stack([np.asarray(r["xout"], np.float32).T for r in res.results])
    return np.ascontiguousarray(out)
